# version 19
# baseline (speedup 1.0000x reference)
import numpy as np
import ml_dtypes
import concourse.bass as bass
import concourse.mybir as mybir
from concourse.bass_utils import run_bass_kernel_spmd
from contextlib import ExitStack

F32 = mybir.dt.float32
BF16 = mybir.dt.bfloat16
AF = mybir.ActivationFunctionType
ALU = mybir.AluOpType
AX = mybir.AxisListType

D = 1024
S = 4096
C = 8
T1 = 512
NT1 = S // T1
T4 = 256
NT4 = S // T4
HS = S // 2
T5 = 256
NT5 = HS // T5
T5B = 512
NT5B = HS // T5B
NSC = HS // 128
NE = 32
GE = 2
NG = NE // GE
LN_EPS = 1e-5
WKV_EPS = 1e-8

ENGINES = ("sync", "vector", "scalar", "tensor", "gpsimd")


class P:
    def __init__(self, nc, es, selfsync=False):
        self.nc = nc
        self.es = es
        self.sems = {}
        self.cnt = {}
        self.q = {e: [] for e in ENGINES}
        self.lastw = {}
        self.selfsync = selfsync
        for s_name in ("mm", "vv", "ss", "pp"):
            self.chan(s_name)
        self.lastmark = {}
        if selfsync:
            for e in ENGINES:
                self.chan(f"ord_{e}")

    def chan(self, name):
        if name in self.sems:
            return name
        self.sems[name] = self.es.enter_context(self.nc.semaphore(name))
        self.cnt[name] = 0
        for e in ENGINES:
            self.lastw[(e, name)] = 0
        return name

    def wait(self, eng, sem, val):
        if not val:
            return
        assert val <= self.cnt[sem], f"wait on future: {eng} {sem} {val} > {self.cnt[sem]}"
        if self.lastw[(eng, sem)] >= val:
            return
        self.lastw[(eng, sem)] = val
        h = self.sems[sem]
        self.q[eng].append(lambda E, h=h, v=val: E.wait_ge(h, v))

    def op(self, eng, make, inc=None):
        if self.selfsync:
            pm = self.lastmark.get(eng)
            if pm is not None:
                self.q[eng].append(lambda E, h=self.sems[pm[0]], v=pm[1]: E.wait_ge(h, v))
        if inc is None:
            if self.selfsync:
                ordname = f"ord_{eng}"
                self.cnt[ordname] += 1
                self.lastmark[eng] = (ordname, self.cnt[ordname])

                def f0(E, make=make, h=self.sems[ordname], n2=1):
                    inst = make(E)
                    if inst is not None:
                        inst.then_inc(h, n2)
                self.q[eng].append(f0)
            else:
                self.q[eng].append(make)
            return None
        n = 1 if inc in ("mm", "vv", "ss", "pp") else 16
        h = self.sems[inc]
        self.cnt[inc] += n
        if self.selfsync:
            self.lastmark[eng] = (inc, self.cnt[inc])

        def f(E, make=make, h=h, n=n):
            make(E).then_inc(h, n)
        self.q[eng].append(f)
        return self.cnt[inc]

    def replay(self, eng, E):
        for f in self.q[eng]:
            f(E)


def build_program(debug=False, selfsync=False):
    nc = bass.Bass("TRN2", target_bir_lowering=False, debug=False)

    dt_in = lambda name, shape: nc.dram_tensor(name, list(shape), F32, kind="ExternalInput").ap()
    xT = dt_in("xT", (D, S))
    w4 = {nm: dt_in(nm, (C, 128, C, 128)) for nm in ("wr4", "ww4", "wk4", "wv4", "wo4", "wrec4")}
    wkey4 = dt_in("wkey4", (32, 128, 8, 128))
    wval4 = dt_in("wval4", (32, 128, 8, 128))
    selT3 = dt_in("selT3", (128, 256))
    keysT = dt_in("keysT", (NE, 128, 4096))
    valsT = dt_in("valsT", (NE, 128, 4096))
    g1v = dt_in("g1v", (128, C)); b1v = dt_in("b1v", (128, C))
    g2v = dt_in("g2v", (128, C)); b2v = dt_in("b2v", (128, C))
    tdv = dt_in("tdv", (128, C))
    onescol_bf = nc.dram_tensor("onescol_bf", [128, 1], BF16, kind="ExternalInput").ap()
    zcol_in = dt_in("zcol_in", (128, 1))
    onecol_in = dt_in("onecol_in", (128, 1))
    onesrow_in = dt_in("onesrow_in", (1, 128))
    epscol_in = dt_in("epscol_in", (128, 1))
    ident_in = dt_in("ident_in", (128, 128))

    kind_sp = "ExternalOutput" if debug else "Internal"
    y0sp = nc.dram_tensor("y0sp", [D, S], F32, kind=kind_sp).ap()
    y1sp = nc.dram_tensor("y1sp", [D, S], F32, kind=kind_sp).ap()
    x1sp = nc.dram_tensor("x1sp", [D, S], F32, kind=kind_sp).ap()
    gdr = nc.dram_tensor("gdr", [32, S], F32, kind=kind_sp).ap()
    rows = [nc.dram_tensor(f"rows{i}", [2, S], F32, kind=kind_sp).ap() for i in range(4)]
    if debug:
        z1dump = nc.dram_tensor("z1dump", [D, S], F32, kind="ExternalOutput").ap()
        rwkvdump = nc.dram_tensor("rwkvdump", [D, S], F32, kind="ExternalOutput").ap()
    outT = nc.dram_tensor("outT", [D, S], F32, kind="ExternalOutput").ap()

    es = ExitStack()
    sb = lambda name, shape, dt=F32: es.enter_context(nc.sbuf_tensor(name, list(shape), dt))
    p = P(nc, es, selfsync=selfsync)

    g1s = sb("g1s", (128, C)); b1s = sb("b1s", (128, C))
    g2s = sb("g2s", (128, C)); b2s = sb("b2s", (128, C))
    tds = sb("tds", (128, C)); decays = sb("decays", (128, C))
    ones_bf = sb("ones_bf", (128, 1), BF16)
    zcol = sb("zcol", (128, 1))
    onecol = sb("onecol", (128, 1))
    onesrow = sb("onesrow", (1, 128))
    epscol = sb("epscol", (128, 1))
    ident = sb("ident", (128, 128))
    selT_sb = sb("selT_sb", (128, 256), BF16)

    padA = sb("padA", (128, 64))
    padB = sb("padB", (128, 64))
    z1_es = ExitStack()
    z1 = [z1_es.enter_context(nc.sbuf_tensor(f"z1_{c}", [128, S], BF16)) for c in range(C)]

    pending = []

    def drain_one():
        if not pending:
            return
        eng, make, inc, waits = pending.pop(0)
        for s_, v_ in waits:
            p.wait(eng, s_, p.cnt[s_] if v_ is None else v_)
        p.op(eng, make, inc=inc)

    def barrier_all():
        bar = dict(p.cnt)
        for eng in ENGINES:
            for s_ in bar:
                p.wait(eng, s_, bar[s_])

    def drain_all_padded():
        while pending:
            drain_one()
            for _ in range(2):
                p.op("vector", lambda E: E.tensor_copy(out=padA[:], in_=padB[:]))
                p.op("vector", lambda E: E.tensor_copy(out=padB[:], in_=padA[:]))

    def ln_smalls_chain(arena, psA, psB, rows_dram, rowch, col0, width, extra_first_waits):
        A = arena[0:1, 0:width]
        B = arena[0:1, width:2 * width]
        Cc = arena[0:1, 2 * width:3 * width]
        steps = []
        steps.append(("vector", lambda E: E.tensor_single_scalar(
            out=A, in_=psA[0:1, :width], scalar=1.0 / D, op=ALU.mult), "vv",
            list(extra_first_waits)))
        steps.append(("vector", lambda E: E.tensor_single_scalar(
            out=B, in_=psB[0:1, :width], scalar=1.0 / D, op=ALU.mult), "vv", []))
        steps.append(("vector", lambda E: E.tensor_tensor(
            out=Cc, in0=A, in1=A, op=ALU.mult), "vv", []))
        steps.append(("vector", lambda E: E.tensor_tensor(
            out=B, in0=B, in1=Cc, op=ALU.subtract), "vv", []))
        steps.append(("vector", lambda E: E.tensor_single_scalar(
            out=B, in_=B, scalar=LN_EPS, op=ALU.add), "vv", []))
        steps.append(("scalar", lambda E: E.activation(
            out=Cc, in_=B, func=AF.Ln), "ss", [("vv", None)]))
        steps.append(("scalar", lambda E: E.activation(
            out=B, in_=Cc, func=AF.Exp, scale=-0.5), "ss", []))
        steps.append(("vector", lambda E: E.tensor_tensor(
            out=A, in0=A, in1=B, op=ALU.mult), "vv", [("ss", None)]))
        if rows_dram is not None:
            steps.append(("scalar", lambda E: E.dma_start(
                out=rows_dram[0:1, col0:col0 + width], in_=B), rowch, [("vv", None)]))
            steps.append(("scalar", lambda E: E.dma_start(
                out=rows_dram[1:2, col0:col0 + width], in_=A), rowch, [("vv", None)]))
        return steps

    with nc.Block() as block:
        for dst, src in ((g1s, g1v), (b1s, b1v), (g2s, g2v), (b2s, b2v), (tds, tdv),
                         (zcol, zcol_in), (onecol, onecol_in), (epscol, epscol_in),
                         (onesrow, onesrow_in), (ident, ident_in)):
            pre_da = p.op("sync", lambda E, d=dst, s=src: E.dma_start(out=d[:], in_=s[:]), inc=p.chan("pre"))
        p.op("gpsimd", lambda E: E.dma_start(out=ones_bf[:], in_=onescol_bf[:]), inc=p.chan("preg"))
        pre_dw = p.op("gpsimd", lambda E: E.dma_start(out=selT_sb[:], in_=selT3[:]), inc="preg")
        p.wait("scalar", "pre", pre_da)
        decays_done = p.op("scalar", lambda E: E.activation(out=decays[:], in_=tds[:], func=AF.Exp), inc="ss")
        p.op("vector", lambda E: E.memset(padA[:], 0.0))
        p.op("vector", lambda E: E.memset(padB[:], 0.0))

        ph1 = ExitStack()
        sb1 = lambda n_, s_, d_=F32: ph1.enter_context(nc.sbuf_tensor(n_, list(s_), d_))
        ps1 = lambda n_, s_: ph1.enter_context(nc.psum_tensor(n_, list(s_), F32))
        XC = [[sb1(f"XC{r}_{c}", (128, T1)) for c in range(C)] for r in range(2)]
        Y0C = [[sb1(f"Y0C{r}_{c}", (128, T1)) for c in range(C)] for r in range(2)]
        xb_r = [sb1(f"xb{i}", (128, T1), BF16) for i in range(2)]
        xq_r = [sb1(f"xq{i}", (128, T1), BF16) for i in range(2)]
        tmp_r = [sb1(f"tmp{i}", (128, T1)) for i in range(2)]
        rt1 = [sb1(f"rt1_{i}", (1, 3 * T1)) for i in range(4)]
        ps_st0 = [ps1(f"ps_st0{i}", (128, T1)) for i in range(2)]
        ps_st1 = [ps1(f"ps_st1{i}", (128, T1)) for i in range(2)]
        ps_bc0 = [ps1(f"ps_bc0{i}", (128, T1)) for i in range(2)]
        ps_bc1 = [ps1(f"ps_bc1{i}", (128, T1)) for i in range(2)]

        for r in range(2):
            for c in range(C):
                p.chan(f"xc{r}_{c}")
                p.chan(f"y0w{c}")
        xc_free = [[0] * C for _ in range(2)]
        y0c_spill = [[("y0w0", 0)] * C for _ in range(2)]
        y0c_sq = [[0] * C for _ in range(2)]
        xb_free = [0] * 2
        xq_free = [0] * 2
        st0_read = 0
        tmp_ss = [0, 0]
        st1_read = 0
        bc0_read = 0
        bc1_read = 0
        st0_rec = [0] * NT1
        st1_rec = [0] * NT1
        ch0_rec = [0] * NT1
        ch1_rec = [0] * NT1
        bc0_mm = [0] * NT1
        bc1_mm = [0] * NT1
        y0sp_done = [[0] * C for _ in range(NT1)]
        z1_done_t = [0] * NT1
        p.wait("tensor", "preg", pre_dw)
        p.wait("tensor", "pre", pre_da)

        def stA(t):
            r = t % 2
            for c in range(C):
                p.wait("sync", "vv", xc_free[r][c])
                p.op("sync", lambda E, t=t, c=c, r=r: E.dma_start(
                    out=XC[r][c][:], in_=xT[c * 128:(c + 1) * 128, t * T1:(t + 1) * T1]), inc=f"xc{r}_{c}")

        def stB(t):
            r = t % 2
            for c in range(C):
                b = c % 2
                p.wait("vector", f"xc{r}_{c}", p.cnt[f"xc{r}_{c}"])
                p.wait("vector", "mm", xb_free[b])
                vb = p.op("vector", lambda E, b=b, r=r, c=c: E.tensor_copy(out=xb_r[b][:], in_=XC[r][c][:]), inc="vv")
                p.wait("scalar", f"xc{r}_{c}", p.cnt[f"xc{r}_{c}"])
                p.wait("scalar", "mm", xq_free[b])
                sq = p.op("scalar", lambda E, b=b, r=r, c=c: E.activation(
                    out=xq_r[b][:], in_=XC[r][c][:], func=AF.Square), inc="ss")
                p.wait("tensor", "vv", vb)
                p.wait("tensor", "ss", sq)
                if c == 0:
                    p.wait("tensor", "vv", st0_read)
                p.op("tensor", lambda E, b=b, st=(c == 0), sp=(c == 7): E.matmul(
                    ps_st0[0][0:1, :], ones_bf[:], xb_r[b][:], start=st, stop=sp))
                mmv = p.op("tensor", lambda E, b=b, st=(c == 0), sp=(c == 7): E.matmul(
                    ps_st0[1][0:1, :], ones_bf[:], xq_r[b][:], start=st, stop=sp), inc="mm")
                xb_free[b] = mmv
                xq_free[b] = mmv
                drain_one()
            st0_rec[t] = mmv
            pending.extend(ln_smalls_chain(rt1[t % 4], ps_st0[0], ps_st0[1], None, None, 0, T1,
                                           [("mm", st0_rec[t])]))

        def stD(t):
            nonlocal st0_read
            ch0_rec[t] = p.cnt["vv"]
            st0_read = ch0_rec[t]
            A_ = rt1[t % 4][0:1, 0:T1]
            B_ = rt1[t % 4][0:1, T1:2 * T1]
            p.wait("tensor", "vv", ch0_rec[t])
            p.wait("tensor", "vv", bc0_read)
            p.op("tensor", lambda E, B_=B_: E.matmul(ps_bc0[0][:], onesrow[:], B_, start=True, stop=True))
            bc0_mm[t] = p.op("tensor", lambda E, A_=A_: E.matmul(
                ps_bc0[1][:], onesrow[:], A_, start=True, stop=True), inc="mm")

        def stE(t):
            nonlocal bc0_read
            r = t % 2
            for c in range(C):
                b = c % 2
                tb = c % 2
                p.wait("vector", "mm", bc0_mm[t])
                wch, wval = y0c_spill[r][c]
                p.wait("vector", wch, wval)
                p.wait("vector", "ss", y0c_sq[r][c])
                p.wait("vector", "ss", tmp_ss[tb])
                p.op("vector", lambda E, r=r, c=c, tb=tb: E.tensor_tensor(
                    out=tmp_r[tb][:], in0=XC[r][c][:], in1=ps_bc0[0][:], op=ALU.mult))
                vy0 = p.op("vector", lambda E, r=r, c=c, tb=tb: E.tensor_tensor(
                    out=Y0C[r][c][:], in0=tmp_r[tb][:], in1=ps_bc0[1][:], op=ALU.subtract), inc="vv")
                xc_free[r][c] = vy0
                p.wait("sync", "vv", vy0)
                y0da = p.op("sync", lambda E, t=t, c=c, r=r: E.dma_start(
                    out=y0sp[c * 128:(c + 1) * 128, t * T1:(t + 1) * T1], in_=Y0C[r][c][:]), inc=f"y0w{c}")
                y0sp_done[t][c] = (f"y0w{c}", y0da)
                y0c_spill[r][c] = (f"y0w{c}", y0da)
                p.wait("vector", "mm", xb_free[b])
                vb = p.op("vector", lambda E, b=b, r=r, c=c: E.tensor_copy(out=xb_r[b][:], in_=Y0C[r][c][:]), inc="vv")
                p.wait("scalar", "vv", vy0)
                p.wait("scalar", "mm", xq_free[b])
                sq = p.op("scalar", lambda E, b=b, r=r, c=c: E.activation(
                    out=xq_r[b][:], in_=Y0C[r][c][:], func=AF.Square), inc="ss")
                y0c_sq[r][c] = sq
                p.wait("tensor", "vv", vb)
                p.wait("tensor", "ss", sq)
                if c == 0:
                    p.wait("tensor", "vv", st1_read)
                p.op("tensor", lambda E, b=b, st=(c == 0), sp=(c == 7): E.matmul(
                    ps_st1[0][0:1, :], ones_bf[:], xb_r[b][:], start=st, stop=sp))
                mmv = p.op("tensor", lambda E, b=b, st=(c == 0), sp=(c == 7): E.matmul(
                    ps_st1[1][0:1, :], ones_bf[:], xq_r[b][:], start=st, stop=sp), inc="mm")
                xb_free[b] = mmv
                xq_free[b] = mmv
                drain_one()
            bc0_read = p.cnt["vv"]
            st1_rec[t] = mmv
            pending.extend(ln_smalls_chain(rt1[2 + t % 2] if False else rt1[(t % 4)], ps_st1[0], ps_st1[1], None, None, 0, T1,
                                           [("mm", st1_rec[t])]))

        def stG(t):
            nonlocal st1_read
            ch1_rec[t] = p.cnt["vv"]
            st1_read = ch1_rec[t]
            A_ = rt1[t % 4][0:1, 0:T1]
            B_ = rt1[t % 4][0:1, T1:2 * T1]
            p.wait("tensor", "vv", ch1_rec[t])
            p.wait("tensor", "vv", bc1_read)
            p.op("tensor", lambda E, B_=B_: E.matmul(ps_bc1[0][:], onesrow[:], B_, start=True, stop=True))
            bc1_mm[t] = p.op("tensor", lambda E, A_=A_: E.matmul(
                ps_bc1[1][:], onesrow[:], A_, start=True, stop=True), inc="mm")

        def stH(t):
            nonlocal bc1_read
            r = t % 2
            for c in range(C):
                tb = c % 2
                p.wait("vector", "mm", bc1_mm[t])
                p.wait("vector", "ss", tmp_ss[tb])
                p.op("vector", lambda E, r=r, c=c, tb=tb: E.tensor_tensor(
                    out=tmp_r[tb][:], in0=Y0C[r][c][:], in1=ps_bc1[0][:], op=ALU.mult))
                vt2 = p.op("vector", lambda E, tb=tb: E.tensor_tensor(
                    out=tmp_r[tb][:], in0=tmp_r[tb][:], in1=ps_bc1[1][:], op=ALU.subtract), inc="vv")
                p.wait("scalar", "vv", vt2)
                sz = p.op("scalar", lambda E, t=t, c=c, tb=tb: E.activation(
                    out=z1[c][:, t * T1:(t + 1) * T1], in_=tmp_r[tb][:], func=AF.Identity,
                    scale=g1s[:, c:c + 1], bias=b1s[:, c:c + 1]), inc="ss")
                tmp_ss[tb] = sz
                drain_one()
            z1_done_t[t] = p.cnt["ss"]
            bc1_read = p.cnt["vv"]

        for s in range(NT1 + 1):
            if s < NT1:
                stA(s)
                stB(s)
                if s == 0:
                    drain_all_padded()
            if s >= 1:
                stG(s - 1)
                stH(s - 1)
            if s < NT1:
                stD(s)
                stE(s)
        drain_all_padded()
        ph1.close()
        barrier_all()

        rwkv_es = ExitStack()
        rwkv = [rwkv_es.enter_context(nc.sbuf_tensor(f"rwkv{c}", [128, S], BF16, side="right")) for c in range(C)]
        ph2 = ExitStack()
        sb2 = lambda n_, s_, d_=F32: ph2.enter_context(nc.sbuf_tensor(n_, list(s_), d_))
        ps2 = lambda n_, s_: ph2.enter_context(nc.psum_tensor(n_, list(s_), F32))
        wnames = ("wr", "ww", "wk", "wv")
        wproj = {nm: [sb2(f"{nm}_{i}", (128, 1024), BF16) for i in range(2)] for nm in wnames}
        EWKD = sb2("EWKD", (128, T1 + 1))
        NUMD = sb2("NUMD", (128, T1 + 1))
        NUMO = sb2("NUMO", (128, T1 + 1))
        DENO = sb2("DENO", (128, T1 + 1))
        Dscan = sb2("Dscan", (128, T1 + 1))
        EWb = [sb2(f"EWb{i}", (128, T1)) for i in range(2)]
        Ub = [sb2(f"Ub{i}", (128, T1)) for i in range(2)]
        RREb = [sb2(f"RREb{i}", (128, T1)) for i in range(2)]
        Rb = [sb2(f"Rb{i}", (128, T1)) for i in range(2)]
        W1 = sb2("W1", (128, T1))
        W2 = sb2("W2", (128, T1))
        PSp = {nm: [ps2(f"PS{nm}{i}", (128, T1)) for i in range(2)] for nm in ("w", "k", "r", "v")}

        proj_src = {"wr": w4["wr4"], "ww": w4["ww4"], "wk": w4["wk4"], "wv": w4["wv4"]}
        wproj_done = [0] * C
        wslot_free = [0] * 2

        p.chan("wp0"); p.chan("wp1")

        def load_proj(ec):
            slot = ec % 2
            p.wait("gpsimd", "mm", wslot_free[slot])
            v = 0
            for nm in wnames:
                v = p.op("gpsimd", lambda E, nm=nm, ec=ec, s=slot: E.dma_start(
                    out=wproj[nm][s][:], in_=proj_src[nm][ec].rearrange("p a b -> p (a b)")), inc=f"wp{slot}")
            wproj_done[ec] = v

        load_proj(0)
        load_proj(1)

        p.wait("vector", "pre", pre_da)
        p.wait("vector", "ss", decays_done)
        bank_free_v = {}
        bank_free_s = {}
        ewkd_read = 0
        wkv_done = 0
        carry_rd = 0
        vRWKV = [[0] * NT1 for _ in range(C)]
        for ec in range(C):
            p.wait("vector", "vv", ewkd_read)
            p.op("vector", lambda E, ec=ec: E.tensor_copy(
                out=Dscan[:, 1:T1 + 1], in_=decays[:, ec:ec + 1].broadcast_to([128, T1])))
            p.op("vector", lambda E: E.tensor_copy(out=Dscan[:, 0:1], in_=zcol[:]))
            p.op("vector", lambda E: E.tensor_copy(out=NUMD[:, 0:1], in_=zcol[:]))
            dscan_v = ewkd_carry = p.op("vector", lambda E: E.tensor_copy(out=EWKD[:, 0:1], in_=zcol[:]), inc="vv")
            if ec + 1 < C:
                load_proj(ec + 1)
            for t in range(NT1):
                tp = t % 2
                prev = (ec, t - 2) if t >= 2 else ((ec - 1, NT1 - 2 + t) if ec > 0 else None)
                p.wait("tensor", f"wp{ec % 2}", wproj_done[ec])
                p.wait("tensor", "ss", z1_done_t[t])
                if prev is not None:
                    p.wait("tensor", "vv", bank_free_v[prev])
                    p.wait("tensor", "ss", bank_free_s[prev])
                mg = {}
                for nm, wsb in (("w", "ww"), ("k", "wk"), ("r", "wr"), ("v", "wv")):
                    for kc in range(C):
                        p.op("tensor", lambda E, nm=nm, wsb=wsb, kc=kc, t=t, tp=tp, ec=ec,
                             st=(kc == 0), sp=(kc == 7): E.matmul(
                            PSp[nm][tp][:], wproj[wsb][ec % 2][:, kc * 128:(kc + 1) * 128],
                            z1[kc][:, t * T1:(t + 1) * T1], start=st, stop=sp),
                            inc="mm" if kc == 7 else None)
                    mg[nm] = p.cnt["mm"]
                p.wait("scalar", "mm", mg["w"])
                sEW = p.op("scalar", lambda E, tp=tp: E.activation(
                    out=EWb[tp][:], in_=PSp["w"][tp][:], func=AF.Exp), inc="ss")
                p.wait("vector", "mm", mg["k"])
                p.wait("vector", "ss", sEW)
                vU = p.op("vector", lambda E, tp=tp: E.tensor_tensor(
                    out=Ub[tp][:], in0=PSp["k"][tp][:], in1=EWb[tp][:], op=ALU.subtract), inc="vv")
                p.wait("scalar", "vv", vU)
                p.wait("scalar", "vv", ewkd_read)
                p.wait("scalar", "vv", ewkd_carry)
                sEWK = p.op("scalar", lambda E, tp=tp: E.activation(
                    out=EWKD[:, 1:T1 + 1], in_=Ub[tp][:], func=AF.Exp), inc="ss")
                p.wait("scalar", "mm", mg["r"])
                sRRE = p.op("scalar", lambda E, tp=tp: E.activation(
                    out=RREb[tp][:], in_=PSp["r"][tp][:], func=AF.Exp, scale=-1.0), inc="ss")
                p.wait("vector", "ss", sRRE)
                p.op("vector", lambda E, tp=tp: E.tensor_single_scalar(
                    out=RREb[tp][:], in_=RREb[tp][:], scalar=1.0, op=ALU.add))
                p.op("vector", lambda E, tp=tp: E.reciprocal(out=Rb[tp][:], in_=RREb[tp][:]))
                p.wait("vector", "ss", sEWK)
                p.wait("vector", "mm", mg["v"])
                vN = p.op("vector", lambda E, tp=tp: E.tensor_tensor(
                    out=NUMD[:, 1:T1 + 1], in0=EWKD[:, 1:T1 + 1], in1=PSp["v"][tp][:], op=ALU.mult), inc="vv")
                p.op("vector", lambda E: E.tensor_tensor_scan(
                    out=NUMO[:], data0=Dscan[:], data1=NUMD[:], initial=0.0, op0=ALU.mult, op1=ALU.add))
                vden = p.op("vector", lambda E: E.tensor_tensor_scan(
                    out=DENO[:], data0=Dscan[:], data1=EWKD[:], initial=0.0, op0=ALU.mult, op1=ALU.add), inc="vv")
                ewkd_read = vden
                p.op("vector", lambda E: E.tensor_single_scalar(
                    out=W1[:], in_=DENO[:, 1:T1 + 1], scalar=WKV_EPS, op=ALU.add))
                p.op("vector", lambda E: E.reciprocal(out=W2[:], in_=W1[:]))
                wkv_done = p.op("vector", lambda E: E.tensor_tensor(
                    out=W1[:], in0=NUMO[:, 1:T1 + 1], in1=W2[:], op=ALU.mult), inc="vv")
                vR = p.op("vector", lambda E, ec=ec, t=t, tp=tp: E.tensor_tensor(
                    out=rwkv[ec][:, t * T1:(t + 1) * T1], in0=Rb[tp][:], in1=W1[:], op=ALU.mult), inc="vv")
                vRWKV[ec][t] = ("vv", vR)
                if t < NT1 - 1:
                    p.op("vector", lambda E: E.tensor_copy(out=NUMD[:, 0:1], in_=NUMO[:, T1:T1 + 1]))
                    carry_rd = ewkd_carry = p.op("vector", lambda E: E.tensor_copy(
                        out=EWKD[:, 0:1], in_=DENO[:, T1:T1 + 1]), inc="vv")
                bank_free_v[(ec, t)] = vN
                bank_free_s[(ec, t)] = sRRE
            wslot_free[ec % 2] = p.cnt["mm"]
        if debug:
            for c in range(C):
                p.wait("gpsimd", vRWKV[c][NT1 - 1][0], vRWKV[c][NT1 - 1][1])
                p.op("gpsimd", lambda E, c=c: E.dma_start(
                    out=rwkvdump[c * 128:(c + 1) * 128, :], in_=rwkv[c][:]), inc=p.chan("dbg"))
        ph2_end_mm = p.cnt["mm"]
        ph2.close()
        z1_es.close()
        ph4w = ExitStack()
        wkey_sb = [ph4w.enter_context(nc.sbuf_tensor(f"wkeysb{fc}", [128, 1024], BF16)) for fc in range(32)]
        barrier_all()
        wkey_done = 0
        p.chan("wk"); p.chan("wv"); p.chan("wr")
        p.wait("gpsimd", "mm", ph2_end_mm)
        for fc in range(32):
            wkey_done = p.op("gpsimd", lambda E, fc=fc: E.dma_start(
                out=wkey_sb[fc][:], in_=wkey4[fc].rearrange("p a b -> p (a b)")), inc="wk")

        ph3 = ExitStack()
        sb3 = lambda n_, s_, d_=F32: ph3.enter_context(nc.sbuf_tensor(n_, list(s_), d_, side="right"))
        ps3 = lambda n_, s_: ph3.enter_context(nc.psum_tensor(n_, list(s_), F32))
        wo_sb = [sb3(f"wo_sb{e}", (128, 1024), BF16) for e in range(C)]
        PSo = [ps3(f"PSo{i}", (128, T1)) for i in range(4)]
        ps_s2 = [ps3(f"ps_s2{i}", (128, T1)) for i in range(4)]
        rt3 = [sb3(f"rt3_{i}", (1, 3 * T1)) for i in range(2)]
        y0l_r = [sb3(f"y0l{i}", (128, T1)) for i in range(4)]
        y1b_r = [sb3(f"y1b{i}", (128, T1)) for i in range(3)]
        xb3_r = [sb3(f"xb3{i}", (128, T1), BF16) for i in range(8)]
        xq3_r = [sb3(f"xq3{i}", (128, T1), BF16) for i in range(8)]

        p.wait("gpsimd", "mm", ph2_end_mm)
        wo_done = 0
        p.chan("wo")
        for e in range(C):
            wo_done = p.op("gpsimd", lambda E, e=e: E.dma_start(
                out=wo_sb[e][:], in_=w4["wo4"][e].rearrange("p a b -> p (a b)")), inc="wo")

        y0l_free3 = [(None, 0)] * 4
        y1b_cons = [(0, 0)] * 3
        xb3_free = [0] * 8
        xq3_free = [0] * 8
        vb_sem3 = [0] * 8
        sq_sem3 = [0] * 8
        pso_free = [0] * 4
        s2_bank_free = [0] * 2
        y1sp_done = [[0] * C for _ in range(NT1)]
        st2 = [0] * NT1

        def stats_mms3(i):
            tq, cq = i // 8, i % 8
            tpar = tq % 2
            p.wait("tensor", "vv", vb_sem3[cq])
            p.wait("tensor", "ss", sq_sem3[cq])
            if cq == 0:
                p.wait("tensor", "vv", s2_bank_free[tpar])
            p.op("tensor", lambda E, cq=cq, tpar=tpar, st=(cq == 0), sp=(cq == 7): E.matmul(
                ps_s2[2 * tpar][0:1, :], ones_bf[:], xb3_r[cq][:], start=st, stop=sp))
            mmv = p.op("tensor", lambda E, cq=cq, tpar=tpar, st=(cq == 0), sp=(cq == 7): E.matmul(
                ps_s2[2 * tpar + 1][0:1, :], ones_bf[:], xq3_r[cq][:], start=st, stop=sp), inc="mm")
            xb3_free[cq] = mmv
            xq3_free[cq] = mmv
            if cq == 7:
                st2[tq] = mmv
                pending.extend(ln_smalls_chain(rt3[tq % 2], ps_s2[2 * (tq % 2)], ps_s2[2 * (tq % 2) + 1],
                                               rows[2], p.chan("rw2"), tq * T1, T1,
                                               [("mm", st2[tq]), ("rw2", None)]))

        for t in range(NT1):
            for c in range(C):
                po = c % 4
                i = t * C + c
                slot = i % 4
                sem, val = y0l_free3[slot]
                if sem is not None:
                    p.wait("sync", sem, val)
                wch3, wv3 = y0sp_done[t][c]
                p.wait("sync", wch3, wv3)
                da = p.op("sync", lambda E, t=t, c=c, s=slot: E.dma_start(
                    out=y0l_r[s][:], in_=y0sp[c * 128:(c + 1) * 128, t * T1:(t + 1) * T1]), inc=p.chan(f"yl{slot}"))
                rch, rval = vRWKV[C - 1][t]
                p.wait("tensor", rch, rval)
                p.wait("tensor", "wo", wo_done)
                p.wait("tensor", "vv", pso_free[po])
                mo = 0
                for kc in range(C):
                    mo = p.op("tensor", lambda E, c=c, kc=kc, t=t, po=po, st=(kc == 0), sp=(kc == 7): E.matmul(
                        PSo[po][:], wo_sb[c][:, kc * 128:(kc + 1) * 128],
                        rwkv[kc][:, t * T1:(t + 1) * T1], start=st, stop=sp),
                        inc="mm" if kc == 7 else None)
                if i >= 2:
                    stats_mms3(i - 2)
                yb = i % 3
                ss1, da1 = y1b_cons[yb]
                p.wait("vector", "mm", mo)
                p.wait("vector", f"yl{slot}", da)
                p.wait("vector", "ss", ss1)
                p.wait("vector", f"y1w{yb}", da1)
                vy1 = p.op("vector", lambda E, po=po, s=slot, y=yb: E.tensor_tensor(
                    out=y1b_r[y][:], in0=PSo[po][:], in1=y0l_r[s][:], op=ALU.add), inc="vv")
                pso_free[po] = vy1
                y0l_free3[slot] = ("vv", vy1)
                drain_one()
                p.wait("sync", "vv", vy1)
                y1da = p.op("sync", lambda E, t=t, c=c, y=yb: E.dma_start(
                    out=y1sp[c * 128:(c + 1) * 128, t * T1:(t + 1) * T1], in_=y1b_r[y][:]), inc=p.chan(f"y1w{yb}"))
                y1sp_done[t][c] = (f"y1w{yb}", y1da)
                p.wait("vector", "mm", xb3_free[c])
                vb = p.op("vector", lambda E, c=c, y=yb: E.tensor_copy(out=xb3_r[c][:], in_=y1b_r[y][:]), inc="vv")
                p.wait("scalar", "vv", vy1)
                p.wait("scalar", "mm", xq3_free[c])
                sq = p.op("scalar", lambda E, c=c, y=yb: E.activation(
                    out=xq3_r[c][:], in_=y1b_r[y][:], func=AF.Square), inc="ss")
                y1b_cons[yb] = (sq, y1da)
                vb_sem3[c] = vb
                sq_sem3[c] = sq
                drain_one()
            if t >= 1:
                s2_bank_free[(t - 1) % 2] = p.cnt["vv"]
        stats_mms3(NT1 * C - 2)
        stats_mms3(NT1 * C - 1)
        drain_all_padded()
        rows2_all_done = p.cnt["rw2"]
        ph3_end_mm = p.cnt["mm"]
        ph3_tail_vv = p.cnt["vv"]
        ph3_tail_ss = p.cnt["ss"]
        y1w_totals = [p.cnt[f"y1w{y}"] for y in range(3)]
        ph3.close()
        rwkv_es.close()
        wval_sb = [ph4w.enter_context(nc.sbuf_tensor(f"wvalsb{fc}", [128, 1024], BF16, side="right")) for fc in range(32)]

        ph4 = ExitStack()
        sb4 = lambda n_, s_, d_=F32: ph4.enter_context(nc.sbuf_tensor(n_, list(s_), d_))
        ps4 = lambda n_, s_: ph4.enter_context(nc.psum_tensor(n_, list(s_), F32))
        wrec_sb = [sb4(f"wrecsb{e}", (128, 1024), BF16) for e in range(C)]
        BC2r = [sb4(f"BC2r{i}", (128, T4)) for i in range(2)]
        BC2m = [sb4(f"BC2m{i}", (128, T4)) for i in range(2)]
        Y1T = [[sb4(f"Y1T{j}_{ec}", (128, T4)) for ec in range(C)] for j in range(2)]
        xring = [sb4(f"xring{i}", (128, T4)) for i in range(2)]
        z2t = [sb4(f"z2t{ec}", (128, T4), BF16) for ec in range(C)]
        tmp4 = [sb4(f"tmp4{i}", (128, T4)) for i in range(1)]
        KRb = sb4("KRb", (128, T4))
        KSb = [sb4(f"KS{i}", (128, T4), BF16) for i in range(32)]
        RC2 = [sb4(f"RC2{i}", (128, T4)) for i in range(8)]
        X1T = [sb4(f"X1T{j}", (128, T4)) for j in range(2)]
        xb4_r = [sb4(f"xb4{i}", (128, T4), BF16) for i in range(4)]
        xq4_r = [sb4(f"xq4{i}", (128, T4), BF16) for i in range(4)]
        PS_k4 = [ps4(f"PS_k4{i}", (128, 512)) for i in range(2)]
        PS_v4 = [ps4(f"PS_v4{i}", (128, 512)) for i in range(4)]
        ps_s3 = [ps4(f"ps_s3{i}", (128, 512)) for i in range(2)]
        rt4 = [sb4(f"rt4_{i}", (1, 3 * T4)) for i in range(1)]

        wrec_done = 0
        for e in range(C):
            wrec_done = p.op("gpsimd", lambda E, e=e: E.dma_start(
                out=wrec_sb[e][:], in_=w4["wrec4"][e].rearrange("p a b -> p (a b)")), inc="wr")


        bc2_free = [0] * 2
        y1t_free = [0] * 2
        xring_free = [0] * 2
        vcmA_free = 0
        z2_done_r = [0] * 2
        ks_free = [0] * 32
        kr_free = 0
        k4_free_v = [pso_free[0], pso_free[1]]
        k4_free_s = [0] * 2
        vcm_free = ph3_tail_vv
        rc2_free = [0] * 8
        x1spill = [0] * 2
        x1t_cons = [0] * 2
        st3 = [0] * NT4
        rows3_done = [0] * NT4
        xb4_free = [0] * 4
        xq4_free = [0] * 4
        vb_sem = [0] * 4
        sq_sem = [0] * 4
        s3_read_gate = 0

        def smalls3_chain(u, st_val):
            return ln_smalls_chain(rt4[0], ps_s3[0], ps_s3[1], rows[3], p.chan("rw3"), u * T4, T4,
                                   [("mm", st_val), ("rw3", None)])

        z2t_free_mm = ph3_end_mm

        def stage_head(u):
            nonlocal bc2_free, y1t_free, z2t_free_mm
            t1i = u // 2
            up = u % 2
            if u == 0:
                p.wait("vector", "mm", ph3_end_mm)
                p.wait("scalar", "mm", ph3_end_mm)
                p.wait("sync", "mm", ph3_end_mm)
                p.wait("sync", "vv", ph3_tail_vv)
                p.wait("sync", "ss", ph3_tail_ss)
                p.wait("sync", "rw2", rows2_all_done)
                for y in range(3):
                    p.wait("sync", f"y1w{y}", y1w_totals[y])
            p.wait("gpsimd", "rw2", rows2_all_done)
            p.wait("gpsimd", "vv", bc2_free[up])
            p.op("gpsimd", lambda E, u=u, up=up: E.dma_start(
                out=BC2r[up][:], in_=rows[2][0:1, u * T4:(u + 1) * T4].broadcast_to([128, T4])), inc=p.chan(f"bc2{up}"))
            bc_da = p.op("gpsimd", lambda E, u=u, up=up: E.dma_start(
                out=BC2m[up][:], in_=rows[2][1:2, u * T4:(u + 1) * T4].broadcast_to([128, T4])), inc=f"bc2{up}")
            p.wait("sync", "vv", y1t_free[up])
            y1_da = 0
            for ec in range(C):
                wch4, wv4 = y1sp_done[t1i][ec]
                p.wait("sync", wch4, wv4)
                y1_da = p.op("sync", lambda E, u=u, up=up, ec=ec: E.dma_start(
                    out=Y1T[up][ec][:], in_=y1sp[ec * 128:(ec + 1) * 128, u * T4:(u + 1) * T4]), inc=p.chan(f"y1t{up}"))
            z2rec = 0
            z2par = [z2_done_r[up]] * 1
            for ec in range(C):
                tb = 0
                xs = ec % 2
                p.wait("sync", "vv", xring_free[xs])
                x_da = p.op("sync", lambda E, u=u, ec=ec, xs=xs: E.dma_start(
                    out=xring[xs][:], in_=xT[ec * 128:(ec + 1) * 128, u * T4:(u + 1) * T4]), inc=p.chan(f"xr{xs}"))
                p.wait("vector", f"y1t{up}", y1_da)
                p.wait("vector", f"bc2{up}", bc_da)
                p.wait("vector", "ss", z2par[tb])
                p.op("vector", lambda E, up=up, ec=ec, tb=tb: E.tensor_tensor(
                    out=tmp4[tb][:], in0=Y1T[up][ec][:], in1=BC2r[up][:], op=ALU.mult))
                vt = p.op("vector", lambda E, up=up, tb=tb: E.tensor_tensor(
                    out=tmp4[tb][:], in0=tmp4[tb][:], in1=BC2m[up][:], op=ALU.subtract), inc="vv")
                p.wait("scalar", "vv", vt)
                p.wait("scalar", "mm", z2t_free_mm)
                z2rec = p.op("scalar", lambda E, ec=ec, tb=tb: E.activation(
                    out=z2t[ec][:], in_=tmp4[tb][:], func=AF.Identity,
                    scale=g2s[:, ec:ec + 1], bias=b2s[:, ec:ec + 1]), inc="ss")
                z2par[tb] = z2rec
                p.wait("vector", f"xr{xs}", x_da)
                xa = p.op("vector", lambda E, up=up, ec=ec, xs=xs: E.tensor_tensor(
                    out=Y1T[up][ec][:], in0=Y1T[up][ec][:], in1=xring[xs][:], op=ALU.add), inc="vv")
                xring_free[xs] = xa
                drain_one()
            z2_done_r[up] = z2rec
            bc2_free[up] = p.cnt["vv"]
            return z2rec

        def kcm_only(u, z2rec):
            nonlocal kr_free
            for fc in range(32):
                fp = fc % 2
                p.wait("tensor", "wk", wkey_done)
                p.wait("tensor", "ss", z2rec)
                p.wait("tensor", "vv", k4_free_v[fp])
                p.wait("tensor", "ss", k4_free_s[fp])
                mk = 0
                for kc in range(C):
                    mk = p.op("tensor", lambda E, fc=fc, kc=kc, fp=fp, st=(kc == 0), sp=(kc == 7): E.matmul(
                        PS_k4[fp][:, :T4], wkey_sb[fc][:, kc * 128:(kc + 1) * 128],
                        z2t[kc][:], start=st, stop=sp), inc="mm" if kc == 7 else None)
                p.wait("vector", "mm", mk)
                p.wait("vector", "ss", kr_free)
                vkr = p.op("vector", lambda E, fp=fp: E.tensor_single_scalar(
                    out=KRb[:], in_=PS_k4[fp][:, :T4], scalar=0.0, op=ALU.max), inc="vv")
                k4_free_v[fp] = vkr
                p.wait("scalar", "vv", vkr)
                p.wait("scalar", "mm", ks_free[fc])
                sks = p.op("scalar", lambda E, fc=fc: E.activation(
                    out=KSb[fc][:], in_=KRb[:], func=AF.Square), inc="ss")
                kr_free = sks
                drain_one()
            return sks

        def rcm_sig_half(u, dh):
            last = 0
            for pair in range(2):
                mrs = [0] * 2
                for ecl2 in range(2):
                    ec = dh * 4 + pair * 2 + ecl2
                    ep = ec % 2
                    p.wait("tensor", "wr", wrec_done)
                    p.wait("tensor", "vv", k4_free_v[ep])
                    p.wait("tensor", "ss", k4_free_s[ep])
                    mr = 0
                    for kc in range(C):
                        mr = p.op("tensor", lambda E, kc=kc, ec=ec, ep=ep, st=(kc == 0), sp=(kc == 7): E.matmul(
                            PS_k4[ep][:, :T4], wrec_sb[ec][:, kc * 128:(kc + 1) * 128],
                            z2t[kc][:], start=st, stop=sp), inc="mm" if kc == 7 else None)
                    k4_free_v[ep] = 0
                    mrs[ecl2] = mr
                srs = [0, 0]
                for ecl2 in range(2):
                    ec = dh * 4 + pair * 2 + ecl2
                    ep = ec % 2
                    slot = dh * 4 + pair * 2 + ecl2
                    p.wait("scalar", "mm", mrs[ecl2])
                    p.wait("scalar", "vv", rc2_free[slot])
                    srs[ecl2] = p.op("scalar", lambda E, ep=ep, slot=slot: E.activation(
                        out=RC2[slot][:], in_=PS_k4[ep][:, :T4], func=AF.Exp, scale=-1.0), inc="ss")
                    k4_free_s[ep] = srs[ecl2]
                    last = srs[ecl2]
                s0_, s1_ = dh * 4 + pair * 2, dh * 4 + pair * 2 + 1
                p.wait("vector", "ss", srs[0])
                p.op("vector", lambda E, s=s0_: E.tensor_single_scalar(
                    out=RC2[s][:], in_=RC2[s][:], scalar=1.0, op=ALU.add))
                p.wait("vector", "ss", srs[1])
                p.op("vector", lambda E, s=s1_: E.tensor_single_scalar(
                    out=RC2[s][:], in_=RC2[s][:], scalar=1.0, op=ALU.add))
                p.op("vector", lambda E, s=s0_: E.reciprocal(out=RC2[s][:], in_=RC2[s][:]))
                p.op("vector", lambda E, s=s1_: E.reciprocal(out=RC2[s][:], in_=RC2[s][:]))
            return last

        def vcm_pass(u, dh, ks_ready, bankfree_vv):
            p.wait("tensor", "ss", ks_ready)
            p.wait("tensor", "vv", bankfree_vv)
            mvcm = 0
            for fc in range(32):
                p.wait("tensor", f"wv{fc // 4}", 64)
                for dcl in range(4):
                    dc = dh * 4 + dcl
                    mvcm = p.op("tensor", lambda E, fc=fc, dc=dc, dcl=dcl,
                                st=(fc == 0), sp=(fc == 31): E.matmul(
                        PS_v4[dcl][:, :T4],
                        wval_sb[fc][:, dc * 128:(dc + 1) * 128], KSb[fc][:],
                        start=st, stop=sp), inc="mm" if dcl == 3 else None)
                if dh == 1:
                    ks_free[fc] = mvcm
            return mvcm

        def combine_half(u, dh, mvcm):
            nonlocal vcm_free, vcmA_free, s3_read_gate
            up = u % 2
            lastvx1 = 0
            if dh == 0:
                drain_one()
                drain_one()
            bank_evac = 0
            for ecl in range(4):
                slot = dh * 4 + ecl
                p.wait("vector", "mm", mvcm)
                bank_evac = p.op("vector", lambda E, slot=slot, ecl=ecl: E.tensor_tensor(
                    out=RC2[slot][:], in0=PS_v4[ecl][:, :T4], in1=RC2[slot][:], op=ALU.mult), inc="vv")
            if dh == 0:
                vcmA_free = bank_evac
            else:
                vcm_free = bank_evac
            for ecl in range(4):
                ec = dh * 4 + ecl
                slot = dh * 4 + ecl
                xs4 = ec % 2
                p.wait("vector", "ss", x1t_cons[xs4])
                p.wait("vector", f"x1w{xs4}", x1spill[xs4])
                vx1 = p.op("vector", lambda E, slot=slot, up=up, ec=ec, xs4=xs4: E.tensor_tensor(
                    out=X1T[xs4][:], in0=RC2[slot][:], in1=Y1T[up][ec][:], op=ALU.add), inc="vv")
                rc2_free[slot] = vx1
                lastvx1 = vx1
                if ec == C - 1:
                    y1t_free[up] = vx1
                p.wait("sync", "vv", vx1)
                x1da = p.op("sync", lambda E, u=u, ec=ec, xs4=xs4: E.dma_start(
                    out=x1sp[ec * 128:(ec + 1) * 128, u * T4:(u + 1) * T4], in_=X1T[xs4][:]), inc=p.chan(f"x1w{xs4}"))
                x1spill[xs4] = x1da
                sl4 = ec % 4
                p.wait("vector", "mm", xb4_free[sl4])
                vb = p.op("vector", lambda E, sl4=sl4, xs4=xs4: E.tensor_copy(out=xb4_r[sl4][:], in_=X1T[xs4][:]), inc="vv")
                p.wait("scalar", "vv", vx1)
                p.wait("scalar", "mm", xq4_free[sl4])
                sq = p.op("scalar", lambda E, sl4=sl4, xs4=xs4: E.activation(
                    out=xq4_r[sl4][:], in_=X1T[xs4][:], func=AF.Square), inc="ss")
                x1t_cons[xs4] = sq
                vb_sem[sl4] = vb
                sq_sem[sl4] = sq
                drain_one()
            return lastvx1

        def stats_mms(u, dh, gate=0):
            mmv = 0
            for ecl in range(4):
                ec = dh * 4 + ecl
                sl4 = ecl
                p.wait("tensor", "vv", vb_sem[sl4])
                p.wait("tensor", "ss", sq_sem[sl4])
                if ec == 0:
                    p.wait("tensor", "vv", gate)
                p.op("tensor", lambda E, sl4=sl4, st=(ec == 0), sp=(ec == 7): E.matmul(
                    ps_s3[0][0:1, :T4], ones_bf[:], xb4_r[sl4][:], start=st, stop=sp))
                mmv = p.op("tensor", lambda E, sl4=sl4, st=(ec == 0), sp=(ec == 7): E.matmul(
                    ps_s3[1][0:1, :T4], ones_bf[:], xq4_r[sl4][:], start=st, stop=sp), inc="mm")
                xb4_free[sl4] = mmv
                xq4_free[sl4] = mmv
                if ec == 7:
                    st3[u] = mmv
            return mmv

        z2r = stage_head(0)
        ksr = kcm_only(0, z2r)
        rcm_sig_half(0, 0)
        p.wait("gpsimd", "mm", ph3_end_mm)
        for fc in range(32):
            p.op("gpsimd", lambda E, fc=fc: E.dma_start(
                out=wval_sb[fc][:], in_=wval4[fc].rearrange("p a b -> p (a b)")), inc=p.chan(f"wv{fc // 4}"))
        for u in range(NT4):
            mvcmA = vcm_pass(u, 0, ksr, vcm_free)
            if u >= 1:
                stats_mms(u - 1, 1)
                pending.extend(smalls3_chain(u - 1, st3[u - 1]))
            rcm_sig_half(u, 1)
            z2t_free_mm = p.cnt["mm"]
            combine_half(u, 0, mvcmA)
            s3gate = p.cnt["vv"]
            if u + 1 < NT4:
                z2r = stage_head(u + 1)
            mvcmB = vcm_pass(u, 1, ksr, vcmA_free)
            stats_mms(u, 0, s3gate)
            if u + 1 < NT4:
                ksr = kcm_only(u + 1, z2r)
                rcm_sig_half(u + 1, 0)
            combine_half(u, 1, mvcmB)
            rows3_done[u] = None
        stats_mms(NT4 - 1, 1)
        pending.extend(smalls3_chain(NT4 - 1, st3[NT4 - 1]))
        drain_all_padded()
        rows3_all_done = p.cnt["rw3"]
        for u in range(NT4):
            rows3_done[u] = rows3_all_done
        ph4.close()
        ph4w.close()
        barrier_all()

        ph5 = ExitStack()
        sb5 = lambda n_, s_, d_=F32: ph5.enter_context(nc.sbuf_tensor(n_, list(s_), d_))
        ps5 = lambda n_, s_: ph5.enter_context(nc.psum_tensor(n_, list(s_), F32))
        OUT = [sb5(f"OUT{ec}", (128, HS)) for ec in range(C)]
        y3h = [sb5(f"y3h{ec}", (128, HS), BF16) for ec in range(C)]
        KEY = [sb5(f"KEY{i}", (128, 4096), BF16) for i in range(4)]
        VAL = [sb5(f"VAL{i}", (128, 4096), BF16) for i in range(4)]
        GFM = sb5("GFM", (32, HS))
        BC3r = [sb5(f"BC3r{i}", (128, T5)) for i in range(2)]
        BC3m = [sb5(f"BC3m{i}", (128, T5)) for i in range(2)]
        tmp5 = [sb5(f"tmp5{i}", (128, T5)) for i in range(2)]
        Gb = [sb5(f"Gb{i}", (128, 32)) for i in range(4)]
        GCb = [sb5(f"GCb{i}", (128, 32)) for i in range(4)]
        Mb = [sb5(f"Mb{i}", (128, 1)) for i in range(4)]
        MKb = [sb5(f"MKb{i}", (128, 32)) for i in range(4)]
        GFb = [sb5(f"GFb{i}", (128, 32)) for i in range(4)]
        GB = [sb5(f"GB{i}", (128, T5B)) for i in range(4)]
        SC = [[[sb5(f"SC{par}_{el}_{fc}", (128, T5B), BF16) for fc in range(4)] for el in range(GE)] for par in range(2)]
        PS_sc = [ps5(f"PS_sc{i}", (128, 512)) for i in range(2)]
        PS_out = [ps5(f"PS_out{i}", (128, 512)) for i in range(4)]

        kv_done = [0] * NE
        kv_free = [0] * 4
        out_done = 0
        p.chan("outw")
        gb_free = [0] * 4
        sc_free = [[[0] * 4 for _ in range(GE)] for _ in range(2)]
        sc_ready = [[[0] * 4 for _ in range(GE)] for _ in range(2)]
        sc_bank_free = [0] * 2
        sc_bank_free_ss = [0] * 2
        gdr_done_prev = 0
        gbatch_free = [0] * 4
        tr_read = 0
        outadd_last = 0
        for h in range(2):
            hoff = h * HS
            outacc_da = 0
            for ec2 in range(2):
                p.wait("sync", f"x1w{ec2}", x1spill[ec2])
            for ec in range(C):
                if h == 1:
                    p.wait("sync", "outw", out_done)
                    p.wait("sync", "vv", outadd_last)
                outacc_da = p.op("sync", lambda E, ec=ec, hoff=hoff: E.dma_start(
                    out=OUT[ec][:], in_=x1sp[ec * 128:(ec + 1) * 128, hoff:hoff + HS]), inc=p.chan("outld"))
            bc3_free = [0] * 2
            y3rec = [[0] * C for _ in range(NT5)]
            for u in range(NT5):
                ug = (hoff + u * T5) // T4
                up = u % 2
                p.wait("gpsimd", "rw3", rows3_done[ug])
                p.wait("gpsimd", "vv", bc3_free[up])
                p.op("gpsimd", lambda E, up=up, hoff=hoff, u=u: E.dma_start(
                    out=BC3r[up][:], in_=rows[3][0:1, hoff + u * T5:hoff + (u + 1) * T5].broadcast_to([128, T5])), inc=p.chan(f"bc3{up}"))
                bc_da = p.op("gpsimd", lambda E, up=up, hoff=hoff, u=u: E.dma_start(
                    out=BC3m[up][:], in_=rows[3][1:2, hoff + u * T5:hoff + (u + 1) * T5].broadcast_to([128, T5])), inc=f"bc3{up}")
                for ec in range(C):
                    tb = ec % 2
                    p.wait("vector", "outld", outacc_da)
                    p.wait("vector", f"bc3{up}", bc_da)
                    p.op("vector", lambda E, ec=ec, u=u, up=up, tb=tb: E.tensor_tensor(
                        out=tmp5[tb][:], in0=OUT[ec][:, u * T5:(u + 1) * T5], in1=BC3r[up][:], op=ALU.mult))
                    y3rec[u][ec] = p.op("vector", lambda E, ec=ec, u=u, up=up, tb=tb: E.tensor_tensor(
                        out=y3h[ec][:, u * T5:(u + 1) * T5], in0=tmp5[tb][:], in1=BC3m[up][:], op=ALU.subtract), inc="vv")
                bc3_free[up] = y3rec[u][C - 1]
            gfm_last = 0
            for scb in range(0, NSC, 4):
                gsig = [0] * 4
                for j in range(4):
                    sc = scb + j
                    sp5 = sc % 2
                    u5 = (sc * 128) // T5
                    p.wait("tensor", "vv", y3rec[u5][C - 1])
                    p.wait("tensor", "preg", pre_dw)
                    p.wait("tensor", "vv", sc_bank_free[sp5])
                    p.wait("tensor", "ss", sc_bank_free_ss[sp5])
                    ms = 0
                    for kc in range(C):
                        ms = p.op("tensor", lambda E, sc=sc, kc=kc, sp5=sp5, st=(kc == 0), sp=(kc == 7): E.matmul(
                            PS_sc[sp5][:, 0:32], y3h[kc][:, sc * 128:(sc + 1) * 128],
                            selT_sb[:, kc * 32:(kc + 1) * 32], start=st, stop=sp),
                            inc="mm" if kc == 7 else None)
                    g4 = sc % 4
                    p.wait("scalar", "mm", ms)
                    p.wait("scalar", "vv", gbatch_free[g4])
                    gsig[j] = p.op("scalar", lambda E, sp5=sp5, g4=g4: E.activation(
                        out=Gb[g4][:], in_=PS_sc[sp5][:, 0:32], func=AF.Exp, scale=-1.0), inc="ss")
                    sc_bank_free_ss[sp5] = gsig[j]
                chains4 = []
                for j in range(4):
                    sc = scb + j
                    g4 = sc % 4
                    q = []
                    q.append(("vector", lambda E, g4=g4: E.tensor_single_scalar(
                        out=Gb[g4][:], in_=Gb[g4][:], scalar=1.0, op=ALU.add), "vv", [("ss", gsig[j])]))
                    q.append(("vector", lambda E, g4=g4: E.reciprocal(out=Gb[g4][:], in_=Gb[g4][:]), "vv", []))
                    q.append(("vector", lambda E, g4=g4: E.tensor_copy(out=GCb[g4][:], in_=Gb[g4][:]), "vv", []))
                    for it in range(4):
                        q.append(("vector", lambda E, g4=g4: E.reduce_max(
                            out=Mb[g4][:], in_=GCb[g4][:], axis=AX.X), "vv", []))
                        if it < 3:
                            q.append(("vector", lambda E, g4=g4: E.tensor_tensor(
                                out=MKb[g4][:], in0=GCb[g4][:], in1=Mb[g4][:, 0:1].broadcast_to([128, 32]),
                                op=ALU.is_ge), "vv", []))
                            q.append(("vector", lambda E, g4=g4: E.scalar_tensor_tensor(
                                out=GCb[g4][:], in0=MKb[g4][:], scalar=-1e30, in1=GCb[g4][:],
                                op0=ALU.mult, op1=ALU.add), "vv", []))
                    q.append(("vector", lambda E, g4=g4: E.tensor_tensor(
                        out=MKb[g4][:], in0=Gb[g4][:], in1=Mb[g4][:, 0:1].broadcast_to([128, 32]),
                        op=ALU.is_ge), "vv", []))
                    q.append(("vector", lambda E, g4=g4: E.tensor_tensor(
                        out=GFb[g4][:], in0=Gb[g4][:], in1=MKb[g4][:], op=ALU.mult), "vv", []))
                    chains4.append(q)
                gf_rec = [0] * 4
                ncs5 = len(chains4[0])
                for si in range(ncs5):
                    for j in range(4):
                        eng, make, inc, waits = chains4[j][si]
                        for s_, v_ in waits:
                            p.wait(eng, s_, p.cnt[s_] if v_ is None else v_)
                        r_ = p.op(eng, make, inc=inc)
                        if si == ncs5 - 1:
                            gf_rec[j] = r_
                for j in range(4):
                    sc = scb + j
                    g4 = sc % 4
                    p.wait("tensor", "vv", gf_rec[j])
                    p.wait("tensor", "ss", tr_read)
                    p.wait("tensor", "vv", outadd_last)
                    mtr = p.op("tensor", lambda E, g4=g4: E.transpose(
                        out=PS_out[0][0:32, 0:128], in_=GFb[g4][:], identity=ident[:]), inc="mm")
                    p.wait("scalar", "mm", mtr)
                    p.wait("scalar", "gdr", gdr_done_prev)
                    tr_read = p.op("scalar", lambda E, sc=sc: E.activation(
                        out=GFM[0:32, sc * 128:(sc + 1) * 128], in_=PS_out[0][0:32, 0:128], func=AF.Copy), inc="ss")
                    gbatch_free[g4] = gf_rec[j]
                gfm_last = tr_read
            p.wait("sync", "ss", gfm_last)
            gdr_done = p.op("sync", lambda E, hoff=hoff: E.dma_start(
                out=gdr[0:32, hoff:hoff + HS], in_=GFM[:]), inc=p.chan("gdr"))
            gdr_done_prev = gdr_done
            seq = [(g, t) for g in range(NG) for t in range(NT5B)]

            def kv_load(g):
                for el in range(GE):
                    e = g * GE + el
                    slot = e % 4
                    p.wait("gpsimd", "mm", kv_free[slot])
                    p.op("gpsimd", lambda E, e=e, slot=slot: E.dma_start(
                        out=KEY[slot][:], in_=keysT[e]), inc=p.chan(f"kv{slot}"))
                    kv_done[e] = p.op("gpsimd", lambda E, e=e, slot=slot: E.dma_start(
                        out=VAL[slot][:], in_=valsT[e]), inc=f"kv{slot}")

            def front(i):
                g, t = seq[i]
                par = i % 2
                for el in range(GE):
                    e = g * GE + el
                    slot = e % 4
                    gbs = (i * GE + el) % 4
                    p.wait("gpsimd", "gdr", gdr_done)
                    p.wait("gpsimd", "vv", gb_free[gbs])
                    gb_da = p.op("gpsimd", lambda E, e=e, gbs=gbs, hoff=hoff, t=t: E.dma_start(
                        out=GB[gbs][:], in_=gdr[e:e + 1, hoff + t * T5B:hoff + (t + 1) * T5B].broadcast_to([128, T5B])), inc=p.chan(f"gb{gbs}"))
                    for fc in range(4):
                        fp = fc % 2
                        p.wait("tensor", f"kv{slot}", kv_done[e])
                        p.wait("tensor", "vv", y3rec[2 * t + 1][C - 1])
                        p.wait("tensor", "vv", sc_bank_free[fp])
                        p.wait("tensor", "ss", sc_bank_free_ss[fp])
                        mk = 0
                        for kc in range(C):
                            mk = p.op("tensor", lambda E, slot=slot, kc=kc, fc=fc, t=t, fp=fp,
                                      st=(kc == 0), sp=(kc == 7): E.matmul(
                                PS_sc[fp][:, 0:T5B],
                                KEY[slot][:, kc * 512 + fc * 128: kc * 512 + (fc + 1) * 128],
                                y3h[kc][:, t * T5B:(t + 1) * T5B], start=st, stop=sp),
                                inc="mm" if kc == 7 else None)
                        p.wait("vector", "mm", mk)
                        p.wait("vector", f"gb{gbs}", gb_da)
                        p.wait("vector", "mm", sc_free[par][el][fc])
                        vsc = p.op("vector", lambda E, par=par, el=el, fc=fc, fp=fp, gbs=gbs: E.scalar_tensor_tensor(
                            out=SC[par][el][fc][:], in0=PS_sc[fp][:, 0:T5B],
                            scalar=0.0, in1=GB[gbs][:], op0=ALU.max, op1=ALU.mult), inc="vv")
                        sc_bank_free[fp] = vsc
                        if fc == 3:
                            gb_free[gbs] = vsc
                        sc_ready[par][el][fc] = vsc
                if t == 1 and g + 1 < NG:
                    kv_load(g + 1)

            def values_pass(i, dh):
                nonlocal outadd_last, mvals_last
                g, t = seq[i]
                par = i % 2
                p.wait("tensor", "vv", sc_ready[par][GE - 1][3])
                p.wait("tensor", "vv", outadd_last)
                p.wait("tensor", "ss", gfm_last)
                mvals = 0
                for el in range(GE):
                    for fc in range(4):
                        for dcl in range(4):
                            dc = dh * 4 + dcl
                            mvals = p.op("tensor", lambda E, par=par, el=el, slot2=(g * GE + el) % 4,
                                         fc=fc, dc=dc, dcl=dcl,
                                         st=(el == 0 and fc == 0), sp=(el == GE - 1 and fc == 3): E.matmul(
                                PS_out[dcl][:, 0:T5B],
                                VAL[slot2][:, fc * 1024 + dc * 128: fc * 1024 + (dc + 1) * 128],
                                SC[par][el][fc][:], start=st, stop=sp), inc="mm" if dcl == 3 else None)
                p.wait("vector", "mm", mvals)
                for dcl in range(4):
                    dc = dh * 4 + dcl
                    outadd_last = p.op("vector", lambda E, dc=dc, dcl=dcl, t=t: E.tensor_tensor(
                        out=OUT[dc][:, t * T5B:(t + 1) * T5B], in0=OUT[dc][:, t * T5B:(t + 1) * T5B],
                        in1=PS_out[dcl][:, 0:T5B], op=ALU.add), inc="vv")
                mvals_last = mvals
                if dh == 1:
                    for el in range(GE):
                        for fc in range(4):
                            sc_free[par][el][fc] = mvals
                        kv_free[(g * GE + el) % 4] = mvals

            kv_load(0)
            mvals_last = 0
            front(0)
            for i in range(len(seq)):
                values_pass(i, 0)
                if i + 1 < len(seq):
                    front(i + 1)
                values_pass(i, 1)
            for ec in range(C):
                p.wait("sync", "vv", outadd_last)
                out_done = p.op("sync", lambda E, ec=ec, hoff=hoff: E.dma_start(
                    out=outT[ec * 128:(ec + 1) * 128, hoff:hoff + HS], in_=OUT[ec][:]), inc=p.chan("outw"))
        ph5.close()

        for s_name in list(p.cnt):
            p.wait("sync", s_name, p.cnt[s_name])

        @block.sync
        def _(E):
            p.replay("sync", E)

        @block.vector
        def _(E):
            p.replay("vector", E)

        @block.scalar
        def _(E):
            p.replay("scalar", E)

        @block.tensor
        def _(E):
            p.replay("tensor", E)

        @block.gpsimd
        def _(E):
            p.replay("gpsimd", E)

    es.close()
    return nc


def _host_layouts(inputs):
    f32 = np.float32
    A = lambda x: np.ascontiguousarray(np.asarray(x), dtype=f32)
    _ = inputs["L0_attn_time_first"]

    def t4(W):
        return np.ascontiguousarray(A(W).T.reshape(8, 128, 8, 128).transpose(2, 1, 0, 3))

    d = {}
    d["wr4"] = t4(inputs["L0_attn_Wr"])
    d["ww4"] = t4(inputs["L0_attn_Ww"])
    d["wk4"] = t4(inputs["L0_attn_Wk"])
    d["wv4"] = t4(inputs["L0_attn_Wv"])
    d["wo4"] = t4(inputs["L0_attn_Wo"])
    d["wrec4"] = t4(inputs["L0_attn_Wrec"])
    d["wkey4"] = np.ascontiguousarray(A(inputs["L0_attn_Wkey"]).T.reshape(8, 128, 32, 128).transpose(2, 1, 0, 3))
    d["wval4"] = np.ascontiguousarray(A(inputs["L0_attn_Wval"]).T.reshape(32, 128, 8, 128))
    d["selT3"] = np.ascontiguousarray(
        A(inputs["L0_ffn_expert_sel"]).T.reshape(8, 128, 32).transpose(1, 0, 2)).reshape(128, 256)
    d["keysT"] = np.ascontiguousarray(
        A(inputs["L0_ffn_keys"]).reshape(32, 8, 128, 512).transpose(0, 2, 1, 3)).reshape(32, 128, 4096)
    d["valsT"] = np.ascontiguousarray(
        A(inputs["L0_ffn_values"]).reshape(32, 4, 128, 1024).transpose(0, 2, 1, 3)).reshape(32, 128, 4096)
    for nm, key in (("g1v", "L0_attn_ln1_g"), ("b1v", "L0_attn_ln1_b"),
                    ("g2v", "L0_attn_ln2_g"), ("b2v", "L0_attn_ln2_b"),
                    ("tdv", "L0_attn_time_decay")):
        d[nm] = np.ascontiguousarray(A(inputs[key]).reshape(8, 128).T)
    d["onescol_bf"] = np.ones((128, 1), dtype=ml_dtypes.bfloat16)
    d["zcol_in"] = np.zeros((128, 1), dtype=f32)
    d["onecol_in"] = np.ones((128, 1), dtype=f32)
    d["onesrow_in"] = np.ones((1, 128), dtype=f32)
    d["epscol_in"] = np.full((128, 1), 1e-8, dtype=f32)
    d["ident_in"] = np.eye(128, dtype=f32)
    return d


def kernel(**inputs):
    nc = build_program()
    shared = _host_layouts(inputs)
    x = np.asarray(inputs["L0_attn_x"], dtype=np.float32)
    in_maps = []
    for b in range(8):
        m = dict(shared)
        m["xT"] = np.ascontiguousarray(x[b].T)
        in_maps.append(m)
    res = run_bass_kernel_spmd(nc, in_maps, core_ids=list(range(8)))
    out = np.stack([np.asarray(r["outT"]).T for r in res.results], axis=0)
    return np.ascontiguousarray(out, dtype=np.float32)


# version 20
# speedup vs baseline: 1.0284x; 1.0284x over previous
import numpy as np
import ml_dtypes
import concourse.bass as bass
import concourse.mybir as mybir
from concourse.bass_utils import run_bass_kernel_spmd
from contextlib import ExitStack

F32 = mybir.dt.float32
BF16 = mybir.dt.bfloat16
AF = mybir.ActivationFunctionType
ALU = mybir.AluOpType
AX = mybir.AxisListType

D = 1024
S = 4096
C = 8
T1 = 512
NT1 = S // T1
T4 = 256
NT4 = S // T4
HS = S // 2
T5 = 256
NT5 = HS // T5
T5B = 512
NT5B = HS // T5B
NSC = HS // 128
NE = 32
GE = 2
NG = NE // GE
LN_EPS = 1e-5
WKV_EPS = 1e-8

ENGINES = ("sync", "vector", "scalar", "tensor", "gpsimd")


class P:
    def __init__(self, nc, es, selfsync=False):
        self.nc = nc
        self.es = es
        self.sems = {}
        self.cnt = {}
        self.q = {e: [] for e in ENGINES}
        self.lastw = {}
        self.selfsync = selfsync
        for s_name in ("mm", "vv", "ss", "pp"):
            self.chan(s_name)
        self.lastmark = {}
        if selfsync:
            for e in ENGINES:
                self.chan(f"ord_{e}")

    def chan(self, name):
        if name in self.sems:
            return name
        self.sems[name] = self.es.enter_context(self.nc.semaphore(name))
        self.cnt[name] = 0
        for e in ENGINES:
            self.lastw[(e, name)] = 0
        return name

    def wait(self, eng, sem, val):
        if not val:
            return
        assert val <= self.cnt[sem], f"wait on future: {eng} {sem} {val} > {self.cnt[sem]}"
        if self.lastw[(eng, sem)] >= val:
            return
        self.lastw[(eng, sem)] = val
        h = self.sems[sem]
        self.q[eng].append(lambda E, h=h, v=val: E.wait_ge(h, v))

    def op(self, eng, make, inc=None):
        if self.selfsync:
            pm = self.lastmark.get(eng)
            if pm is not None:
                self.q[eng].append(lambda E, h=self.sems[pm[0]], v=pm[1]: E.wait_ge(h, v))
        if inc is None:
            if self.selfsync:
                ordname = f"ord_{eng}"
                self.cnt[ordname] += 1
                self.lastmark[eng] = (ordname, self.cnt[ordname])

                def f0(E, make=make, h=self.sems[ordname], n2=1):
                    inst = make(E)
                    if inst is not None:
                        inst.then_inc(h, n2)
                self.q[eng].append(f0)
            else:
                self.q[eng].append(make)
            return None
        n = 1 if inc in ("mm", "vv", "ss", "pp") else 16
        h = self.sems[inc]
        self.cnt[inc] += n
        if self.selfsync:
            self.lastmark[eng] = (inc, self.cnt[inc])

        def f(E, make=make, h=h, n=n):
            make(E).then_inc(h, n)
        self.q[eng].append(f)
        return self.cnt[inc]

    def replay(self, eng, E):
        for f in self.q[eng]:
            f(E)


def build_program(debug=False, selfsync=False):
    nc = bass.Bass("TRN2", target_bir_lowering=False, debug=False)

    dt_in = lambda name, shape: nc.dram_tensor(name, list(shape), F32, kind="ExternalInput").ap()
    xT = dt_in("xT", (D, S))
    w4 = {nm: dt_in(nm, (C, 128, C, 128)) for nm in ("wr4", "ww4", "wk4", "wv4", "wo4", "wrec4")}
    wkey4 = dt_in("wkey4", (32, 128, 8, 128))
    wval4 = dt_in("wval4", (32, 128, 8, 128))
    selT3 = dt_in("selT3", (128, 256))
    keysT = dt_in("keysT", (NE, 128, 4096))
    valsT = dt_in("valsT", (NE, 128, 4096))
    g1v = dt_in("g1v", (128, C)); b1v = dt_in("b1v", (128, C))
    g2v = dt_in("g2v", (128, C)); b2v = dt_in("b2v", (128, C))
    tdv = dt_in("tdv", (128, C))
    onescol_bf = nc.dram_tensor("onescol_bf", [128, 1], BF16, kind="ExternalInput").ap()
    zcol_in = dt_in("zcol_in", (128, 1))
    onecol_in = dt_in("onecol_in", (128, 1))
    onesrow_in = dt_in("onesrow_in", (1, 128))
    epscol_in = dt_in("epscol_in", (128, 1))
    ident_in = dt_in("ident_in", (128, 128))

    kind_sp = "ExternalOutput" if debug else "Internal"
    y0sp = nc.dram_tensor("y0sp", [D, S], F32, kind=kind_sp).ap()
    y1sp = nc.dram_tensor("y1sp", [D, S], F32, kind=kind_sp).ap()
    x1sp = nc.dram_tensor("x1sp", [D, S], F32, kind=kind_sp).ap()
    gdr = nc.dram_tensor("gdr", [32, S], F32, kind=kind_sp).ap()
    rows = [nc.dram_tensor(f"rows{i}", [2, S], F32, kind=kind_sp).ap() for i in range(4)]
    if debug:
        z1dump = nc.dram_tensor("z1dump", [D, S], F32, kind="ExternalOutput").ap()
        rwkvdump = nc.dram_tensor("rwkvdump", [D, S], F32, kind="ExternalOutput").ap()
    outT = nc.dram_tensor("outT", [D, S], F32, kind="ExternalOutput").ap()

    es = ExitStack()
    sb = lambda name, shape, dt=F32: es.enter_context(nc.sbuf_tensor(name, list(shape), dt))
    p = P(nc, es, selfsync=selfsync)

    g1s = sb("g1s", (128, C)); b1s = sb("b1s", (128, C))
    g2s = sb("g2s", (128, C)); b2s = sb("b2s", (128, C))
    tds = sb("tds", (128, C)); decays = sb("decays", (128, C))
    ones_bf = sb("ones_bf", (128, 1), BF16)
    zcol = sb("zcol", (128, 1))
    onecol = sb("onecol", (128, 1))
    onesrow = sb("onesrow", (1, 128))
    epscol = sb("epscol", (128, 1))
    ident = sb("ident", (128, 128))
    selT_sb = sb("selT_sb", (128, 256), BF16)

    padA = sb("padA", (128, 64))
    padB = sb("padB", (128, 64))
    z1_es = ExitStack()
    z1 = [z1_es.enter_context(nc.sbuf_tensor(f"z1_{c}", [128, S], BF16)) for c in range(C)]

    pending = []

    def drain_one():
        if not pending:
            return
        eng, make, inc, waits = pending.pop(0)
        for s_, v_ in waits:
            p.wait(eng, s_, p.cnt[s_] if v_ is None else v_)
        p.op(eng, make, inc=inc)

    def barrier_all():
        bar = dict(p.cnt)
        for eng in ENGINES:
            for s_ in bar:
                p.wait(eng, s_, bar[s_])

    def drain_all_padded():
        while pending:
            drain_one()
            for _ in range(2):
                p.op("vector", lambda E: E.tensor_copy(out=padA[:], in_=padB[:]))
                p.op("vector", lambda E: E.tensor_copy(out=padB[:], in_=padA[:]))

    def ln_smalls_chain(arena, psA, psB, rows_dram, rowch, col0, width, extra_first_waits):
        A = arena[0:1, 0:width]
        B = arena[0:1, width:2 * width]
        Cc = arena[0:1, 2 * width:3 * width]
        steps = []
        steps.append(("vector", lambda E: E.tensor_single_scalar(
            out=A, in_=psA[0:1, :width], scalar=1.0 / D, op=ALU.mult), "vv",
            list(extra_first_waits)))
        steps.append(("vector", lambda E: E.tensor_single_scalar(
            out=B, in_=psB[0:1, :width], scalar=1.0 / D, op=ALU.mult), "vv", []))
        steps.append(("vector", lambda E: E.tensor_tensor(
            out=Cc, in0=A, in1=A, op=ALU.mult), "vv", []))
        steps.append(("vector", lambda E: E.tensor_tensor(
            out=B, in0=B, in1=Cc, op=ALU.subtract), "vv", []))
        steps.append(("vector", lambda E: E.tensor_single_scalar(
            out=B, in_=B, scalar=LN_EPS, op=ALU.add), "vv", []))
        steps.append(("scalar", lambda E: E.activation(
            out=Cc, in_=B, func=AF.Ln), "ss", [("vv", None)]))
        steps.append(("scalar", lambda E: E.activation(
            out=B, in_=Cc, func=AF.Exp, scale=-0.5), "ss", []))
        steps.append(("vector", lambda E: E.tensor_tensor(
            out=A, in0=A, in1=B, op=ALU.mult), "vv", [("ss", None)]))
        if rows_dram is not None:
            steps.append(("scalar", lambda E: E.dma_start(
                out=rows_dram[0:1, col0:col0 + width], in_=B), rowch, [("vv", None)]))
            steps.append(("scalar", lambda E: E.dma_start(
                out=rows_dram[1:2, col0:col0 + width], in_=A), rowch, [("vv", None)]))
        return steps

    with nc.Block() as block:
        for dst, src in ((g1s, g1v), (b1s, b1v), (g2s, g2v), (b2s, b2v), (tds, tdv),
                         (zcol, zcol_in), (onecol, onecol_in), (epscol, epscol_in),
                         (onesrow, onesrow_in), (ident, ident_in)):
            pre_da = p.op("sync", lambda E, d=dst, s=src: E.dma_start(out=d[:], in_=s[:]), inc=p.chan("pre"))
        p.op("gpsimd", lambda E: E.dma_start(out=ones_bf[:], in_=onescol_bf[:]), inc=p.chan("preg"))
        pre_dw = p.op("gpsimd", lambda E: E.dma_start(out=selT_sb[:], in_=selT3[:]), inc="preg")
        p.wait("scalar", "pre", pre_da)
        decays_done = p.op("scalar", lambda E: E.activation(out=decays[:], in_=tds[:], func=AF.Exp), inc="ss")
        p.op("vector", lambda E: E.memset(padA[:], 0.0))
        p.op("vector", lambda E: E.memset(padB[:], 0.0))

        ph1 = ExitStack()
        sb1 = lambda n_, s_, d_=F32: ph1.enter_context(nc.sbuf_tensor(n_, list(s_), d_))
        ps1 = lambda n_, s_: ph1.enter_context(nc.psum_tensor(n_, list(s_), F32))
        XC = [[sb1(f"XC{r}_{c}", (128, T1)) for c in range(C)] for r in range(2)]
        Y0C = [[sb1(f"Y0C{r}_{c}", (128, T1)) for c in range(C)] for r in range(2)]
        xb_r = [sb1(f"xb{i}", (128, T1), BF16) for i in range(2)]
        xq_r = [sb1(f"xq{i}", (128, T1), BF16) for i in range(2)]
        tmp_r = [sb1(f"tmp{i}", (128, T1)) for i in range(2)]
        rt1 = [sb1(f"rt1_{i}", (1, 3 * T1)) for i in range(4)]
        ps_st0 = [ps1(f"ps_st0{i}", (128, T1)) for i in range(2)]
        ps_st1 = [ps1(f"ps_st1{i}", (128, T1)) for i in range(2)]
        ps_bc0 = [ps1(f"ps_bc0{i}", (128, T1)) for i in range(2)]
        ps_bc1 = [ps1(f"ps_bc1{i}", (128, T1)) for i in range(2)]

        for r in range(2):
            for c in range(C):
                p.chan(f"xc{r}_{c}")
                p.chan(f"y0w{c}")
        xc_free = [[0] * C for _ in range(2)]
        y0c_spill = [[("y0w0", 0)] * C for _ in range(2)]
        y0c_sq = [[0] * C for _ in range(2)]
        xb_free = [0] * 2
        xq_free = [0] * 2
        st0_read = 0
        tmp_ss = [0, 0]
        st1_read = 0
        bc0_read = 0
        bc1_read = 0
        st0_rec = [0] * NT1
        st1_rec = [0] * NT1
        ch0_rec = [0] * NT1
        ch1_rec = [0] * NT1
        bc0_mm = [0] * NT1
        bc1_mm = [0] * NT1
        y0sp_done = [[0] * C for _ in range(NT1)]
        z1_done_t = [0] * NT1
        p.wait("tensor", "preg", pre_dw)
        p.wait("tensor", "pre", pre_da)

        def stA(t):
            r = t % 2
            for c in range(C):
                p.wait("sync", "vv", xc_free[r][c])
                p.op("sync", lambda E, t=t, c=c, r=r: E.dma_start(
                    out=XC[r][c][:], in_=xT[c * 128:(c + 1) * 128, t * T1:(t + 1) * T1]), inc=f"xc{r}_{c}")

        def stB(t):
            r = t % 2
            for c in range(C):
                b = c % 2
                p.wait("vector", f"xc{r}_{c}", p.cnt[f"xc{r}_{c}"])
                p.wait("vector", "mm", xb_free[b])
                vb = p.op("vector", lambda E, b=b, r=r, c=c: E.tensor_copy(out=xb_r[b][:], in_=XC[r][c][:]), inc="vv")
                p.wait("scalar", f"xc{r}_{c}", p.cnt[f"xc{r}_{c}"])
                p.wait("scalar", "mm", xq_free[b])
                sq = p.op("scalar", lambda E, b=b, r=r, c=c: E.activation(
                    out=xq_r[b][:], in_=XC[r][c][:], func=AF.Square), inc="ss")
                p.wait("tensor", "vv", vb)
                p.wait("tensor", "ss", sq)
                if c == 0:
                    p.wait("tensor", "vv", st0_read)
                p.op("tensor", lambda E, b=b, st=(c == 0), sp=(c == 7): E.matmul(
                    ps_st0[0][0:1, :], ones_bf[:], xb_r[b][:], start=st, stop=sp))
                mmv = p.op("tensor", lambda E, b=b, st=(c == 0), sp=(c == 7): E.matmul(
                    ps_st0[1][0:1, :], ones_bf[:], xq_r[b][:], start=st, stop=sp), inc="mm")
                xb_free[b] = mmv
                xq_free[b] = mmv
                drain_one()
            st0_rec[t] = mmv
            pending.extend(ln_smalls_chain(rt1[t % 4], ps_st0[0], ps_st0[1], None, None, 0, T1,
                                           [("mm", st0_rec[t])]))

        def stD(t):
            nonlocal st0_read
            ch0_rec[t] = p.cnt["vv"]
            st0_read = ch0_rec[t]
            A_ = rt1[t % 4][0:1, 0:T1]
            B_ = rt1[t % 4][0:1, T1:2 * T1]
            p.wait("tensor", "vv", ch0_rec[t])
            p.wait("tensor", "vv", bc0_read)
            p.op("tensor", lambda E, B_=B_: E.matmul(ps_bc0[0][:], onesrow[:], B_, start=True, stop=True))
            bc0_mm[t] = p.op("tensor", lambda E, A_=A_: E.matmul(
                ps_bc0[1][:], onesrow[:], A_, start=True, stop=True), inc="mm")

        def stE(t):
            nonlocal bc0_read
            r = t % 2
            for c in range(C):
                b = c % 2
                tb = c % 2
                p.wait("vector", "mm", bc0_mm[t])
                wch, wval = y0c_spill[r][c]
                p.wait("vector", wch, wval)
                p.wait("vector", "ss", y0c_sq[r][c])
                p.wait("vector", "ss", tmp_ss[tb])
                p.op("vector", lambda E, r=r, c=c, tb=tb: E.tensor_tensor(
                    out=tmp_r[tb][:], in0=XC[r][c][:], in1=ps_bc0[0][:], op=ALU.mult))
                vy0 = p.op("vector", lambda E, r=r, c=c, tb=tb: E.tensor_tensor(
                    out=Y0C[r][c][:], in0=tmp_r[tb][:], in1=ps_bc0[1][:], op=ALU.subtract), inc="vv")
                xc_free[r][c] = vy0
                p.wait("sync", "vv", vy0)
                y0da = p.op("sync", lambda E, t=t, c=c, r=r: E.dma_start(
                    out=y0sp[c * 128:(c + 1) * 128, t * T1:(t + 1) * T1], in_=Y0C[r][c][:]), inc=f"y0w{c}")
                y0sp_done[t][c] = (f"y0w{c}", y0da)
                y0c_spill[r][c] = (f"y0w{c}", y0da)
                p.wait("vector", "mm", xb_free[b])
                vb = p.op("vector", lambda E, b=b, r=r, c=c: E.tensor_copy(out=xb_r[b][:], in_=Y0C[r][c][:]), inc="vv")
                p.wait("scalar", "vv", vy0)
                p.wait("scalar", "mm", xq_free[b])
                sq = p.op("scalar", lambda E, b=b, r=r, c=c: E.activation(
                    out=xq_r[b][:], in_=Y0C[r][c][:], func=AF.Square), inc="ss")
                y0c_sq[r][c] = sq
                p.wait("tensor", "vv", vb)
                p.wait("tensor", "ss", sq)
                if c == 0:
                    p.wait("tensor", "vv", st1_read)
                p.op("tensor", lambda E, b=b, st=(c == 0), sp=(c == 7): E.matmul(
                    ps_st1[0][0:1, :], ones_bf[:], xb_r[b][:], start=st, stop=sp))
                mmv = p.op("tensor", lambda E, b=b, st=(c == 0), sp=(c == 7): E.matmul(
                    ps_st1[1][0:1, :], ones_bf[:], xq_r[b][:], start=st, stop=sp), inc="mm")
                xb_free[b] = mmv
                xq_free[b] = mmv
                drain_one()
            bc0_read = p.cnt["vv"]
            st1_rec[t] = mmv
            pending.extend(ln_smalls_chain(rt1[2 + t % 2] if False else rt1[(t % 4)], ps_st1[0], ps_st1[1], None, None, 0, T1,
                                           [("mm", st1_rec[t])]))

        def stG(t):
            nonlocal st1_read
            ch1_rec[t] = p.cnt["vv"]
            st1_read = ch1_rec[t]
            A_ = rt1[t % 4][0:1, 0:T1]
            B_ = rt1[t % 4][0:1, T1:2 * T1]
            p.wait("tensor", "vv", ch1_rec[t])
            p.wait("tensor", "vv", bc1_read)
            p.op("tensor", lambda E, B_=B_: E.matmul(ps_bc1[0][:], onesrow[:], B_, start=True, stop=True))
            bc1_mm[t] = p.op("tensor", lambda E, A_=A_: E.matmul(
                ps_bc1[1][:], onesrow[:], A_, start=True, stop=True), inc="mm")

        def stH(t):
            nonlocal bc1_read
            r = t % 2
            for c in range(C):
                tb = c % 2
                p.wait("vector", "mm", bc1_mm[t])
                p.wait("vector", "ss", tmp_ss[tb])
                p.op("vector", lambda E, r=r, c=c, tb=tb: E.tensor_tensor(
                    out=tmp_r[tb][:], in0=Y0C[r][c][:], in1=ps_bc1[0][:], op=ALU.mult))
                vt2 = p.op("vector", lambda E, tb=tb: E.tensor_tensor(
                    out=tmp_r[tb][:], in0=tmp_r[tb][:], in1=ps_bc1[1][:], op=ALU.subtract), inc="vv")
                p.wait("scalar", "vv", vt2)
                sz = p.op("scalar", lambda E, t=t, c=c, tb=tb: E.activation(
                    out=z1[c][:, t * T1:(t + 1) * T1], in_=tmp_r[tb][:], func=AF.Identity,
                    scale=g1s[:, c:c + 1], bias=b1s[:, c:c + 1]), inc="ss")
                tmp_ss[tb] = sz
                drain_one()
            z1_done_t[t] = p.cnt["ss"]
            bc1_read = p.cnt["vv"]

        for s in range(NT1 + 1):
            if s < NT1:
                stA(s)
                stB(s)
                if s == 0:
                    drain_all_padded()
            if s >= 1:
                stG(s - 1)
                stH(s - 1)
            if s < NT1:
                stD(s)
                stE(s)
        drain_all_padded()
        ph1.close()
        barrier_all()

        rwkv_es = ExitStack()
        rwkv = [rwkv_es.enter_context(nc.sbuf_tensor(f"rwkv{c}", [128, S], BF16, side="right")) for c in range(C)]
        ph2 = ExitStack()
        sb2 = lambda n_, s_, d_=F32: ph2.enter_context(nc.sbuf_tensor(n_, list(s_), d_))
        ps2 = lambda n_, s_: ph2.enter_context(nc.psum_tensor(n_, list(s_), F32))
        wnames = ("wr", "ww", "wk", "wv")
        wproj = {nm: [sb2(f"{nm}_{i}", (128, 1024), BF16) for i in range(2)] for nm in wnames}
        EWKD = sb2("EWKD", (128, T1 + 1))
        NUMD = sb2("NUMD", (128, T1 + 1))
        NUMO = sb2("NUMO", (128, T1 + 1))
        DENO = sb2("DENO", (128, T1 + 1))
        Dscan = sb2("Dscan", (128, T1 + 1))
        EWb = [sb2(f"EWb{i}", (128, T1)) for i in range(2)]
        Ub = [sb2(f"Ub{i}", (128, T1)) for i in range(2)]
        RREb = [sb2(f"RREb{i}", (128, T1)) for i in range(2)]
        Rb = [sb2(f"Rb{i}", (128, T1)) for i in range(2)]
        W1 = sb2("W1", (128, T1))
        W2 = sb2("W2", (128, T1))
        PSp = {nm: [ps2(f"PS{nm}{i}", (128, T1)) for i in range(2)] for nm in ("w", "k", "r", "v")}

        proj_src = {"wr": w4["wr4"], "ww": w4["ww4"], "wk": w4["wk4"], "wv": w4["wv4"]}
        wproj_done = [0] * C
        wslot_free = [0] * 2

        p.chan("wp0"); p.chan("wp1")

        def load_proj(ec):
            slot = ec % 2
            p.wait("gpsimd", "mm", wslot_free[slot])
            v = 0
            for nm in wnames:
                v = p.op("gpsimd", lambda E, nm=nm, ec=ec, s=slot: E.dma_start(
                    out=wproj[nm][s][:], in_=proj_src[nm][ec].rearrange("p a b -> p (a b)")), inc=f"wp{slot}")
            wproj_done[ec] = v

        load_proj(0)
        load_proj(1)

        p.wait("vector", "pre", pre_da)
        p.wait("vector", "ss", decays_done)
        bank_free_v = {}
        bank_free_s = {}
        ewkd_read = 0
        wkv_done = 0
        carry_rd = 0
        vRWKV = [[0] * NT1 for _ in range(C)]
        for ec in range(C):
            p.wait("vector", "vv", ewkd_read)
            p.op("vector", lambda E, ec=ec: E.tensor_copy(
                out=Dscan[:, 1:T1 + 1], in_=decays[:, ec:ec + 1].broadcast_to([128, T1])))
            p.op("vector", lambda E: E.tensor_copy(out=Dscan[:, 0:1], in_=zcol[:]))
            p.op("vector", lambda E: E.tensor_copy(out=NUMD[:, 0:1], in_=zcol[:]))
            dscan_v = ewkd_carry = p.op("vector", lambda E: E.tensor_copy(out=EWKD[:, 0:1], in_=zcol[:]), inc="vv")
            if ec + 1 < C:
                load_proj(ec + 1)
            for t in range(NT1):
                tp = t % 2
                prev = (ec, t - 2) if t >= 2 else ((ec - 1, NT1 - 2 + t) if ec > 0 else None)
                p.wait("tensor", f"wp{ec % 2}", wproj_done[ec])
                p.wait("tensor", "ss", z1_done_t[t])
                if prev is not None:
                    p.wait("tensor", "vv", bank_free_v[prev])
                    p.wait("tensor", "ss", bank_free_s[prev])
                mg = {}
                for nm, wsb in (("w", "ww"), ("k", "wk"), ("r", "wr"), ("v", "wv")):
                    for kc in range(C):
                        p.op("tensor", lambda E, nm=nm, wsb=wsb, kc=kc, t=t, tp=tp, ec=ec,
                             st=(kc == 0), sp=(kc == 7): E.matmul(
                            PSp[nm][tp][:], wproj[wsb][ec % 2][:, kc * 128:(kc + 1) * 128],
                            z1[kc][:, t * T1:(t + 1) * T1], start=st, stop=sp),
                            inc="mm" if kc == 7 else None)
                    mg[nm] = p.cnt["mm"]
                p.wait("scalar", "mm", mg["w"])
                sEW = p.op("scalar", lambda E, tp=tp: E.activation(
                    out=EWb[tp][:], in_=PSp["w"][tp][:], func=AF.Exp), inc="ss")
                p.wait("vector", "mm", mg["k"])
                p.wait("vector", "ss", sEW)
                vU = p.op("vector", lambda E, tp=tp: E.tensor_tensor(
                    out=Ub[tp][:], in0=PSp["k"][tp][:], in1=EWb[tp][:], op=ALU.subtract), inc="vv")
                p.wait("scalar", "vv", vU)
                p.wait("scalar", "vv", ewkd_read)
                p.wait("scalar", "vv", ewkd_carry)
                sEWK = p.op("scalar", lambda E, tp=tp: E.activation(
                    out=EWKD[:, 1:T1 + 1], in_=Ub[tp][:], func=AF.Exp), inc="ss")
                p.wait("scalar", "mm", mg["r"])
                sRRE = p.op("scalar", lambda E, tp=tp: E.activation(
                    out=RREb[tp][:], in_=PSp["r"][tp][:], func=AF.Exp, scale=-1.0), inc="ss")
                p.wait("vector", "ss", sRRE)
                p.op("vector", lambda E, tp=tp: E.tensor_single_scalar(
                    out=RREb[tp][:], in_=RREb[tp][:], scalar=1.0, op=ALU.add))
                p.op("vector", lambda E, tp=tp: E.reciprocal(out=Rb[tp][:], in_=RREb[tp][:]))
                p.wait("vector", "ss", sEWK)
                p.wait("vector", "mm", mg["v"])
                vN = p.op("vector", lambda E, tp=tp: E.tensor_tensor(
                    out=NUMD[:, 1:T1 + 1], in0=EWKD[:, 1:T1 + 1], in1=PSp["v"][tp][:], op=ALU.mult), inc="vv")
                p.op("vector", lambda E: E.tensor_tensor_scan(
                    out=NUMO[:], data0=Dscan[:], data1=NUMD[:], initial=0.0, op0=ALU.mult, op1=ALU.add))
                vden = p.op("vector", lambda E: E.tensor_tensor_scan(
                    out=DENO[:], data0=Dscan[:], data1=EWKD[:], initial=0.0, op0=ALU.mult, op1=ALU.add), inc="vv")
                ewkd_read = vden
                p.op("vector", lambda E: E.tensor_single_scalar(
                    out=W1[:], in_=DENO[:, 1:T1 + 1], scalar=WKV_EPS, op=ALU.add))
                p.op("vector", lambda E: E.reciprocal(out=W2[:], in_=W1[:]))
                wkv_done = p.op("vector", lambda E: E.tensor_tensor(
                    out=W1[:], in0=NUMO[:, 1:T1 + 1], in1=W2[:], op=ALU.mult), inc="vv")
                vR = p.op("vector", lambda E, ec=ec, t=t, tp=tp: E.tensor_tensor(
                    out=rwkv[ec][:, t * T1:(t + 1) * T1], in0=Rb[tp][:], in1=W1[:], op=ALU.mult), inc="vv")
                vRWKV[ec][t] = ("vv", vR)
                if t < NT1 - 1:
                    p.op("vector", lambda E: E.tensor_copy(out=NUMD[:, 0:1], in_=NUMO[:, T1:T1 + 1]))
                    carry_rd = ewkd_carry = p.op("vector", lambda E: E.tensor_copy(
                        out=EWKD[:, 0:1], in_=DENO[:, T1:T1 + 1]), inc="vv")
                bank_free_v[(ec, t)] = vN
                bank_free_s[(ec, t)] = sRRE
            wslot_free[ec % 2] = p.cnt["mm"]
        if debug:
            for c in range(C):
                p.wait("gpsimd", vRWKV[c][NT1 - 1][0], vRWKV[c][NT1 - 1][1])
                p.op("gpsimd", lambda E, c=c: E.dma_start(
                    out=rwkvdump[c * 128:(c + 1) * 128, :], in_=rwkv[c][:]), inc=p.chan("dbg"))
        ph2_end_mm = p.cnt["mm"]
        ph2.close()
        z1_es.close()
        ph4w = ExitStack()
        wkey_sb = [ph4w.enter_context(nc.sbuf_tensor(f"wkeysb{fc}", [128, 1024], BF16)) for fc in range(32)]
        barrier_all()
        wkey_done = 0
        p.chan("wk"); p.chan("wv"); p.chan("wr")
        p.wait("gpsimd", "mm", ph2_end_mm)
        for fc in range(32):
            wkey_done = p.op("gpsimd", lambda E, fc=fc: E.dma_start(
                out=wkey_sb[fc][:], in_=wkey4[fc].rearrange("p a b -> p (a b)")), inc="wk")

        ph3 = ExitStack()
        sb3 = lambda n_, s_, d_=F32: ph3.enter_context(nc.sbuf_tensor(n_, list(s_), d_, side="right"))
        ps3 = lambda n_, s_: ph3.enter_context(nc.psum_tensor(n_, list(s_), F32))
        wo_sb = [sb3(f"wo_sb{e}", (128, 1024), BF16) for e in range(C)]
        PSo = [ps3(f"PSo{i}", (128, T1)) for i in range(4)]
        ps_s2 = [ps3(f"ps_s2{i}", (128, T1)) for i in range(4)]
        rt3 = [sb3(f"rt3_{i}", (1, 3 * T1)) for i in range(2)]
        y0l_r = [sb3(f"y0l{i}", (128, T1)) for i in range(4)]
        y1b_r = [sb3(f"y1b{i}", (128, T1)) for i in range(3)]
        xb3_r = [sb3(f"xb3{i}", (128, T1), BF16) for i in range(8)]
        xq3_r = [sb3(f"xq3{i}", (128, T1), BF16) for i in range(8)]

        p.wait("gpsimd", "mm", ph2_end_mm)
        wo_done = 0
        p.chan("wo")
        for e in range(C):
            wo_done = p.op("gpsimd", lambda E, e=e: E.dma_start(
                out=wo_sb[e][:], in_=w4["wo4"][e].rearrange("p a b -> p (a b)")), inc="wo")

        y0l_free3 = [(None, 0)] * 4
        y1b_cons = [(0, 0)] * 3
        xb3_free = [0] * 8
        xq3_free = [0] * 8
        vb_sem3 = [0] * 8
        sq_sem3 = [0] * 8
        pso_free = [0] * 4
        s2_bank_free = [0] * 2
        y1sp_done = [[0] * C for _ in range(NT1)]
        st2 = [0] * NT1

        def stats_mms3(i):
            tq, cq = i // 8, i % 8
            tpar = tq % 2
            p.wait("tensor", "vv", vb_sem3[cq])
            p.wait("tensor", "ss", sq_sem3[cq])
            if cq == 0:
                p.wait("tensor", "vv", s2_bank_free[tpar])
            p.op("tensor", lambda E, cq=cq, tpar=tpar, st=(cq == 0), sp=(cq == 7): E.matmul(
                ps_s2[2 * tpar][0:1, :], ones_bf[:], xb3_r[cq][:], start=st, stop=sp))
            mmv = p.op("tensor", lambda E, cq=cq, tpar=tpar, st=(cq == 0), sp=(cq == 7): E.matmul(
                ps_s2[2 * tpar + 1][0:1, :], ones_bf[:], xq3_r[cq][:], start=st, stop=sp), inc="mm")
            xb3_free[cq] = mmv
            xq3_free[cq] = mmv
            if cq == 7:
                st2[tq] = mmv
                pending.extend(ln_smalls_chain(rt3[tq % 2], ps_s2[2 * (tq % 2)], ps_s2[2 * (tq % 2) + 1],
                                               rows[2], p.chan("rw2"), tq * T1, T1,
                                               [("mm", st2[tq]), ("rw2", None)]))

        for t in range(NT1):
            for c in range(C):
                po = c % 4
                i = t * C + c
                slot = i % 4
                sem, val = y0l_free3[slot]
                if sem is not None:
                    p.wait("sync", sem, val)
                wch3, wv3 = y0sp_done[t][c]
                p.wait("sync", wch3, wv3)
                da = p.op("sync", lambda E, t=t, c=c, s=slot: E.dma_start(
                    out=y0l_r[s][:], in_=y0sp[c * 128:(c + 1) * 128, t * T1:(t + 1) * T1]), inc=p.chan(f"yl{slot}"))
                rch, rval = vRWKV[C - 1][t]
                p.wait("tensor", rch, rval)
                p.wait("tensor", "wo", wo_done)
                p.wait("tensor", "vv", pso_free[po])
                mo = 0
                for kc in range(C):
                    mo = p.op("tensor", lambda E, c=c, kc=kc, t=t, po=po, st=(kc == 0), sp=(kc == 7): E.matmul(
                        PSo[po][:], wo_sb[c][:, kc * 128:(kc + 1) * 128],
                        rwkv[kc][:, t * T1:(t + 1) * T1], start=st, stop=sp),
                        inc="mm" if kc == 7 else None)
                if i >= 2:
                    stats_mms3(i - 2)
                yb = i % 3
                ss1, da1 = y1b_cons[yb]
                p.wait("vector", "mm", mo)
                p.wait("vector", f"yl{slot}", da)
                p.wait("vector", "ss", ss1)
                p.wait("vector", f"y1w{yb}", da1)
                vy1 = p.op("vector", lambda E, po=po, s=slot, y=yb: E.tensor_tensor(
                    out=y1b_r[y][:], in0=PSo[po][:], in1=y0l_r[s][:], op=ALU.add), inc="vv")
                pso_free[po] = vy1
                y0l_free3[slot] = ("vv", vy1)
                drain_one()
                p.wait("sync", "vv", vy1)
                y1da = p.op("sync", lambda E, t=t, c=c, y=yb: E.dma_start(
                    out=y1sp[c * 128:(c + 1) * 128, t * T1:(t + 1) * T1], in_=y1b_r[y][:]), inc=p.chan(f"y1w{yb}"))
                y1sp_done[t][c] = (f"y1w{yb}", y1da)
                p.wait("vector", "mm", xb3_free[c])
                vb = p.op("vector", lambda E, c=c, y=yb: E.tensor_copy(out=xb3_r[c][:], in_=y1b_r[y][:]), inc="vv")
                p.wait("scalar", "vv", vy1)
                p.wait("scalar", "mm", xq3_free[c])
                sq = p.op("scalar", lambda E, c=c, y=yb: E.activation(
                    out=xq3_r[c][:], in_=y1b_r[y][:], func=AF.Square), inc="ss")
                y1b_cons[yb] = (sq, y1da)
                vb_sem3[c] = vb
                sq_sem3[c] = sq
                drain_one()
            if t >= 1:
                s2_bank_free[(t - 1) % 2] = p.cnt["vv"]
        stats_mms3(NT1 * C - 2)
        stats_mms3(NT1 * C - 1)
        drain_all_padded()
        rows2_all_done = p.cnt["rw2"]
        ph3_end_mm = p.cnt["mm"]
        ph3_tail_vv = p.cnt["vv"]
        ph3_tail_ss = p.cnt["ss"]
        y1w_totals = [p.cnt[f"y1w{y}"] for y in range(3)]
        ph3.close()
        rwkv_es.close()
        wval_sb = [ph4w.enter_context(nc.sbuf_tensor(f"wvalsb{fc}", [128, 1024], BF16, side="right")) for fc in range(32)]

        ph4 = ExitStack()
        sb4 = lambda n_, s_, d_=F32: ph4.enter_context(nc.sbuf_tensor(n_, list(s_), d_))
        ps4 = lambda n_, s_: ph4.enter_context(nc.psum_tensor(n_, list(s_), F32))
        wrec_sb = [sb4(f"wrecsb{e}", (128, 1024), BF16) for e in range(C)]
        BC2r = [sb4(f"BC2r{i}", (128, T4)) for i in range(2)]
        BC2m = [sb4(f"BC2m{i}", (128, T4)) for i in range(2)]
        Y1T = [[sb4(f"Y1T{j}_{ec}", (128, T4)) for ec in range(C)] for j in range(2)]
        xring = [sb4(f"xring{i}", (128, T4)) for i in range(2)]
        z2t = [sb4(f"z2t{ec}", (128, T4), BF16) for ec in range(C)]
        tmp4 = [sb4(f"tmp4{i}", (128, T4)) for i in range(1)]
        KRb = sb4("KRb", (128, T4))
        KSb = [sb4(f"KS{i}", (128, T4), BF16) for i in range(32)]
        RC2 = [sb4(f"RC2{i}", (128, T4)) for i in range(8)]
        X1T = [sb4(f"X1T{j}", (128, T4)) for j in range(2)]
        xb4_r = [sb4(f"xb4{i}", (128, T4), BF16) for i in range(4)]
        xq4_r = [sb4(f"xq4{i}", (128, T4), BF16) for i in range(4)]
        PS_k4 = [ps4(f"PS_k4{i}", (128, 512)) for i in range(2)]
        PS_v4 = [ps4(f"PS_v4{i}", (128, 512)) for i in range(4)]
        ps_s3 = [ps4(f"ps_s3{i}", (128, 512)) for i in range(2)]
        rt4 = [sb4(f"rt4_{i}", (1, 3 * T4)) for i in range(1)]

        wrec_done = 0
        for e in range(C):
            wrec_done = p.op("gpsimd", lambda E, e=e: E.dma_start(
                out=wrec_sb[e][:], in_=w4["wrec4"][e].rearrange("p a b -> p (a b)")), inc="wr")


        bc2_free = [0] * 2
        y1t_free = [0] * 2
        xring_free = [0] * 2
        xa_fold = [[0] * C for _ in range(2)]
        vcmA_free = 0
        z2_done_r = [0] * 2
        ks_free = [0] * 32
        kr_free = 0
        k4_free_v = [pso_free[0], pso_free[1]]
        k4_free_s = [0] * 2
        vcm_free = ph3_tail_vv
        rc2_free = [0] * 8
        x1spill = [0] * 2
        x1t_cons = [0] * 2
        st3 = [0] * NT4
        rows3_done = [0] * NT4
        xb4_free = [0] * 4
        xq4_free = [0] * 4
        vb_sem = [0] * 4
        sq_sem = [0] * 4
        s3_read_gate = 0

        def smalls3_chain(u, st_val):
            return ln_smalls_chain(rt4[0], ps_s3[0], ps_s3[1], rows[3], p.chan("rw3"), u * T4, T4,
                                   [("mm", st_val), ("rw3", None)])

        z2t_free_mm = ph3_end_mm

        def stage_head(u):
            nonlocal bc2_free, y1t_free, z2t_free_mm
            t1i = u // 2
            up = u % 2
            if u == 0:
                p.wait("vector", "mm", ph3_end_mm)
                p.wait("scalar", "mm", ph3_end_mm)
                p.wait("sync", "mm", ph3_end_mm)
                p.wait("sync", "vv", ph3_tail_vv)
                p.wait("sync", "ss", ph3_tail_ss)
                p.wait("sync", "rw2", rows2_all_done)
                for y in range(3):
                    p.wait("sync", f"y1w{y}", y1w_totals[y])
            p.wait("gpsimd", "rw2", rows2_all_done)
            p.wait("gpsimd", "pp", bc2_free[up])
            p.op("gpsimd", lambda E, u=u, up=up: E.dma_start(
                out=BC2r[up][:], in_=rows[2][0:1, u * T4:(u + 1) * T4].broadcast_to([128, T4])), inc=p.chan(f"bc2{up}"))
            bc_da = p.op("gpsimd", lambda E, u=u, up=up: E.dma_start(
                out=BC2m[up][:], in_=rows[2][1:2, u * T4:(u + 1) * T4].broadcast_to([128, T4])), inc=f"bc2{up}")
            p.wait("sync", "vv", y1t_free[up])
            y1_da = 0
            for ec in range(C):
                wch4, wv4 = y1sp_done[t1i][ec]
                p.wait("sync", wch4, wv4)
                y1_da = p.op("sync", lambda E, u=u, up=up, ec=ec: E.dma_start(
                    out=Y1T[up][ec][:], in_=y1sp[ec * 128:(ec + 1) * 128, u * T4:(u + 1) * T4]), inc=p.chan(f"y1t{up}"))
            z2rec = 0
            z2par = [z2_done_r[up]] * 1
            for ec in range(C):
                tb = 0
                xs = ec % 2
                p.wait("sync", "pp", xring_free[xs])
                x_da = p.op("sync", lambda E, u=u, ec=ec, xs=xs: E.dma_start(
                    out=xring[xs][:], in_=xT[ec * 128:(ec + 1) * 128, u * T4:(u + 1) * T4]), inc=p.chan(f"xr{xs}"))
                p.wait("gpsimd", f"y1t{up}", y1_da)
                p.wait("gpsimd", f"bc2{up}", bc_da)
                p.wait("gpsimd", "ss", z2par[tb])
                p.op("gpsimd", lambda E, up=up, ec=ec, tb=tb: E.tensor_tensor(
                    out=tmp4[tb][:], in0=Y1T[up][ec][:], in1=BC2r[up][:], op=ALU.mult))
                vt = p.op("gpsimd", lambda E, up=up, tb=tb: E.tensor_tensor(
                    out=tmp4[tb][:], in0=tmp4[tb][:], in1=BC2m[up][:], op=ALU.subtract), inc="pp")
                p.wait("scalar", "pp", vt)
                p.wait("scalar", "mm", z2t_free_mm)
                z2rec = p.op("scalar", lambda E, ec=ec, tb=tb: E.activation(
                    out=z2t[ec][:], in_=tmp4[tb][:], func=AF.Identity,
                    scale=g2s[:, ec:ec + 1], bias=b2s[:, ec:ec + 1]), inc="ss")
                z2par[tb] = z2rec
                p.wait("gpsimd", f"xr{xs}", x_da)
                xa = p.op("gpsimd", lambda E, up=up, ec=ec, xs=xs: E.tensor_tensor(
                    out=Y1T[up][ec][:], in0=Y1T[up][ec][:], in1=xring[xs][:], op=ALU.add), inc="pp")
                xring_free[xs] = xa
                xa_fold[up][ec] = xa
                drain_one()
            z2_done_r[up] = z2rec
            bc2_free[up] = p.cnt["pp"]
            return z2rec

        def kcm_only(u, z2rec):
            nonlocal kr_free
            for fc in range(32):
                fp = fc % 2
                p.wait("tensor", "wk", wkey_done)
                p.wait("tensor", "ss", z2rec)
                p.wait("tensor", "vv", k4_free_v[fp])
                p.wait("tensor", "ss", k4_free_s[fp])
                mk = 0
                for kc in range(C):
                    mk = p.op("tensor", lambda E, fc=fc, kc=kc, fp=fp, st=(kc == 0), sp=(kc == 7): E.matmul(
                        PS_k4[fp][:, :T4], wkey_sb[fc][:, kc * 128:(kc + 1) * 128],
                        z2t[kc][:], start=st, stop=sp), inc="mm" if kc == 7 else None)
                p.wait("vector", "mm", mk)
                p.wait("vector", "ss", kr_free)
                vkr = p.op("vector", lambda E, fp=fp: E.tensor_single_scalar(
                    out=KRb[:], in_=PS_k4[fp][:, :T4], scalar=0.0, op=ALU.max), inc="vv")
                k4_free_v[fp] = vkr
                p.wait("scalar", "vv", vkr)
                p.wait("scalar", "mm", ks_free[fc])
                sks = p.op("scalar", lambda E, fc=fc: E.activation(
                    out=KSb[fc][:], in_=KRb[:], func=AF.Square), inc="ss")
                kr_free = sks
                drain_one()
            return sks

        def rcm_sig_half(u, dh):
            last = 0
            for pair in range(2):
                mrs = [0] * 2
                for ecl2 in range(2):
                    ec = dh * 4 + pair * 2 + ecl2
                    ep = ec % 2
                    p.wait("tensor", "wr", wrec_done)
                    p.wait("tensor", "vv", k4_free_v[ep])
                    p.wait("tensor", "ss", k4_free_s[ep])
                    mr = 0
                    for kc in range(C):
                        mr = p.op("tensor", lambda E, kc=kc, ec=ec, ep=ep, st=(kc == 0), sp=(kc == 7): E.matmul(
                            PS_k4[ep][:, :T4], wrec_sb[ec][:, kc * 128:(kc + 1) * 128],
                            z2t[kc][:], start=st, stop=sp), inc="mm" if kc == 7 else None)
                    k4_free_v[ep] = 0
                    mrs[ecl2] = mr
                srs = [0, 0]
                for ecl2 in range(2):
                    ec = dh * 4 + pair * 2 + ecl2
                    ep = ec % 2
                    slot = dh * 4 + pair * 2 + ecl2
                    p.wait("scalar", "mm", mrs[ecl2])
                    p.wait("scalar", "vv", rc2_free[slot])
                    srs[ecl2] = p.op("scalar", lambda E, ep=ep, slot=slot: E.activation(
                        out=RC2[slot][:], in_=PS_k4[ep][:, :T4], func=AF.Exp, scale=-1.0), inc="ss")
                    k4_free_s[ep] = srs[ecl2]
                    last = srs[ecl2]
                s0_, s1_ = dh * 4 + pair * 2, dh * 4 + pair * 2 + 1
                p.wait("vector", "ss", srs[0])
                p.op("vector", lambda E, s=s0_: E.tensor_single_scalar(
                    out=RC2[s][:], in_=RC2[s][:], scalar=1.0, op=ALU.add))
                p.wait("vector", "ss", srs[1])
                p.op("vector", lambda E, s=s1_: E.tensor_single_scalar(
                    out=RC2[s][:], in_=RC2[s][:], scalar=1.0, op=ALU.add))
                p.op("vector", lambda E, s=s0_: E.reciprocal(out=RC2[s][:], in_=RC2[s][:]))
                p.op("vector", lambda E, s=s1_: E.reciprocal(out=RC2[s][:], in_=RC2[s][:]))
            return last

        def vcm_pass(u, dh, ks_ready, bankfree_vv):
            p.wait("tensor", "ss", ks_ready)
            p.wait("tensor", "vv", bankfree_vv)
            mvcm = 0
            for fc in range(32):
                p.wait("tensor", f"wv{fc // 4}", 64)
                for dcl in range(4):
                    dc = dh * 4 + dcl
                    mvcm = p.op("tensor", lambda E, fc=fc, dc=dc, dcl=dcl,
                                st=(fc == 0), sp=(fc == 31): E.matmul(
                        PS_v4[dcl][:, :T4],
                        wval_sb[fc][:, dc * 128:(dc + 1) * 128], KSb[fc][:],
                        start=st, stop=sp), inc="mm" if dcl == 3 else None)
                if dh == 1:
                    ks_free[fc] = mvcm
            return mvcm

        def combine_half(u, dh, mvcm):
            nonlocal vcm_free, vcmA_free, s3_read_gate
            up = u % 2
            lastvx1 = 0
            if dh == 0:
                drain_one()
                drain_one()
            bank_evac = 0
            for ecl in range(4):
                slot = dh * 4 + ecl
                p.wait("vector", "mm", mvcm)
                bank_evac = p.op("vector", lambda E, slot=slot, ecl=ecl: E.tensor_tensor(
                    out=RC2[slot][:], in0=PS_v4[ecl][:, :T4], in1=RC2[slot][:], op=ALU.mult), inc="vv")
            if dh == 0:
                vcmA_free = bank_evac
            else:
                vcm_free = bank_evac
            for ecl in range(4):
                ec = dh * 4 + ecl
                slot = dh * 4 + ecl
                xs4 = ec % 2
                p.wait("vector", "ss", x1t_cons[xs4])
                p.wait("vector", f"x1w{xs4}", x1spill[xs4])
                p.wait("vector", "pp", xa_fold[up][ec])
                vx1 = p.op("vector", lambda E, slot=slot, up=up, ec=ec, xs4=xs4: E.tensor_tensor(
                    out=X1T[xs4][:], in0=RC2[slot][:], in1=Y1T[up][ec][:], op=ALU.add), inc="vv")
                rc2_free[slot] = vx1
                lastvx1 = vx1
                if ec == C - 1:
                    y1t_free[up] = vx1
                p.wait("sync", "vv", vx1)
                x1da = p.op("sync", lambda E, u=u, ec=ec, xs4=xs4: E.dma_start(
                    out=x1sp[ec * 128:(ec + 1) * 128, u * T4:(u + 1) * T4], in_=X1T[xs4][:]), inc=p.chan(f"x1w{xs4}"))
                x1spill[xs4] = x1da
                sl4 = ec % 4
                p.wait("vector", "mm", xb4_free[sl4])
                vb = p.op("vector", lambda E, sl4=sl4, xs4=xs4: E.tensor_copy(out=xb4_r[sl4][:], in_=X1T[xs4][:]), inc="vv")
                p.wait("scalar", "vv", vx1)
                p.wait("scalar", "mm", xq4_free[sl4])
                sq = p.op("scalar", lambda E, sl4=sl4, xs4=xs4: E.activation(
                    out=xq4_r[sl4][:], in_=X1T[xs4][:], func=AF.Square), inc="ss")
                x1t_cons[xs4] = sq
                vb_sem[sl4] = vb
                sq_sem[sl4] = sq
                drain_one()
            return lastvx1

        def stats_mms(u, dh, gate=0):
            mmv = 0
            for ecl in range(4):
                ec = dh * 4 + ecl
                sl4 = ecl
                p.wait("tensor", "vv", vb_sem[sl4])
                p.wait("tensor", "ss", sq_sem[sl4])
                if ec == 0:
                    p.wait("tensor", "vv", gate)
                p.op("tensor", lambda E, sl4=sl4, st=(ec == 0), sp=(ec == 7): E.matmul(
                    ps_s3[0][0:1, :T4], ones_bf[:], xb4_r[sl4][:], start=st, stop=sp))
                mmv = p.op("tensor", lambda E, sl4=sl4, st=(ec == 0), sp=(ec == 7): E.matmul(
                    ps_s3[1][0:1, :T4], ones_bf[:], xq4_r[sl4][:], start=st, stop=sp), inc="mm")
                xb4_free[sl4] = mmv
                xq4_free[sl4] = mmv
                if ec == 7:
                    st3[u] = mmv
            return mmv

        z2r = stage_head(0)
        ksr = kcm_only(0, z2r)
        rcm_sig_half(0, 0)
        p.wait("gpsimd", "mm", ph3_end_mm)
        for fc in range(32):
            p.op("gpsimd", lambda E, fc=fc: E.dma_start(
                out=wval_sb[fc][:], in_=wval4[fc].rearrange("p a b -> p (a b)")), inc=p.chan(f"wv{fc // 4}"))
        for u in range(NT4):
            mvcmA = vcm_pass(u, 0, ksr, vcm_free)
            if u >= 1:
                stats_mms(u - 1, 1)
                pending.extend(smalls3_chain(u - 1, st3[u - 1]))
            rcm_sig_half(u, 1)
            z2t_free_mm = p.cnt["mm"]
            combine_half(u, 0, mvcmA)
            s3gate = p.cnt["vv"]
            if u + 1 < NT4:
                z2r = stage_head(u + 1)
            mvcmB = vcm_pass(u, 1, ksr, vcmA_free)
            stats_mms(u, 0, s3gate)
            if u + 1 < NT4:
                ksr = kcm_only(u + 1, z2r)
                rcm_sig_half(u + 1, 0)
            combine_half(u, 1, mvcmB)
            rows3_done[u] = None
        stats_mms(NT4 - 1, 1)
        pending.extend(smalls3_chain(NT4 - 1, st3[NT4 - 1]))
        drain_all_padded()
        rows3_all_done = p.cnt["rw3"]
        for u in range(NT4):
            rows3_done[u] = rows3_all_done
        ph4.close()
        ph4w.close()
        barrier_all()

        ph5 = ExitStack()
        sb5 = lambda n_, s_, d_=F32: ph5.enter_context(nc.sbuf_tensor(n_, list(s_), d_))
        ps5 = lambda n_, s_: ph5.enter_context(nc.psum_tensor(n_, list(s_), F32))
        OUT = [sb5(f"OUT{ec}", (128, HS)) for ec in range(C)]
        y3h = [sb5(f"y3h{ec}", (128, HS), BF16) for ec in range(C)]
        KEY = [sb5(f"KEY{i}", (128, 4096), BF16) for i in range(4)]
        VAL = [sb5(f"VAL{i}", (128, 4096), BF16) for i in range(4)]
        GFM = sb5("GFM", (32, HS))
        BC3r = [sb5(f"BC3r{i}", (128, T5)) for i in range(2)]
        BC3m = [sb5(f"BC3m{i}", (128, T5)) for i in range(2)]
        tmp5 = [sb5(f"tmp5{i}", (128, T5)) for i in range(2)]
        Gb = [sb5(f"Gb{i}", (128, 32)) for i in range(4)]
        GCb = [sb5(f"GCb{i}", (128, 32)) for i in range(4)]
        Mb = [sb5(f"Mb{i}", (128, 1)) for i in range(4)]
        MKb = [sb5(f"MKb{i}", (128, 32)) for i in range(4)]
        GFb = [sb5(f"GFb{i}", (128, 32)) for i in range(4)]
        GB = [sb5(f"GB{i}", (128, T5B)) for i in range(4)]
        SC = [[[sb5(f"SC{par}_{el}_{fc}", (128, T5B), BF16) for fc in range(4)] for el in range(GE)] for par in range(2)]
        PS_sc = [ps5(f"PS_sc{i}", (128, 512)) for i in range(2)]
        PS_out = [ps5(f"PS_out{i}", (128, 512)) for i in range(4)]

        kv_done = [0] * NE
        kv_free = [0] * 4
        out_done = 0
        p.chan("outw")
        gb_free = [0] * 4
        sc_free = [[[0] * 4 for _ in range(GE)] for _ in range(2)]
        sc_ready = [[[0] * 4 for _ in range(GE)] for _ in range(2)]
        sc_bank_free = [0] * 2
        sc_bank_free_ss = [0] * 2
        gdr_done_prev = 0
        gbatch_free = [0] * 4
        tr_read = 0
        outadd_last = 0
        for h in range(2):
            hoff = h * HS
            outacc_da = 0
            for ec2 in range(2):
                p.wait("sync", f"x1w{ec2}", x1spill[ec2])
            for ec in range(C):
                if h == 1:
                    p.wait("sync", "outw", out_done)
                    p.wait("sync", "vv", outadd_last)
                outacc_da = p.op("sync", lambda E, ec=ec, hoff=hoff: E.dma_start(
                    out=OUT[ec][:], in_=x1sp[ec * 128:(ec + 1) * 128, hoff:hoff + HS]), inc=p.chan("outld"))
            bc3_free = [0] * 2
            y3rec = [[0] * C for _ in range(NT5)]
            for u in range(NT5):
                ug = (hoff + u * T5) // T4
                up = u % 2
                p.wait("gpsimd", "rw3", rows3_done[ug])
                p.wait("gpsimd", "vv", bc3_free[up])
                p.op("gpsimd", lambda E, up=up, hoff=hoff, u=u: E.dma_start(
                    out=BC3r[up][:], in_=rows[3][0:1, hoff + u * T5:hoff + (u + 1) * T5].broadcast_to([128, T5])), inc=p.chan(f"bc3{up}"))
                bc_da = p.op("gpsimd", lambda E, up=up, hoff=hoff, u=u: E.dma_start(
                    out=BC3m[up][:], in_=rows[3][1:2, hoff + u * T5:hoff + (u + 1) * T5].broadcast_to([128, T5])), inc=f"bc3{up}")
                for ec in range(C):
                    tb = ec % 2
                    p.wait("vector", "outld", outacc_da)
                    p.wait("vector", f"bc3{up}", bc_da)
                    p.op("vector", lambda E, ec=ec, u=u, up=up, tb=tb: E.tensor_tensor(
                        out=tmp5[tb][:], in0=OUT[ec][:, u * T5:(u + 1) * T5], in1=BC3r[up][:], op=ALU.mult))
                    y3rec[u][ec] = p.op("vector", lambda E, ec=ec, u=u, up=up, tb=tb: E.tensor_tensor(
                        out=y3h[ec][:, u * T5:(u + 1) * T5], in0=tmp5[tb][:], in1=BC3m[up][:], op=ALU.subtract), inc="vv")
                bc3_free[up] = y3rec[u][C - 1]
            gfm_last = 0
            for scb in range(0, NSC, 4):
                gsig = [0] * 4
                for j in range(4):
                    sc = scb + j
                    sp5 = sc % 2
                    u5 = (sc * 128) // T5
                    p.wait("tensor", "vv", y3rec[u5][C - 1])
                    p.wait("tensor", "preg", pre_dw)
                    p.wait("tensor", "vv", sc_bank_free[sp5])
                    p.wait("tensor", "ss", sc_bank_free_ss[sp5])
                    ms = 0
                    for kc in range(C):
                        ms = p.op("tensor", lambda E, sc=sc, kc=kc, sp5=sp5, st=(kc == 0), sp=(kc == 7): E.matmul(
                            PS_sc[sp5][:, 0:32], y3h[kc][:, sc * 128:(sc + 1) * 128],
                            selT_sb[:, kc * 32:(kc + 1) * 32], start=st, stop=sp),
                            inc="mm" if kc == 7 else None)
                    g4 = sc % 4
                    p.wait("scalar", "mm", ms)
                    p.wait("scalar", "vv", gbatch_free[g4])
                    gsig[j] = p.op("scalar", lambda E, sp5=sp5, g4=g4: E.activation(
                        out=Gb[g4][:], in_=PS_sc[sp5][:, 0:32], func=AF.Exp, scale=-1.0), inc="ss")
                    sc_bank_free_ss[sp5] = gsig[j]
                chains4 = []
                for j in range(4):
                    sc = scb + j
                    g4 = sc % 4
                    q = []
                    q.append(("vector", lambda E, g4=g4: E.tensor_single_scalar(
                        out=Gb[g4][:], in_=Gb[g4][:], scalar=1.0, op=ALU.add), "vv", [("ss", gsig[j])]))
                    q.append(("vector", lambda E, g4=g4: E.reciprocal(out=Gb[g4][:], in_=Gb[g4][:]), "vv", []))
                    q.append(("vector", lambda E, g4=g4: E.tensor_copy(out=GCb[g4][:], in_=Gb[g4][:]), "vv", []))
                    for it in range(4):
                        q.append(("vector", lambda E, g4=g4: E.reduce_max(
                            out=Mb[g4][:], in_=GCb[g4][:], axis=AX.X), "vv", []))
                        if it < 3:
                            q.append(("vector", lambda E, g4=g4: E.tensor_tensor(
                                out=MKb[g4][:], in0=GCb[g4][:], in1=Mb[g4][:, 0:1].broadcast_to([128, 32]),
                                op=ALU.is_ge), "vv", []))
                            q.append(("vector", lambda E, g4=g4: E.scalar_tensor_tensor(
                                out=GCb[g4][:], in0=MKb[g4][:], scalar=-1e30, in1=GCb[g4][:],
                                op0=ALU.mult, op1=ALU.add), "vv", []))
                    q.append(("vector", lambda E, g4=g4: E.tensor_tensor(
                        out=MKb[g4][:], in0=Gb[g4][:], in1=Mb[g4][:, 0:1].broadcast_to([128, 32]),
                        op=ALU.is_ge), "vv", []))
                    q.append(("vector", lambda E, g4=g4: E.tensor_tensor(
                        out=GFb[g4][:], in0=Gb[g4][:], in1=MKb[g4][:], op=ALU.mult), "vv", []))
                    chains4.append(q)
                gf_rec = [0] * 4
                ncs5 = len(chains4[0])
                for si in range(ncs5):
                    for j in range(4):
                        eng, make, inc, waits = chains4[j][si]
                        for s_, v_ in waits:
                            p.wait(eng, s_, p.cnt[s_] if v_ is None else v_)
                        r_ = p.op(eng, make, inc=inc)
                        if si == ncs5 - 1:
                            gf_rec[j] = r_
                for j in range(4):
                    sc = scb + j
                    g4 = sc % 4
                    p.wait("tensor", "vv", gf_rec[j])
                    p.wait("tensor", "ss", tr_read)
                    p.wait("tensor", "vv", outadd_last)
                    mtr = p.op("tensor", lambda E, g4=g4: E.transpose(
                        out=PS_out[0][0:32, 0:128], in_=GFb[g4][:], identity=ident[:]), inc="mm")
                    p.wait("scalar", "mm", mtr)
                    p.wait("scalar", "gdr", gdr_done_prev)
                    tr_read = p.op("scalar", lambda E, sc=sc: E.activation(
                        out=GFM[0:32, sc * 128:(sc + 1) * 128], in_=PS_out[0][0:32, 0:128], func=AF.Copy), inc="ss")
                    gbatch_free[g4] = gf_rec[j]
                gfm_last = tr_read
            p.wait("sync", "ss", gfm_last)
            gdr_done = p.op("sync", lambda E, hoff=hoff: E.dma_start(
                out=gdr[0:32, hoff:hoff + HS], in_=GFM[:]), inc=p.chan("gdr"))
            gdr_done_prev = gdr_done
            seq = [(g, t) for g in range(NG) for t in range(NT5B)]

            def kv_load(g):
                for el in range(GE):
                    e = g * GE + el
                    slot = e % 4
                    p.wait("gpsimd", "mm", kv_free[slot])
                    p.op("gpsimd", lambda E, e=e, slot=slot: E.dma_start(
                        out=KEY[slot][:], in_=keysT[e]), inc=p.chan(f"kv{slot}"))
                    kv_done[e] = p.op("gpsimd", lambda E, e=e, slot=slot: E.dma_start(
                        out=VAL[slot][:], in_=valsT[e]), inc=f"kv{slot}")

            def front(i):
                g, t = seq[i]
                par = i % 2
                for el in range(GE):
                    e = g * GE + el
                    slot = e % 4
                    gbs = (i * GE + el) % 4
                    p.wait("gpsimd", "gdr", gdr_done)
                    p.wait("gpsimd", "vv", gb_free[gbs])
                    gb_da = p.op("gpsimd", lambda E, e=e, gbs=gbs, hoff=hoff, t=t: E.dma_start(
                        out=GB[gbs][:], in_=gdr[e:e + 1, hoff + t * T5B:hoff + (t + 1) * T5B].broadcast_to([128, T5B])), inc=p.chan(f"gb{gbs}"))
                    for fc in range(4):
                        fp = fc % 2
                        p.wait("tensor", f"kv{slot}", kv_done[e])
                        p.wait("tensor", "vv", y3rec[2 * t + 1][C - 1])
                        p.wait("tensor", "vv", sc_bank_free[fp])
                        p.wait("tensor", "ss", sc_bank_free_ss[fp])
                        mk = 0
                        for kc in range(C):
                            mk = p.op("tensor", lambda E, slot=slot, kc=kc, fc=fc, t=t, fp=fp,
                                      st=(kc == 0), sp=(kc == 7): E.matmul(
                                PS_sc[fp][:, 0:T5B],
                                KEY[slot][:, kc * 512 + fc * 128: kc * 512 + (fc + 1) * 128],
                                y3h[kc][:, t * T5B:(t + 1) * T5B], start=st, stop=sp),
                                inc="mm" if kc == 7 else None)
                        p.wait("vector", "mm", mk)
                        p.wait("vector", f"gb{gbs}", gb_da)
                        p.wait("vector", "mm", sc_free[par][el][fc])
                        vsc = p.op("vector", lambda E, par=par, el=el, fc=fc, fp=fp, gbs=gbs: E.scalar_tensor_tensor(
                            out=SC[par][el][fc][:], in0=PS_sc[fp][:, 0:T5B],
                            scalar=0.0, in1=GB[gbs][:], op0=ALU.max, op1=ALU.mult), inc="vv")
                        sc_bank_free[fp] = vsc
                        if fc == 3:
                            gb_free[gbs] = vsc
                        sc_ready[par][el][fc] = vsc
                if t == 1 and g + 1 < NG:
                    kv_load(g + 1)

            def values_pass(i, dh):
                nonlocal outadd_last, mvals_last
                g, t = seq[i]
                par = i % 2
                p.wait("tensor", "vv", sc_ready[par][GE - 1][3])
                p.wait("tensor", "vv", outadd_last)
                p.wait("tensor", "ss", gfm_last)
                mvals = 0
                for el in range(GE):
                    for fc in range(4):
                        for dcl in range(4):
                            dc = dh * 4 + dcl
                            mvals = p.op("tensor", lambda E, par=par, el=el, slot2=(g * GE + el) % 4,
                                         fc=fc, dc=dc, dcl=dcl,
                                         st=(el == 0 and fc == 0), sp=(el == GE - 1 and fc == 3): E.matmul(
                                PS_out[dcl][:, 0:T5B],
                                VAL[slot2][:, fc * 1024 + dc * 128: fc * 1024 + (dc + 1) * 128],
                                SC[par][el][fc][:], start=st, stop=sp), inc="mm" if dcl == 3 else None)
                p.wait("vector", "mm", mvals)
                for dcl in range(4):
                    dc = dh * 4 + dcl
                    outadd_last = p.op("vector", lambda E, dc=dc, dcl=dcl, t=t: E.tensor_tensor(
                        out=OUT[dc][:, t * T5B:(t + 1) * T5B], in0=OUT[dc][:, t * T5B:(t + 1) * T5B],
                        in1=PS_out[dcl][:, 0:T5B], op=ALU.add), inc="vv")
                mvals_last = mvals
                if dh == 1:
                    for el in range(GE):
                        for fc in range(4):
                            sc_free[par][el][fc] = mvals
                        kv_free[(g * GE + el) % 4] = mvals

            kv_load(0)
            mvals_last = 0
            front(0)
            for i in range(len(seq)):
                values_pass(i, 0)
                if i + 1 < len(seq):
                    front(i + 1)
                values_pass(i, 1)
            for ec in range(C):
                p.wait("sync", "vv", outadd_last)
                out_done = p.op("sync", lambda E, ec=ec, hoff=hoff: E.dma_start(
                    out=outT[ec * 128:(ec + 1) * 128, hoff:hoff + HS], in_=OUT[ec][:]), inc=p.chan("outw"))
        ph5.close()

        for s_name in list(p.cnt):
            p.wait("sync", s_name, p.cnt[s_name])

        @block.sync
        def _(E):
            p.replay("sync", E)

        @block.vector
        def _(E):
            p.replay("vector", E)

        @block.scalar
        def _(E):
            p.replay("scalar", E)

        @block.tensor
        def _(E):
            p.replay("tensor", E)

        @block.gpsimd
        def _(E):
            p.replay("gpsimd", E)

    es.close()
    return nc


def _host_layouts(inputs):
    f32 = np.float32
    A = lambda x: np.ascontiguousarray(np.asarray(x), dtype=f32)
    _ = inputs["L0_attn_time_first"]

    def t4(W):
        return np.ascontiguousarray(A(W).T.reshape(8, 128, 8, 128).transpose(2, 1, 0, 3))

    d = {}
    d["wr4"] = t4(inputs["L0_attn_Wr"])
    d["ww4"] = t4(inputs["L0_attn_Ww"])
    d["wk4"] = t4(inputs["L0_attn_Wk"])
    d["wv4"] = t4(inputs["L0_attn_Wv"])
    d["wo4"] = t4(inputs["L0_attn_Wo"])
    d["wrec4"] = t4(inputs["L0_attn_Wrec"])
    d["wkey4"] = np.ascontiguousarray(A(inputs["L0_attn_Wkey"]).T.reshape(8, 128, 32, 128).transpose(2, 1, 0, 3))
    d["wval4"] = np.ascontiguousarray(A(inputs["L0_attn_Wval"]).T.reshape(32, 128, 8, 128))
    d["selT3"] = np.ascontiguousarray(
        A(inputs["L0_ffn_expert_sel"]).T.reshape(8, 128, 32).transpose(1, 0, 2)).reshape(128, 256)
    d["keysT"] = np.ascontiguousarray(
        A(inputs["L0_ffn_keys"]).reshape(32, 8, 128, 512).transpose(0, 2, 1, 3)).reshape(32, 128, 4096)
    d["valsT"] = np.ascontiguousarray(
        A(inputs["L0_ffn_values"]).reshape(32, 4, 128, 1024).transpose(0, 2, 1, 3)).reshape(32, 128, 4096)
    for nm, key in (("g1v", "L0_attn_ln1_g"), ("b1v", "L0_attn_ln1_b"),
                    ("g2v", "L0_attn_ln2_g"), ("b2v", "L0_attn_ln2_b"),
                    ("tdv", "L0_attn_time_decay")):
        d[nm] = np.ascontiguousarray(A(inputs[key]).reshape(8, 128).T)
    d["onescol_bf"] = np.ones((128, 1), dtype=ml_dtypes.bfloat16)
    d["zcol_in"] = np.zeros((128, 1), dtype=f32)
    d["onecol_in"] = np.ones((128, 1), dtype=f32)
    d["onesrow_in"] = np.ones((1, 128), dtype=f32)
    d["epscol_in"] = np.full((128, 1), 1e-8, dtype=f32)
    d["ident_in"] = np.eye(128, dtype=f32)
    return d


def kernel(**inputs):
    nc = build_program()
    shared = _host_layouts(inputs)
    x = np.asarray(inputs["L0_attn_x"], dtype=np.float32)
    in_maps = []
    for b in range(8):
        m = dict(shared)
        m["xT"] = np.ascontiguousarray(x[b].T)
        in_maps.append(m)
    res = run_bass_kernel_spmd(nc, in_maps, core_ids=list(range(8)))
    out = np.stack([np.asarray(r["outT"]).T for r in res.results], axis=0)
    return np.ascontiguousarray(out, dtype=np.float32)


# version 21
# speedup vs baseline: 1.0709x; 1.0413x over previous
import numpy as np
import ml_dtypes
import concourse.bass as bass
import concourse.mybir as mybir
from concourse.bass_utils import run_bass_kernel_spmd
from contextlib import ExitStack

F32 = mybir.dt.float32
BF16 = mybir.dt.bfloat16
AF = mybir.ActivationFunctionType
ALU = mybir.AluOpType
AX = mybir.AxisListType

D = 1024
S = 4096
C = 8
T1 = 512
NT1 = S // T1
T4 = 256
NT4 = S // T4
HS = S // 2
T5 = 256
NT5 = HS // T5
T5B = 512
NT5B = HS // T5B
NSC = HS // 128
NE = 32
GE = 2
NG = NE // GE
LN_EPS = 1e-5
WKV_EPS = 1e-8

ENGINES = ("sync", "vector", "scalar", "tensor", "gpsimd")


class P:
    def __init__(self, nc, es, selfsync=False):
        self.nc = nc
        self.es = es
        self.sems = {}
        self.cnt = {}
        self.q = {e: [] for e in ENGINES}
        self.lastw = {}
        self.selfsync = selfsync
        for s_name in ("mm", "vv", "ss", "pp"):
            self.chan(s_name)
        self.lastmark = {}
        if selfsync:
            for e in ENGINES:
                self.chan(f"ord_{e}")

    def chan(self, name):
        if name in self.sems:
            return name
        self.sems[name] = self.es.enter_context(self.nc.semaphore(name))
        self.cnt[name] = 0
        for e in ENGINES:
            self.lastw[(e, name)] = 0
        return name

    def wait(self, eng, sem, val):
        if not val:
            return
        assert val <= self.cnt[sem], f"wait on future: {eng} {sem} {val} > {self.cnt[sem]}"
        if self.lastw[(eng, sem)] >= val:
            return
        self.lastw[(eng, sem)] = val
        h = self.sems[sem]
        self.q[eng].append(lambda E, h=h, v=val: E.wait_ge(h, v))

    def op(self, eng, make, inc=None):
        if self.selfsync:
            pm = self.lastmark.get(eng)
            if pm is not None:
                self.q[eng].append(lambda E, h=self.sems[pm[0]], v=pm[1]: E.wait_ge(h, v))
        if inc is None:
            if self.selfsync:
                ordname = f"ord_{eng}"
                self.cnt[ordname] += 1
                self.lastmark[eng] = (ordname, self.cnt[ordname])

                def f0(E, make=make, h=self.sems[ordname], n2=1):
                    inst = make(E)
                    if inst is not None:
                        inst.then_inc(h, n2)
                self.q[eng].append(f0)
            else:
                self.q[eng].append(make)
            return None
        n = 1 if inc in ("mm", "vv", "ss", "pp") else 16
        h = self.sems[inc]
        self.cnt[inc] += n
        if self.selfsync:
            self.lastmark[eng] = (inc, self.cnt[inc])

        def f(E, make=make, h=h, n=n):
            make(E).then_inc(h, n)
        self.q[eng].append(f)
        return self.cnt[inc]

    def replay(self, eng, E):
        for f in self.q[eng]:
            f(E)


def build_program(debug=False, selfsync=False):
    nc = bass.Bass("TRN2", target_bir_lowering=False, debug=False)

    dt_in = lambda name, shape: nc.dram_tensor(name, list(shape), F32, kind="ExternalInput").ap()
    xT = dt_in("xT", (D, S))
    w4 = {nm: dt_in(nm, (C, 128, C, 128)) for nm in ("wr4", "ww4", "wk4", "wv4", "wo4", "wrec4")}
    wkey4 = dt_in("wkey4", (32, 128, 8, 128))
    wval4 = dt_in("wval4", (32, 128, 8, 128))
    selT3 = dt_in("selT3", (128, 256))
    keysT = dt_in("keysT", (NE, 128, 4096))
    valsT = dt_in("valsT", (NE, 128, 4096))
    g1v = dt_in("g1v", (128, C)); b1v = dt_in("b1v", (128, C))
    g2v = dt_in("g2v", (128, C)); b2v = dt_in("b2v", (128, C))
    tdv = dt_in("tdv", (128, C))
    onescol_bf = nc.dram_tensor("onescol_bf", [128, 1], BF16, kind="ExternalInput").ap()
    zcol_in = dt_in("zcol_in", (128, 1))
    onecol_in = dt_in("onecol_in", (128, 1))
    onesrow_in = dt_in("onesrow_in", (1, 128))
    epscol_in = dt_in("epscol_in", (128, 1))
    ident_in = dt_in("ident_in", (128, 128))

    kind_sp = "ExternalOutput" if debug else "Internal"
    y0sp = nc.dram_tensor("y0sp", [D, S], F32, kind=kind_sp).ap()
    y1sp = nc.dram_tensor("y1sp", [D, S], F32, kind=kind_sp).ap()
    x1sp = nc.dram_tensor("x1sp", [D, S], F32, kind=kind_sp).ap()
    gdr = nc.dram_tensor("gdr", [32, S], F32, kind=kind_sp).ap()
    rows = [nc.dram_tensor(f"rows{i}", [2, S], F32, kind=kind_sp).ap() for i in range(4)]
    if debug:
        z1dump = nc.dram_tensor("z1dump", [D, S], F32, kind="ExternalOutput").ap()
        rwkvdump = nc.dram_tensor("rwkvdump", [D, S], F32, kind="ExternalOutput").ap()
    outT = nc.dram_tensor("outT", [D, S], F32, kind="ExternalOutput").ap()

    es = ExitStack()
    sb = lambda name, shape, dt=F32: es.enter_context(nc.sbuf_tensor(name, list(shape), dt))
    p = P(nc, es, selfsync=selfsync)

    g1s = sb("g1s", (128, C)); b1s = sb("b1s", (128, C))
    g2s = sb("g2s", (128, C)); b2s = sb("b2s", (128, C))
    tds = sb("tds", (128, C)); decays = sb("decays", (128, C))
    ones_bf = sb("ones_bf", (128, 1), BF16)
    zcol = sb("zcol", (128, 1))
    onecol = sb("onecol", (128, 1))
    onesrow = sb("onesrow", (1, 128))
    epscol = sb("epscol", (128, 1))
    ident = sb("ident", (128, 128))
    selT_sb = sb("selT_sb", (128, 256), BF16)

    padA = sb("padA", (128, 64))
    padB = sb("padB", (128, 64))
    z1_es = ExitStack()
    z1 = [z1_es.enter_context(nc.sbuf_tensor(f"z1_{c}", [128, S], BF16)) for c in range(C)]

    pending = []

    def drain_one():
        if not pending:
            return
        eng, make, inc, waits = pending.pop(0)
        for s_, v_ in waits:
            p.wait(eng, s_, p.cnt[s_] if v_ is None else v_)
        p.op(eng, make, inc=inc)

    def barrier_all():
        bar = dict(p.cnt)
        for eng in ENGINES:
            for s_ in bar:
                p.wait(eng, s_, bar[s_])

    def drain_all_padded():
        while pending:
            drain_one()
            for _ in range(2):
                p.op("vector", lambda E: E.tensor_copy(out=padA[:], in_=padB[:]))
                p.op("vector", lambda E: E.tensor_copy(out=padB[:], in_=padA[:]))

    def ln_smalls_chain(arena, psA, psB, rows_dram, rowch, col0, width, extra_first_waits):
        A = arena[0:1, 0:width]
        B = arena[0:1, width:2 * width]
        Cc = arena[0:1, 2 * width:3 * width]
        steps = []
        steps.append(("vector", lambda E: E.tensor_single_scalar(
            out=A, in_=psA[0:1, :width], scalar=1.0 / D, op=ALU.mult), "vv",
            list(extra_first_waits)))
        steps.append(("vector", lambda E: E.tensor_single_scalar(
            out=B, in_=psB[0:1, :width], scalar=1.0 / D, op=ALU.mult), "vv", []))
        steps.append(("vector", lambda E: E.tensor_tensor(
            out=Cc, in0=A, in1=A, op=ALU.mult), "vv", []))
        steps.append(("vector", lambda E: E.tensor_tensor(
            out=B, in0=B, in1=Cc, op=ALU.subtract), "vv", []))
        steps.append(("vector", lambda E: E.tensor_single_scalar(
            out=B, in_=B, scalar=LN_EPS, op=ALU.add), "vv", []))
        steps.append(("scalar", lambda E: E.activation(
            out=Cc, in_=B, func=AF.Ln), "ss", [("vv", None)]))
        steps.append(("scalar", lambda E: E.activation(
            out=B, in_=Cc, func=AF.Exp, scale=-0.5), "ss", []))
        steps.append(("vector", lambda E: E.tensor_tensor(
            out=A, in0=A, in1=B, op=ALU.mult), "vv", [("ss", None)]))
        if rows_dram is not None:
            steps.append(("scalar", lambda E: E.dma_start(
                out=rows_dram[0:1, col0:col0 + width], in_=B), rowch, [("vv", None)]))
            steps.append(("scalar", lambda E: E.dma_start(
                out=rows_dram[1:2, col0:col0 + width], in_=A), rowch, [("vv", None)]))
        return steps

    with nc.Block() as block:
        for dst, src in ((g1s, g1v), (b1s, b1v), (g2s, g2v), (b2s, b2v), (tds, tdv),
                         (zcol, zcol_in), (onecol, onecol_in), (epscol, epscol_in),
                         (onesrow, onesrow_in), (ident, ident_in)):
            pre_da = p.op("sync", lambda E, d=dst, s=src: E.dma_start(out=d[:], in_=s[:]), inc=p.chan("pre"))
        p.op("gpsimd", lambda E: E.dma_start(out=ones_bf[:], in_=onescol_bf[:]), inc=p.chan("preg"))
        pre_dw = p.op("gpsimd", lambda E: E.dma_start(out=selT_sb[:], in_=selT3[:]), inc="preg")
        p.wait("scalar", "pre", pre_da)
        decays_done = p.op("scalar", lambda E: E.activation(out=decays[:], in_=tds[:], func=AF.Exp), inc="ss")
        p.op("vector", lambda E: E.memset(padA[:], 0.0))
        p.op("vector", lambda E: E.memset(padB[:], 0.0))

        ph1 = ExitStack()
        sb1 = lambda n_, s_, d_=F32: ph1.enter_context(nc.sbuf_tensor(n_, list(s_), d_))
        ps1 = lambda n_, s_: ph1.enter_context(nc.psum_tensor(n_, list(s_), F32))
        XC = [[sb1(f"XC{r}_{c}", (128, T1)) for c in range(C)] for r in range(2)]
        Y0C = [[sb1(f"Y0C{r}_{c}", (128, T1)) for c in range(C)] for r in range(2)]
        xb_r = [sb1(f"xb{i}", (128, T1), BF16) for i in range(2)]
        xq_r = [sb1(f"xq{i}", (128, T1), BF16) for i in range(2)]
        tmp_r = [sb1(f"tmp{i}", (128, T1)) for i in range(2)]
        rt1 = [sb1(f"rt1_{i}", (1, 3 * T1)) for i in range(4)]
        ps_st0 = [ps1(f"ps_st0{i}", (128, T1)) for i in range(2)]
        ps_st1 = [ps1(f"ps_st1{i}", (128, T1)) for i in range(2)]
        ps_bc0 = [ps1(f"ps_bc0{i}", (128, T1)) for i in range(2)]
        ps_bc1 = [ps1(f"ps_bc1{i}", (128, T1)) for i in range(2)]

        for r in range(2):
            for c in range(C):
                p.chan(f"xc{r}_{c}")
                p.chan(f"y0w{c}")
        xc_free = [[0] * C for _ in range(2)]
        y0c_spill = [[("y0w0", 0)] * C for _ in range(2)]
        y0c_sq = [[0] * C for _ in range(2)]
        xb_free = [0] * 2
        xq_free = [0] * 2
        st0_read = 0
        tmp_ss = [0, 0]
        st1_read = 0
        bc0_read = 0
        bc1_read = 0
        st0_rec = [0] * NT1
        st1_rec = [0] * NT1
        ch0_rec = [0] * NT1
        ch1_rec = [0] * NT1
        bc0_mm = [0] * NT1
        bc1_mm = [0] * NT1
        y0sp_done = [[0] * C for _ in range(NT1)]
        z1_done_t = [0] * NT1
        p.wait("tensor", "preg", pre_dw)
        p.wait("tensor", "pre", pre_da)

        def stA(t):
            r = t % 2
            for c in range(C):
                p.wait("sync", "vv", xc_free[r][c])
                p.op("sync", lambda E, t=t, c=c, r=r: E.dma_start(
                    out=XC[r][c][:], in_=xT[c * 128:(c + 1) * 128, t * T1:(t + 1) * T1]), inc=f"xc{r}_{c}")

        def stB(t):
            r = t % 2
            for c in range(C):
                b = c % 2
                p.wait("vector", f"xc{r}_{c}", p.cnt[f"xc{r}_{c}"])
                p.wait("vector", "mm", xb_free[b])
                vb = p.op("vector", lambda E, b=b, r=r, c=c: E.tensor_copy(out=xb_r[b][:], in_=XC[r][c][:]), inc="vv")
                p.wait("scalar", f"xc{r}_{c}", p.cnt[f"xc{r}_{c}"])
                p.wait("scalar", "mm", xq_free[b])
                sq = p.op("scalar", lambda E, b=b, r=r, c=c: E.activation(
                    out=xq_r[b][:], in_=XC[r][c][:], func=AF.Square), inc="ss")
                p.wait("tensor", "vv", vb)
                p.wait("tensor", "ss", sq)
                if c == 0:
                    p.wait("tensor", "vv", st0_read)
                p.op("tensor", lambda E, b=b, st=(c == 0), sp=(c == 7): E.matmul(
                    ps_st0[0][0:1, :], ones_bf[:], xb_r[b][:], start=st, stop=sp))
                mmv = p.op("tensor", lambda E, b=b, st=(c == 0), sp=(c == 7): E.matmul(
                    ps_st0[1][0:1, :], ones_bf[:], xq_r[b][:], start=st, stop=sp), inc="mm")
                xb_free[b] = mmv
                xq_free[b] = mmv
                drain_one()
            st0_rec[t] = mmv
            pending.extend(ln_smalls_chain(rt1[t % 4], ps_st0[0], ps_st0[1], None, None, 0, T1,
                                           [("mm", st0_rec[t])]))

        def stD(t):
            nonlocal st0_read
            ch0_rec[t] = p.cnt["vv"]
            st0_read = ch0_rec[t]
            A_ = rt1[t % 4][0:1, 0:T1]
            B_ = rt1[t % 4][0:1, T1:2 * T1]
            p.wait("tensor", "vv", ch0_rec[t])
            p.wait("tensor", "vv", bc0_read)
            p.op("tensor", lambda E, B_=B_: E.matmul(ps_bc0[0][:], onesrow[:], B_, start=True, stop=True))
            bc0_mm[t] = p.op("tensor", lambda E, A_=A_: E.matmul(
                ps_bc0[1][:], onesrow[:], A_, start=True, stop=True), inc="mm")

        def stE(t):
            nonlocal bc0_read
            r = t % 2
            for c in range(C):
                b = c % 2
                tb = c % 2
                p.wait("vector", "mm", bc0_mm[t])
                wch, wval = y0c_spill[r][c]
                p.wait("vector", wch, wval)
                p.wait("vector", "ss", y0c_sq[r][c])
                p.wait("vector", "ss", tmp_ss[tb])
                p.op("vector", lambda E, r=r, c=c, tb=tb: E.tensor_tensor(
                    out=tmp_r[tb][:], in0=XC[r][c][:], in1=ps_bc0[0][:], op=ALU.mult))
                vy0 = p.op("vector", lambda E, r=r, c=c, tb=tb: E.tensor_tensor(
                    out=Y0C[r][c][:], in0=tmp_r[tb][:], in1=ps_bc0[1][:], op=ALU.subtract), inc="vv")
                xc_free[r][c] = vy0
                p.wait("sync", "vv", vy0)
                y0da = p.op("sync", lambda E, t=t, c=c, r=r: E.dma_start(
                    out=y0sp[c * 128:(c + 1) * 128, t * T1:(t + 1) * T1], in_=Y0C[r][c][:]), inc=f"y0w{c}")
                y0sp_done[t][c] = (f"y0w{c}", y0da)
                y0c_spill[r][c] = (f"y0w{c}", y0da)
                p.wait("vector", "mm", xb_free[b])
                vb = p.op("vector", lambda E, b=b, r=r, c=c: E.tensor_copy(out=xb_r[b][:], in_=Y0C[r][c][:]), inc="vv")
                p.wait("scalar", "vv", vy0)
                p.wait("scalar", "mm", xq_free[b])
                sq = p.op("scalar", lambda E, b=b, r=r, c=c: E.activation(
                    out=xq_r[b][:], in_=Y0C[r][c][:], func=AF.Square), inc="ss")
                y0c_sq[r][c] = sq
                p.wait("tensor", "vv", vb)
                p.wait("tensor", "ss", sq)
                if c == 0:
                    p.wait("tensor", "vv", st1_read)
                p.op("tensor", lambda E, b=b, st=(c == 0), sp=(c == 7): E.matmul(
                    ps_st1[0][0:1, :], ones_bf[:], xb_r[b][:], start=st, stop=sp))
                mmv = p.op("tensor", lambda E, b=b, st=(c == 0), sp=(c == 7): E.matmul(
                    ps_st1[1][0:1, :], ones_bf[:], xq_r[b][:], start=st, stop=sp), inc="mm")
                xb_free[b] = mmv
                xq_free[b] = mmv
                drain_one()
            bc0_read = p.cnt["vv"]
            st1_rec[t] = mmv
            pending.extend(ln_smalls_chain(rt1[2 + t % 2] if False else rt1[(t % 4)], ps_st1[0], ps_st1[1], None, None, 0, T1,
                                           [("mm", st1_rec[t])]))

        def stG(t):
            nonlocal st1_read
            ch1_rec[t] = p.cnt["vv"]
            st1_read = ch1_rec[t]
            A_ = rt1[t % 4][0:1, 0:T1]
            B_ = rt1[t % 4][0:1, T1:2 * T1]
            p.wait("tensor", "vv", ch1_rec[t])
            p.wait("tensor", "vv", bc1_read)
            p.op("tensor", lambda E, B_=B_: E.matmul(ps_bc1[0][:], onesrow[:], B_, start=True, stop=True))
            bc1_mm[t] = p.op("tensor", lambda E, A_=A_: E.matmul(
                ps_bc1[1][:], onesrow[:], A_, start=True, stop=True), inc="mm")

        def stH(t):
            nonlocal bc1_read
            r = t % 2
            for c in range(C):
                tb = c % 2
                p.wait("vector", "mm", bc1_mm[t])
                p.wait("vector", "ss", tmp_ss[tb])
                p.op("vector", lambda E, r=r, c=c, tb=tb: E.tensor_tensor(
                    out=tmp_r[tb][:], in0=Y0C[r][c][:], in1=ps_bc1[0][:], op=ALU.mult))
                vt2 = p.op("vector", lambda E, tb=tb: E.tensor_tensor(
                    out=tmp_r[tb][:], in0=tmp_r[tb][:], in1=ps_bc1[1][:], op=ALU.subtract), inc="vv")
                p.wait("scalar", "vv", vt2)
                sz = p.op("scalar", lambda E, t=t, c=c, tb=tb: E.activation(
                    out=z1[c][:, t * T1:(t + 1) * T1], in_=tmp_r[tb][:], func=AF.Identity,
                    scale=g1s[:, c:c + 1], bias=b1s[:, c:c + 1]), inc="ss")
                tmp_ss[tb] = sz
                drain_one()
            z1_done_t[t] = p.cnt["ss"]
            bc1_read = p.cnt["vv"]

        for s in range(NT1 + 1):
            if s < NT1:
                stA(s)
                stB(s)
                if s == 0:
                    drain_all_padded()
            if s >= 1:
                stG(s - 1)
                stH(s - 1)
            if s < NT1:
                stD(s)
                stE(s)
        drain_all_padded()
        ph1.close()
        barrier_all()

        rwkv_es = ExitStack()
        rwkv = [rwkv_es.enter_context(nc.sbuf_tensor(f"rwkv{c}", [128, S], BF16, side="right")) for c in range(C)]
        ph2 = ExitStack()
        sb2 = lambda n_, s_, d_=F32: ph2.enter_context(nc.sbuf_tensor(n_, list(s_), d_))
        ps2 = lambda n_, s_: ph2.enter_context(nc.psum_tensor(n_, list(s_), F32))
        wnames = ("wr", "ww", "wk", "wv")
        wproj = {nm: [sb2(f"{nm}_{i}", (128, 1024), BF16) for i in range(2)] for nm in wnames}
        EWKD = sb2("EWKD", (128, T1 + 1))
        NUMD = sb2("NUMD", (128, T1 + 1))
        NUMO = sb2("NUMO", (128, T1 + 1))
        DENO = sb2("DENO", (128, T1 + 1))
        Dscan = sb2("Dscan", (128, T1 + 1))
        EWb = [sb2(f"EWb{i}", (128, T1)) for i in range(2)]
        Ub = [sb2(f"Ub{i}", (128, T1)) for i in range(2)]
        RREb = [sb2(f"RREb{i}", (128, T1)) for i in range(2)]
        Rb = [sb2(f"Rb{i}", (128, T1)) for i in range(2)]
        W1 = sb2("W1", (128, T1))
        W2 = sb2("W2", (128, T1))
        PSp = {nm: [ps2(f"PS{nm}{i}", (128, T1)) for i in range(2)] for nm in ("w", "k", "r", "v")}

        proj_src = {"wr": w4["wr4"], "ww": w4["ww4"], "wk": w4["wk4"], "wv": w4["wv4"]}
        wproj_done = [0] * C
        wslot_free = [0] * 2

        p.chan("wp0"); p.chan("wp1")

        def load_proj(ec):
            slot = ec % 2
            p.wait("gpsimd", "mm", wslot_free[slot])
            v = 0
            for nm in wnames:
                v = p.op("gpsimd", lambda E, nm=nm, ec=ec, s=slot: E.dma_start(
                    out=wproj[nm][s][:], in_=proj_src[nm][ec].rearrange("p a b -> p (a b)")), inc=f"wp{slot}")
            wproj_done[ec] = v

        load_proj(0)
        load_proj(1)

        p.wait("vector", "pre", pre_da)
        p.wait("vector", "ss", decays_done)
        bank_free_v = {}
        bank_free_s = {}
        ewkd_read = 0
        wkv_done = 0
        carry_rd = 0
        vRWKV = [[0] * NT1 for _ in range(C)]
        for ec in range(C):
            p.wait("vector", "vv", ewkd_read)
            p.op("vector", lambda E, ec=ec: E.tensor_copy(
                out=Dscan[:, 1:T1 + 1], in_=decays[:, ec:ec + 1].broadcast_to([128, T1])))
            p.op("vector", lambda E: E.tensor_copy(out=Dscan[:, 0:1], in_=zcol[:]))
            p.op("vector", lambda E: E.tensor_copy(out=NUMD[:, 0:1], in_=zcol[:]))
            dscan_v = ewkd_carry = p.op("vector", lambda E: E.tensor_copy(out=EWKD[:, 0:1], in_=zcol[:]), inc="vv")
            if ec + 1 < C:
                load_proj(ec + 1)
            for t in range(NT1):
                tp = t % 2
                prev = (ec, t - 2) if t >= 2 else ((ec - 1, NT1 - 2 + t) if ec > 0 else None)
                p.wait("tensor", f"wp{ec % 2}", wproj_done[ec])
                p.wait("tensor", "ss", z1_done_t[t])
                if prev is not None:
                    p.wait("tensor", "vv", bank_free_v[prev])
                    p.wait("tensor", "ss", bank_free_s[prev])
                mg = {}
                for nm, wsb in (("w", "ww"), ("k", "wk"), ("r", "wr"), ("v", "wv")):
                    for kc in range(C):
                        p.op("tensor", lambda E, nm=nm, wsb=wsb, kc=kc, t=t, tp=tp, ec=ec,
                             st=(kc == 0), sp=(kc == 7): E.matmul(
                            PSp[nm][tp][:], wproj[wsb][ec % 2][:, kc * 128:(kc + 1) * 128],
                            z1[kc][:, t * T1:(t + 1) * T1], start=st, stop=sp),
                            inc="mm" if kc == 7 else None)
                    mg[nm] = p.cnt["mm"]
                p.wait("scalar", "mm", mg["w"])
                sEW = p.op("scalar", lambda E, tp=tp: E.activation(
                    out=EWb[tp][:], in_=PSp["w"][tp][:], func=AF.Exp), inc="ss")
                p.wait("vector", "mm", mg["k"])
                p.wait("vector", "ss", sEW)
                vU = p.op("vector", lambda E, tp=tp: E.tensor_tensor(
                    out=Ub[tp][:], in0=PSp["k"][tp][:], in1=EWb[tp][:], op=ALU.subtract), inc="vv")
                p.wait("scalar", "vv", vU)
                p.wait("scalar", "vv", ewkd_read)
                p.wait("scalar", "vv", ewkd_carry)
                sEWK = p.op("scalar", lambda E, tp=tp: E.activation(
                    out=EWKD[:, 1:T1 + 1], in_=Ub[tp][:], func=AF.Exp), inc="ss")
                p.wait("scalar", "mm", mg["r"])
                sRRE = p.op("scalar", lambda E, tp=tp: E.activation(
                    out=RREb[tp][:], in_=PSp["r"][tp][:], func=AF.Exp, scale=-1.0), inc="ss")
                p.wait("vector", "ss", sRRE)
                p.op("vector", lambda E, tp=tp: E.tensor_single_scalar(
                    out=RREb[tp][:], in_=RREb[tp][:], scalar=1.0, op=ALU.add))
                p.op("vector", lambda E, tp=tp: E.reciprocal(out=Rb[tp][:], in_=RREb[tp][:]))
                p.wait("vector", "ss", sEWK)
                p.wait("vector", "mm", mg["v"])
                vN = p.op("vector", lambda E, tp=tp: E.tensor_tensor(
                    out=NUMD[:, 1:T1 + 1], in0=EWKD[:, 1:T1 + 1], in1=PSp["v"][tp][:], op=ALU.mult), inc="vv")
                p.op("vector", lambda E: E.tensor_tensor_scan(
                    out=NUMO[:], data0=Dscan[:], data1=NUMD[:], initial=0.0, op0=ALU.mult, op1=ALU.add))
                vden = p.op("vector", lambda E: E.tensor_tensor_scan(
                    out=DENO[:], data0=Dscan[:], data1=EWKD[:], initial=0.0, op0=ALU.mult, op1=ALU.add), inc="vv")
                ewkd_read = vden
                p.op("vector", lambda E: E.tensor_single_scalar(
                    out=W1[:], in_=DENO[:, 1:T1 + 1], scalar=WKV_EPS, op=ALU.add))
                p.op("vector", lambda E: E.reciprocal(out=W2[:], in_=W1[:]))
                wkv_done = p.op("vector", lambda E: E.tensor_tensor(
                    out=W1[:], in0=NUMO[:, 1:T1 + 1], in1=W2[:], op=ALU.mult), inc="vv")
                vR = p.op("vector", lambda E, ec=ec, t=t, tp=tp: E.tensor_tensor(
                    out=rwkv[ec][:, t * T1:(t + 1) * T1], in0=Rb[tp][:], in1=W1[:], op=ALU.mult), inc="vv")
                vRWKV[ec][t] = ("vv", vR)
                if t < NT1 - 1:
                    p.op("vector", lambda E: E.tensor_copy(out=NUMD[:, 0:1], in_=NUMO[:, T1:T1 + 1]))
                    carry_rd = ewkd_carry = p.op("vector", lambda E: E.tensor_copy(
                        out=EWKD[:, 0:1], in_=DENO[:, T1:T1 + 1]), inc="vv")
                bank_free_v[(ec, t)] = vN
                bank_free_s[(ec, t)] = sRRE
            wslot_free[ec % 2] = p.cnt["mm"]
        if debug:
            for c in range(C):
                p.wait("gpsimd", vRWKV[c][NT1 - 1][0], vRWKV[c][NT1 - 1][1])
                p.op("gpsimd", lambda E, c=c: E.dma_start(
                    out=rwkvdump[c * 128:(c + 1) * 128, :], in_=rwkv[c][:]), inc=p.chan("dbg"))
        ph2_end_mm = p.cnt["mm"]
        ph2.close()
        z1_es.close()
        ph4w = ExitStack()
        wkey_sb = [ph4w.enter_context(nc.sbuf_tensor(f"wkeysb{fc}", [128, 1024], BF16)) for fc in range(32)]
        barrier_all()
        wkey_done = 0
        p.chan("wk"); p.chan("wv"); p.chan("wr")
        p.wait("gpsimd", "mm", ph2_end_mm)
        for fc in range(32):
            wkey_done = p.op("gpsimd", lambda E, fc=fc: E.dma_start(
                out=wkey_sb[fc][:], in_=wkey4[fc].rearrange("p a b -> p (a b)")), inc="wk")

        ph3 = ExitStack()
        sb3 = lambda n_, s_, d_=F32: ph3.enter_context(nc.sbuf_tensor(n_, list(s_), d_, side="right"))
        ps3 = lambda n_, s_: ph3.enter_context(nc.psum_tensor(n_, list(s_), F32))
        wo_sb = [sb3(f"wo_sb{e}", (128, 1024), BF16) for e in range(C)]
        PSo = [ps3(f"PSo{i}", (128, T1)) for i in range(4)]
        ps_s2 = [ps3(f"ps_s2{i}", (128, T1)) for i in range(4)]
        rt3 = [sb3(f"rt3_{i}", (1, 3 * T1)) for i in range(2)]
        y0l_r = [sb3(f"y0l{i}", (128, T1)) for i in range(4)]
        y1b_r = [sb3(f"y1b{i}", (128, T1)) for i in range(3)]
        xb3_r = [sb3(f"xb3{i}", (128, T1), BF16) for i in range(8)]
        xq3_r = [sb3(f"xq3{i}", (128, T1), BF16) for i in range(8)]

        p.wait("gpsimd", "mm", ph2_end_mm)
        wo_done = 0
        p.chan("wo")
        for e in range(C):
            wo_done = p.op("gpsimd", lambda E, e=e: E.dma_start(
                out=wo_sb[e][:], in_=w4["wo4"][e].rearrange("p a b -> p (a b)")), inc="wo")

        y0l_free3 = [(None, 0)] * 4
        y1b_cons = [(0, 0)] * 3
        xb3_free = [0] * 8
        xq3_free = [0] * 8
        vb_sem3 = [0] * 8
        sq_sem3 = [0] * 8
        pso_free = [0] * 4
        s2_bank_free = [0] * 2
        y1sp_done = [[0] * C for _ in range(NT1)]
        st2 = [0] * NT1

        def stats_mms3(i):
            tq, cq = i // 8, i % 8
            tpar = tq % 2
            p.wait("tensor", "vv", vb_sem3[cq])
            p.wait("tensor", "ss", sq_sem3[cq])
            if cq == 0:
                p.wait("tensor", "vv", s2_bank_free[tpar])
            p.op("tensor", lambda E, cq=cq, tpar=tpar, st=(cq == 0), sp=(cq == 7): E.matmul(
                ps_s2[2 * tpar][0:1, :], ones_bf[:], xb3_r[cq][:], start=st, stop=sp))
            mmv = p.op("tensor", lambda E, cq=cq, tpar=tpar, st=(cq == 0), sp=(cq == 7): E.matmul(
                ps_s2[2 * tpar + 1][0:1, :], ones_bf[:], xq3_r[cq][:], start=st, stop=sp), inc="mm")
            xb3_free[cq] = mmv
            xq3_free[cq] = mmv
            if cq == 7:
                st2[tq] = mmv
                pending.extend(ln_smalls_chain(rt3[tq % 2], ps_s2[2 * (tq % 2)], ps_s2[2 * (tq % 2) + 1],
                                               rows[2], p.chan("rw2"), tq * T1, T1,
                                               [("mm", st2[tq]), ("rw2", None)]))

        for t in range(NT1):
            for c in range(C):
                po = c % 4
                i = t * C + c
                slot = i % 4
                sem, val = y0l_free3[slot]
                if sem is not None:
                    p.wait("sync", sem, val)
                wch3, wv3 = y0sp_done[t][c]
                p.wait("sync", wch3, wv3)
                da = p.op("sync", lambda E, t=t, c=c, s=slot: E.dma_start(
                    out=y0l_r[s][:], in_=y0sp[c * 128:(c + 1) * 128, t * T1:(t + 1) * T1]), inc=p.chan(f"yl{slot}"))
                rch, rval = vRWKV[C - 1][t]
                p.wait("tensor", rch, rval)
                p.wait("tensor", "wo", wo_done)
                p.wait("tensor", "vv", pso_free[po])
                mo = 0
                for kc in range(C):
                    mo = p.op("tensor", lambda E, c=c, kc=kc, t=t, po=po, st=(kc == 0), sp=(kc == 7): E.matmul(
                        PSo[po][:], wo_sb[c][:, kc * 128:(kc + 1) * 128],
                        rwkv[kc][:, t * T1:(t + 1) * T1], start=st, stop=sp),
                        inc="mm" if kc == 7 else None)
                if i >= 2:
                    stats_mms3(i - 2)
                yb = i % 3
                ss1, da1 = y1b_cons[yb]
                p.wait("vector", "mm", mo)
                p.wait("vector", f"yl{slot}", da)
                p.wait("vector", "ss", ss1)
                p.wait("vector", f"y1w{yb}", da1)
                vy1 = p.op("vector", lambda E, po=po, s=slot, y=yb: E.tensor_tensor(
                    out=y1b_r[y][:], in0=PSo[po][:], in1=y0l_r[s][:], op=ALU.add), inc="vv")
                pso_free[po] = vy1
                y0l_free3[slot] = ("vv", vy1)
                drain_one()
                p.wait("sync", "vv", vy1)
                y1da = p.op("sync", lambda E, t=t, c=c, y=yb: E.dma_start(
                    out=y1sp[c * 128:(c + 1) * 128, t * T1:(t + 1) * T1], in_=y1b_r[y][:]), inc=p.chan(f"y1w{yb}"))
                y1sp_done[t][c] = (f"y1w{yb}", y1da)
                p.wait("vector", "mm", xb3_free[c])
                vb = p.op("vector", lambda E, c=c, y=yb: E.tensor_copy(out=xb3_r[c][:], in_=y1b_r[y][:]), inc="vv")
                p.wait("scalar", "vv", vy1)
                p.wait("scalar", "mm", xq3_free[c])
                sq = p.op("scalar", lambda E, c=c, y=yb: E.activation(
                    out=xq3_r[c][:], in_=y1b_r[y][:], func=AF.Square), inc="ss")
                y1b_cons[yb] = (sq, y1da)
                vb_sem3[c] = vb
                sq_sem3[c] = sq
                drain_one()
            if t >= 1:
                s2_bank_free[(t - 1) % 2] = p.cnt["vv"]
        stats_mms3(NT1 * C - 2)
        stats_mms3(NT1 * C - 1)
        drain_all_padded()
        rows2_all_done = p.cnt["rw2"]
        ph3_end_mm = p.cnt["mm"]
        ph3_tail_vv = p.cnt["vv"]
        ph3_tail_ss = p.cnt["ss"]
        y1w_totals = [p.cnt[f"y1w{y}"] for y in range(3)]
        ph3.close()
        rwkv_es.close()
        wval_sb = [ph4w.enter_context(nc.sbuf_tensor(f"wvalsb{fc}", [128, 1024], BF16, side="right")) for fc in range(32)]

        ph4 = ExitStack()
        sb4 = lambda n_, s_, d_=F32: ph4.enter_context(nc.sbuf_tensor(n_, list(s_), d_))
        ps4 = lambda n_, s_: ph4.enter_context(nc.psum_tensor(n_, list(s_), F32))
        wrec_sb = [sb4(f"wrecsb{e}", (128, 1024), BF16) for e in range(C)]
        BC2r = [sb4(f"BC2r{i}", (128, T4)) for i in range(2)]
        BC2m = [sb4(f"BC2m{i}", (128, T4)) for i in range(2)]
        Y1T = [[sb4(f"Y1T{j}_{ec}", (128, T4)) for ec in range(C)] for j in range(2)]
        xring = [sb4(f"xring{i}", (128, T4)) for i in range(2)]
        z2t = [sb4(f"z2t{ec}", (128, T4), BF16) for ec in range(C)]
        tmp4 = [sb4(f"tmp4{i}", (128, T4)) for i in range(1)]
        KRb = sb4("KRb", (128, T4))
        KSb = [sb4(f"KS{i}", (128, T4), BF16) for i in range(32)]
        RC2 = [sb4(f"RC2{i}", (128, T4)) for i in range(8)]
        X1T = [sb4(f"X1T{j}", (128, T4)) for j in range(2)]
        xb4_r = [sb4(f"xb4{i}", (128, T4), BF16) for i in range(4)]
        xq4_r = [sb4(f"xq4{i}", (128, T4), BF16) for i in range(4)]
        PS_k4 = [ps4(f"PS_k4{i}", (128, 512)) for i in range(2)]
        PS_v4 = [ps4(f"PS_v4{i}", (128, 512)) for i in range(4)]
        ps_s3 = [ps4(f"ps_s3{i}", (128, 512)) for i in range(2)]
        rt4 = [sb4(f"rt4_{i}", (1, 3 * T4)) for i in range(1)]

        wrec_done = 0
        for e in range(C):
            wrec_done = p.op("gpsimd", lambda E, e=e: E.dma_start(
                out=wrec_sb[e][:], in_=w4["wrec4"][e].rearrange("p a b -> p (a b)")), inc="wr")


        bc2_free = [0] * 2
        y1t_free = [0] * 2
        xring_free = [0] * 2
        xa_fold = [[0] * C for _ in range(2)]
        vcmA_free = 0
        z2_done_r = [0] * 2
        ks_free = [0] * 32
        kr_free = 0
        k4_free_v = [pso_free[0], pso_free[1]]
        k4_free_s = [0] * 2
        vcm_free = ph3_tail_vv
        rc2_free = [0] * 8
        rc2_sig = [0] * 8
        x1spill = [0] * 2
        x1t_cons = [0] * 2
        st3 = [0] * NT4
        rows3_done = [0] * NT4
        xb4_free = [0] * 4
        xq4_free = [0] * 4
        vb_sem = [0] * 4
        sq_sem = [0] * 4
        s3_read_gate = 0

        def smalls3_chain(u, st_val):
            return ln_smalls_chain(rt4[0], ps_s3[0], ps_s3[1], rows[3], p.chan("rw3"), u * T4, T4,
                                   [("mm", st_val), ("rw3", None)])

        z2t_free_mm = ph3_end_mm

        def stage_head(u):
            nonlocal bc2_free, y1t_free, z2t_free_mm
            t1i = u // 2
            up = u % 2
            if u == 0:
                p.wait("vector", "mm", ph3_end_mm)
                p.wait("scalar", "mm", ph3_end_mm)
                p.wait("sync", "mm", ph3_end_mm)
                p.wait("sync", "vv", ph3_tail_vv)
                p.wait("sync", "ss", ph3_tail_ss)
                p.wait("sync", "rw2", rows2_all_done)
                for y in range(3):
                    p.wait("sync", f"y1w{y}", y1w_totals[y])
            p.wait("gpsimd", "rw2", rows2_all_done)
            p.wait("gpsimd", "vv", bc2_free[up])
            p.op("gpsimd", lambda E, u=u, up=up: E.dma_start(
                out=BC2r[up][:], in_=rows[2][0:1, u * T4:(u + 1) * T4].broadcast_to([128, T4])), inc=p.chan(f"bc2{up}"))
            bc_da = p.op("gpsimd", lambda E, u=u, up=up: E.dma_start(
                out=BC2m[up][:], in_=rows[2][1:2, u * T4:(u + 1) * T4].broadcast_to([128, T4])), inc=f"bc2{up}")
            p.wait("sync", "vv", y1t_free[up])
            y1_da = 0
            for ec in range(C):
                wch4, wv4 = y1sp_done[t1i][ec]
                p.wait("sync", wch4, wv4)
                y1_da = p.op("sync", lambda E, u=u, up=up, ec=ec: E.dma_start(
                    out=Y1T[up][ec][:], in_=y1sp[ec * 128:(ec + 1) * 128, u * T4:(u + 1) * T4]), inc=p.chan(f"y1t{up}"))
            z2rec = 0
            z2par = [z2_done_r[up]] * 1
            for ec in range(C):
                tb = 0
                xs = ec % 2
                p.wait("sync", "vv", xring_free[xs])
                x_da = p.op("sync", lambda E, u=u, ec=ec, xs=xs: E.dma_start(
                    out=xring[xs][:], in_=xT[ec * 128:(ec + 1) * 128, u * T4:(u + 1) * T4]), inc=p.chan(f"xr{xs}"))
                p.wait("vector", f"y1t{up}", y1_da)
                p.wait("vector", f"bc2{up}", bc_da)
                p.wait("vector", "ss", z2par[tb])
                p.op("vector", lambda E, up=up, ec=ec, tb=tb: E.tensor_tensor(
                    out=tmp4[tb][:], in0=Y1T[up][ec][:], in1=BC2r[up][:], op=ALU.mult))
                vt = p.op("vector", lambda E, up=up, tb=tb: E.tensor_tensor(
                    out=tmp4[tb][:], in0=tmp4[tb][:], in1=BC2m[up][:], op=ALU.subtract), inc="vv")
                p.wait("scalar", "vv", vt)
                p.wait("scalar", "mm", z2t_free_mm)
                z2rec = p.op("scalar", lambda E, ec=ec, tb=tb: E.activation(
                    out=z2t[ec][:], in_=tmp4[tb][:], func=AF.Identity,
                    scale=g2s[:, ec:ec + 1], bias=b2s[:, ec:ec + 1]), inc="ss")
                z2par[tb] = z2rec
                p.wait("vector", f"xr{xs}", x_da)
                xa = p.op("vector", lambda E, up=up, ec=ec, xs=xs: E.tensor_tensor(
                    out=Y1T[up][ec][:], in0=Y1T[up][ec][:], in1=xring[xs][:], op=ALU.add), inc="vv")
                xring_free[xs] = xa
                xa_fold[up][ec] = xa
                drain_one()
            z2_done_r[up] = z2rec
            bc2_free[up] = p.cnt["vv"]
            return z2rec

        def kcm_only(u, z2rec):
            nonlocal kr_free
            for fc in range(32):
                fp = fc % 2
                p.wait("tensor", "wk", wkey_done)
                p.wait("tensor", "ss", z2rec)
                p.wait("tensor", "vv", k4_free_v[fp])
                p.wait("tensor", "ss", k4_free_s[fp])
                mk = 0
                for kc in range(C):
                    mk = p.op("tensor", lambda E, fc=fc, kc=kc, fp=fp, st=(kc == 0), sp=(kc == 7): E.matmul(
                        PS_k4[fp][:, :T4], wkey_sb[fc][:, kc * 128:(kc + 1) * 128],
                        z2t[kc][:], start=st, stop=sp), inc="mm" if kc == 7 else None)
                p.wait("vector", "mm", mk)
                p.wait("vector", "ss", kr_free)
                vkr = p.op("vector", lambda E, fp=fp: E.tensor_single_scalar(
                    out=KRb[:], in_=PS_k4[fp][:, :T4], scalar=0.0, op=ALU.max), inc="vv")
                k4_free_v[fp] = vkr
                p.wait("scalar", "vv", vkr)
                p.wait("scalar", "mm", ks_free[fc])
                sks = p.op("scalar", lambda E, fc=fc: E.activation(
                    out=KSb[fc][:], in_=KRb[:], func=AF.Square), inc="ss")
                kr_free = sks
                drain_one()
            return sks

        def rcm_sig_half(u, dh):
            last = 0
            for pair in range(2):
                mrs = [0] * 2
                for ecl2 in range(2):
                    ec = dh * 4 + pair * 2 + ecl2
                    ep = ec % 2
                    p.wait("tensor", "wr", wrec_done)
                    p.wait("tensor", "vv", k4_free_v[ep])
                    p.wait("tensor", "ss", k4_free_s[ep])
                    mr = 0
                    for kc in range(C):
                        mr = p.op("tensor", lambda E, kc=kc, ec=ec, ep=ep, st=(kc == 0), sp=(kc == 7): E.matmul(
                            PS_k4[ep][:, :T4], wrec_sb[ec][:, kc * 128:(kc + 1) * 128],
                            z2t[kc][:], start=st, stop=sp), inc="mm" if kc == 7 else None)
                    k4_free_v[ep] = 0
                    mrs[ecl2] = mr
                srs = [0, 0]
                for ecl2 in range(2):
                    ec = dh * 4 + pair * 2 + ecl2
                    ep = ec % 2
                    slot = dh * 4 + pair * 2 + ecl2
                    p.wait("scalar", "mm", mrs[ecl2])
                    p.wait("scalar", "vv", rc2_free[slot])
                    srs[ecl2] = p.op("scalar", lambda E, ep=ep, slot=slot: E.activation(
                        out=RC2[slot][:], in_=PS_k4[ep][:, :T4], func=AF.Sigmoid, scale=1.0), inc="ss")
                    k4_free_s[ep] = srs[ecl2]
                    rc2_sig[slot] = srs[ecl2]
                    last = srs[ecl2]
            return last

        def vcm_pass(u, dh, ks_ready, bankfree_vv):
            p.wait("tensor", "ss", ks_ready)
            p.wait("tensor", "vv", bankfree_vv)
            mvcm = 0
            for fc in range(32):
                p.wait("tensor", f"wv{fc // 4}", 64)
                for dcl in range(4):
                    dc = dh * 4 + dcl
                    mvcm = p.op("tensor", lambda E, fc=fc, dc=dc, dcl=dcl,
                                st=(fc == 0), sp=(fc == 31): E.matmul(
                        PS_v4[dcl][:, :T4],
                        wval_sb[fc][:, dc * 128:(dc + 1) * 128], KSb[fc][:],
                        start=st, stop=sp), inc="mm" if dcl == 3 else None)
                if dh == 1:
                    ks_free[fc] = mvcm
            return mvcm

        def combine_half(u, dh, mvcm):
            nonlocal vcm_free, vcmA_free, s3_read_gate
            up = u % 2
            lastvx1 = 0
            if dh == 0:
                drain_one()
                drain_one()
            bank_evac = 0
            for ecl in range(4):
                slot = dh * 4 + ecl
                p.wait("vector", "mm", mvcm)
                p.wait("vector", "ss", rc2_sig[slot])
                bank_evac = p.op("vector", lambda E, slot=slot, ecl=ecl: E.tensor_tensor(
                    out=RC2[slot][:], in0=PS_v4[ecl][:, :T4], in1=RC2[slot][:], op=ALU.mult), inc="vv")
            if dh == 0:
                vcmA_free = bank_evac
            else:
                vcm_free = bank_evac
            for ecl in range(4):
                ec = dh * 4 + ecl
                slot = dh * 4 + ecl
                xs4 = ec % 2
                p.wait("vector", "ss", x1t_cons[xs4])
                p.wait("vector", f"x1w{xs4}", x1spill[xs4])
                vx1 = p.op("vector", lambda E, slot=slot, up=up, ec=ec, xs4=xs4: E.tensor_tensor(
                    out=X1T[xs4][:], in0=RC2[slot][:], in1=Y1T[up][ec][:], op=ALU.add), inc="vv")
                rc2_free[slot] = vx1
                lastvx1 = vx1
                if ec == C - 1:
                    y1t_free[up] = vx1
                p.wait("sync", "vv", vx1)
                x1da = p.op("sync", lambda E, u=u, ec=ec, xs4=xs4: E.dma_start(
                    out=x1sp[ec * 128:(ec + 1) * 128, u * T4:(u + 1) * T4], in_=X1T[xs4][:]), inc=p.chan(f"x1w{xs4}"))
                x1spill[xs4] = x1da
                sl4 = ec % 4
                p.wait("vector", "mm", xb4_free[sl4])
                vb = p.op("vector", lambda E, sl4=sl4, xs4=xs4: E.tensor_copy(out=xb4_r[sl4][:], in_=X1T[xs4][:]), inc="vv")
                p.wait("scalar", "vv", vx1)
                p.wait("scalar", "mm", xq4_free[sl4])
                sq = p.op("scalar", lambda E, sl4=sl4, xs4=xs4: E.activation(
                    out=xq4_r[sl4][:], in_=X1T[xs4][:], func=AF.Square), inc="ss")
                x1t_cons[xs4] = sq
                vb_sem[sl4] = vb
                sq_sem[sl4] = sq
                drain_one()
            return lastvx1

        def stats_mms(u, dh, gate=0):
            mmv = 0
            for ecl in range(4):
                ec = dh * 4 + ecl
                sl4 = ecl
                p.wait("tensor", "vv", vb_sem[sl4])
                p.wait("tensor", "ss", sq_sem[sl4])
                if ec == 0:
                    p.wait("tensor", "vv", gate)
                p.op("tensor", lambda E, sl4=sl4, st=(ec == 0), sp=(ec == 7): E.matmul(
                    ps_s3[0][0:1, :T4], ones_bf[:], xb4_r[sl4][:], start=st, stop=sp))
                mmv = p.op("tensor", lambda E, sl4=sl4, st=(ec == 0), sp=(ec == 7): E.matmul(
                    ps_s3[1][0:1, :T4], ones_bf[:], xq4_r[sl4][:], start=st, stop=sp), inc="mm")
                xb4_free[sl4] = mmv
                xq4_free[sl4] = mmv
                if ec == 7:
                    st3[u] = mmv
            return mmv

        z2r = stage_head(0)
        ksr = kcm_only(0, z2r)
        rcm_sig_half(0, 0)
        p.wait("gpsimd", "mm", ph3_end_mm)
        for fc in range(32):
            p.op("gpsimd", lambda E, fc=fc: E.dma_start(
                out=wval_sb[fc][:], in_=wval4[fc].rearrange("p a b -> p (a b)")), inc=p.chan(f"wv{fc // 4}"))
        for u in range(NT4):
            mvcmA = vcm_pass(u, 0, ksr, vcm_free)
            if u >= 1:
                stats_mms(u - 1, 1)
                pending.extend(smalls3_chain(u - 1, st3[u - 1]))
            rcm_sig_half(u, 1)
            z2t_free_mm = p.cnt["mm"]
            combine_half(u, 0, mvcmA)
            s3gate = p.cnt["vv"]
            if u + 1 < NT4:
                z2r = stage_head(u + 1)
            mvcmB = vcm_pass(u, 1, ksr, vcmA_free)
            stats_mms(u, 0, s3gate)
            if u + 1 < NT4:
                ksr = kcm_only(u + 1, z2r)
                rcm_sig_half(u + 1, 0)
            combine_half(u, 1, mvcmB)
            rows3_done[u] = None
        stats_mms(NT4 - 1, 1)
        pending.extend(smalls3_chain(NT4 - 1, st3[NT4 - 1]))
        drain_all_padded()
        rows3_all_done = p.cnt["rw3"]
        for u in range(NT4):
            rows3_done[u] = rows3_all_done
        ph4.close()
        ph4w.close()
        barrier_all()

        ph5 = ExitStack()
        sb5 = lambda n_, s_, d_=F32: ph5.enter_context(nc.sbuf_tensor(n_, list(s_), d_))
        ps5 = lambda n_, s_: ph5.enter_context(nc.psum_tensor(n_, list(s_), F32))
        OUT = [sb5(f"OUT{ec}", (128, HS)) for ec in range(C)]
        y3h = [sb5(f"y3h{ec}", (128, HS), BF16) for ec in range(C)]
        KEY = [sb5(f"KEY{i}", (128, 4096), BF16) for i in range(4)]
        VAL = [sb5(f"VAL{i}", (128, 4096), BF16) for i in range(4)]
        GFM = sb5("GFM", (32, HS))
        BC3r = [sb5(f"BC3r{i}", (128, T5)) for i in range(2)]
        BC3m = [sb5(f"BC3m{i}", (128, T5)) for i in range(2)]
        tmp5 = [sb5(f"tmp5{i}", (128, T5)) for i in range(2)]
        Gb = [sb5(f"Gb{i}", (128, 32)) for i in range(4)]
        GCb = [sb5(f"GCb{i}", (128, 32)) for i in range(4)]
        Mb = [sb5(f"Mb{i}", (128, 1)) for i in range(4)]
        MKb = [sb5(f"MKb{i}", (128, 32)) for i in range(4)]
        GFb = [sb5(f"GFb{i}", (128, 32)) for i in range(4)]
        GB = [sb5(f"GB{i}", (128, T5B)) for i in range(4)]
        SC = [[[sb5(f"SC{par}_{el}_{fc}", (128, T5B), BF16) for fc in range(4)] for el in range(GE)] for par in range(2)]
        PS_sc = [ps5(f"PS_sc{i}", (128, 512)) for i in range(2)]
        PS_out = [ps5(f"PS_out{i}", (128, 512)) for i in range(4)]

        kv_done = [0] * NE
        kv_free = [0] * 4
        out_done = 0
        p.chan("outw")
        gb_free = [0] * 4
        sc_free = [[[0] * 4 for _ in range(GE)] for _ in range(2)]
        sc_ready = [[[0] * 4 for _ in range(GE)] for _ in range(2)]
        sc_bank_free = [0] * 2
        sc_bank_free_ss = [0] * 2
        gdr_done_prev = 0
        gbatch_free = [0] * 4
        tr_read = 0
        outadd_last = 0
        for h in range(2):
            hoff = h * HS
            outacc_da = 0
            for ec2 in range(2):
                p.wait("sync", f"x1w{ec2}", x1spill[ec2])
            for ec in range(C):
                if h == 1:
                    p.wait("sync", "outw", out_done)
                    p.wait("sync", "vv", outadd_last)
                outacc_da = p.op("sync", lambda E, ec=ec, hoff=hoff: E.dma_start(
                    out=OUT[ec][:], in_=x1sp[ec * 128:(ec + 1) * 128, hoff:hoff + HS]), inc=p.chan("outld"))
            bc3_free = [0] * 2
            y3rec = [[0] * C for _ in range(NT5)]
            for u in range(NT5):
                ug = (hoff + u * T5) // T4
                up = u % 2
                p.wait("gpsimd", "rw3", rows3_done[ug])
                p.wait("gpsimd", "vv", bc3_free[up])
                p.op("gpsimd", lambda E, up=up, hoff=hoff, u=u: E.dma_start(
                    out=BC3r[up][:], in_=rows[3][0:1, hoff + u * T5:hoff + (u + 1) * T5].broadcast_to([128, T5])), inc=p.chan(f"bc3{up}"))
                bc_da = p.op("gpsimd", lambda E, up=up, hoff=hoff, u=u: E.dma_start(
                    out=BC3m[up][:], in_=rows[3][1:2, hoff + u * T5:hoff + (u + 1) * T5].broadcast_to([128, T5])), inc=f"bc3{up}")
                for ec in range(C):
                    tb = ec % 2
                    p.wait("vector", "outld", outacc_da)
                    p.wait("vector", f"bc3{up}", bc_da)
                    p.op("vector", lambda E, ec=ec, u=u, up=up, tb=tb: E.tensor_tensor(
                        out=tmp5[tb][:], in0=OUT[ec][:, u * T5:(u + 1) * T5], in1=BC3r[up][:], op=ALU.mult))
                    y3rec[u][ec] = p.op("vector", lambda E, ec=ec, u=u, up=up, tb=tb: E.tensor_tensor(
                        out=y3h[ec][:, u * T5:(u + 1) * T5], in0=tmp5[tb][:], in1=BC3m[up][:], op=ALU.subtract), inc="vv")
                bc3_free[up] = y3rec[u][C - 1]
            gfm_last = 0
            for scb in range(0, NSC, 4):
                gsig = [0] * 4
                for j in range(4):
                    sc = scb + j
                    sp5 = sc % 2
                    u5 = (sc * 128) // T5
                    p.wait("tensor", "vv", y3rec[u5][C - 1])
                    p.wait("tensor", "preg", pre_dw)
                    p.wait("tensor", "vv", sc_bank_free[sp5])
                    p.wait("tensor", "ss", sc_bank_free_ss[sp5])
                    ms = 0
                    for kc in range(C):
                        ms = p.op("tensor", lambda E, sc=sc, kc=kc, sp5=sp5, st=(kc == 0), sp=(kc == 7): E.matmul(
                            PS_sc[sp5][:, 0:32], y3h[kc][:, sc * 128:(sc + 1) * 128],
                            selT_sb[:, kc * 32:(kc + 1) * 32], start=st, stop=sp),
                            inc="mm" if kc == 7 else None)
                    g4 = sc % 4
                    p.wait("scalar", "mm", ms)
                    p.wait("scalar", "vv", gbatch_free[g4])
                    gsig[j] = p.op("scalar", lambda E, sp5=sp5, g4=g4: E.activation(
                        out=Gb[g4][:], in_=PS_sc[sp5][:, 0:32], func=AF.Exp, scale=-1.0), inc="ss")
                    sc_bank_free_ss[sp5] = gsig[j]
                chains4 = []
                for j in range(4):
                    sc = scb + j
                    g4 = sc % 4
                    q = []
                    q.append(("vector", lambda E, g4=g4: E.tensor_single_scalar(
                        out=Gb[g4][:], in_=Gb[g4][:], scalar=1.0, op=ALU.add), "vv", [("ss", gsig[j])]))
                    q.append(("vector", lambda E, g4=g4: E.reciprocal(out=Gb[g4][:], in_=Gb[g4][:]), "vv", []))
                    q.append(("vector", lambda E, g4=g4: E.tensor_copy(out=GCb[g4][:], in_=Gb[g4][:]), "vv", []))
                    for it in range(4):
                        q.append(("vector", lambda E, g4=g4: E.reduce_max(
                            out=Mb[g4][:], in_=GCb[g4][:], axis=AX.X), "vv", []))
                        if it < 3:
                            q.append(("vector", lambda E, g4=g4: E.tensor_tensor(
                                out=MKb[g4][:], in0=GCb[g4][:], in1=Mb[g4][:, 0:1].broadcast_to([128, 32]),
                                op=ALU.is_ge), "vv", []))
                            q.append(("vector", lambda E, g4=g4: E.scalar_tensor_tensor(
                                out=GCb[g4][:], in0=MKb[g4][:], scalar=-1e30, in1=GCb[g4][:],
                                op0=ALU.mult, op1=ALU.add), "vv", []))
                    q.append(("vector", lambda E, g4=g4: E.tensor_tensor(
                        out=MKb[g4][:], in0=Gb[g4][:], in1=Mb[g4][:, 0:1].broadcast_to([128, 32]),
                        op=ALU.is_ge), "vv", []))
                    q.append(("vector", lambda E, g4=g4: E.tensor_tensor(
                        out=GFb[g4][:], in0=Gb[g4][:], in1=MKb[g4][:], op=ALU.mult), "vv", []))
                    chains4.append(q)
                gf_rec = [0] * 4
                ncs5 = len(chains4[0])
                for si in range(ncs5):
                    for j in range(4):
                        eng, make, inc, waits = chains4[j][si]
                        for s_, v_ in waits:
                            p.wait(eng, s_, p.cnt[s_] if v_ is None else v_)
                        r_ = p.op(eng, make, inc=inc)
                        if si == ncs5 - 1:
                            gf_rec[j] = r_
                for j in range(4):
                    sc = scb + j
                    g4 = sc % 4
                    p.wait("tensor", "vv", gf_rec[j])
                    p.wait("tensor", "ss", tr_read)
                    p.wait("tensor", "vv", outadd_last)
                    mtr = p.op("tensor", lambda E, g4=g4: E.transpose(
                        out=PS_out[0][0:32, 0:128], in_=GFb[g4][:], identity=ident[:]), inc="mm")
                    p.wait("scalar", "mm", mtr)
                    p.wait("scalar", "gdr", gdr_done_prev)
                    tr_read = p.op("scalar", lambda E, sc=sc: E.activation(
                        out=GFM[0:32, sc * 128:(sc + 1) * 128], in_=PS_out[0][0:32, 0:128], func=AF.Copy), inc="ss")
                    gbatch_free[g4] = gf_rec[j]
                gfm_last = tr_read
            p.wait("sync", "ss", gfm_last)
            gdr_done = p.op("sync", lambda E, hoff=hoff: E.dma_start(
                out=gdr[0:32, hoff:hoff + HS], in_=GFM[:]), inc=p.chan("gdr"))
            gdr_done_prev = gdr_done
            seq = [(g, t) for g in range(NG) for t in range(NT5B)]

            def kv_load(g):
                for el in range(GE):
                    e = g * GE + el
                    slot = e % 4
                    p.wait("gpsimd", "mm", kv_free[slot])
                    p.op("gpsimd", lambda E, e=e, slot=slot: E.dma_start(
                        out=KEY[slot][:], in_=keysT[e]), inc=p.chan(f"kv{slot}"))
                    kv_done[e] = p.op("gpsimd", lambda E, e=e, slot=slot: E.dma_start(
                        out=VAL[slot][:], in_=valsT[e]), inc=f"kv{slot}")

            def front(i):
                g, t = seq[i]
                par = i % 2
                for el in range(GE):
                    e = g * GE + el
                    slot = e % 4
                    gbs = (i * GE + el) % 4
                    p.wait("gpsimd", "gdr", gdr_done)
                    p.wait("gpsimd", "vv", gb_free[gbs])
                    gb_da = p.op("gpsimd", lambda E, e=e, gbs=gbs, hoff=hoff, t=t: E.dma_start(
                        out=GB[gbs][:], in_=gdr[e:e + 1, hoff + t * T5B:hoff + (t + 1) * T5B].broadcast_to([128, T5B])), inc=p.chan(f"gb{gbs}"))
                    for fc in range(4):
                        fp = fc % 2
                        p.wait("tensor", f"kv{slot}", kv_done[e])
                        p.wait("tensor", "vv", y3rec[2 * t + 1][C - 1])
                        p.wait("tensor", "vv", sc_bank_free[fp])
                        p.wait("tensor", "ss", sc_bank_free_ss[fp])
                        mk = 0
                        for kc in range(C):
                            mk = p.op("tensor", lambda E, slot=slot, kc=kc, fc=fc, t=t, fp=fp,
                                      st=(kc == 0), sp=(kc == 7): E.matmul(
                                PS_sc[fp][:, 0:T5B],
                                KEY[slot][:, kc * 512 + fc * 128: kc * 512 + (fc + 1) * 128],
                                y3h[kc][:, t * T5B:(t + 1) * T5B], start=st, stop=sp),
                                inc="mm" if kc == 7 else None)
                        p.wait("vector", "mm", mk)
                        p.wait("vector", f"gb{gbs}", gb_da)
                        p.wait("vector", "mm", sc_free[par][el][fc])
                        vsc = p.op("vector", lambda E, par=par, el=el, fc=fc, fp=fp, gbs=gbs: E.scalar_tensor_tensor(
                            out=SC[par][el][fc][:], in0=PS_sc[fp][:, 0:T5B],
                            scalar=0.0, in1=GB[gbs][:], op0=ALU.max, op1=ALU.mult), inc="vv")
                        sc_bank_free[fp] = vsc
                        if fc == 3:
                            gb_free[gbs] = vsc
                        sc_ready[par][el][fc] = vsc
                if t == 1 and g + 1 < NG:
                    kv_load(g + 1)

            def values_pass(i, dh):
                nonlocal outadd_last, mvals_last
                g, t = seq[i]
                par = i % 2
                p.wait("tensor", "vv", sc_ready[par][GE - 1][3])
                p.wait("tensor", "vv", outadd_last)
                p.wait("tensor", "ss", gfm_last)
                mvals = 0
                for el in range(GE):
                    for fc in range(4):
                        for dcl in range(4):
                            dc = dh * 4 + dcl
                            mvals = p.op("tensor", lambda E, par=par, el=el, slot2=(g * GE + el) % 4,
                                         fc=fc, dc=dc, dcl=dcl,
                                         st=(el == 0 and fc == 0), sp=(el == GE - 1 and fc == 3): E.matmul(
                                PS_out[dcl][:, 0:T5B],
                                VAL[slot2][:, fc * 1024 + dc * 128: fc * 1024 + (dc + 1) * 128],
                                SC[par][el][fc][:], start=st, stop=sp), inc="mm" if dcl == 3 else None)
                p.wait("vector", "mm", mvals)
                for dcl in range(4):
                    dc = dh * 4 + dcl
                    outadd_last = p.op("vector", lambda E, dc=dc, dcl=dcl, t=t: E.tensor_tensor(
                        out=OUT[dc][:, t * T5B:(t + 1) * T5B], in0=OUT[dc][:, t * T5B:(t + 1) * T5B],
                        in1=PS_out[dcl][:, 0:T5B], op=ALU.add), inc="vv")
                mvals_last = mvals
                if dh == 1:
                    for el in range(GE):
                        for fc in range(4):
                            sc_free[par][el][fc] = mvals
                        kv_free[(g * GE + el) % 4] = mvals

            kv_load(0)
            mvals_last = 0
            front(0)
            for i in range(len(seq)):
                values_pass(i, 0)
                if i + 1 < len(seq):
                    front(i + 1)
                values_pass(i, 1)
            for ec in range(C):
                p.wait("sync", "vv", outadd_last)
                out_done = p.op("sync", lambda E, ec=ec, hoff=hoff: E.dma_start(
                    out=outT[ec * 128:(ec + 1) * 128, hoff:hoff + HS], in_=OUT[ec][:]), inc=p.chan("outw"))
        ph5.close()

        for s_name in list(p.cnt):
            p.wait("sync", s_name, p.cnt[s_name])

        @block.sync
        def _(E):
            p.replay("sync", E)

        @block.vector
        def _(E):
            p.replay("vector", E)

        @block.scalar
        def _(E):
            p.replay("scalar", E)

        @block.tensor
        def _(E):
            p.replay("tensor", E)

        @block.gpsimd
        def _(E):
            p.replay("gpsimd", E)

    es.close()
    return nc


def _host_layouts(inputs):
    f32 = np.float32
    A = lambda x: np.ascontiguousarray(np.asarray(x), dtype=f32)
    _ = inputs["L0_attn_time_first"]

    def t4(W):
        return np.ascontiguousarray(A(W).T.reshape(8, 128, 8, 128).transpose(2, 1, 0, 3))

    d = {}
    d["wr4"] = t4(inputs["L0_attn_Wr"])
    d["ww4"] = t4(inputs["L0_attn_Ww"])
    d["wk4"] = t4(inputs["L0_attn_Wk"])
    d["wv4"] = t4(inputs["L0_attn_Wv"])
    d["wo4"] = t4(inputs["L0_attn_Wo"])
    d["wrec4"] = t4(inputs["L0_attn_Wrec"])
    d["wkey4"] = np.ascontiguousarray(A(inputs["L0_attn_Wkey"]).T.reshape(8, 128, 32, 128).transpose(2, 1, 0, 3))
    d["wval4"] = np.ascontiguousarray(A(inputs["L0_attn_Wval"]).T.reshape(32, 128, 8, 128))
    d["selT3"] = np.ascontiguousarray(
        A(inputs["L0_ffn_expert_sel"]).T.reshape(8, 128, 32).transpose(1, 0, 2)).reshape(128, 256)
    d["keysT"] = np.ascontiguousarray(
        A(inputs["L0_ffn_keys"]).reshape(32, 8, 128, 512).transpose(0, 2, 1, 3)).reshape(32, 128, 4096)
    d["valsT"] = np.ascontiguousarray(
        A(inputs["L0_ffn_values"]).reshape(32, 4, 128, 1024).transpose(0, 2, 1, 3)).reshape(32, 128, 4096)
    for nm, key in (("g1v", "L0_attn_ln1_g"), ("b1v", "L0_attn_ln1_b"),
                    ("g2v", "L0_attn_ln2_g"), ("b2v", "L0_attn_ln2_b"),
                    ("tdv", "L0_attn_time_decay")):
        d[nm] = np.ascontiguousarray(A(inputs[key]).reshape(8, 128).T)
    d["onescol_bf"] = np.ones((128, 1), dtype=ml_dtypes.bfloat16)
    d["zcol_in"] = np.zeros((128, 1), dtype=f32)
    d["onecol_in"] = np.ones((128, 1), dtype=f32)
    d["onesrow_in"] = np.ones((1, 128), dtype=f32)
    d["epscol_in"] = np.full((128, 1), 1e-8, dtype=f32)
    d["ident_in"] = np.eye(128, dtype=f32)
    return d


def kernel(**inputs):
    nc = build_program()
    shared = _host_layouts(inputs)
    x = np.asarray(inputs["L0_attn_x"], dtype=np.float32)
    in_maps = []
    for b in range(8):
        m = dict(shared)
        m["xT"] = np.ascontiguousarray(x[b].T)
        in_maps.append(m)
    res = run_bass_kernel_spmd(nc, in_maps, core_ids=list(range(8)))
    out = np.stack([np.asarray(r["outT"]).T for r in res.results], axis=0)
    return np.ascontiguousarray(out, dtype=np.float32)


# version 22
# speedup vs baseline: 1.0829x; 1.0113x over previous
import numpy as np
import ml_dtypes
import concourse.bass as bass
import concourse.mybir as mybir
from concourse.bass_utils import run_bass_kernel_spmd
from contextlib import ExitStack

F32 = mybir.dt.float32
BF16 = mybir.dt.bfloat16
AF = mybir.ActivationFunctionType
ALU = mybir.AluOpType
AX = mybir.AxisListType

D = 1024
S = 4096
C = 8
T1 = 512
NT1 = S // T1
T4 = 256
NT4 = S // T4
HS = S // 2
T5 = 256
NT5 = HS // T5
T5B = 512
NT5B = HS // T5B
NSC = HS // 128
NE = 32
GE = 2
NG = NE // GE
LN_EPS = 1e-5
WKV_EPS = 1e-8

ENGINES = ("sync", "vector", "scalar", "tensor", "gpsimd")


class P:
    def __init__(self, nc, es, selfsync=False):
        self.nc = nc
        self.es = es
        self.sems = {}
        self.cnt = {}
        self.q = {e: [] for e in ENGINES}
        self.lastw = {}
        self.selfsync = selfsync
        for s_name in ("mm", "vv", "ss", "pp"):
            self.chan(s_name)
        self.lastmark = {}
        if selfsync:
            for e in ENGINES:
                self.chan(f"ord_{e}")

    def chan(self, name):
        if name in self.sems:
            return name
        self.sems[name] = self.es.enter_context(self.nc.semaphore(name))
        self.cnt[name] = 0
        for e in ENGINES:
            self.lastw[(e, name)] = 0
        return name

    def wait(self, eng, sem, val):
        if not val:
            return
        assert val <= self.cnt[sem], f"wait on future: {eng} {sem} {val} > {self.cnt[sem]}"
        if self.lastw[(eng, sem)] >= val:
            return
        self.lastw[(eng, sem)] = val
        h = self.sems[sem]
        self.q[eng].append(lambda E, h=h, v=val: E.wait_ge(h, v))

    def op(self, eng, make, inc=None):
        if self.selfsync:
            pm = self.lastmark.get(eng)
            if pm is not None:
                self.q[eng].append(lambda E, h=self.sems[pm[0]], v=pm[1]: E.wait_ge(h, v))
        if inc is None:
            if self.selfsync:
                ordname = f"ord_{eng}"
                self.cnt[ordname] += 1
                self.lastmark[eng] = (ordname, self.cnt[ordname])

                def f0(E, make=make, h=self.sems[ordname], n2=1):
                    inst = make(E)
                    if inst is not None:
                        inst.then_inc(h, n2)
                self.q[eng].append(f0)
            else:
                self.q[eng].append(make)
            return None
        n = 1 if inc in ("mm", "vv", "ss", "pp") else 16
        h = self.sems[inc]
        self.cnt[inc] += n
        if self.selfsync:
            self.lastmark[eng] = (inc, self.cnt[inc])

        def f(E, make=make, h=h, n=n):
            make(E).then_inc(h, n)
        self.q[eng].append(f)
        return self.cnt[inc]

    def replay(self, eng, E):
        for f in self.q[eng]:
            f(E)


def build_program(debug=False, selfsync=False):
    nc = bass.Bass("TRN2", target_bir_lowering=False, debug=False)

    dt_in = lambda name, shape: nc.dram_tensor(name, list(shape), F32, kind="ExternalInput").ap()
    xT = dt_in("xT", (D, S))
    w4 = {nm: dt_in(nm, (C, 128, C, 128)) for nm in ("wr4", "ww4", "wk4", "wv4", "wo4", "wrec4")}
    wkey4 = dt_in("wkey4", (32, 128, 8, 128))
    wval4 = dt_in("wval4", (32, 128, 8, 128))
    selT3 = dt_in("selT3", (128, 256))
    keysT = dt_in("keysT", (NE, 128, 4096))
    valsT = dt_in("valsT", (NE, 128, 4096))
    g1v = dt_in("g1v", (128, C)); b1v = dt_in("b1v", (128, C))
    g2v = dt_in("g2v", (128, C)); b2v = dt_in("b2v", (128, C))
    tdv = dt_in("tdv", (128, C))
    onescol_bf = nc.dram_tensor("onescol_bf", [128, 1], BF16, kind="ExternalInput").ap()
    zcol_in = dt_in("zcol_in", (128, 1))
    onecol_in = dt_in("onecol_in", (128, 1))
    onesrow_in = dt_in("onesrow_in", (1, 128))
    epscol_in = dt_in("epscol_in", (128, 1))
    ident_in = dt_in("ident_in", (128, 128))

    kind_sp = "ExternalOutput" if debug else "Internal"
    y0sp = nc.dram_tensor("y0sp", [D, S], F32, kind=kind_sp).ap()
    y1sp = nc.dram_tensor("y1sp", [D, S], F32, kind=kind_sp).ap()
    x1sp = nc.dram_tensor("x1sp", [D, S], F32, kind=kind_sp).ap()
    gdr = nc.dram_tensor("gdr", [32, S], F32, kind=kind_sp).ap()
    rows = [nc.dram_tensor(f"rows{i}", [2, S], F32, kind=kind_sp).ap() for i in range(4)]
    if debug:
        z1dump = nc.dram_tensor("z1dump", [D, S], F32, kind="ExternalOutput").ap()
        rwkvdump = nc.dram_tensor("rwkvdump", [D, S], F32, kind="ExternalOutput").ap()
    outT = nc.dram_tensor("outT", [D, S], F32, kind="ExternalOutput").ap()

    es = ExitStack()
    sb = lambda name, shape, dt=F32: es.enter_context(nc.sbuf_tensor(name, list(shape), dt))
    p = P(nc, es, selfsync=selfsync)

    g1s = sb("g1s", (128, C)); b1s = sb("b1s", (128, C))
    g2s = sb("g2s", (128, C)); b2s = sb("b2s", (128, C))
    tds = sb("tds", (128, C)); decays = sb("decays", (128, C))
    ones_bf = sb("ones_bf", (128, 1), BF16)
    zcol = sb("zcol", (128, 1))
    onecol = sb("onecol", (128, 1))
    onesrow = sb("onesrow", (1, 128))
    epscol = sb("epscol", (128, 1))
    ident = sb("ident", (128, 128))
    selT_sb = sb("selT_sb", (128, 256), BF16)

    padA = sb("padA", (128, 64))
    padB = sb("padB", (128, 64))
    z1_es = ExitStack()
    z1 = [z1_es.enter_context(nc.sbuf_tensor(f"z1_{c}", [128, S], BF16)) for c in range(C)]

    pending = []

    def drain_one():
        if not pending:
            return
        eng, make, inc, waits = pending.pop(0)
        for s_, v_ in waits:
            p.wait(eng, s_, p.cnt[s_] if v_ is None else v_)
        p.op(eng, make, inc=inc)

    def barrier_all():
        bar = dict(p.cnt)
        for eng in ENGINES:
            for s_ in bar:
                p.wait(eng, s_, bar[s_])

    def drain_all_padded():
        while pending:
            drain_one()
            for _ in range(2):
                p.op("vector", lambda E: E.tensor_copy(out=padA[:], in_=padB[:]))
                p.op("vector", lambda E: E.tensor_copy(out=padB[:], in_=padA[:]))

    def ln_smalls_chain(arena, psA, psB, rows_dram, rowch, col0, width, extra_first_waits):
        A = arena[0:1, 0:width]
        B = arena[0:1, width:2 * width]
        Cc = arena[0:1, 2 * width:3 * width]
        steps = []
        steps.append(("vector", lambda E: E.tensor_single_scalar(
            out=A, in_=psA[0:1, :width], scalar=1.0 / D, op=ALU.mult), "vv",
            list(extra_first_waits)))
        steps.append(("vector", lambda E: E.tensor_single_scalar(
            out=B, in_=psB[0:1, :width], scalar=1.0 / D, op=ALU.mult), "vv", []))
        steps.append(("vector", lambda E: E.tensor_tensor(
            out=Cc, in0=A, in1=A, op=ALU.mult), "vv", []))
        steps.append(("vector", lambda E: E.tensor_tensor(
            out=B, in0=B, in1=Cc, op=ALU.subtract), "vv", []))
        steps.append(("vector", lambda E: E.tensor_single_scalar(
            out=B, in_=B, scalar=LN_EPS, op=ALU.add), "vv", []))
        steps.append(("scalar", lambda E: E.activation(
            out=Cc, in_=B, func=AF.Ln), "ss", [("vv", None)]))
        steps.append(("scalar", lambda E: E.activation(
            out=B, in_=Cc, func=AF.Exp, scale=-0.5), "ss", []))
        steps.append(("vector", lambda E: E.tensor_tensor(
            out=A, in0=A, in1=B, op=ALU.mult), "vv", [("ss", None)]))
        if rows_dram is not None:
            steps.append(("scalar", lambda E: E.dma_start(
                out=rows_dram[0:1, col0:col0 + width], in_=B), rowch, [("vv", None)]))
            steps.append(("scalar", lambda E: E.dma_start(
                out=rows_dram[1:2, col0:col0 + width], in_=A), rowch, [("vv", None)]))
        return steps

    with nc.Block() as block:
        for dst, src in ((g1s, g1v), (b1s, b1v), (g2s, g2v), (b2s, b2v), (tds, tdv),
                         (zcol, zcol_in), (onecol, onecol_in), (epscol, epscol_in),
                         (onesrow, onesrow_in), (ident, ident_in)):
            pre_da = p.op("sync", lambda E, d=dst, s=src: E.dma_start(out=d[:], in_=s[:]), inc=p.chan("pre"))
        p.op("gpsimd", lambda E: E.dma_start(out=ones_bf[:], in_=onescol_bf[:]), inc=p.chan("preg"))
        pre_dw = p.op("gpsimd", lambda E: E.dma_start(out=selT_sb[:], in_=selT3[:]), inc="preg")
        p.wait("scalar", "pre", pre_da)
        decays_done = p.op("scalar", lambda E: E.activation(out=decays[:], in_=tds[:], func=AF.Exp), inc="ss")
        p.op("vector", lambda E: E.memset(padA[:], 0.0))
        p.op("vector", lambda E: E.memset(padB[:], 0.0))

        ph1 = ExitStack()
        sb1 = lambda n_, s_, d_=F32: ph1.enter_context(nc.sbuf_tensor(n_, list(s_), d_))
        ps1 = lambda n_, s_: ph1.enter_context(nc.psum_tensor(n_, list(s_), F32))
        XC = [[sb1(f"XC{r}_{c}", (128, T1)) for c in range(C)] for r in range(2)]
        Y0C = [[sb1(f"Y0C{r}_{c}", (128, T1)) for c in range(C)] for r in range(2)]
        xb_r = [sb1(f"xb{i}", (128, T1), BF16) for i in range(2)]
        xq_r = [sb1(f"xq{i}", (128, T1), BF16) for i in range(2)]
        tmp_r = [sb1(f"tmp{i}", (128, T1)) for i in range(2)]
        rt1 = [sb1(f"rt1_{i}", (1, 3 * T1)) for i in range(4)]
        ps_st0 = [ps1(f"ps_st0{i}", (128, T1)) for i in range(2)]
        ps_st1 = [ps1(f"ps_st1{i}", (128, T1)) for i in range(2)]
        ps_bc0 = [ps1(f"ps_bc0{i}", (128, T1)) for i in range(2)]
        ps_bc1 = [ps1(f"ps_bc1{i}", (128, T1)) for i in range(2)]

        for r in range(2):
            for c in range(C):
                p.chan(f"xc{r}_{c}")
                p.chan(f"y0w{c}")
        xc_free = [[0] * C for _ in range(2)]
        y0c_spill = [[("y0w0", 0)] * C for _ in range(2)]
        y0c_sq = [[0] * C for _ in range(2)]
        xb_free = [0] * 2
        xq_free = [0] * 2
        st0_read = 0
        tmp_ss = [0, 0]
        st1_read = 0
        bc0_read = 0
        bc1_read = 0
        st0_rec = [0] * NT1
        st1_rec = [0] * NT1
        ch0_rec = [0] * NT1
        ch1_rec = [0] * NT1
        bc0_mm = [0] * NT1
        bc1_mm = [0] * NT1
        y0sp_done = [[0] * C for _ in range(NT1)]
        z1_done_t = [0] * NT1
        p.wait("tensor", "preg", pre_dw)
        p.wait("tensor", "pre", pre_da)

        def stA(t):
            r = t % 2
            for c in range(C):
                p.wait("sync", "vv", xc_free[r][c])
                p.op("sync", lambda E, t=t, c=c, r=r: E.dma_start(
                    out=XC[r][c][:], in_=xT[c * 128:(c + 1) * 128, t * T1:(t + 1) * T1]), inc=f"xc{r}_{c}")

        def stB(t):
            r = t % 2
            for c in range(C):
                b = c % 2
                p.wait("vector", f"xc{r}_{c}", p.cnt[f"xc{r}_{c}"])
                p.wait("vector", "mm", xb_free[b])
                vb = p.op("vector", lambda E, b=b, r=r, c=c: E.tensor_copy(out=xb_r[b][:], in_=XC[r][c][:]), inc="vv")
                p.wait("scalar", f"xc{r}_{c}", p.cnt[f"xc{r}_{c}"])
                p.wait("scalar", "mm", xq_free[b])
                sq = p.op("scalar", lambda E, b=b, r=r, c=c: E.activation(
                    out=xq_r[b][:], in_=XC[r][c][:], func=AF.Square), inc="ss")
                p.wait("tensor", "vv", vb)
                p.wait("tensor", "ss", sq)
                if c == 0:
                    p.wait("tensor", "vv", st0_read)
                p.op("tensor", lambda E, b=b, st=(c == 0), sp=(c == 7): E.matmul(
                    ps_st0[0][0:1, :], ones_bf[:], xb_r[b][:], start=st, stop=sp))
                mmv = p.op("tensor", lambda E, b=b, st=(c == 0), sp=(c == 7): E.matmul(
                    ps_st0[1][0:1, :], ones_bf[:], xq_r[b][:], start=st, stop=sp), inc="mm")
                xb_free[b] = mmv
                xq_free[b] = mmv
                drain_one()
            st0_rec[t] = mmv
            pending.extend(ln_smalls_chain(rt1[t % 4], ps_st0[0], ps_st0[1], None, None, 0, T1,
                                           [("mm", st0_rec[t])]))

        def stD(t):
            nonlocal st0_read
            ch0_rec[t] = p.cnt["vv"]
            st0_read = ch0_rec[t]
            A_ = rt1[t % 4][0:1, 0:T1]
            B_ = rt1[t % 4][0:1, T1:2 * T1]
            p.wait("tensor", "vv", ch0_rec[t])
            p.wait("tensor", "vv", bc0_read)
            p.op("tensor", lambda E, B_=B_: E.matmul(ps_bc0[0][:], onesrow[:], B_, start=True, stop=True))
            bc0_mm[t] = p.op("tensor", lambda E, A_=A_: E.matmul(
                ps_bc0[1][:], onesrow[:], A_, start=True, stop=True), inc="mm")

        def stE(t):
            nonlocal bc0_read
            r = t % 2
            for c in range(C):
                b = c % 2
                tb = c % 2
                p.wait("vector", "mm", bc0_mm[t])
                wch, wval = y0c_spill[r][c]
                p.wait("vector", wch, wval)
                p.wait("vector", "ss", y0c_sq[r][c])
                p.wait("vector", "ss", tmp_ss[tb])
                p.op("vector", lambda E, r=r, c=c, tb=tb: E.tensor_tensor(
                    out=tmp_r[tb][:], in0=XC[r][c][:], in1=ps_bc0[0][:], op=ALU.mult))
                vy0 = p.op("vector", lambda E, r=r, c=c, tb=tb: E.tensor_tensor(
                    out=Y0C[r][c][:], in0=tmp_r[tb][:], in1=ps_bc0[1][:], op=ALU.subtract), inc="vv")
                xc_free[r][c] = vy0
                p.wait("sync", "vv", vy0)
                y0da = p.op("sync", lambda E, t=t, c=c, r=r: E.dma_start(
                    out=y0sp[c * 128:(c + 1) * 128, t * T1:(t + 1) * T1], in_=Y0C[r][c][:]), inc=f"y0w{c}")
                y0sp_done[t][c] = (f"y0w{c}", y0da)
                y0c_spill[r][c] = (f"y0w{c}", y0da)
                p.wait("vector", "mm", xb_free[b])
                vb = p.op("vector", lambda E, b=b, r=r, c=c: E.tensor_copy(out=xb_r[b][:], in_=Y0C[r][c][:]), inc="vv")
                p.wait("scalar", "vv", vy0)
                p.wait("scalar", "mm", xq_free[b])
                sq = p.op("scalar", lambda E, b=b, r=r, c=c: E.activation(
                    out=xq_r[b][:], in_=Y0C[r][c][:], func=AF.Square), inc="ss")
                y0c_sq[r][c] = sq
                p.wait("tensor", "vv", vb)
                p.wait("tensor", "ss", sq)
                if c == 0:
                    p.wait("tensor", "vv", st1_read)
                p.op("tensor", lambda E, b=b, st=(c == 0), sp=(c == 7): E.matmul(
                    ps_st1[0][0:1, :], ones_bf[:], xb_r[b][:], start=st, stop=sp))
                mmv = p.op("tensor", lambda E, b=b, st=(c == 0), sp=(c == 7): E.matmul(
                    ps_st1[1][0:1, :], ones_bf[:], xq_r[b][:], start=st, stop=sp), inc="mm")
                xb_free[b] = mmv
                xq_free[b] = mmv
                drain_one()
            bc0_read = p.cnt["vv"]
            st1_rec[t] = mmv
            pending.extend(ln_smalls_chain(rt1[2 + t % 2] if False else rt1[(t % 4)], ps_st1[0], ps_st1[1], None, None, 0, T1,
                                           [("mm", st1_rec[t])]))

        def stG(t):
            nonlocal st1_read
            ch1_rec[t] = p.cnt["vv"]
            st1_read = ch1_rec[t]
            A_ = rt1[t % 4][0:1, 0:T1]
            B_ = rt1[t % 4][0:1, T1:2 * T1]
            p.wait("tensor", "vv", ch1_rec[t])
            p.wait("tensor", "vv", bc1_read)
            p.op("tensor", lambda E, B_=B_: E.matmul(ps_bc1[0][:], onesrow[:], B_, start=True, stop=True))
            bc1_mm[t] = p.op("tensor", lambda E, A_=A_: E.matmul(
                ps_bc1[1][:], onesrow[:], A_, start=True, stop=True), inc="mm")

        def stH(t):
            nonlocal bc1_read
            r = t % 2
            for c in range(C):
                tb = c % 2
                p.wait("vector", "mm", bc1_mm[t])
                p.wait("vector", "ss", tmp_ss[tb])
                p.op("vector", lambda E, r=r, c=c, tb=tb: E.tensor_tensor(
                    out=tmp_r[tb][:], in0=Y0C[r][c][:], in1=ps_bc1[0][:], op=ALU.mult))
                vt2 = p.op("vector", lambda E, tb=tb: E.tensor_tensor(
                    out=tmp_r[tb][:], in0=tmp_r[tb][:], in1=ps_bc1[1][:], op=ALU.subtract), inc="vv")
                p.wait("scalar", "vv", vt2)
                sz = p.op("scalar", lambda E, t=t, c=c, tb=tb: E.activation(
                    out=z1[c][:, t * T1:(t + 1) * T1], in_=tmp_r[tb][:], func=AF.Identity,
                    scale=g1s[:, c:c + 1], bias=b1s[:, c:c + 1]), inc="ss")
                tmp_ss[tb] = sz
                drain_one()
            z1_done_t[t] = p.cnt["ss"]
            bc1_read = p.cnt["vv"]

        for s in range(NT1 + 1):
            if s < NT1:
                stA(s)
                stB(s)
                if s == 0:
                    drain_all_padded()
            if s >= 1:
                stG(s - 1)
                stH(s - 1)
            if s < NT1:
                stD(s)
                stE(s)
        drain_all_padded()
        ph1.close()
        barrier_all()

        rwkv_es = ExitStack()
        rwkv = [rwkv_es.enter_context(nc.sbuf_tensor(f"rwkv{c}", [128, S], BF16, side="right")) for c in range(C)]
        ph2 = ExitStack()
        sb2 = lambda n_, s_, d_=F32: ph2.enter_context(nc.sbuf_tensor(n_, list(s_), d_))
        ps2 = lambda n_, s_: ph2.enter_context(nc.psum_tensor(n_, list(s_), F32))
        wnames = ("wr", "ww", "wk", "wv")
        wproj = {nm: [sb2(f"{nm}_{i}", (128, 1024), BF16) for i in range(2)] for nm in wnames}
        EWKD = sb2("EWKD", (128, T1 + 1))
        NUMD = sb2("NUMD", (128, T1 + 1))
        NUMO = sb2("NUMO", (128, T1 + 1))
        DENO = sb2("DENO", (128, T1 + 1))
        Dscan = sb2("Dscan", (128, T1 + 1))
        EWb = [sb2(f"EWb{i}", (128, T1)) for i in range(2)]
        Ub = [sb2(f"Ub{i}", (128, T1)) for i in range(2)]
        RREb = [sb2(f"RREb{i}", (128, T1)) for i in range(2)]
        Rb = [sb2(f"Rb{i}", (128, T1)) for i in range(2)]
        W1 = sb2("W1", (128, T1))
        W2 = sb2("W2", (128, T1))
        PSp = {nm: [ps2(f"PS{nm}{i}", (128, T1)) for i in range(2)] for nm in ("w", "k", "r", "v")}

        proj_src = {"wr": w4["wr4"], "ww": w4["ww4"], "wk": w4["wk4"], "wv": w4["wv4"]}
        wproj_done = [0] * C
        wslot_free = [0] * 2

        p.chan("wp0"); p.chan("wp1")

        def load_proj(ec):
            slot = ec % 2
            p.wait("gpsimd", "mm", wslot_free[slot])
            v = 0
            for nm in wnames:
                v = p.op("gpsimd", lambda E, nm=nm, ec=ec, s=slot: E.dma_start(
                    out=wproj[nm][s][:], in_=proj_src[nm][ec].rearrange("p a b -> p (a b)")), inc=f"wp{slot}")
            wproj_done[ec] = v

        load_proj(0)
        load_proj(1)

        p.wait("vector", "pre", pre_da)
        p.wait("vector", "ss", decays_done)
        bank_free_v = {}
        bank_free_s = {}
        ewkd_read = 0
        wkv_done = 0
        carry_rd = 0
        vRWKV = [[0] * NT1 for _ in range(C)]
        for ec in range(C):
            p.wait("vector", "vv", ewkd_read)
            p.op("vector", lambda E, ec=ec: E.tensor_copy(
                out=Dscan[:, 1:T1 + 1], in_=decays[:, ec:ec + 1].broadcast_to([128, T1])))
            p.op("vector", lambda E: E.tensor_copy(out=Dscan[:, 0:1], in_=zcol[:]))
            p.op("vector", lambda E: E.tensor_copy(out=NUMD[:, 0:1], in_=zcol[:]))
            dscan_v = ewkd_carry = p.op("vector", lambda E: E.tensor_copy(out=EWKD[:, 0:1], in_=zcol[:]), inc="vv")
            if ec + 1 < C:
                load_proj(ec + 1)
            for t in range(NT1):
                tp = t % 2
                prev = (ec, t - 2) if t >= 2 else ((ec - 1, NT1 - 2 + t) if ec > 0 else None)
                p.wait("tensor", f"wp{ec % 2}", wproj_done[ec])
                p.wait("tensor", "ss", z1_done_t[t])
                if prev is not None:
                    p.wait("tensor", "vv", bank_free_v[prev])
                    p.wait("tensor", "ss", bank_free_s[prev])
                mg = {}
                for nm, wsb in (("w", "ww"), ("k", "wk"), ("r", "wr"), ("v", "wv")):
                    for kc in range(C):
                        p.op("tensor", lambda E, nm=nm, wsb=wsb, kc=kc, t=t, tp=tp, ec=ec,
                             st=(kc == 0), sp=(kc == 7): E.matmul(
                            PSp[nm][tp][:], wproj[wsb][ec % 2][:, kc * 128:(kc + 1) * 128],
                            z1[kc][:, t * T1:(t + 1) * T1], start=st, stop=sp),
                            inc="mm" if kc == 7 else None)
                    mg[nm] = p.cnt["mm"]
                p.wait("scalar", "mm", mg["w"])
                sEW = p.op("scalar", lambda E, tp=tp: E.activation(
                    out=EWb[tp][:], in_=PSp["w"][tp][:], func=AF.Exp), inc="ss")
                p.wait("vector", "mm", mg["k"])
                p.wait("vector", "ss", sEW)
                vU = p.op("vector", lambda E, tp=tp: E.tensor_tensor(
                    out=Ub[tp][:], in0=PSp["k"][tp][:], in1=EWb[tp][:], op=ALU.subtract), inc="vv")
                p.wait("scalar", "vv", vU)
                p.wait("scalar", "vv", ewkd_read)
                p.wait("scalar", "vv", ewkd_carry)
                sEWK = p.op("scalar", lambda E, tp=tp: E.activation(
                    out=EWKD[:, 1:T1 + 1], in_=Ub[tp][:], func=AF.Exp), inc="ss")
                p.wait("scalar", "mm", mg["r"])
                sRRE = p.op("scalar", lambda E, tp=tp: E.activation(
                    out=RREb[tp][:], in_=PSp["r"][tp][:], func=AF.Tanh, scale=0.5), inc="ss")
                p.wait("vector", "ss", sRRE)
                p.op("vector", lambda E, tp=tp: E.tensor_scalar(
                    out=Rb[tp][:], in0=RREb[tp][:], scalar1=0.5, scalar2=0.5, op0=ALU.mult, op1=ALU.add))
                p.wait("vector", "ss", sEWK)
                p.wait("vector", "mm", mg["v"])
                vN = p.op("vector", lambda E, tp=tp: E.tensor_tensor(
                    out=NUMD[:, 1:T1 + 1], in0=EWKD[:, 1:T1 + 1], in1=PSp["v"][tp][:], op=ALU.mult), inc="vv")
                p.op("vector", lambda E: E.tensor_tensor_scan(
                    out=NUMO[:], data0=Dscan[:], data1=NUMD[:], initial=0.0, op0=ALU.mult, op1=ALU.add))
                vden = p.op("vector", lambda E: E.tensor_tensor_scan(
                    out=DENO[:], data0=Dscan[:], data1=EWKD[:], initial=0.0, op0=ALU.mult, op1=ALU.add), inc="vv")
                ewkd_read = vden
                p.op("vector", lambda E: E.tensor_single_scalar(
                    out=W1[:], in_=DENO[:, 1:T1 + 1], scalar=WKV_EPS, op=ALU.add))
                p.op("vector", lambda E: E.reciprocal(out=W2[:], in_=W1[:]))
                wkv_done = p.op("vector", lambda E: E.tensor_tensor(
                    out=W1[:], in0=NUMO[:, 1:T1 + 1], in1=W2[:], op=ALU.mult), inc="vv")
                vR = p.op("vector", lambda E, ec=ec, t=t, tp=tp: E.tensor_tensor(
                    out=rwkv[ec][:, t * T1:(t + 1) * T1], in0=Rb[tp][:], in1=W1[:], op=ALU.mult), inc="vv")
                vRWKV[ec][t] = ("vv", vR)
                if t < NT1 - 1:
                    p.op("vector", lambda E: E.tensor_copy(out=NUMD[:, 0:1], in_=NUMO[:, T1:T1 + 1]))
                    carry_rd = ewkd_carry = p.op("vector", lambda E: E.tensor_copy(
                        out=EWKD[:, 0:1], in_=DENO[:, T1:T1 + 1]), inc="vv")
                bank_free_v[(ec, t)] = vN
                bank_free_s[(ec, t)] = sRRE
            wslot_free[ec % 2] = p.cnt["mm"]
        if debug:
            for c in range(C):
                p.wait("gpsimd", vRWKV[c][NT1 - 1][0], vRWKV[c][NT1 - 1][1])
                p.op("gpsimd", lambda E, c=c: E.dma_start(
                    out=rwkvdump[c * 128:(c + 1) * 128, :], in_=rwkv[c][:]), inc=p.chan("dbg"))
        ph2_end_mm = p.cnt["mm"]
        ph2.close()
        z1_es.close()
        ph4w = ExitStack()
        wkey_sb = [ph4w.enter_context(nc.sbuf_tensor(f"wkeysb{fc}", [128, 1024], BF16)) for fc in range(32)]
        barrier_all()
        wkey_done = 0
        p.chan("wk"); p.chan("wv"); p.chan("wr")
        p.wait("gpsimd", "mm", ph2_end_mm)
        for fc in range(32):
            wkey_done = p.op("gpsimd", lambda E, fc=fc: E.dma_start(
                out=wkey_sb[fc][:], in_=wkey4[fc].rearrange("p a b -> p (a b)")), inc="wk")

        ph3 = ExitStack()
        sb3 = lambda n_, s_, d_=F32: ph3.enter_context(nc.sbuf_tensor(n_, list(s_), d_, side="right"))
        ps3 = lambda n_, s_: ph3.enter_context(nc.psum_tensor(n_, list(s_), F32))
        wo_sb = [sb3(f"wo_sb{e}", (128, 1024), BF16) for e in range(C)]
        PSo = [ps3(f"PSo{i}", (128, T1)) for i in range(4)]
        ps_s2 = [ps3(f"ps_s2{i}", (128, T1)) for i in range(4)]
        rt3 = [sb3(f"rt3_{i}", (1, 3 * T1)) for i in range(2)]
        y0l_r = [sb3(f"y0l{i}", (128, T1)) for i in range(4)]
        y1b_r = [sb3(f"y1b{i}", (128, T1)) for i in range(3)]
        xb3_r = [sb3(f"xb3{i}", (128, T1), BF16) for i in range(8)]
        xq3_r = [sb3(f"xq3{i}", (128, T1), BF16) for i in range(8)]

        p.wait("gpsimd", "mm", ph2_end_mm)
        wo_done = 0
        p.chan("wo")
        for e in range(C):
            wo_done = p.op("gpsimd", lambda E, e=e: E.dma_start(
                out=wo_sb[e][:], in_=w4["wo4"][e].rearrange("p a b -> p (a b)")), inc="wo")

        y0l_free3 = [(None, 0)] * 4
        y1b_cons = [(0, 0)] * 3
        xb3_free = [0] * 8
        xq3_free = [0] * 8
        vb_sem3 = [0] * 8
        sq_sem3 = [0] * 8
        pso_free = [0] * 4
        s2_bank_free = [0] * 2
        y1sp_done = [[0] * C for _ in range(NT1)]
        st2 = [0] * NT1

        def stats_mms3(i):
            tq, cq = i // 8, i % 8
            tpar = tq % 2
            p.wait("tensor", "vv", vb_sem3[cq])
            p.wait("tensor", "ss", sq_sem3[cq])
            if cq == 0:
                p.wait("tensor", "vv", s2_bank_free[tpar])
            p.op("tensor", lambda E, cq=cq, tpar=tpar, st=(cq == 0), sp=(cq == 7): E.matmul(
                ps_s2[2 * tpar][0:1, :], ones_bf[:], xb3_r[cq][:], start=st, stop=sp))
            mmv = p.op("tensor", lambda E, cq=cq, tpar=tpar, st=(cq == 0), sp=(cq == 7): E.matmul(
                ps_s2[2 * tpar + 1][0:1, :], ones_bf[:], xq3_r[cq][:], start=st, stop=sp), inc="mm")
            xb3_free[cq] = mmv
            xq3_free[cq] = mmv
            if cq == 7:
                st2[tq] = mmv
                pending.extend(ln_smalls_chain(rt3[tq % 2], ps_s2[2 * (tq % 2)], ps_s2[2 * (tq % 2) + 1],
                                               rows[2], p.chan("rw2"), tq * T1, T1,
                                               [("mm", st2[tq]), ("rw2", None)]))

        for t in range(NT1):
            for c in range(C):
                po = c % 4
                i = t * C + c
                slot = i % 4
                sem, val = y0l_free3[slot]
                if sem is not None:
                    p.wait("sync", sem, val)
                wch3, wv3 = y0sp_done[t][c]
                p.wait("sync", wch3, wv3)
                da = p.op("sync", lambda E, t=t, c=c, s=slot: E.dma_start(
                    out=y0l_r[s][:], in_=y0sp[c * 128:(c + 1) * 128, t * T1:(t + 1) * T1]), inc=p.chan(f"yl{slot}"))
                rch, rval = vRWKV[C - 1][t]
                p.wait("tensor", rch, rval)
                p.wait("tensor", "wo", wo_done)
                p.wait("tensor", "vv", pso_free[po])
                mo = 0
                for kc in range(C):
                    mo = p.op("tensor", lambda E, c=c, kc=kc, t=t, po=po, st=(kc == 0), sp=(kc == 7): E.matmul(
                        PSo[po][:], wo_sb[c][:, kc * 128:(kc + 1) * 128],
                        rwkv[kc][:, t * T1:(t + 1) * T1], start=st, stop=sp),
                        inc="mm" if kc == 7 else None)
                if i >= 2:
                    stats_mms3(i - 2)
                yb = i % 3
                ss1, da1 = y1b_cons[yb]
                p.wait("vector", "mm", mo)
                p.wait("vector", f"yl{slot}", da)
                p.wait("vector", "ss", ss1)
                p.wait("vector", f"y1w{yb}", da1)
                vy1 = p.op("vector", lambda E, po=po, s=slot, y=yb: E.tensor_tensor(
                    out=y1b_r[y][:], in0=PSo[po][:], in1=y0l_r[s][:], op=ALU.add), inc="vv")
                pso_free[po] = vy1
                y0l_free3[slot] = ("vv", vy1)
                drain_one()
                p.wait("sync", "vv", vy1)
                y1da = p.op("sync", lambda E, t=t, c=c, y=yb: E.dma_start(
                    out=y1sp[c * 128:(c + 1) * 128, t * T1:(t + 1) * T1], in_=y1b_r[y][:]), inc=p.chan(f"y1w{yb}"))
                y1sp_done[t][c] = (f"y1w{yb}", y1da)
                p.wait("vector", "mm", xb3_free[c])
                vb = p.op("vector", lambda E, c=c, y=yb: E.tensor_copy(out=xb3_r[c][:], in_=y1b_r[y][:]), inc="vv")
                p.wait("scalar", "vv", vy1)
                p.wait("scalar", "mm", xq3_free[c])
                sq = p.op("scalar", lambda E, c=c, y=yb: E.activation(
                    out=xq3_r[c][:], in_=y1b_r[y][:], func=AF.Square), inc="ss")
                y1b_cons[yb] = (sq, y1da)
                vb_sem3[c] = vb
                sq_sem3[c] = sq
                drain_one()
            if t >= 1:
                s2_bank_free[(t - 1) % 2] = p.cnt["vv"]
        stats_mms3(NT1 * C - 2)
        stats_mms3(NT1 * C - 1)
        drain_all_padded()
        rows2_all_done = p.cnt["rw2"]
        ph3_end_mm = p.cnt["mm"]
        ph3_tail_vv = p.cnt["vv"]
        ph3_tail_ss = p.cnt["ss"]
        y1w_totals = [p.cnt[f"y1w{y}"] for y in range(3)]
        ph3.close()
        rwkv_es.close()
        wval_sb = [ph4w.enter_context(nc.sbuf_tensor(f"wvalsb{fc}", [128, 1024], BF16, side="right")) for fc in range(32)]

        ph4 = ExitStack()
        sb4 = lambda n_, s_, d_=F32: ph4.enter_context(nc.sbuf_tensor(n_, list(s_), d_))
        ps4 = lambda n_, s_: ph4.enter_context(nc.psum_tensor(n_, list(s_), F32))
        wrec_sb = [sb4(f"wrecsb{e}", (128, 1024), BF16) for e in range(C)]
        BC2r = [sb4(f"BC2r{i}", (128, T4)) for i in range(2)]
        BC2m = [sb4(f"BC2m{i}", (128, T4)) for i in range(2)]
        Y1T = [[sb4(f"Y1T{j}_{ec}", (128, T4)) for ec in range(C)] for j in range(2)]
        xring = [sb4(f"xring{i}", (128, T4)) for i in range(2)]
        z2t = [sb4(f"z2t{ec}", (128, T4), BF16) for ec in range(C)]
        tmp4 = [sb4(f"tmp4{i}", (128, T4)) for i in range(1)]
        KRb = sb4("KRb", (128, T4))
        KSb = [sb4(f"KS{i}", (128, T4), BF16) for i in range(32)]
        RC2 = [sb4(f"RC2{i}", (128, T4)) for i in range(8)]
        X1T = [sb4(f"X1T{j}", (128, T4)) for j in range(2)]
        xb4_r = [sb4(f"xb4{i}", (128, T4), BF16) for i in range(4)]
        xq4_r = [sb4(f"xq4{i}", (128, T4), BF16) for i in range(4)]
        PS_k4 = [ps4(f"PS_k4{i}", (128, 512)) for i in range(2)]
        PS_v4 = [ps4(f"PS_v4{i}", (128, 512)) for i in range(4)]
        ps_s3 = [ps4(f"ps_s3{i}", (128, 512)) for i in range(2)]
        rt4 = [sb4(f"rt4_{i}", (1, 3 * T4)) for i in range(1)]

        wrec_done = 0
        for e in range(C):
            wrec_done = p.op("gpsimd", lambda E, e=e: E.dma_start(
                out=wrec_sb[e][:], in_=w4["wrec4"][e].rearrange("p a b -> p (a b)")), inc="wr")


        bc2_free = [0] * 2
        y1t_free = [0] * 2
        xring_free = [0] * 2
        xa_fold = [[0] * C for _ in range(2)]
        vcmA_free = 0
        z2_done_r = [0] * 2
        ks_free = [0] * 32
        kr_free = 0
        k4_free_v = [pso_free[0], pso_free[1]]
        k4_free_s = [0] * 2
        vcm_free = ph3_tail_vv
        rc2_free = [0] * 8
        rc2_sig = [0] * 8
        x1spill = [0] * 2
        x1t_cons = [0] * 2
        st3 = [0] * NT4
        rows3_done = [0] * NT4
        xb4_free = [0] * 4
        xq4_free = [0] * 4
        vb_sem = [0] * 4
        sq_sem = [0] * 4
        s3_read_gate = 0

        def smalls3_chain(u, st_val):
            return ln_smalls_chain(rt4[0], ps_s3[0], ps_s3[1], rows[3], p.chan("rw3"), u * T4, T4,
                                   [("mm", st_val), ("rw3", None)])

        z2t_free_mm = ph3_end_mm

        def stage_head(u):
            nonlocal bc2_free, y1t_free, z2t_free_mm
            t1i = u // 2
            up = u % 2
            if u == 0:
                p.wait("vector", "mm", ph3_end_mm)
                p.wait("scalar", "mm", ph3_end_mm)
                p.wait("sync", "mm", ph3_end_mm)
                p.wait("sync", "vv", ph3_tail_vv)
                p.wait("sync", "ss", ph3_tail_ss)
                p.wait("sync", "rw2", rows2_all_done)
                for y in range(3):
                    p.wait("sync", f"y1w{y}", y1w_totals[y])
            p.wait("gpsimd", "rw2", rows2_all_done)
            p.wait("gpsimd", "vv", bc2_free[up])
            p.op("gpsimd", lambda E, u=u, up=up: E.dma_start(
                out=BC2r[up][:], in_=rows[2][0:1, u * T4:(u + 1) * T4].broadcast_to([128, T4])), inc=p.chan(f"bc2{up}"))
            bc_da = p.op("gpsimd", lambda E, u=u, up=up: E.dma_start(
                out=BC2m[up][:], in_=rows[2][1:2, u * T4:(u + 1) * T4].broadcast_to([128, T4])), inc=f"bc2{up}")
            p.wait("sync", "vv", y1t_free[up])
            y1_da = 0
            for ec in range(C):
                wch4, wv4 = y1sp_done[t1i][ec]
                p.wait("sync", wch4, wv4)
                y1_da = p.op("sync", lambda E, u=u, up=up, ec=ec: E.dma_start(
                    out=Y1T[up][ec][:], in_=y1sp[ec * 128:(ec + 1) * 128, u * T4:(u + 1) * T4]), inc=p.chan(f"y1t{up}"))
            z2rec = 0
            z2par = [z2_done_r[up]] * 1
            for ec in range(C):
                tb = 0
                xs = ec % 2
                p.wait("sync", "vv", xring_free[xs])
                x_da = p.op("sync", lambda E, u=u, ec=ec, xs=xs: E.dma_start(
                    out=xring[xs][:], in_=xT[ec * 128:(ec + 1) * 128, u * T4:(u + 1) * T4]), inc=p.chan(f"xr{xs}"))
                p.wait("vector", f"y1t{up}", y1_da)
                p.wait("vector", f"bc2{up}", bc_da)
                p.wait("vector", "ss", z2par[tb])
                p.op("vector", lambda E, up=up, ec=ec, tb=tb: E.tensor_tensor(
                    out=tmp4[tb][:], in0=Y1T[up][ec][:], in1=BC2r[up][:], op=ALU.mult))
                vt = p.op("vector", lambda E, up=up, tb=tb: E.tensor_tensor(
                    out=tmp4[tb][:], in0=tmp4[tb][:], in1=BC2m[up][:], op=ALU.subtract), inc="vv")
                p.wait("scalar", "vv", vt)
                p.wait("scalar", "mm", z2t_free_mm)
                z2rec = p.op("scalar", lambda E, ec=ec, tb=tb: E.activation(
                    out=z2t[ec][:], in_=tmp4[tb][:], func=AF.Identity,
                    scale=g2s[:, ec:ec + 1], bias=b2s[:, ec:ec + 1]), inc="ss")
                z2par[tb] = z2rec
                p.wait("vector", f"xr{xs}", x_da)
                xa = p.op("vector", lambda E, up=up, ec=ec, xs=xs: E.tensor_tensor(
                    out=Y1T[up][ec][:], in0=Y1T[up][ec][:], in1=xring[xs][:], op=ALU.add), inc="vv")
                xring_free[xs] = xa
                xa_fold[up][ec] = xa
                drain_one()
            z2_done_r[up] = z2rec
            bc2_free[up] = p.cnt["vv"]
            return z2rec

        def kcm_only(u, z2rec):
            nonlocal kr_free
            for fc in range(32):
                fp = fc % 2
                p.wait("tensor", "wk", wkey_done)
                p.wait("tensor", "ss", z2rec)
                p.wait("tensor", "vv", k4_free_v[fp])
                p.wait("tensor", "ss", k4_free_s[fp])
                mk = 0
                for kc in range(C):
                    mk = p.op("tensor", lambda E, fc=fc, kc=kc, fp=fp, st=(kc == 0), sp=(kc == 7): E.matmul(
                        PS_k4[fp][:, :T4], wkey_sb[fc][:, kc * 128:(kc + 1) * 128],
                        z2t[kc][:], start=st, stop=sp), inc="mm" if kc == 7 else None)
                p.wait("vector", "mm", mk)
                p.wait("vector", "ss", kr_free)
                vkr = p.op("vector", lambda E, fp=fp: E.tensor_single_scalar(
                    out=KRb[:], in_=PS_k4[fp][:, :T4], scalar=0.0, op=ALU.max), inc="vv")
                k4_free_v[fp] = vkr
                p.wait("scalar", "vv", vkr)
                p.wait("scalar", "mm", ks_free[fc])
                sks = p.op("scalar", lambda E, fc=fc: E.activation(
                    out=KSb[fc][:], in_=KRb[:], func=AF.Square), inc="ss")
                kr_free = sks
                drain_one()
            return sks

        def rcm_sig_half(u, dh):
            last = 0
            for pair in range(2):
                mrs = [0] * 2
                for ecl2 in range(2):
                    ec = dh * 4 + pair * 2 + ecl2
                    ep = ec % 2
                    p.wait("tensor", "wr", wrec_done)
                    p.wait("tensor", "vv", k4_free_v[ep])
                    p.wait("tensor", "ss", k4_free_s[ep])
                    mr = 0
                    for kc in range(C):
                        mr = p.op("tensor", lambda E, kc=kc, ec=ec, ep=ep, st=(kc == 0), sp=(kc == 7): E.matmul(
                            PS_k4[ep][:, :T4], wrec_sb[ec][:, kc * 128:(kc + 1) * 128],
                            z2t[kc][:], start=st, stop=sp), inc="mm" if kc == 7 else None)
                    k4_free_v[ep] = 0
                    mrs[ecl2] = mr
                srs = [0, 0]
                for ecl2 in range(2):
                    ec = dh * 4 + pair * 2 + ecl2
                    ep = ec % 2
                    slot = dh * 4 + pair * 2 + ecl2
                    p.wait("scalar", "mm", mrs[ecl2])
                    p.wait("scalar", "vv", rc2_free[slot])
                    srs[ecl2] = p.op("scalar", lambda E, ep=ep, slot=slot: E.activation(
                        out=RC2[slot][:], in_=PS_k4[ep][:, :T4], func=AF.Sigmoid, scale=1.0), inc="ss")
                    k4_free_s[ep] = srs[ecl2]
                    rc2_sig[slot] = srs[ecl2]
                    last = srs[ecl2]
            return last

        def vcm_pass(u, dh, ks_ready, bankfree_vv):
            p.wait("tensor", "ss", ks_ready)
            p.wait("tensor", "vv", bankfree_vv)
            mvcm = 0
            for fc in range(32):
                p.wait("tensor", f"wv{fc // 4}", 64)
                for dcl in range(4):
                    dc = dh * 4 + dcl
                    mvcm = p.op("tensor", lambda E, fc=fc, dc=dc, dcl=dcl,
                                st=(fc == 0), sp=(fc == 31): E.matmul(
                        PS_v4[dcl][:, :T4],
                        wval_sb[fc][:, dc * 128:(dc + 1) * 128], KSb[fc][:],
                        start=st, stop=sp), inc="mm" if dcl == 3 else None)
                if dh == 1:
                    ks_free[fc] = mvcm
            return mvcm

        def combine_half(u, dh, mvcm):
            nonlocal vcm_free, vcmA_free, s3_read_gate
            up = u % 2
            lastvx1 = 0
            if dh == 0:
                drain_one()
                drain_one()
            bank_evac = 0
            for ecl in range(4):
                slot = dh * 4 + ecl
                p.wait("vector", "mm", mvcm)
                p.wait("vector", "ss", rc2_sig[slot])
                bank_evac = p.op("vector", lambda E, slot=slot, ecl=ecl: E.tensor_tensor(
                    out=RC2[slot][:], in0=PS_v4[ecl][:, :T4], in1=RC2[slot][:], op=ALU.mult), inc="vv")
            if dh == 0:
                vcmA_free = bank_evac
            else:
                vcm_free = bank_evac
            for ecl in range(4):
                ec = dh * 4 + ecl
                slot = dh * 4 + ecl
                xs4 = ec % 2
                p.wait("vector", "ss", x1t_cons[xs4])
                p.wait("vector", f"x1w{xs4}", x1spill[xs4])
                vx1 = p.op("vector", lambda E, slot=slot, up=up, ec=ec, xs4=xs4: E.tensor_tensor(
                    out=X1T[xs4][:], in0=RC2[slot][:], in1=Y1T[up][ec][:], op=ALU.add), inc="vv")
                rc2_free[slot] = vx1
                lastvx1 = vx1
                if ec == C - 1:
                    y1t_free[up] = vx1
                p.wait("sync", "vv", vx1)
                x1da = p.op("sync", lambda E, u=u, ec=ec, xs4=xs4: E.dma_start(
                    out=x1sp[ec * 128:(ec + 1) * 128, u * T4:(u + 1) * T4], in_=X1T[xs4][:]), inc=p.chan(f"x1w{xs4}"))
                x1spill[xs4] = x1da
                sl4 = ec % 4
                p.wait("vector", "mm", xb4_free[sl4])
                vb = p.op("vector", lambda E, sl4=sl4, xs4=xs4: E.tensor_copy(out=xb4_r[sl4][:], in_=X1T[xs4][:]), inc="vv")
                p.wait("scalar", "vv", vx1)
                p.wait("scalar", "mm", xq4_free[sl4])
                sq = p.op("scalar", lambda E, sl4=sl4, xs4=xs4: E.activation(
                    out=xq4_r[sl4][:], in_=X1T[xs4][:], func=AF.Square), inc="ss")
                x1t_cons[xs4] = sq
                vb_sem[sl4] = vb
                sq_sem[sl4] = sq
                drain_one()
            return lastvx1

        def stats_mms(u, dh, gate=0):
            mmv = 0
            for ecl in range(4):
                ec = dh * 4 + ecl
                sl4 = ecl
                p.wait("tensor", "vv", vb_sem[sl4])
                p.wait("tensor", "ss", sq_sem[sl4])
                if ec == 0:
                    p.wait("tensor", "vv", gate)
                p.op("tensor", lambda E, sl4=sl4, st=(ec == 0), sp=(ec == 7): E.matmul(
                    ps_s3[0][0:1, :T4], ones_bf[:], xb4_r[sl4][:], start=st, stop=sp))
                mmv = p.op("tensor", lambda E, sl4=sl4, st=(ec == 0), sp=(ec == 7): E.matmul(
                    ps_s3[1][0:1, :T4], ones_bf[:], xq4_r[sl4][:], start=st, stop=sp), inc="mm")
                xb4_free[sl4] = mmv
                xq4_free[sl4] = mmv
                if ec == 7:
                    st3[u] = mmv
            return mmv

        z2r = stage_head(0)
        ksr = kcm_only(0, z2r)
        rcm_sig_half(0, 0)
        p.wait("gpsimd", "mm", ph3_end_mm)
        for fc in range(32):
            p.op("gpsimd", lambda E, fc=fc: E.dma_start(
                out=wval_sb[fc][:], in_=wval4[fc].rearrange("p a b -> p (a b)")), inc=p.chan(f"wv{fc // 4}"))
        for u in range(NT4):
            mvcmA = vcm_pass(u, 0, ksr, vcm_free)
            if u >= 1:
                stats_mms(u - 1, 1)
                pending.extend(smalls3_chain(u - 1, st3[u - 1]))
            rcm_sig_half(u, 1)
            z2t_free_mm = p.cnt["mm"]
            combine_half(u, 0, mvcmA)
            s3gate = p.cnt["vv"]
            if u + 1 < NT4:
                z2r = stage_head(u + 1)
            mvcmB = vcm_pass(u, 1, ksr, vcmA_free)
            stats_mms(u, 0, s3gate)
            if u + 1 < NT4:
                ksr = kcm_only(u + 1, z2r)
                rcm_sig_half(u + 1, 0)
            combine_half(u, 1, mvcmB)
            rows3_done[u] = None
        stats_mms(NT4 - 1, 1)
        pending.extend(smalls3_chain(NT4 - 1, st3[NT4 - 1]))
        drain_all_padded()
        rows3_all_done = p.cnt["rw3"]
        for u in range(NT4):
            rows3_done[u] = rows3_all_done
        ph4.close()
        ph4w.close()
        barrier_all()

        ph5 = ExitStack()
        sb5 = lambda n_, s_, d_=F32: ph5.enter_context(nc.sbuf_tensor(n_, list(s_), d_))
        ps5 = lambda n_, s_: ph5.enter_context(nc.psum_tensor(n_, list(s_), F32))
        OUT = [sb5(f"OUT{ec}", (128, HS)) for ec in range(C)]
        y3h = [sb5(f"y3h{ec}", (128, HS), BF16) for ec in range(C)]
        KEY = [sb5(f"KEY{i}", (128, 4096), BF16) for i in range(4)]
        VAL = [sb5(f"VAL{i}", (128, 4096), BF16) for i in range(4)]
        GFM = sb5("GFM", (32, HS))
        BC3r = [sb5(f"BC3r{i}", (128, T5)) for i in range(2)]
        BC3m = [sb5(f"BC3m{i}", (128, T5)) for i in range(2)]
        tmp5 = [sb5(f"tmp5{i}", (128, T5)) for i in range(2)]
        Gb = [sb5(f"Gb{i}", (128, 32)) for i in range(4)]
        GCb = [sb5(f"GCb{i}", (128, 32)) for i in range(4)]
        Mb = [sb5(f"Mb{i}", (128, 1)) for i in range(4)]
        MKb = [sb5(f"MKb{i}", (128, 32)) for i in range(4)]
        GFb = [sb5(f"GFb{i}", (128, 32)) for i in range(4)]
        GB = [sb5(f"GB{i}", (128, T5B)) for i in range(4)]
        SC = [[[sb5(f"SC{par}_{el}_{fc}", (128, T5B), BF16) for fc in range(4)] for el in range(GE)] for par in range(2)]
        PS_sc = [ps5(f"PS_sc{i}", (128, 512)) for i in range(2)]
        PS_out = [ps5(f"PS_out{i}", (128, 512)) for i in range(4)]

        kv_done = [0] * NE
        kv_free = [0] * 4
        out_done = 0
        p.chan("outw")
        gb_free = [0] * 4
        sc_free = [[[0] * 4 for _ in range(GE)] for _ in range(2)]
        sc_ready = [[[0] * 4 for _ in range(GE)] for _ in range(2)]
        sc_bank_free = [0] * 2
        sc_bank_free_ss = [0] * 2
        gdr_done_prev = 0
        gbatch_free = [0] * 4
        tr_read = 0
        outadd_last = 0
        for h in range(2):
            hoff = h * HS
            outacc_da = 0
            for ec2 in range(2):
                p.wait("sync", f"x1w{ec2}", x1spill[ec2])
            for ec in range(C):
                if h == 1:
                    p.wait("sync", "outw", out_done)
                    p.wait("sync", "vv", outadd_last)
                outacc_da = p.op("sync", lambda E, ec=ec, hoff=hoff: E.dma_start(
                    out=OUT[ec][:], in_=x1sp[ec * 128:(ec + 1) * 128, hoff:hoff + HS]), inc=p.chan("outld"))
            bc3_free = [0] * 2
            y3rec = [[0] * C for _ in range(NT5)]
            for u in range(NT5):
                ug = (hoff + u * T5) // T4
                up = u % 2
                p.wait("gpsimd", "rw3", rows3_done[ug])
                p.wait("gpsimd", "vv", bc3_free[up])
                p.op("gpsimd", lambda E, up=up, hoff=hoff, u=u: E.dma_start(
                    out=BC3r[up][:], in_=rows[3][0:1, hoff + u * T5:hoff + (u + 1) * T5].broadcast_to([128, T5])), inc=p.chan(f"bc3{up}"))
                bc_da = p.op("gpsimd", lambda E, up=up, hoff=hoff, u=u: E.dma_start(
                    out=BC3m[up][:], in_=rows[3][1:2, hoff + u * T5:hoff + (u + 1) * T5].broadcast_to([128, T5])), inc=f"bc3{up}")
                for ec in range(C):
                    tb = ec % 2
                    p.wait("vector", "outld", outacc_da)
                    p.wait("vector", f"bc3{up}", bc_da)
                    p.op("vector", lambda E, ec=ec, u=u, up=up, tb=tb: E.tensor_tensor(
                        out=tmp5[tb][:], in0=OUT[ec][:, u * T5:(u + 1) * T5], in1=BC3r[up][:], op=ALU.mult))
                    y3rec[u][ec] = p.op("vector", lambda E, ec=ec, u=u, up=up, tb=tb: E.tensor_tensor(
                        out=y3h[ec][:, u * T5:(u + 1) * T5], in0=tmp5[tb][:], in1=BC3m[up][:], op=ALU.subtract), inc="vv")
                bc3_free[up] = y3rec[u][C - 1]
            gfm_last = 0
            for scb in range(0, NSC, 4):
                gsig = [0] * 4
                for j in range(4):
                    sc = scb + j
                    sp5 = sc % 2
                    u5 = (sc * 128) // T5
                    p.wait("tensor", "vv", y3rec[u5][C - 1])
                    p.wait("tensor", "preg", pre_dw)
                    p.wait("tensor", "vv", sc_bank_free[sp5])
                    p.wait("tensor", "ss", sc_bank_free_ss[sp5])
                    ms = 0
                    for kc in range(C):
                        ms = p.op("tensor", lambda E, sc=sc, kc=kc, sp5=sp5, st=(kc == 0), sp=(kc == 7): E.matmul(
                            PS_sc[sp5][:, 0:32], y3h[kc][:, sc * 128:(sc + 1) * 128],
                            selT_sb[:, kc * 32:(kc + 1) * 32], start=st, stop=sp),
                            inc="mm" if kc == 7 else None)
                    g4 = sc % 4
                    p.wait("scalar", "mm", ms)
                    p.wait("scalar", "vv", gbatch_free[g4])
                    gsig[j] = p.op("scalar", lambda E, sp5=sp5, g4=g4: E.activation(
                        out=Gb[g4][:], in_=PS_sc[sp5][:, 0:32], func=AF.Sigmoid, scale=1.0), inc="ss")
                    sc_bank_free_ss[sp5] = gsig[j]
                chains4 = []
                for j in range(4):
                    sc = scb + j
                    g4 = sc % 4
                    q = []
                    q.append(("vector", lambda E, g4=g4: E.tensor_copy(
                        out=GCb[g4][:], in_=Gb[g4][:]), "vv", [("ss", gsig[j])]))
                    for it in range(4):
                        q.append(("vector", lambda E, g4=g4: E.reduce_max(
                            out=Mb[g4][:], in_=GCb[g4][:], axis=AX.X), "vv", []))
                        if it < 3:
                            q.append(("vector", lambda E, g4=g4: E.tensor_tensor(
                                out=MKb[g4][:], in0=GCb[g4][:], in1=Mb[g4][:, 0:1].broadcast_to([128, 32]),
                                op=ALU.is_ge), "vv", []))
                            q.append(("vector", lambda E, g4=g4: E.scalar_tensor_tensor(
                                out=GCb[g4][:], in0=MKb[g4][:], scalar=-1e30, in1=GCb[g4][:],
                                op0=ALU.mult, op1=ALU.add), "vv", []))
                    q.append(("vector", lambda E, g4=g4: E.tensor_tensor(
                        out=MKb[g4][:], in0=Gb[g4][:], in1=Mb[g4][:, 0:1].broadcast_to([128, 32]),
                        op=ALU.is_ge), "vv", []))
                    q.append(("vector", lambda E, g4=g4: E.tensor_tensor(
                        out=GFb[g4][:], in0=Gb[g4][:], in1=MKb[g4][:], op=ALU.mult), "vv", []))
                    chains4.append(q)
                gf_rec = [0] * 4
                ncs5 = len(chains4[0])
                for si in range(ncs5):
                    for j in range(4):
                        eng, make, inc, waits = chains4[j][si]
                        for s_, v_ in waits:
                            p.wait(eng, s_, p.cnt[s_] if v_ is None else v_)
                        r_ = p.op(eng, make, inc=inc)
                        if si == ncs5 - 1:
                            gf_rec[j] = r_
                for j in range(4):
                    sc = scb + j
                    g4 = sc % 4
                    p.wait("tensor", "vv", gf_rec[j])
                    p.wait("tensor", "ss", tr_read)
                    p.wait("tensor", "vv", outadd_last)
                    mtr = p.op("tensor", lambda E, g4=g4: E.transpose(
                        out=PS_out[0][0:32, 0:128], in_=GFb[g4][:], identity=ident[:]), inc="mm")
                    p.wait("scalar", "mm", mtr)
                    p.wait("scalar", "gdr", gdr_done_prev)
                    tr_read = p.op("scalar", lambda E, sc=sc: E.activation(
                        out=GFM[0:32, sc * 128:(sc + 1) * 128], in_=PS_out[0][0:32, 0:128], func=AF.Copy), inc="ss")
                    gbatch_free[g4] = gf_rec[j]
                gfm_last = tr_read
            p.wait("sync", "ss", gfm_last)
            gdr_done = p.op("sync", lambda E, hoff=hoff: E.dma_start(
                out=gdr[0:32, hoff:hoff + HS], in_=GFM[:]), inc=p.chan("gdr"))
            gdr_done_prev = gdr_done
            seq = [(g, t) for g in range(NG) for t in range(NT5B)]

            def kv_load(g):
                for el in range(GE):
                    e = g * GE + el
                    slot = e % 4
                    p.wait("gpsimd", "mm", kv_free[slot])
                    p.op("gpsimd", lambda E, e=e, slot=slot: E.dma_start(
                        out=KEY[slot][:], in_=keysT[e]), inc=p.chan(f"kv{slot}"))
                    kv_done[e] = p.op("gpsimd", lambda E, e=e, slot=slot: E.dma_start(
                        out=VAL[slot][:], in_=valsT[e]), inc=f"kv{slot}")

            def front(i):
                g, t = seq[i]
                par = i % 2
                for el in range(GE):
                    e = g * GE + el
                    slot = e % 4
                    gbs = (i * GE + el) % 4
                    p.wait("gpsimd", "gdr", gdr_done)
                    p.wait("gpsimd", "vv", gb_free[gbs])
                    gb_da = p.op("gpsimd", lambda E, e=e, gbs=gbs, hoff=hoff, t=t: E.dma_start(
                        out=GB[gbs][:], in_=gdr[e:e + 1, hoff + t * T5B:hoff + (t + 1) * T5B].broadcast_to([128, T5B])), inc=p.chan(f"gb{gbs}"))
                    for fc in range(4):
                        fp = fc % 2
                        p.wait("tensor", f"kv{slot}", kv_done[e])
                        p.wait("tensor", "vv", y3rec[2 * t + 1][C - 1])
                        p.wait("tensor", "vv", sc_bank_free[fp])
                        p.wait("tensor", "ss", sc_bank_free_ss[fp])
                        mk = 0
                        for kc in range(C):
                            mk = p.op("tensor", lambda E, slot=slot, kc=kc, fc=fc, t=t, fp=fp,
                                      st=(kc == 0), sp=(kc == 7): E.matmul(
                                PS_sc[fp][:, 0:T5B],
                                KEY[slot][:, kc * 512 + fc * 128: kc * 512 + (fc + 1) * 128],
                                y3h[kc][:, t * T5B:(t + 1) * T5B], start=st, stop=sp),
                                inc="mm" if kc == 7 else None)
                        p.wait("vector", "mm", mk)
                        p.wait("vector", f"gb{gbs}", gb_da)
                        p.wait("vector", "mm", sc_free[par][el][fc])
                        vsc = p.op("vector", lambda E, par=par, el=el, fc=fc, fp=fp, gbs=gbs: E.scalar_tensor_tensor(
                            out=SC[par][el][fc][:], in0=PS_sc[fp][:, 0:T5B],
                            scalar=0.0, in1=GB[gbs][:], op0=ALU.max, op1=ALU.mult), inc="vv")
                        sc_bank_free[fp] = vsc
                        if fc == 3:
                            gb_free[gbs] = vsc
                        sc_ready[par][el][fc] = vsc
                if t == 1 and g + 1 < NG:
                    kv_load(g + 1)

            def values_pass(i, dh):
                nonlocal outadd_last, mvals_last
                g, t = seq[i]
                par = i % 2
                p.wait("tensor", "vv", sc_ready[par][GE - 1][3])
                p.wait("tensor", "vv", outadd_last)
                p.wait("tensor", "ss", gfm_last)
                mvals = 0
                for el in range(GE):
                    for fc in range(4):
                        for dcl in range(4):
                            dc = dh * 4 + dcl
                            mvals = p.op("tensor", lambda E, par=par, el=el, slot2=(g * GE + el) % 4,
                                         fc=fc, dc=dc, dcl=dcl,
                                         st=(el == 0 and fc == 0), sp=(el == GE - 1 and fc == 3): E.matmul(
                                PS_out[dcl][:, 0:T5B],
                                VAL[slot2][:, fc * 1024 + dc * 128: fc * 1024 + (dc + 1) * 128],
                                SC[par][el][fc][:], start=st, stop=sp), inc="mm" if dcl == 3 else None)
                p.wait("vector", "mm", mvals)
                for dcl in range(4):
                    dc = dh * 4 + dcl
                    outadd_last = p.op("vector", lambda E, dc=dc, dcl=dcl, t=t: E.tensor_tensor(
                        out=OUT[dc][:, t * T5B:(t + 1) * T5B], in0=OUT[dc][:, t * T5B:(t + 1) * T5B],
                        in1=PS_out[dcl][:, 0:T5B], op=ALU.add), inc="vv")
                mvals_last = mvals
                if dh == 1:
                    for el in range(GE):
                        for fc in range(4):
                            sc_free[par][el][fc] = mvals
                        kv_free[(g * GE + el) % 4] = mvals

            kv_load(0)
            mvals_last = 0
            front(0)
            for i in range(len(seq)):
                values_pass(i, 0)
                if i + 1 < len(seq):
                    front(i + 1)
                values_pass(i, 1)
            for ec in range(C):
                p.wait("sync", "vv", outadd_last)
                out_done = p.op("sync", lambda E, ec=ec, hoff=hoff: E.dma_start(
                    out=outT[ec * 128:(ec + 1) * 128, hoff:hoff + HS], in_=OUT[ec][:]), inc=p.chan("outw"))
        ph5.close()

        for s_name in list(p.cnt):
            p.wait("sync", s_name, p.cnt[s_name])

        @block.sync
        def _(E):
            p.replay("sync", E)

        @block.vector
        def _(E):
            p.replay("vector", E)

        @block.scalar
        def _(E):
            p.replay("scalar", E)

        @block.tensor
        def _(E):
            p.replay("tensor", E)

        @block.gpsimd
        def _(E):
            p.replay("gpsimd", E)

    es.close()
    return nc


def _host_layouts(inputs):
    f32 = np.float32
    A = lambda x: np.ascontiguousarray(np.asarray(x), dtype=f32)
    _ = inputs["L0_attn_time_first"]

    def t4(W):
        return np.ascontiguousarray(A(W).T.reshape(8, 128, 8, 128).transpose(2, 1, 0, 3))

    d = {}
    d["wr4"] = t4(inputs["L0_attn_Wr"])
    d["ww4"] = t4(inputs["L0_attn_Ww"])
    d["wk4"] = t4(inputs["L0_attn_Wk"])
    d["wv4"] = t4(inputs["L0_attn_Wv"])
    d["wo4"] = t4(inputs["L0_attn_Wo"])
    d["wrec4"] = t4(inputs["L0_attn_Wrec"])
    d["wkey4"] = np.ascontiguousarray(A(inputs["L0_attn_Wkey"]).T.reshape(8, 128, 32, 128).transpose(2, 1, 0, 3))
    d["wval4"] = np.ascontiguousarray(A(inputs["L0_attn_Wval"]).T.reshape(32, 128, 8, 128))
    d["selT3"] = np.ascontiguousarray(
        A(inputs["L0_ffn_expert_sel"]).T.reshape(8, 128, 32).transpose(1, 0, 2)).reshape(128, 256)
    d["keysT"] = np.ascontiguousarray(
        A(inputs["L0_ffn_keys"]).reshape(32, 8, 128, 512).transpose(0, 2, 1, 3)).reshape(32, 128, 4096)
    d["valsT"] = np.ascontiguousarray(
        A(inputs["L0_ffn_values"]).reshape(32, 4, 128, 1024).transpose(0, 2, 1, 3)).reshape(32, 128, 4096)
    for nm, key in (("g1v", "L0_attn_ln1_g"), ("b1v", "L0_attn_ln1_b"),
                    ("g2v", "L0_attn_ln2_g"), ("b2v", "L0_attn_ln2_b"),
                    ("tdv", "L0_attn_time_decay")):
        d[nm] = np.ascontiguousarray(A(inputs[key]).reshape(8, 128).T)
    d["onescol_bf"] = np.ones((128, 1), dtype=ml_dtypes.bfloat16)
    d["zcol_in"] = np.zeros((128, 1), dtype=f32)
    d["onecol_in"] = np.ones((128, 1), dtype=f32)
    d["onesrow_in"] = np.ones((1, 128), dtype=f32)
    d["epscol_in"] = np.full((128, 1), 1e-8, dtype=f32)
    d["ident_in"] = np.eye(128, dtype=f32)
    return d


def kernel(**inputs):
    nc = build_program()
    shared = _host_layouts(inputs)
    x = np.asarray(inputs["L0_attn_x"], dtype=np.float32)
    in_maps = []
    for b in range(8):
        m = dict(shared)
        m["xT"] = np.ascontiguousarray(x[b].T)
        in_maps.append(m)
    res = run_bass_kernel_spmd(nc, in_maps, core_ids=list(range(8)))
    out = np.stack([np.asarray(r["outT"]).T for r in res.results], axis=0)
    return np.ascontiguousarray(out, dtype=np.float32)


# version 24
# speedup vs baseline: 1.1118x; 1.0266x over previous
import numpy as np
import ml_dtypes
import concourse.bass as bass
from concourse import dve_ops
import concourse.mybir as mybir
from concourse.bass_utils import run_bass_kernel_spmd
from contextlib import ExitStack

F32 = mybir.dt.float32
BF16 = mybir.dt.bfloat16
AF = mybir.ActivationFunctionType
ALU = mybir.AluOpType
AX = mybir.AxisListType

D = 1024
S = 4096
C = 8
T1 = 512
NT1 = S // T1
T4 = 256
NT4 = S // T4
HS = S // 2
T5 = 256
NT5 = HS // T5
T5B = 512
NT5B = HS // T5B
NSC = HS // 128
NE = 32
GE = 2
NG = NE // GE
LN_EPS = 1e-5
WKV_EPS = 1e-8

ENGINES = ("sync", "vector", "scalar", "tensor", "gpsimd")


class P:
    def __init__(self, nc, es, selfsync=False):
        self.nc = nc
        self.es = es
        self.sems = {}
        self.cnt = {}
        self.q = {e: [] for e in ENGINES}
        self.lastw = {}
        self.selfsync = selfsync
        for s_name in ("mm", "vv", "ss", "pp"):
            self.chan(s_name)
        self.lastmark = {}
        if selfsync:
            for e in ENGINES:
                self.chan(f"ord_{e}")

    def chan(self, name):
        if name in self.sems:
            return name
        self.sems[name] = self.es.enter_context(self.nc.semaphore(name))
        self.cnt[name] = 0
        for e in ENGINES:
            self.lastw[(e, name)] = 0
        return name

    def wait(self, eng, sem, val):
        if not val:
            return
        assert val <= self.cnt[sem], f"wait on future: {eng} {sem} {val} > {self.cnt[sem]}"
        if self.lastw[(eng, sem)] >= val:
            return
        self.lastw[(eng, sem)] = val
        h = self.sems[sem]
        self.q[eng].append(lambda E, h=h, v=val: E.wait_ge(h, v))

    def op(self, eng, make, inc=None):
        if self.selfsync:
            pm = self.lastmark.get(eng)
            if pm is not None:
                self.q[eng].append(lambda E, h=self.sems[pm[0]], v=pm[1]: E.wait_ge(h, v))
        if inc is None:
            if self.selfsync:
                ordname = f"ord_{eng}"
                self.cnt[ordname] += 1
                self.lastmark[eng] = (ordname, self.cnt[ordname])

                def f0(E, make=make, h=self.sems[ordname], n2=1):
                    inst = make(E)
                    if inst is not None:
                        inst.then_inc(h, n2)
                self.q[eng].append(f0)
            else:
                self.q[eng].append(make)
            return None
        n = 1 if inc in ("mm", "vv", "ss", "pp") else 16
        h = self.sems[inc]
        self.cnt[inc] += n
        if self.selfsync:
            self.lastmark[eng] = (inc, self.cnt[inc])

        def f(E, make=make, h=h, n=n):
            make(E).then_inc(h, n)
        self.q[eng].append(f)
        return self.cnt[inc]

    def replay(self, eng, E):
        for f in self.q[eng]:
            f(E)


def build_program(debug=False, selfsync=False):
    nc = bass.Bass("TRN2", target_bir_lowering=False, debug=False)

    dt_in = lambda name, shape: nc.dram_tensor(name, list(shape), F32, kind="ExternalInput").ap()
    xT = dt_in("xT", (D, S))
    w4 = {nm: dt_in(nm, (C, 128, C, 128)) for nm in ("wr4", "ww4", "wk4", "wv4", "wo4", "wrec4")}
    wkey4 = dt_in("wkey4", (32, 128, 8, 128))
    wval4 = dt_in("wval4", (32, 128, 8, 128))
    selT3 = dt_in("selT3", (128, 256))
    keysT = dt_in("keysT", (NE, 128, 4096))
    valsT = dt_in("valsT", (NE, 128, 4096))
    g1v = dt_in("g1v", (128, C)); b1v = dt_in("b1v", (128, C))
    g2v = dt_in("g2v", (128, C)); b2v = dt_in("b2v", (128, C))
    tdv = dt_in("tdv", (128, C))
    onescol_bf = nc.dram_tensor("onescol_bf", [128, 1], BF16, kind="ExternalInput").ap()
    zcol_in = dt_in("zcol_in", (128, 1))
    onecol_in = dt_in("onecol_in", (128, 1))
    onesrow_in = dt_in("onesrow_in", (1, 128))
    epscol_in = dt_in("epscol_in", (128, 1))
    ident_in = dt_in("ident_in", (128, 128))

    kind_sp = "ExternalOutput" if debug else "Internal"
    y0sp = nc.dram_tensor("y0sp", [D, S], F32, kind=kind_sp).ap()
    y1sp = nc.dram_tensor("y1sp", [D, S], F32, kind=kind_sp).ap()
    x1sp = nc.dram_tensor("x1sp", [D, S], F32, kind=kind_sp).ap()
    gdr = nc.dram_tensor("gdr", [32, S], F32, kind=kind_sp).ap()
    rows = [nc.dram_tensor(f"rows{i}", [2, S], F32, kind=kind_sp).ap() for i in range(4)]
    if debug:
        z1dump = nc.dram_tensor("z1dump", [D, S], F32, kind="ExternalOutput").ap()
        rwkvdump = nc.dram_tensor("rwkvdump", [D, S], F32, kind="ExternalOutput").ap()
    outT = nc.dram_tensor("outT", [D, S], F32, kind="ExternalOutput").ap()

    es = ExitStack()
    sb = lambda name, shape, dt=F32: es.enter_context(nc.sbuf_tensor(name, list(shape), dt))
    p = P(nc, es, selfsync=selfsync)

    g1s = sb("g1s", (128, C)); b1s = sb("b1s", (128, C))
    g2s = sb("g2s", (128, C)); b2s = sb("b2s", (128, C))
    tds = sb("tds", (128, C)); decays = sb("decays", (128, C))
    ones_bf = sb("ones_bf", (128, 1), BF16)
    zcol = sb("zcol", (128, 1))
    onecol = sb("onecol", (128, 1))
    onesrow = sb("onesrow", (1, 128))
    epscol = sb("epscol", (128, 1))
    ident = sb("ident", (128, 128))
    selT_sb = sb("selT_sb", (128, 256), BF16)

    padA = sb("padA", (128, 64))
    padB = sb("padB", (128, 64))
    z1_es = ExitStack()
    z1 = [z1_es.enter_context(nc.sbuf_tensor(f"z1_{c}", [128, S], BF16)) for c in range(C)]

    pending = []

    def drain_one():
        if not pending:
            return
        eng, make, inc, waits = pending.pop(0)
        for s_, v_ in waits:
            p.wait(eng, s_, p.cnt[s_] if v_ is None else v_)
        p.op(eng, make, inc=inc)

    def barrier_all():
        bar = dict(p.cnt)
        for eng in ENGINES:
            for s_ in bar:
                p.wait(eng, s_, bar[s_])

    def drain_all_padded():
        while pending:
            drain_one()
            for _ in range(2):
                p.op("vector", lambda E: E.tensor_copy(out=padA[:], in_=padB[:]))
                p.op("vector", lambda E: E.tensor_copy(out=padB[:], in_=padA[:]))

    def ln_smalls_chain(arena, psA, psB, rows_dram, rowch, col0, width, extra_first_waits):
        A = arena[0:1, 0:width]
        B = arena[0:1, width:2 * width]
        Cc = arena[0:1, 2 * width:3 * width]
        steps = []
        steps.append(("vector", lambda E: E.tensor_single_scalar(
            out=A, in_=psA[0:1, :width], scalar=1.0 / D, op=ALU.mult), "vv",
            list(extra_first_waits)))
        steps.append(("vector", lambda E: E.tensor_single_scalar(
            out=B, in_=psB[0:1, :width], scalar=1.0 / D, op=ALU.mult), "vv", []))
        steps.append(("vector", lambda E: E.tensor_tensor(
            out=Cc, in0=A, in1=A, op=ALU.mult), "vv", []))
        steps.append(("vector", lambda E: E.tensor_tensor(
            out=B, in0=B, in1=Cc, op=ALU.subtract), "vv", []))
        steps.append(("vector", lambda E: E.tensor_single_scalar(
            out=B, in_=B, scalar=LN_EPS, op=ALU.add), "vv", []))
        steps.append(("scalar", lambda E: E.activation(
            out=Cc, in_=B, func=AF.Ln), "ss", [("vv", None)]))
        steps.append(("scalar", lambda E: E.activation(
            out=B, in_=Cc, func=AF.Exp, scale=-0.5), "ss", []))
        steps.append(("vector", lambda E: E.tensor_tensor(
            out=A, in0=A, in1=B, op=ALU.mult), "vv", [("ss", None)]))
        if rows_dram is not None:
            steps.append(("scalar", lambda E: E.dma_start(
                out=rows_dram[0:1, col0:col0 + width], in_=B), rowch, [("vv", None)]))
            steps.append(("scalar", lambda E: E.dma_start(
                out=rows_dram[1:2, col0:col0 + width], in_=A), rowch, [("vv", None)]))
        return steps

    with nc.Block() as block:
        for dst, src in ((g1s, g1v), (b1s, b1v), (g2s, g2v), (b2s, b2v), (tds, tdv),
                         (zcol, zcol_in), (onecol, onecol_in), (epscol, epscol_in),
                         (onesrow, onesrow_in), (ident, ident_in)):
            pre_da = p.op("sync", lambda E, d=dst, s=src: E.dma_start(out=d[:], in_=s[:]), inc=p.chan("pre"))
        p.op("gpsimd", lambda E: E.dma_start(out=ones_bf[:], in_=onescol_bf[:]), inc=p.chan("preg"))
        pre_dw = p.op("gpsimd", lambda E: E.dma_start(out=selT_sb[:], in_=selT3[:]), inc="preg")
        p.wait("scalar", "pre", pre_da)
        decays_done = p.op("scalar", lambda E: E.activation(out=decays[:], in_=tds[:], func=AF.Exp), inc="ss")
        p.op("vector", lambda E: E.memset(padA[:], 0.0))
        p.op("vector", lambda E: E.memset(padB[:], 0.0))

        ph1 = ExitStack()
        sb1 = lambda n_, s_, d_=F32: ph1.enter_context(nc.sbuf_tensor(n_, list(s_), d_))
        ps1 = lambda n_, s_: ph1.enter_context(nc.psum_tensor(n_, list(s_), F32))
        XC = [[sb1(f"XC{r}_{c}", (128, T1)) for c in range(C)] for r in range(2)]
        Y0C = [[sb1(f"Y0C{r}_{c}", (128, T1)) for c in range(C)] for r in range(2)]
        xb_r = [sb1(f"xb{i}", (128, T1), BF16) for i in range(2)]
        xq_r = [sb1(f"xq{i}", (128, T1), BF16) for i in range(2)]
        tmp_r = [sb1(f"tmp{i}", (128, T1)) for i in range(2)]
        rt1 = [sb1(f"rt1_{i}", (1, 3 * T1)) for i in range(4)]
        ps_st0 = [ps1(f"ps_st0{i}", (128, T1)) for i in range(2)]
        ps_st1 = [ps1(f"ps_st1{i}", (128, T1)) for i in range(2)]
        ps_bc0 = [ps1(f"ps_bc0{i}", (128, T1)) for i in range(2)]
        ps_bc1 = [ps1(f"ps_bc1{i}", (128, T1)) for i in range(2)]

        for r in range(2):
            for c in range(C):
                p.chan(f"xc{r}_{c}")
                p.chan(f"y0w{c}")
        xc_free = [[0] * C for _ in range(2)]
        y0c_spill = [[("y0w0", 0)] * C for _ in range(2)]
        y0c_sq = [[0] * C for _ in range(2)]
        xb_free = [0] * 2
        xq_free = [0] * 2
        st0_read = 0
        tmp_ss = [0, 0]
        st1_read = 0
        bc0_read = 0
        bc1_read = 0
        st0_rec = [0] * NT1
        st1_rec = [0] * NT1
        ch0_rec = [0] * NT1
        ch1_rec = [0] * NT1
        bc0_mm = [0] * NT1
        bc1_mm = [0] * NT1
        y0sp_done = [[0] * C for _ in range(NT1)]
        z1_done_t = [0] * NT1
        p.wait("tensor", "preg", pre_dw)
        p.wait("tensor", "pre", pre_da)

        def stA(t):
            r = t % 2
            for c in range(C):
                p.wait("sync", "vv", xc_free[r][c])
                p.op("sync", lambda E, t=t, c=c, r=r: E.dma_start(
                    out=XC[r][c][:], in_=xT[c * 128:(c + 1) * 128, t * T1:(t + 1) * T1]), inc=f"xc{r}_{c}")

        def stB(t):
            r = t % 2
            for c in range(C):
                b = c % 2
                p.wait("vector", f"xc{r}_{c}", p.cnt[f"xc{r}_{c}"])
                p.wait("vector", "mm", xb_free[b])
                vb = p.op("vector", lambda E, b=b, r=r, c=c: E.tensor_copy(out=xb_r[b][:], in_=XC[r][c][:]), inc="vv")
                p.wait("scalar", f"xc{r}_{c}", p.cnt[f"xc{r}_{c}"])
                p.wait("scalar", "mm", xq_free[b])
                sq = p.op("scalar", lambda E, b=b, r=r, c=c: E.activation(
                    out=xq_r[b][:], in_=XC[r][c][:], func=AF.Square), inc="ss")
                p.wait("tensor", "vv", vb)
                p.wait("tensor", "ss", sq)
                if c == 0:
                    p.wait("tensor", "vv", st0_read)
                p.op("tensor", lambda E, b=b, st=(c == 0), sp=(c == 7): E.matmul(
                    ps_st0[0][0:1, :], ones_bf[:], xb_r[b][:], start=st, stop=sp))
                mmv = p.op("tensor", lambda E, b=b, st=(c == 0), sp=(c == 7): E.matmul(
                    ps_st0[1][0:1, :], ones_bf[:], xq_r[b][:], start=st, stop=sp), inc="mm")
                xb_free[b] = mmv
                xq_free[b] = mmv
                drain_one()
            st0_rec[t] = mmv
            pending.extend(ln_smalls_chain(rt1[t % 4], ps_st0[0], ps_st0[1], None, None, 0, T1,
                                           [("mm", st0_rec[t])]))

        def stD(t):
            nonlocal st0_read
            ch0_rec[t] = p.cnt["vv"]
            st0_read = ch0_rec[t]
            A_ = rt1[t % 4][0:1, 0:T1]
            B_ = rt1[t % 4][0:1, T1:2 * T1]
            p.wait("tensor", "vv", ch0_rec[t])
            p.wait("tensor", "vv", bc0_read)
            p.op("tensor", lambda E, B_=B_: E.matmul(ps_bc0[0][:], onesrow[:], B_, start=True, stop=True))
            bc0_mm[t] = p.op("tensor", lambda E, A_=A_: E.matmul(
                ps_bc0[1][:], onesrow[:], A_, start=True, stop=True), inc="mm")

        def stE(t):
            nonlocal bc0_read
            r = t % 2
            for c in range(C):
                b = c % 2
                tb = c % 2
                p.wait("vector", "mm", bc0_mm[t])
                wch, wval = y0c_spill[r][c]
                p.wait("vector", wch, wval)
                p.wait("vector", "ss", y0c_sq[r][c])
                p.wait("vector", "ss", tmp_ss[tb])
                p.op("vector", lambda E, r=r, c=c, tb=tb: E.tensor_tensor(
                    out=tmp_r[tb][:], in0=XC[r][c][:], in1=ps_bc0[0][:], op=ALU.mult))
                vy0 = p.op("vector", lambda E, r=r, c=c, tb=tb: E.tensor_tensor(
                    out=Y0C[r][c][:], in0=tmp_r[tb][:], in1=ps_bc0[1][:], op=ALU.subtract), inc="vv")
                xc_free[r][c] = vy0
                p.wait("sync", "vv", vy0)
                y0da = p.op("sync", lambda E, t=t, c=c, r=r: E.dma_start(
                    out=y0sp[c * 128:(c + 1) * 128, t * T1:(t + 1) * T1], in_=Y0C[r][c][:]), inc=f"y0w{c}")
                y0sp_done[t][c] = (f"y0w{c}", y0da)
                y0c_spill[r][c] = (f"y0w{c}", y0da)
                p.wait("vector", "mm", xb_free[b])
                vb = p.op("vector", lambda E, b=b, r=r, c=c: E.tensor_copy(out=xb_r[b][:], in_=Y0C[r][c][:]), inc="vv")
                p.wait("scalar", "vv", vy0)
                p.wait("scalar", "mm", xq_free[b])
                sq = p.op("scalar", lambda E, b=b, r=r, c=c: E.activation(
                    out=xq_r[b][:], in_=Y0C[r][c][:], func=AF.Square), inc="ss")
                y0c_sq[r][c] = sq
                p.wait("tensor", "vv", vb)
                p.wait("tensor", "ss", sq)
                if c == 0:
                    p.wait("tensor", "vv", st1_read)
                p.op("tensor", lambda E, b=b, st=(c == 0), sp=(c == 7): E.matmul(
                    ps_st1[0][0:1, :], ones_bf[:], xb_r[b][:], start=st, stop=sp))
                mmv = p.op("tensor", lambda E, b=b, st=(c == 0), sp=(c == 7): E.matmul(
                    ps_st1[1][0:1, :], ones_bf[:], xq_r[b][:], start=st, stop=sp), inc="mm")
                xb_free[b] = mmv
                xq_free[b] = mmv
                drain_one()
            bc0_read = p.cnt["vv"]
            st1_rec[t] = mmv
            pending.extend(ln_smalls_chain(rt1[2 + t % 2] if False else rt1[(t % 4)], ps_st1[0], ps_st1[1], None, None, 0, T1,
                                           [("mm", st1_rec[t])]))

        def stG(t):
            nonlocal st1_read
            ch1_rec[t] = p.cnt["vv"]
            st1_read = ch1_rec[t]
            A_ = rt1[t % 4][0:1, 0:T1]
            B_ = rt1[t % 4][0:1, T1:2 * T1]
            p.wait("tensor", "vv", ch1_rec[t])
            p.wait("tensor", "vv", bc1_read)
            p.op("tensor", lambda E, B_=B_: E.matmul(ps_bc1[0][:], onesrow[:], B_, start=True, stop=True))
            bc1_mm[t] = p.op("tensor", lambda E, A_=A_: E.matmul(
                ps_bc1[1][:], onesrow[:], A_, start=True, stop=True), inc="mm")

        def stH(t):
            nonlocal bc1_read
            r = t % 2
            for c in range(C):
                tb = c % 2
                p.wait("vector", "mm", bc1_mm[t])
                p.wait("vector", "ss", tmp_ss[tb])
                p.op("vector", lambda E, r=r, c=c, tb=tb: E.tensor_tensor(
                    out=tmp_r[tb][:], in0=Y0C[r][c][:], in1=ps_bc1[0][:], op=ALU.mult))
                vt2 = p.op("vector", lambda E, tb=tb: E.tensor_tensor(
                    out=tmp_r[tb][:], in0=tmp_r[tb][:], in1=ps_bc1[1][:], op=ALU.subtract), inc="vv")
                p.wait("scalar", "vv", vt2)
                sz = p.op("scalar", lambda E, t=t, c=c, tb=tb: E.activation(
                    out=z1[c][:, t * T1:(t + 1) * T1], in_=tmp_r[tb][:], func=AF.Identity,
                    scale=g1s[:, c:c + 1], bias=b1s[:, c:c + 1]), inc="ss")
                tmp_ss[tb] = sz
                drain_one()
            z1_done_t[t] = p.cnt["ss"]
            bc1_read = p.cnt["vv"]

        for s in range(NT1 + 1):
            if s < NT1:
                stA(s)
                stB(s)
                if s == 0:
                    drain_all_padded()
            if s >= 1:
                stG(s - 1)
                stH(s - 1)
            if s < NT1:
                stD(s)
                stE(s)
        drain_all_padded()
        ph1.close()
        barrier_all()

        rwkv_es = ExitStack()
        rwkv = [rwkv_es.enter_context(nc.sbuf_tensor(f"rwkv{c}", [128, S], BF16, side="right")) for c in range(C)]
        ph2 = ExitStack()
        sb2 = lambda n_, s_, d_=F32: ph2.enter_context(nc.sbuf_tensor(n_, list(s_), d_))
        ps2 = lambda n_, s_: ph2.enter_context(nc.psum_tensor(n_, list(s_), F32))
        wnames = ("wr", "ww", "wk", "wv")
        wproj = {nm: [sb2(f"{nm}_{i}", (128, 1024), BF16) for i in range(2)] for nm in wnames}
        EWKD = sb2("EWKD", (128, T1 + 1))
        NUMD = sb2("NUMD", (128, T1 + 1))
        NUMO = sb2("NUMO", (128, T1 + 1))
        DENO = sb2("DENO", (128, T1 + 1))
        Dscan = sb2("Dscan", (128, T1 + 1))
        EWb = [sb2(f"EWb{i}", (128, T1)) for i in range(2)]
        Ub = [sb2(f"Ub{i}", (128, T1)) for i in range(2)]
        RREb = [sb2(f"RREb{i}", (128, T1)) for i in range(2)]
        Rb = [sb2(f"Rb{i}", (128, T1)) for i in range(2)]
        W1 = sb2("W1", (128, T1))
        W2 = sb2("W2", (128, T1))
        PSp = {nm: [ps2(f"PS{nm}{i}", (128, T1)) for i in range(2)] for nm in ("w", "k", "r", "v")}

        proj_src = {"wr": w4["wr4"], "ww": w4["ww4"], "wk": w4["wk4"], "wv": w4["wv4"]}
        wproj_done = [0] * C
        wslot_free = [0] * 2

        p.chan("wp0"); p.chan("wp1")

        def load_proj(ec):
            slot = ec % 2
            p.wait("gpsimd", "mm", wslot_free[slot])
            v = 0
            for nm in wnames:
                v = p.op("gpsimd", lambda E, nm=nm, ec=ec, s=slot: E.dma_start(
                    out=wproj[nm][s][:], in_=proj_src[nm][ec].rearrange("p a b -> p (a b)")), inc=f"wp{slot}")
            wproj_done[ec] = v

        load_proj(0)
        load_proj(1)

        p.wait("vector", "pre", pre_da)
        p.wait("vector", "ss", decays_done)
        bank_free_v = {}
        bank_free_s = {}
        ewkd_read = 0
        wkv_prev = 0
        wkv_done = 0
        carry_rd = 0
        vRWKV = [[0] * NT1 for _ in range(C)]
        for ec in range(C):
            p.wait("vector", "vv", ewkd_read)
            p.op("vector", lambda E, ec=ec: E.tensor_copy(
                out=Dscan[:, 1:T1 + 1], in_=decays[:, ec:ec + 1].broadcast_to([128, T1])))
            p.op("vector", lambda E: E.tensor_copy(out=Dscan[:, 0:1], in_=zcol[:]))
            p.op("vector", lambda E: E.tensor_copy(out=NUMD[:, 0:1], in_=zcol[:]))
            dscan_v = ewkd_carry = p.op("vector", lambda E: E.tensor_copy(out=EWKD[:, 0:1], in_=zcol[:]), inc="vv")
            if ec + 1 < C:
                load_proj(ec + 1)
            for t in range(NT1):
                tp = t % 2
                prev = (ec, t - 2) if t >= 2 else ((ec - 1, NT1 - 2 + t) if ec > 0 else None)
                p.wait("tensor", f"wp{ec % 2}", wproj_done[ec])
                p.wait("tensor", "ss", z1_done_t[t])
                if prev is not None:
                    p.wait("tensor", "vv", bank_free_v[prev])
                    p.wait("tensor", "ss", bank_free_s[prev])
                mg = {}
                for nm, wsb in (("w", "ww"), ("k", "wk"), ("r", "wr"), ("v", "wv")):
                    for kc in range(C):
                        p.op("tensor", lambda E, nm=nm, wsb=wsb, kc=kc, t=t, tp=tp, ec=ec,
                             st=(kc == 0), sp=(kc == 7): E.matmul(
                            PSp[nm][tp][:], wproj[wsb][ec % 2][:, kc * 128:(kc + 1) * 128],
                            z1[kc][:, t * T1:(t + 1) * T1], start=st, stop=sp),
                            inc="mm" if kc == 7 else None)
                    mg[nm] = p.cnt["mm"]
                p.wait("scalar", "mm", mg["w"])
                sEW = p.op("scalar", lambda E, tp=tp: E.activation(
                    out=EWb[tp][:], in_=PSp["w"][tp][:], func=AF.Exp), inc="ss")
                p.wait("vector", "mm", mg["k"])
                p.wait("vector", "ss", sEW)
                vU = p.op("vector", lambda E, tp=tp: E.tensor_tensor(
                    out=Ub[tp][:], in0=PSp["k"][tp][:], in1=EWb[tp][:], op=ALU.subtract), inc="vv")
                p.wait("scalar", "vv", vU)
                p.wait("scalar", "vv", ewkd_read)
                p.wait("scalar", "vv", ewkd_carry)
                sEWK = p.op("scalar", lambda E, tp=tp: E.activation(
                    out=EWKD[:, 1:T1 + 1], in_=Ub[tp][:], func=AF.Exp), inc="ss")
                p.wait("scalar", "mm", mg["r"])
                sRRE = p.op("scalar", lambda E, tp=tp: E.activation(
                    out=RREb[tp][:], in_=PSp["r"][tp][:], func=AF.Tanh, scale=0.5), inc="ss")
                p.wait("vector", "ss", sRRE)
                p.op("vector", lambda E, tp=tp: E.tensor_scalar(
                    out=Rb[tp][:], in0=RREb[tp][:], scalar1=0.5, scalar2=0.5, op0=ALU.mult, op1=ALU.add))
                p.wait("vector", "ss", sEWK)
                p.wait("vector", "mm", mg["v"])
                vN = p.op("vector", lambda E, tp=tp: E.tensor_tensor(
                    out=NUMD[:, 1:T1 + 1], in0=EWKD[:, 1:T1 + 1], in1=PSp["v"][tp][:], op=ALU.mult), inc="vv")
                p.op("vector", lambda E: E.tensor_tensor_scan(
                    out=NUMO[:], data0=Dscan[:], data1=NUMD[:], initial=0.0, op0=ALU.mult, op1=ALU.add))
                vden = p.op("vector", lambda E: E.tensor_tensor_scan(
                    out=DENO[:], data0=Dscan[:], data1=EWKD[:], initial=0.0, op0=ALU.mult, op1=ALU.add), inc="vv")
                ewkd_read = vden
                p.op("vector", lambda E: E.tensor_single_scalar(
                    out=W1[:], in_=DENO[:, 1:T1 + 1], scalar=WKV_EPS, op=ALU.add))
                p.op("vector", lambda E: E.reciprocal(out=W2[:], in_=W1[:]))
                wkv_done = p.op("vector", lambda E: E.tensor_tensor(
                    out=W1[:], in0=NUMO[:, 1:T1 + 1], in1=W2[:], op=ALU.mult), inc="vv")
                vR = p.op("vector", lambda E, ec=ec, t=t, tp=tp: E.tensor_tensor(
                    out=rwkv[ec][:, t * T1:(t + 1) * T1], in0=Rb[tp][:], in1=W1[:], op=ALU.mult), inc="vv")
                vRWKV[ec][t] = ("vv", vR)
                if t < NT1 - 1:
                    p.op("vector", lambda E: E.tensor_copy(out=NUMD[:, 0:1], in_=NUMO[:, T1:T1 + 1]))
                    carry_rd = ewkd_carry = p.op("vector", lambda E: E.tensor_copy(
                        out=EWKD[:, 0:1], in_=DENO[:, T1:T1 + 1]), inc="vv")
                bank_free_v[(ec, t)] = vN
                bank_free_s[(ec, t)] = sRRE
            wslot_free[ec % 2] = p.cnt["mm"]
        if debug:
            for c in range(C):
                p.wait("gpsimd", vRWKV[c][NT1 - 1][0], vRWKV[c][NT1 - 1][1])
                p.op("gpsimd", lambda E, c=c: E.dma_start(
                    out=rwkvdump[c * 128:(c + 1) * 128, :], in_=rwkv[c][:]), inc=p.chan("dbg"))
        ph2_end_mm = p.cnt["mm"]
        ph2.close()
        z1_es.close()
        ph4w = ExitStack()
        wkey_sb = [ph4w.enter_context(nc.sbuf_tensor(f"wkeysb{fc}", [128, 1024], BF16)) for fc in range(32)]
        barrier_all()
        wkey_done = 0
        p.chan("wk"); p.chan("wv"); p.chan("wr")
        p.wait("gpsimd", "mm", ph2_end_mm)
        for fc in range(32):
            wkey_done = p.op("gpsimd", lambda E, fc=fc: E.dma_start(
                out=wkey_sb[fc][:], in_=wkey4[fc].rearrange("p a b -> p (a b)")), inc="wk")

        ph3 = ExitStack()
        sb3 = lambda n_, s_, d_=F32: ph3.enter_context(nc.sbuf_tensor(n_, list(s_), d_, side="right"))
        ps3 = lambda n_, s_: ph3.enter_context(nc.psum_tensor(n_, list(s_), F32))
        wo_sb = [sb3(f"wo_sb{e}", (128, 1024), BF16) for e in range(C)]
        PSo = [ps3(f"PSo{i}", (128, T1)) for i in range(4)]
        ps_s2 = [ps3(f"ps_s2{i}", (128, T1)) for i in range(4)]
        rt3 = [sb3(f"rt3_{i}", (1, 3 * T1)) for i in range(2)]
        y0l_r = [sb3(f"y0l{i}", (128, T1)) for i in range(4)]
        y1b_r = [sb3(f"y1b{i}", (128, T1)) for i in range(3)]
        xb3_r = [sb3(f"xb3{i}", (128, T1), BF16) for i in range(8)]
        xq3_r = [sb3(f"xq3{i}", (128, T1), BF16) for i in range(8)]

        p.wait("gpsimd", "mm", ph2_end_mm)
        wo_done = 0
        p.chan("wo")
        for e in range(C):
            wo_done = p.op("gpsimd", lambda E, e=e: E.dma_start(
                out=wo_sb[e][:], in_=w4["wo4"][e].rearrange("p a b -> p (a b)")), inc="wo")

        y0l_free3 = [(None, 0)] * 4
        y1b_cons = [(0, 0)] * 3
        xb3_free = [0] * 8
        xq3_free = [0] * 8
        vb_sem3 = [0] * 8
        sq_sem3 = [0] * 8
        pso_free = [0] * 4
        s2_bank_free = [0] * 2
        y1sp_done = [[0] * C for _ in range(NT1)]
        st2 = [0] * NT1

        def stats_mms3(i):
            tq, cq = i // 8, i % 8
            tpar = tq % 2
            p.wait("tensor", "vv", vb_sem3[cq])
            p.wait("tensor", "ss", sq_sem3[cq])
            if cq == 0:
                p.wait("tensor", "vv", s2_bank_free[tpar])
            p.op("tensor", lambda E, cq=cq, tpar=tpar, st=(cq == 0), sp=(cq == 7): E.matmul(
                ps_s2[2 * tpar][0:1, :], ones_bf[:], xb3_r[cq][:], start=st, stop=sp))
            mmv = p.op("tensor", lambda E, cq=cq, tpar=tpar, st=(cq == 0), sp=(cq == 7): E.matmul(
                ps_s2[2 * tpar + 1][0:1, :], ones_bf[:], xq3_r[cq][:], start=st, stop=sp), inc="mm")
            xb3_free[cq] = mmv
            xq3_free[cq] = mmv
            if cq == 7:
                st2[tq] = mmv
                pending.extend(ln_smalls_chain(rt3[tq % 2], ps_s2[2 * (tq % 2)], ps_s2[2 * (tq % 2) + 1],
                                               rows[2], p.chan("rw2"), tq * T1, T1,
                                               [("mm", st2[tq]), ("rw2", None)]))

        for t in range(NT1):
            for c in range(C):
                po = c % 4
                i = t * C + c
                slot = i % 4
                sem, val = y0l_free3[slot]
                if sem is not None:
                    p.wait("sync", sem, val)
                wch3, wv3 = y0sp_done[t][c]
                p.wait("sync", wch3, wv3)
                da = p.op("sync", lambda E, t=t, c=c, s=slot: E.dma_start(
                    out=y0l_r[s][:], in_=y0sp[c * 128:(c + 1) * 128, t * T1:(t + 1) * T1]), inc=p.chan(f"yl{slot}"))
                rch, rval = vRWKV[C - 1][t]
                p.wait("tensor", rch, rval)
                p.wait("tensor", "wo", wo_done)
                p.wait("tensor", "vv", pso_free[po])
                mo = 0
                for kc in range(C):
                    mo = p.op("tensor", lambda E, c=c, kc=kc, t=t, po=po, st=(kc == 0), sp=(kc == 7): E.matmul(
                        PSo[po][:], wo_sb[c][:, kc * 128:(kc + 1) * 128],
                        rwkv[kc][:, t * T1:(t + 1) * T1], start=st, stop=sp),
                        inc="mm" if kc == 7 else None)
                if i >= 2:
                    stats_mms3(i - 2)
                yb = i % 3
                ss1, da1 = y1b_cons[yb]
                p.wait("vector", "mm", mo)
                p.wait("vector", f"yl{slot}", da)
                p.wait("vector", "ss", ss1)
                p.wait("vector", f"y1w{yb}", da1)
                vy1 = p.op("vector", lambda E, po=po, s=slot, y=yb: E.tensor_tensor(
                    out=y1b_r[y][:], in0=PSo[po][:], in1=y0l_r[s][:], op=ALU.add), inc="vv")
                pso_free[po] = vy1
                y0l_free3[slot] = ("vv", vy1)
                drain_one()
                p.wait("sync", "vv", vy1)
                y1da = p.op("sync", lambda E, t=t, c=c, y=yb: E.dma_start(
                    out=y1sp[c * 128:(c + 1) * 128, t * T1:(t + 1) * T1], in_=y1b_r[y][:]), inc=p.chan(f"y1w{yb}"))
                y1sp_done[t][c] = (f"y1w{yb}", y1da)
                p.wait("vector", "mm", xb3_free[c])
                vb = p.op("vector", lambda E, c=c, y=yb: E.tensor_copy(out=xb3_r[c][:], in_=y1b_r[y][:]), inc="vv")
                p.wait("scalar", "vv", vy1)
                p.wait("scalar", "mm", xq3_free[c])
                sq = p.op("scalar", lambda E, c=c, y=yb: E.activation(
                    out=xq3_r[c][:], in_=y1b_r[y][:], func=AF.Square), inc="ss")
                y1b_cons[yb] = (sq, y1da)
                vb_sem3[c] = vb
                sq_sem3[c] = sq
                drain_one()
            if t >= 1:
                s2_bank_free[(t - 1) % 2] = p.cnt["vv"]
        stats_mms3(NT1 * C - 2)
        stats_mms3(NT1 * C - 1)
        drain_all_padded()
        rows2_all_done = p.cnt["rw2"]
        ph3_end_mm = p.cnt["mm"]
        ph3_tail_vv = p.cnt["vv"]
        ph3_tail_ss = p.cnt["ss"]
        y1w_totals = [p.cnt[f"y1w{y}"] for y in range(3)]
        ph3.close()
        rwkv_es.close()
        wval_sb = [ph4w.enter_context(nc.sbuf_tensor(f"wvalsb{fc}", [128, 1024], BF16, side="right")) for fc in range(32)]

        ph4 = ExitStack()
        sb4 = lambda n_, s_, d_=F32: ph4.enter_context(nc.sbuf_tensor(n_, list(s_), d_))
        ps4 = lambda n_, s_: ph4.enter_context(nc.psum_tensor(n_, list(s_), F32))
        wrec_sb = [sb4(f"wrecsb{e}", (128, 1024), BF16) for e in range(C)]
        BC2r = [sb4(f"BC2r{i}", (128, T4)) for i in range(2)]
        BC2m = [sb4(f"BC2m{i}", (128, T4)) for i in range(2)]
        Y1T = [[sb4(f"Y1T{j}_{ec}", (128, T4)) for ec in range(C)] for j in range(2)]
        xring = [sb4(f"xring{i}", (128, T4)) for i in range(2)]
        z2t = [sb4(f"z2t{ec}", (128, T4), BF16) for ec in range(C)]
        tmp4 = [sb4(f"tmp4{i}", (128, T4)) for i in range(1)]
        KRb = sb4("KRb", (128, T4))
        KSb = [sb4(f"KS{i}", (128, T4), BF16) for i in range(32)]
        RC2 = [sb4(f"RC2{i}", (128, T4)) for i in range(8)]
        X1T = [sb4(f"X1T{j}", (128, T4)) for j in range(2)]
        xb4_r = [sb4(f"xb4{i}", (128, T4), BF16) for i in range(4)]
        xq4_r = [sb4(f"xq4{i}", (128, T4), BF16) for i in range(4)]
        PS_k4 = [ps4(f"PS_k4{i}", (128, 512)) for i in range(2)]
        PS_v4 = [ps4(f"PS_v4{i}", (128, 512)) for i in range(4)]
        ps_s3 = [ps4(f"ps_s3{i}", (128, 512)) for i in range(2)]
        rt4 = [sb4(f"rt4_{i}", (1, 3 * T4)) for i in range(1)]

        wrec_done = 0
        for e in range(C):
            wrec_done = p.op("gpsimd", lambda E, e=e: E.dma_start(
                out=wrec_sb[e][:], in_=w4["wrec4"][e].rearrange("p a b -> p (a b)")), inc="wr")


        bc2_free = [0] * 2
        y1t_free = [0] * 2
        xring_free = [0] * 2
        xa_fold = [[0] * C for _ in range(2)]
        vcmA_free = 0
        z2_done_r = [0] * 2
        ks_free = [0] * 32
        kr_free = 0
        k4_free_v = [pso_free[0], pso_free[1]]
        k4_free_s = [0] * 2
        vcm_free = ph3_tail_vv
        rc2_free = [0] * 8
        rc2_sig = [0] * 8
        x1spill = [0] * 2
        x1t_cons = [0] * 2
        st3 = [0] * NT4
        rows3_done = [0] * NT4
        xb4_free = [0] * 4
        xq4_free = [0] * 4
        vb_sem = [0] * 4
        sq_sem = [0] * 4
        s3_read_gate = 0

        def smalls3_chain(u, st_val):
            return ln_smalls_chain(rt4[0], ps_s3[0], ps_s3[1], rows[3], p.chan("rw3"), u * T4, T4,
                                   [("mm", st_val), ("rw3", None)])

        z2t_free_mm = ph3_end_mm

        def stage_head(u):
            nonlocal bc2_free, y1t_free, z2t_free_mm
            t1i = u // 2
            up = u % 2
            if u == 0:
                p.wait("vector", "mm", ph3_end_mm)
                p.wait("scalar", "mm", ph3_end_mm)
                p.wait("sync", "mm", ph3_end_mm)
                p.wait("sync", "vv", ph3_tail_vv)
                p.wait("sync", "ss", ph3_tail_ss)
                p.wait("sync", "rw2", rows2_all_done)
                for y in range(3):
                    p.wait("sync", f"y1w{y}", y1w_totals[y])
            p.wait("gpsimd", "rw2", rows2_all_done)
            p.wait("gpsimd", "vv", bc2_free[up])
            p.op("gpsimd", lambda E, u=u, up=up: E.dma_start(
                out=BC2r[up][:], in_=rows[2][0:1, u * T4:(u + 1) * T4].broadcast_to([128, T4])), inc=p.chan(f"bc2{up}"))
            bc_da = p.op("gpsimd", lambda E, u=u, up=up: E.dma_start(
                out=BC2m[up][:], in_=rows[2][1:2, u * T4:(u + 1) * T4].broadcast_to([128, T4])), inc=f"bc2{up}")
            p.wait("sync", "vv", y1t_free[up])
            y1_da = 0
            for ec in range(C):
                wch4, wv4 = y1sp_done[t1i][ec]
                p.wait("sync", wch4, wv4)
                y1_da = p.op("sync", lambda E, u=u, up=up, ec=ec: E.dma_start(
                    out=Y1T[up][ec][:], in_=y1sp[ec * 128:(ec + 1) * 128, u * T4:(u + 1) * T4]), inc=p.chan(f"y1t{up}"))
            z2rec = 0
            z2par = [z2_done_r[up]] * 1
            for ec in range(C):
                tb = 0
                xs = ec % 2
                p.wait("sync", "vv", xring_free[xs])
                x_da = p.op("sync", lambda E, u=u, ec=ec, xs=xs: E.dma_start(
                    out=xring[xs][:], in_=xT[ec * 128:(ec + 1) * 128, u * T4:(u + 1) * T4]), inc=p.chan(f"xr{xs}"))
                p.wait("vector", f"y1t{up}", y1_da)
                p.wait("vector", f"bc2{up}", bc_da)
                p.wait("vector", "ss", z2par[tb])
                p.op("vector", lambda E, up=up, ec=ec, tb=tb: E.tensor_tensor(
                    out=tmp4[tb][:], in0=Y1T[up][ec][:], in1=BC2r[up][:], op=ALU.mult))
                vt = p.op("vector", lambda E, up=up, tb=tb: E.tensor_tensor(
                    out=tmp4[tb][:], in0=tmp4[tb][:], in1=BC2m[up][:], op=ALU.subtract), inc="vv")
                p.wait("scalar", "vv", vt)
                p.wait("scalar", "mm", z2t_free_mm)
                z2rec = p.op("scalar", lambda E, ec=ec, tb=tb: E.activation(
                    out=z2t[ec][:], in_=tmp4[tb][:], func=AF.Identity,
                    scale=g2s[:, ec:ec + 1], bias=b2s[:, ec:ec + 1]), inc="ss")
                z2par[tb] = z2rec
                p.wait("vector", f"xr{xs}", x_da)
                xa = p.op("vector", lambda E, up=up, ec=ec, xs=xs: E.tensor_tensor(
                    out=Y1T[up][ec][:], in0=Y1T[up][ec][:], in1=xring[xs][:], op=ALU.add), inc="vv")
                xring_free[xs] = xa
                xa_fold[up][ec] = xa
                drain_one()
            z2_done_r[up] = z2rec
            bc2_free[up] = p.cnt["vv"]
            return z2rec

        def kcm_only(u, z2rec):
            nonlocal kr_free
            for fc in range(32):
                fp = fc % 2
                p.wait("tensor", "wk", wkey_done)
                p.wait("tensor", "ss", z2rec)
                p.wait("tensor", "vv", k4_free_v[fp])
                p.wait("tensor", "ss", k4_free_s[fp])
                mk = 0
                for kc in range(C):
                    mk = p.op("tensor", lambda E, fc=fc, kc=kc, fp=fp, st=(kc == 0), sp=(kc == 7): E.matmul(
                        PS_k4[fp][:, :T4], wkey_sb[fc][:, kc * 128:(kc + 1) * 128],
                        z2t[kc][:], start=st, stop=sp), inc="mm" if kc == 7 else None)
                p.wait("vector", "mm", mk)
                p.wait("vector", "ss", kr_free)
                vkr = p.op("vector", lambda E, fp=fp: E.tensor_single_scalar(
                    out=KRb[:], in_=PS_k4[fp][:, :T4], scalar=0.0, op=ALU.max), inc="vv")
                k4_free_v[fp] = vkr
                p.wait("scalar", "vv", vkr)
                p.wait("scalar", "mm", ks_free[fc])
                sks = p.op("scalar", lambda E, fc=fc: E.activation(
                    out=KSb[fc][:], in_=KRb[:], func=AF.Square), inc="ss")
                kr_free = sks
                drain_one()
            return sks

        def rcm_sig_half(u, dh):
            last = 0
            for pair in range(2):
                mrs = [0] * 2
                for ecl2 in range(2):
                    ec = dh * 4 + pair * 2 + ecl2
                    ep = ec % 2
                    p.wait("tensor", "wr", wrec_done)
                    p.wait("tensor", "vv", k4_free_v[ep])
                    p.wait("tensor", "ss", k4_free_s[ep])
                    mr = 0
                    for kc in range(C):
                        mr = p.op("tensor", lambda E, kc=kc, ec=ec, ep=ep, st=(kc == 0), sp=(kc == 7): E.matmul(
                            PS_k4[ep][:, :T4], wrec_sb[ec][:, kc * 128:(kc + 1) * 128],
                            z2t[kc][:], start=st, stop=sp), inc="mm" if kc == 7 else None)
                    k4_free_v[ep] = 0
                    mrs[ecl2] = mr
                srs = [0, 0]
                for ecl2 in range(2):
                    ec = dh * 4 + pair * 2 + ecl2
                    ep = ec % 2
                    slot = dh * 4 + pair * 2 + ecl2
                    p.wait("scalar", "mm", mrs[ecl2])
                    p.wait("scalar", "vv", rc2_free[slot])
                    srs[ecl2] = p.op("scalar", lambda E, ep=ep, slot=slot: E.activation(
                        out=RC2[slot][:], in_=PS_k4[ep][:, :T4], func=AF.Sigmoid, scale=1.0), inc="ss")
                    k4_free_s[ep] = srs[ecl2]
                    rc2_sig[slot] = srs[ecl2]
                    last = srs[ecl2]
            return last

        def vcm_pass(u, dh, ks_ready, bankfree_vv):
            p.wait("tensor", "ss", ks_ready)
            p.wait("tensor", "vv", bankfree_vv)
            mvcm = 0
            for fc in range(32):
                p.wait("tensor", f"wv{fc // 4}", 64)
                for dcl in range(4):
                    dc = dh * 4 + dcl
                    mvcm = p.op("tensor", lambda E, fc=fc, dc=dc, dcl=dcl,
                                st=(fc == 0), sp=(fc == 31): E.matmul(
                        PS_v4[dcl][:, :T4],
                        wval_sb[fc][:, dc * 128:(dc + 1) * 128], KSb[fc][:],
                        start=st, stop=sp), inc="mm" if dcl == 3 else None)
                if dh == 1:
                    ks_free[fc] = mvcm
            return mvcm

        def combine_half(u, dh, mvcm):
            nonlocal vcm_free, vcmA_free, s3_read_gate
            up = u % 2
            lastvx1 = 0
            if dh == 0:
                drain_one()
                drain_one()
            bank_evac = 0
            for ecl in range(4):
                slot = dh * 4 + ecl
                p.wait("vector", "mm", mvcm)
                p.wait("vector", "ss", rc2_sig[slot])
                bank_evac = p.op("vector", lambda E, slot=slot, ecl=ecl: E.tensor_tensor(
                    out=RC2[slot][:], in0=PS_v4[ecl][:, :T4], in1=RC2[slot][:], op=ALU.mult), inc="vv")
            if dh == 0:
                vcmA_free = bank_evac
            else:
                vcm_free = bank_evac
            for ecl in range(4):
                ec = dh * 4 + ecl
                slot = dh * 4 + ecl
                xs4 = ec % 2
                p.wait("vector", "ss", x1t_cons[xs4])
                p.wait("vector", f"x1w{xs4}", x1spill[xs4])
                vx1 = p.op("vector", lambda E, slot=slot, up=up, ec=ec, xs4=xs4: E.tensor_tensor(
                    out=X1T[xs4][:], in0=RC2[slot][:], in1=Y1T[up][ec][:], op=ALU.add), inc="vv")
                rc2_free[slot] = vx1
                lastvx1 = vx1
                if ec == C - 1:
                    y1t_free[up] = vx1
                p.wait("sync", "vv", vx1)
                x1da = p.op("sync", lambda E, u=u, ec=ec, xs4=xs4: E.dma_start(
                    out=x1sp[ec * 128:(ec + 1) * 128, u * T4:(u + 1) * T4], in_=X1T[xs4][:]), inc=p.chan(f"x1w{xs4}"))
                x1spill[xs4] = x1da
                sl4 = ec % 4
                p.wait("vector", "mm", xb4_free[sl4])
                vb = p.op("vector", lambda E, sl4=sl4, xs4=xs4: E.tensor_copy(out=xb4_r[sl4][:], in_=X1T[xs4][:]), inc="vv")
                p.wait("scalar", "vv", vx1)
                p.wait("scalar", "mm", xq4_free[sl4])
                sq = p.op("scalar", lambda E, sl4=sl4, xs4=xs4: E.activation(
                    out=xq4_r[sl4][:], in_=X1T[xs4][:], func=AF.Square), inc="ss")
                x1t_cons[xs4] = sq
                vb_sem[sl4] = vb
                sq_sem[sl4] = sq
                drain_one()
            return lastvx1

        def stats_mms(u, dh, gate=0):
            mmv = 0
            for ecl in range(4):
                ec = dh * 4 + ecl
                sl4 = ecl
                p.wait("tensor", "vv", vb_sem[sl4])
                p.wait("tensor", "ss", sq_sem[sl4])
                if ec == 0:
                    p.wait("tensor", "vv", gate)
                p.op("tensor", lambda E, sl4=sl4, st=(ec == 0), sp=(ec == 7): E.matmul(
                    ps_s3[0][0:1, :T4], ones_bf[:], xb4_r[sl4][:], start=st, stop=sp))
                mmv = p.op("tensor", lambda E, sl4=sl4, st=(ec == 0), sp=(ec == 7): E.matmul(
                    ps_s3[1][0:1, :T4], ones_bf[:], xq4_r[sl4][:], start=st, stop=sp), inc="mm")
                xb4_free[sl4] = mmv
                xq4_free[sl4] = mmv
                if ec == 7:
                    st3[u] = mmv
            return mmv

        z2r = stage_head(0)
        ksr = kcm_only(0, z2r)
        rcm_sig_half(0, 0)
        p.wait("gpsimd", "mm", ph3_end_mm)
        for fc in range(32):
            p.op("gpsimd", lambda E, fc=fc: E.dma_start(
                out=wval_sb[fc][:], in_=wval4[fc].rearrange("p a b -> p (a b)")), inc=p.chan(f"wv{fc // 4}"))
        for u in range(NT4):
            mvcmA = vcm_pass(u, 0, ksr, vcm_free)
            if u >= 1:
                stats_mms(u - 1, 1)
                pending.extend(smalls3_chain(u - 1, st3[u - 1]))
            rcm_sig_half(u, 1)
            z2t_free_mm = p.cnt["mm"]
            combine_half(u, 0, mvcmA)
            s3gate = p.cnt["vv"]
            if u + 1 < NT4:
                z2r = stage_head(u + 1)
            mvcmB = vcm_pass(u, 1, ksr, vcmA_free)
            stats_mms(u, 0, s3gate)
            if u + 1 < NT4:
                ksr = kcm_only(u + 1, z2r)
                rcm_sig_half(u + 1, 0)
            combine_half(u, 1, mvcmB)
            rows3_done[u] = None
        stats_mms(NT4 - 1, 1)
        pending.extend(smalls3_chain(NT4 - 1, st3[NT4 - 1]))
        drain_all_padded()
        rows3_all_done = p.cnt["rw3"]
        for u in range(NT4):
            rows3_done[u] = rows3_all_done
        ph4.close()
        ph4w.close()
        barrier_all()

        ph5 = ExitStack()
        sb5 = lambda n_, s_, d_=F32: ph5.enter_context(nc.sbuf_tensor(n_, list(s_), d_))
        ps5 = lambda n_, s_: ph5.enter_context(nc.psum_tensor(n_, list(s_), F32))
        OUT = [sb5(f"OUT{ec}", (128, HS)) for ec in range(C)]
        y3h = [sb5(f"y3h{ec}", (128, HS), BF16) for ec in range(C)]
        KEY = [sb5(f"KEY{i}", (128, 4096), BF16) for i in range(4)]
        VAL = [sb5(f"VAL{i}", (128, 4096), BF16) for i in range(4)]
        GFM = sb5("GFM", (32, HS))
        BC3r = [sb5(f"BC3r{i}", (128, T5)) for i in range(2)]
        BC3m = [sb5(f"BC3m{i}", (128, T5)) for i in range(2)]
        tmp5 = [sb5(f"tmp5{i}", (128, T5)) for i in range(2)]
        Gb = [sb5(f"Gb{i}", (128, 32)) for i in range(4)]
        GCb = [sb5(f"GCb{i}", (128, 32)) for i in range(4)]
        Mb = [sb5(f"Mb{i}", (128, 1)) for i in range(4)]
        MKb = [sb5(f"MKb{i}", (128, 32)) for i in range(4)]
        GFb = [sb5(f"GFb{i}", (128, 32)) for i in range(4)]
        GB = [sb5(f"GB{i}", (128, T5B)) for i in range(4)]
        SC = [[[sb5(f"SC{par}_{el}_{fc}", (128, T5B), BF16) for fc in range(4)] for el in range(GE)] for par in range(2)]
        PS_sc = [ps5(f"PS_sc{i}", (128, 512)) for i in range(2)]
        PS_out = [ps5(f"PS_out{i}", (128, 512)) for i in range(4)]

        kv_done = [0] * NE
        kv_free = [0] * 4
        out_done = 0
        p.chan("outw")
        gb_free = [0] * 4
        sc_free = [[[0] * 4 for _ in range(GE)] for _ in range(2)]
        sc_ready = [[[0] * 4 for _ in range(GE)] for _ in range(2)]
        sc_bank_free = [0] * 2
        sc_bank_free_ss = [0] * 2
        gdr_done_prev = 0
        gbatch_free = [0] * 4
        tr_read = 0
        outadd_last = 0
        for h in range(2):
            hoff = h * HS
            outacc_da = 0
            for ec2 in range(2):
                p.wait("sync", f"x1w{ec2}", x1spill[ec2])
            for ec in range(C):
                if h == 1:
                    p.wait("sync", "outw", out_done)
                    p.wait("sync", "vv", outadd_last)
                outacc_da = p.op("sync", lambda E, ec=ec, hoff=hoff: E.dma_start(
                    out=OUT[ec][:], in_=x1sp[ec * 128:(ec + 1) * 128, hoff:hoff + HS]), inc=p.chan("outld"))
            bc3_free = [0] * 2
            y3rec = [[0] * C for _ in range(NT5)]
            for u in range(NT5):
                ug = (hoff + u * T5) // T4
                up = u % 2
                p.wait("gpsimd", "rw3", rows3_done[ug])
                p.wait("gpsimd", "vv", bc3_free[up])
                p.op("gpsimd", lambda E, up=up, hoff=hoff, u=u: E.dma_start(
                    out=BC3r[up][:], in_=rows[3][0:1, hoff + u * T5:hoff + (u + 1) * T5].broadcast_to([128, T5])), inc=p.chan(f"bc3{up}"))
                bc_da = p.op("gpsimd", lambda E, up=up, hoff=hoff, u=u: E.dma_start(
                    out=BC3m[up][:], in_=rows[3][1:2, hoff + u * T5:hoff + (u + 1) * T5].broadcast_to([128, T5])), inc=f"bc3{up}")
                for ec in range(C):
                    tb = ec % 2
                    p.wait("vector", "outld", outacc_da)
                    p.wait("vector", f"bc3{up}", bc_da)
                    p.op("vector", lambda E, ec=ec, u=u, up=up, tb=tb: E.tensor_tensor(
                        out=tmp5[tb][:], in0=OUT[ec][:, u * T5:(u + 1) * T5], in1=BC3r[up][:], op=ALU.mult))
                    y3rec[u][ec] = p.op("vector", lambda E, ec=ec, u=u, up=up, tb=tb: E.tensor_tensor(
                        out=y3h[ec][:, u * T5:(u + 1) * T5], in0=tmp5[tb][:], in1=BC3m[up][:], op=ALU.subtract), inc="vv")
                bc3_free[up] = y3rec[u][C - 1]
            gfm_last = 0
            for scb in range(0, NSC, 4):
                gsig = [0] * 4
                for j in range(4):
                    sc = scb + j
                    sp5 = sc % 2
                    u5 = (sc * 128) // T5
                    p.wait("tensor", "vv", y3rec[u5][C - 1])
                    p.wait("tensor", "preg", pre_dw)
                    p.wait("tensor", "vv", sc_bank_free[sp5])
                    p.wait("tensor", "ss", sc_bank_free_ss[sp5])
                    ms = 0
                    for kc in range(C):
                        ms = p.op("tensor", lambda E, sc=sc, kc=kc, sp5=sp5, st=(kc == 0), sp=(kc == 7): E.matmul(
                            PS_sc[sp5][:, 0:32], y3h[kc][:, sc * 128:(sc + 1) * 128],
                            selT_sb[:, kc * 32:(kc + 1) * 32], start=st, stop=sp),
                            inc="mm" if kc == 7 else None)
                    g4 = sc % 4
                    p.wait("scalar", "mm", ms)
                    p.wait("scalar", "vv", gbatch_free[g4])
                    gsig[j] = p.op("scalar", lambda E, sp5=sp5, g4=g4: E.activation(
                        out=Gb[g4][:], in_=PS_sc[sp5][:, 0:32], func=AF.Sigmoid, scale=1.0), inc="ss")
                    sc_bank_free_ss[sp5] = gsig[j]
                chains4 = []
                for j in range(4):
                    sc = scb + j
                    g4 = sc % 4
                    q = []
                    q.append(("vector", lambda E, g4=g4: E.tensor_copy(
                        out=GCb[g4][:], in_=Gb[g4][:]), "vv", [("ss", gsig[j])]))
                    for it in range(4):
                        q.append(("vector", lambda E, g4=g4: E.reduce_max(
                            out=Mb[g4][:], in_=GCb[g4][:], axis=AX.X), "vv", []))
                        if it < 3:
                            q.append(("vector", lambda E, g4=g4: E.tensor_tensor(
                                out=MKb[g4][:], in0=GCb[g4][:], in1=Mb[g4][:, 0:1].broadcast_to([128, 32]),
                                op=ALU.is_ge), "vv", []))
                            q.append(("vector", lambda E, g4=g4: E.scalar_tensor_tensor(
                                out=GCb[g4][:], in0=MKb[g4][:], scalar=-1e30, in1=GCb[g4][:],
                                op0=ALU.mult, op1=ALU.add), "vv", []))
                    q.append(("vector", lambda E, g4=g4: E.tensor_tensor(
                        out=MKb[g4][:], in0=Gb[g4][:], in1=Mb[g4][:, 0:1].broadcast_to([128, 32]),
                        op=ALU.is_ge), "vv", []))
                    q.append(("vector", lambda E, g4=g4: E.tensor_tensor(
                        out=GFb[g4][:], in0=Gb[g4][:], in1=MKb[g4][:], op=ALU.mult), "vv", []))
                    chains4.append(q)
                gf_rec = [0] * 4
                ncs5 = len(chains4[0])
                for si in range(ncs5):
                    for j in range(4):
                        eng, make, inc, waits = chains4[j][si]
                        for s_, v_ in waits:
                            p.wait(eng, s_, p.cnt[s_] if v_ is None else v_)
                        r_ = p.op(eng, make, inc=inc)
                        if si == ncs5 - 1:
                            gf_rec[j] = r_
                for j in range(4):
                    sc = scb + j
                    g4 = sc % 4
                    p.wait("tensor", "vv", gf_rec[j])
                    p.wait("tensor", "ss", tr_read)
                    p.wait("tensor", "vv", outadd_last)
                    mtr = p.op("tensor", lambda E, g4=g4: E.transpose(
                        out=PS_out[0][0:32, 0:128], in_=GFb[g4][:], identity=ident[:]), inc="mm")
                    p.wait("scalar", "mm", mtr)
                    p.wait("scalar", "gdr", gdr_done_prev)
                    tr_read = p.op("scalar", lambda E, sc=sc: E.activation(
                        out=GFM[0:32, sc * 128:(sc + 1) * 128], in_=PS_out[0][0:32, 0:128], func=AF.Copy), inc="ss")
                    gbatch_free[g4] = gf_rec[j]
                gfm_last = tr_read
            p.wait("sync", "ss", gfm_last)
            gdr_done = p.op("sync", lambda E, hoff=hoff: E.dma_start(
                out=gdr[0:32, hoff:hoff + HS], in_=GFM[:]), inc=p.chan("gdr"))
            gdr_done_prev = gdr_done
            seq = [(g, t) for g in range(NG) for t in range(NT5B)]

            def kv_load(g):
                for el in range(GE):
                    e = g * GE + el
                    slot = e % 4
                    p.wait("gpsimd", "mm", kv_free[slot])
                    p.op("gpsimd", lambda E, e=e, slot=slot: E.dma_start(
                        out=KEY[slot][:], in_=keysT[e]), inc=p.chan(f"kv{slot}"))
                    kv_done[e] = p.op("gpsimd", lambda E, e=e, slot=slot: E.dma_start(
                        out=VAL[slot][:], in_=valsT[e]), inc=f"kv{slot}")

            def front(i):
                g, t = seq[i]
                par = i % 2
                for el in range(GE):
                    e = g * GE + el
                    slot = e % 4
                    gbs = (i * GE + el) % 4
                    p.wait("gpsimd", "gdr", gdr_done)
                    p.wait("gpsimd", "vv", gb_free[gbs])
                    gb_da = p.op("gpsimd", lambda E, e=e, gbs=gbs, hoff=hoff, t=t: E.dma_start(
                        out=GB[gbs][:], in_=gdr[e:e + 1, hoff + t * T5B:hoff + (t + 1) * T5B].broadcast_to([128, T5B])), inc=p.chan(f"gb{gbs}"))
                    for fc in range(4):
                        fp = fc % 2
                        p.wait("tensor", f"kv{slot}", kv_done[e])
                        p.wait("tensor", "vv", y3rec[2 * t + 1][C - 1])
                        p.wait("tensor", "vv", sc_bank_free[fp])
                        p.wait("tensor", "ss", sc_bank_free_ss[fp])
                        mk = 0
                        for kc in range(C):
                            mk = p.op("tensor", lambda E, slot=slot, kc=kc, fc=fc, t=t, fp=fp,
                                      st=(kc == 0), sp=(kc == 7): E.matmul(
                                PS_sc[fp][:, 0:T5B],
                                KEY[slot][:, kc * 512 + fc * 128: kc * 512 + (fc + 1) * 128],
                                y3h[kc][:, t * T5B:(t + 1) * T5B], start=st, stop=sp),
                                inc="mm" if kc == 7 else None)
                        p.wait("vector", "mm", mk)
                        p.wait("vector", f"gb{gbs}", gb_da)
                        p.wait("vector", "mm", sc_free[par][el][fc])
                        vsc = p.op("vector", lambda E, par=par, el=el, fc=fc, fp=fp, gbs=gbs: E.scalar_tensor_tensor(
                            out=SC[par][el][fc][:], in0=PS_sc[fp][:, 0:T5B],
                            scalar=0.0, in1=GB[gbs][:], op0=ALU.max, op1=ALU.mult), inc="vv")
                        sc_bank_free[fp] = vsc
                        if fc == 3:
                            gb_free[gbs] = vsc
                        sc_ready[par][el][fc] = vsc
                if t == 1 and g + 1 < NG:
                    kv_load(g + 1)

            def values_pass(i, dh):
                nonlocal outadd_last, mvals_last
                g, t = seq[i]
                par = i % 2
                p.wait("tensor", "vv", sc_ready[par][GE - 1][3])
                p.wait("tensor", "vv", outadd_last)
                p.wait("tensor", "ss", gfm_last)
                mvals = 0
                for el in range(GE):
                    for fc in range(4):
                        for dcl in range(4):
                            dc = dh * 4 + dcl
                            mvals = p.op("tensor", lambda E, par=par, el=el, slot2=(g * GE + el) % 4,
                                         fc=fc, dc=dc, dcl=dcl,
                                         st=(el == 0 and fc == 0), sp=(el == GE - 1 and fc == 3): E.matmul(
                                PS_out[dcl][:, 0:T5B],
                                VAL[slot2][:, fc * 1024 + dc * 128: fc * 1024 + (dc + 1) * 128],
                                SC[par][el][fc][:], start=st, stop=sp), inc="mm" if dcl == 3 else None)
                p.wait("vector", "mm", mvals)
                for dcl in range(4):
                    dc = dh * 4 + dcl
                    outadd_last = p.op("vector", lambda E, dc=dc, dcl=dcl, t=t: E.tensor_tensor(
                        out=OUT[dc][:, t * T5B:(t + 1) * T5B], in0=OUT[dc][:, t * T5B:(t + 1) * T5B],
                        in1=PS_out[dcl][:, 0:T5B], op=ALU.add), inc="vv")
                mvals_last = mvals
                if dh == 1:
                    for el in range(GE):
                        for fc in range(4):
                            sc_free[par][el][fc] = mvals
                        kv_free[(g * GE + el) % 4] = mvals

            kv_load(0)
            mvals_last = 0
            front(0)
            for i in range(len(seq)):
                values_pass(i, 0)
                if i + 1 < len(seq):
                    front(i + 1)
                values_pass(i, 1)
            for ec in range(C):
                p.wait("sync", "vv", outadd_last)
                out_done = p.op("sync", lambda E, ec=ec, hoff=hoff: E.dma_start(
                    out=outT[ec * 128:(ec + 1) * 128, hoff:hoff + HS], in_=OUT[ec][:]), inc=p.chan("outw"))
        ph5.close()

        for s_name in list(p.cnt):
            p.wait("sync", s_name, p.cnt[s_name])

        @block.sync
        def _(E):
            p.replay("sync", E)

        @block.vector
        def _(E):
            p.replay("vector", E)

        @block.scalar
        def _(E):
            p.replay("scalar", E)

        @block.tensor
        def _(E):
            p.replay("tensor", E)

        @block.gpsimd
        def _(E):
            p.replay("gpsimd", E)

    es.close()
    return nc


def _host_layouts(inputs):
    f32 = np.float32
    A = lambda x: np.ascontiguousarray(np.asarray(x), dtype=f32)
    _ = inputs["L0_attn_time_first"]

    def t4(W):
        return np.ascontiguousarray(A(W).T.reshape(8, 128, 8, 128).transpose(2, 1, 0, 3))

    d = {}
    d["wr4"] = t4(inputs["L0_attn_Wr"])
    d["ww4"] = t4(inputs["L0_attn_Ww"])
    d["wk4"] = t4(inputs["L0_attn_Wk"])
    d["wv4"] = t4(inputs["L0_attn_Wv"])
    d["wo4"] = t4(inputs["L0_attn_Wo"])
    d["wrec4"] = t4(inputs["L0_attn_Wrec"])
    d["wkey4"] = np.ascontiguousarray(A(inputs["L0_attn_Wkey"]).T.reshape(8, 128, 32, 128).transpose(2, 1, 0, 3))
    d["wval4"] = np.ascontiguousarray(A(inputs["L0_attn_Wval"]).T.reshape(32, 128, 8, 128))
    d["selT3"] = np.ascontiguousarray(
        A(inputs["L0_ffn_expert_sel"]).T.reshape(8, 128, 32).transpose(1, 0, 2)).reshape(128, 256)
    d["keysT"] = np.ascontiguousarray(
        A(inputs["L0_ffn_keys"]).reshape(32, 8, 128, 512).transpose(0, 2, 1, 3)).reshape(32, 128, 4096)
    d["valsT"] = np.ascontiguousarray(
        A(inputs["L0_ffn_values"]).reshape(32, 4, 128, 1024).transpose(0, 2, 1, 3)).reshape(32, 128, 4096)
    for nm, key in (("g1v", "L0_attn_ln1_g"), ("b1v", "L0_attn_ln1_b"),
                    ("g2v", "L0_attn_ln2_g"), ("b2v", "L0_attn_ln2_b"),
                    ("tdv", "L0_attn_time_decay")):
        d[nm] = np.ascontiguousarray(A(inputs[key]).reshape(8, 128).T)
    d["onescol_bf"] = np.ones((128, 1), dtype=ml_dtypes.bfloat16)
    d["zcol_in"] = np.zeros((128, 1), dtype=f32)
    d["onecol_in"] = np.ones((128, 1), dtype=f32)
    d["onesrow_in"] = np.ones((1, 128), dtype=f32)
    d["epscol_in"] = np.full((128, 1), 1e-8, dtype=f32)
    d["ident_in"] = np.eye(128, dtype=f32)
    return d


def kernel(**inputs):
    nc = build_program()
    shared = _host_layouts(inputs)
    x = np.asarray(inputs["L0_attn_x"], dtype=np.float32)
    in_maps = []
    for b in range(8):
        m = dict(shared)
        m["xT"] = np.ascontiguousarray(x[b].T)
        in_maps.append(m)
    res = run_bass_kernel_spmd(nc, in_maps, core_ids=list(range(8)))
    out = np.stack([np.asarray(r["outT"]).T for r in res.results], axis=0)
    return np.ascontiguousarray(out, dtype=np.float32)


# version 25
# speedup vs baseline: 1.1242x; 1.0112x over previous
import numpy as np
import ml_dtypes
import concourse.bass as bass
from concourse import dve_ops
import concourse.mybir as mybir
from concourse.bass_utils import run_bass_kernel_spmd
from contextlib import ExitStack

F32 = mybir.dt.float32
BF16 = mybir.dt.bfloat16
AF = mybir.ActivationFunctionType
ALU = mybir.AluOpType
AX = mybir.AxisListType

D = 1024
S = 4096
C = 8
T1 = 512
NT1 = S // T1
T4 = 256
NT4 = S // T4
HS = S // 2
T5 = 256
NT5 = HS // T5
T5B = 512
NT5B = HS // T5B
NSC = HS // 128
NE = 32
GE = 2
NG = NE // GE
LN_EPS = 1e-5
WKV_EPS = 1e-8

ENGINES = ("sync", "vector", "scalar", "tensor", "gpsimd")


class P:
    def __init__(self, nc, es, selfsync=False):
        self.nc = nc
        self.es = es
        self.sems = {}
        self.cnt = {}
        self.q = {e: [] for e in ENGINES}
        self.lastw = {}
        self.selfsync = selfsync
        for s_name in ("mm", "vv", "ss", "pp"):
            self.chan(s_name)
        self.lastmark = {}
        if selfsync:
            for e in ENGINES:
                self.chan(f"ord_{e}")

    def chan(self, name):
        if name in self.sems:
            return name
        self.sems[name] = self.es.enter_context(self.nc.semaphore(name))
        self.cnt[name] = 0
        for e in ENGINES:
            self.lastw[(e, name)] = 0
        return name

    def wait(self, eng, sem, val):
        if not val:
            return
        assert val <= self.cnt[sem], f"wait on future: {eng} {sem} {val} > {self.cnt[sem]}"
        if self.lastw[(eng, sem)] >= val:
            return
        self.lastw[(eng, sem)] = val
        h = self.sems[sem]
        self.q[eng].append(lambda E, h=h, v=val: E.wait_ge(h, v))

    def op(self, eng, make, inc=None):
        if self.selfsync:
            pm = self.lastmark.get(eng)
            if pm is not None:
                self.q[eng].append(lambda E, h=self.sems[pm[0]], v=pm[1]: E.wait_ge(h, v))
        if inc is None:
            if self.selfsync:
                ordname = f"ord_{eng}"
                self.cnt[ordname] += 1
                self.lastmark[eng] = (ordname, self.cnt[ordname])

                def f0(E, make=make, h=self.sems[ordname], n2=1):
                    inst = make(E)
                    if inst is not None:
                        inst.then_inc(h, n2)
                self.q[eng].append(f0)
            else:
                self.q[eng].append(make)
            return None
        n = 1 if inc in ("mm", "vv", "ss", "pp") else 16
        h = self.sems[inc]
        self.cnt[inc] += n
        if self.selfsync:
            self.lastmark[eng] = (inc, self.cnt[inc])

        def f(E, make=make, h=h, n=n):
            make(E).then_inc(h, n)
        self.q[eng].append(f)
        return self.cnt[inc]

    def replay(self, eng, E):
        for f in self.q[eng]:
            f(E)


def build_program(debug=False, selfsync=False):
    nc = bass.Bass("TRN2", target_bir_lowering=False, debug=False)

    dt_in = lambda name, shape: nc.dram_tensor(name, list(shape), F32, kind="ExternalInput").ap()
    xT = dt_in("xT", (D, S))
    w4 = {nm: dt_in(nm, (C, 128, C, 128)) for nm in ("wr4", "ww4", "wk4", "wv4", "wo4", "wrec4")}
    wkey4 = dt_in("wkey4", (32, 128, 8, 128))
    wval4 = dt_in("wval4", (32, 128, 8, 128))
    selT3 = dt_in("selT3", (128, 256))
    keysT = dt_in("keysT", (NE, 128, 4096))
    valsT = dt_in("valsT", (NE, 128, 4096))
    g1v = dt_in("g1v", (128, C)); b1v = dt_in("b1v", (128, C))
    g2v = dt_in("g2v", (128, C)); b2v = dt_in("b2v", (128, C))
    tdv = dt_in("tdv", (128, C))
    onescol_bf = nc.dram_tensor("onescol_bf", [128, 1], BF16, kind="ExternalInput").ap()
    zcol_in = dt_in("zcol_in", (128, 1))
    onecol_in = dt_in("onecol_in", (128, 1))
    onesrow_in = dt_in("onesrow_in", (1, 128))
    epscol_in = dt_in("epscol_in", (128, 1))
    ident_in = dt_in("ident_in", (128, 128))

    kind_sp = "ExternalOutput" if debug else "Internal"
    y0sp = nc.dram_tensor("y0sp", [D, S], F32, kind=kind_sp).ap()
    y1sp = nc.dram_tensor("y1sp", [D, S], F32, kind=kind_sp).ap()
    x1sp = nc.dram_tensor("x1sp", [D, S], F32, kind=kind_sp).ap()
    gdr = nc.dram_tensor("gdr", [32, S], F32, kind=kind_sp).ap()
    rows = [nc.dram_tensor(f"rows{i}", [2, S], F32, kind=kind_sp).ap() for i in range(4)]
    if debug:
        z1dump = nc.dram_tensor("z1dump", [D, S], F32, kind="ExternalOutput").ap()
        rwkvdump = nc.dram_tensor("rwkvdump", [D, S], F32, kind="ExternalOutput").ap()
    outT = nc.dram_tensor("outT", [D, S], F32, kind="ExternalOutput").ap()

    es = ExitStack()
    sb = lambda name, shape, dt=F32: es.enter_context(nc.sbuf_tensor(name, list(shape), dt))
    p = P(nc, es, selfsync=selfsync)

    g1s = sb("g1s", (128, C)); b1s = sb("b1s", (128, C))
    g2s = sb("g2s", (128, C)); b2s = sb("b2s", (128, C))
    tds = sb("tds", (128, C)); decays = sb("decays", (128, C))
    ones_bf = sb("ones_bf", (128, 1), BF16)
    zcol = sb("zcol", (128, 1))
    onecol = sb("onecol", (128, 1))
    onesrow = sb("onesrow", (1, 128))
    epscol = sb("epscol", (128, 1))
    ident = sb("ident", (128, 128))
    selT_sb = sb("selT_sb", (128, 256), BF16)

    padA = sb("padA", (128, 64))
    padB = sb("padB", (128, 64))
    z1_es = ExitStack()
    z1 = [z1_es.enter_context(nc.sbuf_tensor(f"z1_{c}", [128, S], BF16)) for c in range(C)]

    pending = []

    def drain_one():
        if not pending:
            return
        eng, make, inc, waits = pending.pop(0)
        for s_, v_ in waits:
            p.wait(eng, s_, p.cnt[s_] if v_ is None else v_)
        p.op(eng, make, inc=inc)

    def barrier_all():
        bar = dict(p.cnt)
        for eng in ENGINES:
            for s_ in bar:
                p.wait(eng, s_, bar[s_])

    def drain_all_padded():
        while pending:
            drain_one()
            for _ in range(2):
                p.op("vector", lambda E: E.tensor_copy(out=padA[:], in_=padB[:]))
                p.op("vector", lambda E: E.tensor_copy(out=padB[:], in_=padA[:]))

    def ln_smalls_chain(arena, psA, psB, rows_dram, rowch, col0, width, extra_first_waits):
        A = arena[0:1, 0:width]
        B = arena[0:1, width:2 * width]
        Cc = arena[0:1, 2 * width:3 * width]
        steps = []
        steps.append(("vector", lambda E: E.tensor_single_scalar(
            out=A, in_=psA[0:1, :width], scalar=1.0 / D, op=ALU.mult), "vv",
            list(extra_first_waits)))
        steps.append(("vector", lambda E: E.tensor_single_scalar(
            out=B, in_=psB[0:1, :width], scalar=1.0 / D, op=ALU.mult), "vv", []))
        steps.append(("vector", lambda E: E.tensor_tensor(
            out=Cc, in0=A, in1=A, op=ALU.mult), "vv", []))
        steps.append(("vector", lambda E: E.tensor_tensor(
            out=B, in0=B, in1=Cc, op=ALU.subtract), "vv", []))
        steps.append(("vector", lambda E: E.tensor_single_scalar(
            out=B, in_=B, scalar=LN_EPS, op=ALU.add), "vv", []))
        steps.append(("scalar", lambda E: E.activation(
            out=Cc, in_=B, func=AF.Ln), "ss", [("vv", None)]))
        steps.append(("scalar", lambda E: E.activation(
            out=B, in_=Cc, func=AF.Exp, scale=-0.5), "ss", []))
        steps.append(("vector", lambda E: E.tensor_tensor(
            out=A, in0=A, in1=B, op=ALU.mult), "vv", [("ss", None)]))
        if rows_dram is not None:
            steps.append(("scalar", lambda E: E.dma_start(
                out=rows_dram[0:1, col0:col0 + width], in_=B), rowch, [("vv", None)]))
            steps.append(("scalar", lambda E: E.dma_start(
                out=rows_dram[1:2, col0:col0 + width], in_=A), rowch, [("vv", None)]))
        return steps

    with nc.Block() as block:
        for dst, src in ((g1s, g1v), (b1s, b1v), (g2s, g2v), (b2s, b2v), (tds, tdv),
                         (zcol, zcol_in), (onecol, onecol_in), (epscol, epscol_in),
                         (onesrow, onesrow_in), (ident, ident_in)):
            pre_da = p.op("sync", lambda E, d=dst, s=src: E.dma_start(out=d[:], in_=s[:]), inc=p.chan("pre"))
        p.op("gpsimd", lambda E: E.dma_start(out=ones_bf[:], in_=onescol_bf[:]), inc=p.chan("preg"))
        pre_dw = p.op("gpsimd", lambda E: E.dma_start(out=selT_sb[:], in_=selT3[:]), inc="preg")
        p.wait("scalar", "pre", pre_da)
        decays_done = p.op("scalar", lambda E: E.activation(out=decays[:], in_=tds[:], func=AF.Exp), inc="ss")
        p.op("vector", lambda E: E.memset(padA[:], 0.0))
        p.op("vector", lambda E: E.memset(padB[:], 0.0))

        ph1 = ExitStack()
        sb1 = lambda n_, s_, d_=F32: ph1.enter_context(nc.sbuf_tensor(n_, list(s_), d_))
        ps1 = lambda n_, s_: ph1.enter_context(nc.psum_tensor(n_, list(s_), F32))
        XC = [[sb1(f"XC{r}_{c}", (128, T1)) for c in range(C)] for r in range(2)]
        Y0C = [[sb1(f"Y0C{r}_{c}", (128, T1)) for c in range(C)] for r in range(2)]
        xb_r = [sb1(f"xb{i}", (128, T1), BF16) for i in range(2)]
        xq_r = [sb1(f"xq{i}", (128, T1), BF16) for i in range(2)]
        tmp_r = [sb1(f"tmp{i}", (128, T1)) for i in range(2)]
        rt1 = [sb1(f"rt1_{i}", (1, 3 * T1)) for i in range(4)]
        ps_st0 = [ps1(f"ps_st0{i}", (128, T1)) for i in range(2)]
        ps_st1 = [ps1(f"ps_st1{i}", (128, T1)) for i in range(2)]
        ps_bc0 = [ps1(f"ps_bc0{i}", (128, T1)) for i in range(2)]
        ps_bc1 = [ps1(f"ps_bc1{i}", (128, T1)) for i in range(2)]

        for r in range(2):
            for c in range(C):
                p.chan(f"xc{r}_{c}")
                p.chan(f"y0w{c}")
        xc_free = [[0] * C for _ in range(2)]
        y0c_spill = [[("y0w0", 0)] * C for _ in range(2)]
        y0c_sq = [[0] * C for _ in range(2)]
        xb_free = [0] * 2
        xq_free = [0] * 2
        st0_read = 0
        tmp_ss = [0, 0]
        st1_read = 0
        bc0_read = 0
        bc1_read = 0
        st0_rec = [0] * NT1
        st1_rec = [0] * NT1
        ch0_rec = [0] * NT1
        ch1_rec = [0] * NT1
        bc0_mm = [0] * NT1
        bc1_mm = [0] * NT1
        y0sp_done = [[0] * C for _ in range(NT1)]
        z1_done_t = [0] * NT1
        p.wait("tensor", "preg", pre_dw)
        p.wait("tensor", "pre", pre_da)

        def stA(t):
            r = t % 2
            for c in range(C):
                p.wait("sync", "vv", xc_free[r][c])
                p.op("sync", lambda E, t=t, c=c, r=r: E.dma_start(
                    out=XC[r][c][:], in_=xT[c * 128:(c + 1) * 128, t * T1:(t + 1) * T1]), inc=f"xc{r}_{c}")

        def stB(t):
            r = t % 2
            for c in range(C):
                b = c % 2
                p.wait("vector", f"xc{r}_{c}", p.cnt[f"xc{r}_{c}"])
                p.wait("vector", "mm", xb_free[b])
                vb = p.op("vector", lambda E, b=b, r=r, c=c: E.tensor_copy(out=xb_r[b][:], in_=XC[r][c][:]), inc="vv")
                p.wait("scalar", f"xc{r}_{c}", p.cnt[f"xc{r}_{c}"])
                p.wait("scalar", "mm", xq_free[b])
                sq = p.op("scalar", lambda E, b=b, r=r, c=c: E.activation(
                    out=xq_r[b][:], in_=XC[r][c][:], func=AF.Square), inc="ss")
                p.wait("tensor", "vv", vb)
                p.wait("tensor", "ss", sq)
                if c == 0:
                    p.wait("tensor", "vv", st0_read)
                p.op("tensor", lambda E, b=b, st=(c == 0), sp=(c == 7): E.matmul(
                    ps_st0[0][0:1, :], ones_bf[:], xb_r[b][:], start=st, stop=sp))
                mmv = p.op("tensor", lambda E, b=b, st=(c == 0), sp=(c == 7): E.matmul(
                    ps_st0[1][0:1, :], ones_bf[:], xq_r[b][:], start=st, stop=sp), inc="mm")
                xb_free[b] = mmv
                xq_free[b] = mmv
                drain_one()
            st0_rec[t] = mmv
            pending.extend(ln_smalls_chain(rt1[t % 4], ps_st0[0], ps_st0[1], None, None, 0, T1,
                                           [("mm", st0_rec[t])]))

        def stD(t):
            nonlocal st0_read
            ch0_rec[t] = p.cnt["vv"]
            st0_read = ch0_rec[t]
            A_ = rt1[t % 4][0:1, 0:T1]
            B_ = rt1[t % 4][0:1, T1:2 * T1]
            p.wait("tensor", "vv", ch0_rec[t])
            p.wait("tensor", "vv", bc0_read)
            p.op("tensor", lambda E, B_=B_: E.matmul(ps_bc0[0][:], onesrow[:], B_, start=True, stop=True))
            bc0_mm[t] = p.op("tensor", lambda E, A_=A_: E.matmul(
                ps_bc0[1][:], onesrow[:], A_, start=True, stop=True), inc="mm")

        def stE(t):
            nonlocal bc0_read
            r = t % 2
            for c in range(C):
                b = c % 2
                tb = c % 2
                p.wait("vector", "mm", bc0_mm[t])
                wch, wval = y0c_spill[r][c]
                p.wait("vector", wch, wval)
                p.wait("vector", "ss", y0c_sq[r][c])
                p.wait("vector", "ss", tmp_ss[tb])
                p.op("vector", lambda E, r=r, c=c, tb=tb: E.tensor_tensor(
                    out=tmp_r[tb][:], in0=XC[r][c][:], in1=ps_bc0[0][:], op=ALU.mult))
                vy0 = p.op("vector", lambda E, r=r, c=c, tb=tb: E.tensor_tensor(
                    out=Y0C[r][c][:], in0=tmp_r[tb][:], in1=ps_bc0[1][:], op=ALU.subtract), inc="vv")
                xc_free[r][c] = vy0
                p.wait("sync", "vv", vy0)
                y0da = p.op("sync", lambda E, t=t, c=c, r=r: E.dma_start(
                    out=y0sp[c * 128:(c + 1) * 128, t * T1:(t + 1) * T1], in_=Y0C[r][c][:]), inc=f"y0w{c}")
                y0sp_done[t][c] = (f"y0w{c}", y0da)
                y0c_spill[r][c] = (f"y0w{c}", y0da)
                p.wait("vector", "mm", xb_free[b])
                vb = p.op("vector", lambda E, b=b, r=r, c=c: E.tensor_copy(out=xb_r[b][:], in_=Y0C[r][c][:]), inc="vv")
                p.wait("scalar", "vv", vy0)
                p.wait("scalar", "mm", xq_free[b])
                sq = p.op("scalar", lambda E, b=b, r=r, c=c: E.activation(
                    out=xq_r[b][:], in_=Y0C[r][c][:], func=AF.Square), inc="ss")
                y0c_sq[r][c] = sq
                p.wait("tensor", "vv", vb)
                p.wait("tensor", "ss", sq)
                if c == 0:
                    p.wait("tensor", "vv", st1_read)
                p.op("tensor", lambda E, b=b, st=(c == 0), sp=(c == 7): E.matmul(
                    ps_st1[0][0:1, :], ones_bf[:], xb_r[b][:], start=st, stop=sp))
                mmv = p.op("tensor", lambda E, b=b, st=(c == 0), sp=(c == 7): E.matmul(
                    ps_st1[1][0:1, :], ones_bf[:], xq_r[b][:], start=st, stop=sp), inc="mm")
                xb_free[b] = mmv
                xq_free[b] = mmv
                drain_one()
            bc0_read = p.cnt["vv"]
            st1_rec[t] = mmv
            pending.extend(ln_smalls_chain(rt1[2 + t % 2] if False else rt1[(t % 4)], ps_st1[0], ps_st1[1], None, None, 0, T1,
                                           [("mm", st1_rec[t])]))

        def stG(t):
            nonlocal st1_read
            ch1_rec[t] = p.cnt["vv"]
            st1_read = ch1_rec[t]
            A_ = rt1[t % 4][0:1, 0:T1]
            B_ = rt1[t % 4][0:1, T1:2 * T1]
            p.wait("tensor", "vv", ch1_rec[t])
            p.wait("tensor", "vv", bc1_read)
            p.op("tensor", lambda E, B_=B_: E.matmul(ps_bc1[0][:], onesrow[:], B_, start=True, stop=True))
            bc1_mm[t] = p.op("tensor", lambda E, A_=A_: E.matmul(
                ps_bc1[1][:], onesrow[:], A_, start=True, stop=True), inc="mm")

        def stH(t):
            nonlocal bc1_read
            r = t % 2
            for c in range(C):
                tb = c % 2
                p.wait("vector", "mm", bc1_mm[t])
                p.wait("vector", "ss", tmp_ss[tb])
                p.op("vector", lambda E, r=r, c=c, tb=tb: E.tensor_tensor(
                    out=tmp_r[tb][:], in0=Y0C[r][c][:], in1=ps_bc1[0][:], op=ALU.mult))
                vt2 = p.op("vector", lambda E, tb=tb: E.tensor_tensor(
                    out=tmp_r[tb][:], in0=tmp_r[tb][:], in1=ps_bc1[1][:], op=ALU.subtract), inc="vv")
                p.wait("scalar", "vv", vt2)
                sz = p.op("scalar", lambda E, t=t, c=c, tb=tb: E.activation(
                    out=z1[c][:, t * T1:(t + 1) * T1], in_=tmp_r[tb][:], func=AF.Identity,
                    scale=g1s[:, c:c + 1], bias=b1s[:, c:c + 1]), inc="ss")
                tmp_ss[tb] = sz
                drain_one()
            z1_done_t[t] = p.cnt["ss"]
            bc1_read = p.cnt["vv"]

        for s in range(NT1 + 1):
            if s < NT1:
                stA(s)
                stB(s)
                if s == 0:
                    drain_all_padded()
            if s >= 1:
                stG(s - 1)
                stH(s - 1)
            if s < NT1:
                stD(s)
                stE(s)
        drain_all_padded()
        ph1.close()
        barrier_all()

        rwkv_es = ExitStack()
        rwkv = [rwkv_es.enter_context(nc.sbuf_tensor(f"rwkv{c}", [128, S], BF16, side="right")) for c in range(C)]
        ph2 = ExitStack()
        sb2 = lambda n_, s_, d_=F32: ph2.enter_context(nc.sbuf_tensor(n_, list(s_), d_))
        ps2 = lambda n_, s_: ph2.enter_context(nc.psum_tensor(n_, list(s_), F32))
        wnames = ("wr", "ww", "wk", "wv")
        wproj = {nm: [sb2(f"{nm}_{i}", (128, 1024), BF16) for i in range(2)] for nm in wnames}
        EWKD = sb2("EWKD", (128, T1 + 1))
        NUMD = sb2("NUMD", (128, T1 + 1))
        NUMO = sb2("NUMO", (128, T1 + 1))
        DENO = sb2("DENO", (128, T1 + 1))
        Dscan = sb2("Dscan", (128, T1 + 1))
        EWb = [sb2(f"EWb{i}", (128, T1)) for i in range(2)]
        Ub = [sb2(f"Ub{i}", (128, T1)) for i in range(2)]
        RREb = [sb2(f"RREb{i}", (128, T1)) for i in range(2)]
        Rb = [sb2(f"Rb{i}", (128, T1)) for i in range(2)]
        W1 = sb2("W1", (128, T1))
        W2 = sb2("W2", (128, T1))
        PSp = {nm: [ps2(f"PS{nm}{i}", (128, T1)) for i in range(2)] for nm in ("w", "k", "r", "v")}

        proj_src = {"wr": w4["wr4"], "ww": w4["ww4"], "wk": w4["wk4"], "wv": w4["wv4"]}
        wproj_done = [0] * C
        wslot_free = [0] * 2

        p.chan("wp0"); p.chan("wp1")

        def load_proj(ec):
            slot = ec % 2
            p.wait("gpsimd", "mm", wslot_free[slot])
            v = 0
            for nm in wnames:
                v = p.op("gpsimd", lambda E, nm=nm, ec=ec, s=slot: E.dma_start(
                    out=wproj[nm][s][:], in_=proj_src[nm][ec].rearrange("p a b -> p (a b)")), inc=f"wp{slot}")
            wproj_done[ec] = v

        load_proj(0)
        load_proj(1)

        p.wait("vector", "pre", pre_da)
        p.wait("vector", "ss", decays_done)
        bank_free_v = {}
        bank_free_s = {}
        ewkd_read = 0
        wkv_prev = 0
        wkv_done = 0
        carry_rd = 0
        vRWKV = [[0] * NT1 for _ in range(C)]
        for ec in range(C):
            p.wait("vector", "vv", ewkd_read)
            p.op("vector", lambda E, ec=ec: E.tensor_copy(
                out=Dscan[:, 1:T1 + 1], in_=decays[:, ec:ec + 1].broadcast_to([128, T1])))
            p.op("vector", lambda E: E.tensor_copy(out=Dscan[:, 0:1], in_=zcol[:]))
            p.op("vector", lambda E: E.tensor_copy(out=NUMD[:, 0:1], in_=zcol[:]))
            dscan_v = ewkd_carry = p.op("vector", lambda E: E.tensor_copy(out=EWKD[:, 0:1], in_=zcol[:]), inc="vv")
            if ec + 1 < C:
                load_proj(ec + 1)
            for t in range(NT1):
                tp = t % 2
                prev = (ec, t - 2) if t >= 2 else ((ec - 1, NT1 - 2 + t) if ec > 0 else None)
                p.wait("tensor", f"wp{ec % 2}", wproj_done[ec])
                p.wait("tensor", "ss", z1_done_t[t])
                if prev is not None:
                    p.wait("tensor", "vv", bank_free_v[prev])
                    p.wait("tensor", "ss", bank_free_s[prev])
                mg = {}
                for nm, wsb in (("w", "ww"), ("k", "wk"), ("r", "wr"), ("v", "wv")):
                    for kc in range(C):
                        p.op("tensor", lambda E, nm=nm, wsb=wsb, kc=kc, t=t, tp=tp, ec=ec,
                             st=(kc == 0), sp=(kc == 7): E.matmul(
                            PSp[nm][tp][:], wproj[wsb][ec % 2][:, kc * 128:(kc + 1) * 128],
                            z1[kc][:, t * T1:(t + 1) * T1], start=st, stop=sp),
                            inc="mm" if kc == 7 else None)
                    mg[nm] = p.cnt["mm"]
                p.wait("scalar", "mm", mg["w"])
                sEW = p.op("scalar", lambda E, tp=tp: E.activation(
                    out=EWb[tp][:], in_=PSp["w"][tp][:], func=AF.Exp), inc="ss")
                p.wait("vector", "mm", mg["k"])
                p.wait("vector", "ss", sEW)
                vU = p.op("vector", lambda E, tp=tp: E.tensor_tensor(
                    out=Ub[tp][:], in0=PSp["k"][tp][:], in1=EWb[tp][:], op=ALU.subtract), inc="vv")
                p.wait("scalar", "vv", vU)
                p.wait("scalar", "vv", ewkd_read)
                p.wait("scalar", "vv", ewkd_carry)
                sEWK = p.op("scalar", lambda E, tp=tp: E.activation(
                    out=EWKD[:, 1:T1 + 1], in_=Ub[tp][:], func=AF.Exp), inc="ss")
                p.wait("scalar", "mm", mg["r"])
                sRRE = p.op("scalar", lambda E, tp=tp: E.activation(
                    out=RREb[tp][:], in_=PSp["r"][tp][:], func=AF.Tanh, scale=0.5), inc="ss")
                p.wait("vector", "ss", sRRE)
                p.op("vector", lambda E, tp=tp: E.tensor_scalar(
                    out=Rb[tp][:], in0=RREb[tp][:], scalar1=0.5, scalar2=0.5, op0=ALU.mult, op1=ALU.add))
                p.wait("vector", "ss", sEWK)
                p.wait("vector", "mm", mg["v"])
                vN = p.op("vector", lambda E, tp=tp: E.tensor_tensor(
                    out=NUMD[:, 1:T1 + 1], in0=EWKD[:, 1:T1 + 1], in1=PSp["v"][tp][:], op=ALU.mult), inc="vv")
                p.op("vector", lambda E: E.tensor_tensor_scan(
                    out=NUMO[:], data0=Dscan[:], data1=NUMD[:], initial=0.0, op0=ALU.mult, op1=ALU.add))
                vden = p.op("vector", lambda E: E.tensor_tensor_scan(
                    out=DENO[:], data0=Dscan[:], data1=EWKD[:], initial=0.0, op0=ALU.mult, op1=ALU.add), inc="vv")
                ewkd_read = vden
                p.op("vector", lambda E: E.tensor_single_scalar(
                    out=W1[:], in_=DENO[:, 1:T1 + 1], scalar=WKV_EPS, op=ALU.add))
                p.op("vector", lambda E: E.reciprocal(out=W2[:], in_=W1[:]))
                wkv_done = p.op("vector", lambda E: E.tensor_tensor(
                    out=W1[:], in0=NUMO[:, 1:T1 + 1], in1=W2[:], op=ALU.mult), inc="vv")
                vR = p.op("vector", lambda E, ec=ec, t=t, tp=tp: E.tensor_tensor(
                    out=rwkv[ec][:, t * T1:(t + 1) * T1], in0=Rb[tp][:], in1=W1[:], op=ALU.mult), inc="vv")
                vRWKV[ec][t] = ("vv", vR)
                if t < NT1 - 1:
                    p.op("vector", lambda E: E.tensor_copy(out=NUMD[:, 0:1], in_=NUMO[:, T1:T1 + 1]))
                    carry_rd = ewkd_carry = p.op("vector", lambda E: E.tensor_copy(
                        out=EWKD[:, 0:1], in_=DENO[:, T1:T1 + 1]), inc="vv")
                bank_free_v[(ec, t)] = vN
                bank_free_s[(ec, t)] = sRRE
            wslot_free[ec % 2] = p.cnt["mm"]
        if debug:
            for c in range(C):
                p.wait("gpsimd", vRWKV[c][NT1 - 1][0], vRWKV[c][NT1 - 1][1])
                p.op("gpsimd", lambda E, c=c: E.dma_start(
                    out=rwkvdump[c * 128:(c + 1) * 128, :], in_=rwkv[c][:]), inc=p.chan("dbg"))
        ph2_end_mm = p.cnt["mm"]
        ph2.close()
        z1_es.close()
        ph4w = ExitStack()
        wkey_sb = [ph4w.enter_context(nc.sbuf_tensor(f"wkeysb{fc}", [128, 1024], BF16)) for fc in range(32)]
        barrier_all()
        wkey_done = 0
        p.chan("wk"); p.chan("wv"); p.chan("wr")
        p.wait("gpsimd", "mm", ph2_end_mm)
        for fc in range(32):
            wkey_done = p.op("gpsimd", lambda E, fc=fc: E.dma_start(
                out=wkey_sb[fc][:], in_=wkey4[fc].rearrange("p a b -> p (a b)")), inc="wk")

        ph3 = ExitStack()
        sb3 = lambda n_, s_, d_=F32: ph3.enter_context(nc.sbuf_tensor(n_, list(s_), d_, side="right"))
        ps3 = lambda n_, s_: ph3.enter_context(nc.psum_tensor(n_, list(s_), F32))
        wo_sb = [sb3(f"wo_sb{e}", (128, 1024), BF16) for e in range(C)]
        PSo = [ps3(f"PSo{i}", (128, T1)) for i in range(4)]
        ps_s2 = [ps3(f"ps_s2{i}", (128, T1)) for i in range(4)]
        rt3 = [sb3(f"rt3_{i}", (1, 3 * T1)) for i in range(2)]
        y0l_r = [sb3(f"y0l{i}", (128, T1)) for i in range(4)]
        y1b_r = [sb3(f"y1b{i}", (128, T1)) for i in range(3)]
        xb3_r = [sb3(f"xb3{i}", (128, T1), BF16) for i in range(8)]
        xq3_r = [sb3(f"xq3{i}", (128, T1), BF16) for i in range(8)]

        p.wait("gpsimd", "mm", ph2_end_mm)
        wo_done = 0
        p.chan("wo")
        for e in range(C):
            wo_done = p.op("gpsimd", lambda E, e=e: E.dma_start(
                out=wo_sb[e][:], in_=w4["wo4"][e].rearrange("p a b -> p (a b)")), inc="wo")

        y0l_free3 = [(None, 0)] * 4
        y1b_cons = [(0, 0)] * 3
        xb3_free = [0] * 8
        xq3_free = [0] * 8
        vb_sem3 = [0] * 8
        sq_sem3 = [0] * 8
        pso_free = [0] * 4
        s2_bank_free = [0] * 2
        y1sp_done = [[0] * C for _ in range(NT1)]
        st2 = [0] * NT1

        def stats_mms3(i):
            tq, cq = i // 8, i % 8
            tpar = tq % 2
            p.wait("tensor", "vv", vb_sem3[cq])
            p.wait("tensor", "ss", sq_sem3[cq])
            if cq == 0:
                p.wait("tensor", "vv", s2_bank_free[tpar])
            p.op("tensor", lambda E, cq=cq, tpar=tpar, st=(cq == 0), sp=(cq == 7): E.matmul(
                ps_s2[2 * tpar][0:1, :], ones_bf[:], xb3_r[cq][:], start=st, stop=sp))
            mmv = p.op("tensor", lambda E, cq=cq, tpar=tpar, st=(cq == 0), sp=(cq == 7): E.matmul(
                ps_s2[2 * tpar + 1][0:1, :], ones_bf[:], xq3_r[cq][:], start=st, stop=sp), inc="mm")
            xb3_free[cq] = mmv
            xq3_free[cq] = mmv
            if cq == 7:
                st2[tq] = mmv
                pending.extend(ln_smalls_chain(rt3[tq % 2], ps_s2[2 * (tq % 2)], ps_s2[2 * (tq % 2) + 1],
                                               rows[2], p.chan("rw2"), tq * T1, T1,
                                               [("mm", st2[tq]), ("rw2", None)]))

        for t in range(NT1):
            for c in range(C):
                po = c % 4
                i = t * C + c
                slot = i % 4
                sem, val = y0l_free3[slot]
                if sem is not None:
                    p.wait("sync", sem, val)
                wch3, wv3 = y0sp_done[t][c]
                p.wait("sync", wch3, wv3)
                da = p.op("sync", lambda E, t=t, c=c, s=slot: E.dma_start(
                    out=y0l_r[s][:], in_=y0sp[c * 128:(c + 1) * 128, t * T1:(t + 1) * T1]), inc=p.chan(f"yl{slot}"))
                rch, rval = vRWKV[C - 1][t]
                p.wait("tensor", rch, rval)
                p.wait("tensor", "wo", wo_done)
                p.wait("tensor", "vv", pso_free[po])
                mo = 0
                for kc in range(C):
                    mo = p.op("tensor", lambda E, c=c, kc=kc, t=t, po=po, st=(kc == 0), sp=(kc == 7): E.matmul(
                        PSo[po][:], wo_sb[c][:, kc * 128:(kc + 1) * 128],
                        rwkv[kc][:, t * T1:(t + 1) * T1], start=st, stop=sp),
                        inc="mm" if kc == 7 else None)
                if i >= 2:
                    stats_mms3(i - 2)
                yb = i % 3
                ss1, da1 = y1b_cons[yb]
                p.wait("vector", "mm", mo)
                p.wait("vector", f"yl{slot}", da)
                p.wait("vector", "ss", ss1)
                p.wait("vector", f"y1w{yb}", da1)
                vy1 = p.op("vector", lambda E, po=po, s=slot, y=yb: E.tensor_tensor(
                    out=y1b_r[y][:], in0=PSo[po][:], in1=y0l_r[s][:], op=ALU.add), inc="vv")
                pso_free[po] = vy1
                y0l_free3[slot] = ("vv", vy1)
                drain_one()
                p.wait("sync", "vv", vy1)
                y1da = p.op("sync", lambda E, t=t, c=c, y=yb: E.dma_start(
                    out=y1sp[c * 128:(c + 1) * 128, t * T1:(t + 1) * T1], in_=y1b_r[y][:]), inc=p.chan(f"y1w{yb}"))
                y1sp_done[t][c] = (f"y1w{yb}", y1da)
                p.wait("vector", "mm", xb3_free[c])
                vb = p.op("vector", lambda E, c=c, y=yb: E.tensor_copy(out=xb3_r[c][:], in_=y1b_r[y][:]), inc="vv")
                p.wait("scalar", "vv", vy1)
                p.wait("scalar", "mm", xq3_free[c])
                sq = p.op("scalar", lambda E, c=c, y=yb: E.activation(
                    out=xq3_r[c][:], in_=y1b_r[y][:], func=AF.Square), inc="ss")
                y1b_cons[yb] = (sq, y1da)
                vb_sem3[c] = vb
                sq_sem3[c] = sq
                drain_one()
            if t >= 1:
                s2_bank_free[(t - 1) % 2] = p.cnt["vv"]
        stats_mms3(NT1 * C - 2)
        stats_mms3(NT1 * C - 1)
        drain_all_padded()
        rows2_all_done = p.cnt["rw2"]
        ph3_end_mm = p.cnt["mm"]
        ph3_tail_vv = p.cnt["vv"]
        ph3_tail_ss = p.cnt["ss"]
        y1w_totals = [p.cnt[f"y1w{y}"] for y in range(3)]
        ph3.close()
        rwkv_es.close()
        wval_sb = [ph4w.enter_context(nc.sbuf_tensor(f"wvalsb{fc}", [128, 1024], BF16, side="right")) for fc in range(32)]

        ph4 = ExitStack()
        sb4 = lambda n_, s_, d_=F32: ph4.enter_context(nc.sbuf_tensor(n_, list(s_), d_))
        ps4 = lambda n_, s_: ph4.enter_context(nc.psum_tensor(n_, list(s_), F32))
        wrec_sb = [sb4(f"wrecsb{e}", (128, 1024), BF16) for e in range(C)]
        BC2r = [sb4(f"BC2r{i}", (128, T4)) for i in range(2)]
        BC2m = [sb4(f"BC2m{i}", (128, T4)) for i in range(2)]
        Y1T = [[sb4(f"Y1T{j}_{ec}", (128, T4)) for ec in range(C)] for j in range(2)]
        xring = [sb4(f"xring{i}", (128, T4)) for i in range(2)]
        z2t = [sb4(f"z2t{ec}", (128, T4), BF16) for ec in range(C)]
        tmp4 = [sb4(f"tmp4{i}", (128, T4)) for i in range(1)]
        KRb = sb4("KRb", (128, T4))
        KSb = [sb4(f"KS{i}", (128, T4), BF16) for i in range(32)]
        RC2 = [sb4(f"RC2{i}", (128, T4)) for i in range(8)]
        X1T = [sb4(f"X1T{j}", (128, T4)) for j in range(2)]
        xb4_r = [sb4(f"xb4{i}", (128, T4), BF16) for i in range(4)]
        xq4_r = [sb4(f"xq4{i}", (128, T4), BF16) for i in range(4)]
        PS_k4 = [ps4(f"PS_k4{i}", (128, 512)) for i in range(2)]
        PS_v4 = [ps4(f"PS_v4{i}", (128, 512)) for i in range(4)]
        ps_s3 = [ps4(f"ps_s3{i}", (128, 512)) for i in range(2)]
        rt4 = [sb4(f"rt4_{i}", (1, 3 * T4)) for i in range(1)]

        wrec_done = 0
        for e in range(C):
            wrec_done = p.op("gpsimd", lambda E, e=e: E.dma_start(
                out=wrec_sb[e][:], in_=w4["wrec4"][e].rearrange("p a b -> p (a b)")), inc="wr")


        bc2_free = [0] * 2
        y1t_free = [0] * 2
        xring_free = [0] * 2
        xa_fold = [[0] * C for _ in range(2)]
        vcmA_free = 0
        z2_done_r = [0] * 2
        ks_free = [0] * 32
        kr_free = 0
        k4_free_v = [pso_free[0], pso_free[1]]
        k4_free_s = [0] * 2
        vcm_free = ph3_tail_vv
        rc2_free = [0] * 8
        rc2_sig = [0] * 8
        x1spill = [0] * 2
        x1t_cons = [0] * 2
        st3 = [0] * NT4
        rows3_done = [0] * NT4
        xb4_free = [0] * 4
        xq4_free = [0] * 4
        vb_sem = [0] * 4
        sq_sem = [0] * 4
        s3_read_gate = 0

        def smalls3_chain(u, st_val):
            return ln_smalls_chain(rt4[0], ps_s3[0], ps_s3[1], rows[3], p.chan("rw3"), u * T4, T4,
                                   [("mm", st_val), ("rw3", None)])

        z2t_free_mm = ph3_end_mm

        def stage_head(u):
            nonlocal bc2_free, y1t_free, z2t_free_mm
            t1i = u // 2
            up = u % 2
            if u == 0:
                p.wait("vector", "mm", ph3_end_mm)
                p.wait("scalar", "mm", ph3_end_mm)
                p.wait("sync", "mm", ph3_end_mm)
                p.wait("sync", "vv", ph3_tail_vv)
                p.wait("sync", "ss", ph3_tail_ss)
                p.wait("sync", "rw2", rows2_all_done)
                for y in range(3):
                    p.wait("sync", f"y1w{y}", y1w_totals[y])
            p.wait("gpsimd", "rw2", rows2_all_done)
            p.wait("gpsimd", "vv", bc2_free[up])
            p.op("gpsimd", lambda E, u=u, up=up: E.dma_start(
                out=BC2r[up][:], in_=rows[2][0:1, u * T4:(u + 1) * T4].broadcast_to([128, T4])), inc=p.chan(f"bc2{up}"))
            bc_da = p.op("gpsimd", lambda E, u=u, up=up: E.dma_start(
                out=BC2m[up][:], in_=rows[2][1:2, u * T4:(u + 1) * T4].broadcast_to([128, T4])), inc=f"bc2{up}")
            p.wait("sync", "vv", y1t_free[up])
            y1_da = 0
            for ec in range(C):
                wch4, wv4 = y1sp_done[t1i][ec]
                p.wait("sync", wch4, wv4)
                y1_da = p.op("sync", lambda E, u=u, up=up, ec=ec: E.dma_start(
                    out=Y1T[up][ec][:], in_=y1sp[ec * 128:(ec + 1) * 128, u * T4:(u + 1) * T4]), inc=p.chan(f"y1t{up}"))
            z2rec = 0
            z2par = [z2_done_r[up]] * 1
            for ec in range(C):
                tb = 0
                xs = ec % 2
                p.wait("sync", "vv", xring_free[xs])
                x_da = p.op("sync", lambda E, u=u, ec=ec, xs=xs: E.dma_start(
                    out=xring[xs][:], in_=xT[ec * 128:(ec + 1) * 128, u * T4:(u + 1) * T4]), inc=p.chan(f"xr{xs}"))
                p.wait("vector", f"y1t{up}", y1_da)
                p.wait("vector", f"bc2{up}", bc_da)
                p.wait("vector", "ss", z2par[tb])
                p.op("vector", lambda E, up=up, ec=ec, tb=tb: E.tensor_tensor(
                    out=tmp4[tb][:], in0=Y1T[up][ec][:], in1=BC2r[up][:], op=ALU.mult))
                vt = p.op("vector", lambda E, up=up, tb=tb: E.tensor_tensor(
                    out=tmp4[tb][:], in0=tmp4[tb][:], in1=BC2m[up][:], op=ALU.subtract), inc="vv")
                p.wait("scalar", "vv", vt)
                p.wait("scalar", "mm", z2t_free_mm)
                z2rec = p.op("scalar", lambda E, ec=ec, tb=tb: E.activation(
                    out=z2t[ec][:], in_=tmp4[tb][:], func=AF.Identity,
                    scale=g2s[:, ec:ec + 1], bias=b2s[:, ec:ec + 1]), inc="ss")
                z2par[tb] = z2rec
                p.wait("vector", f"xr{xs}", x_da)
                xa = p.op("vector", lambda E, up=up, ec=ec, xs=xs: E.tensor_tensor(
                    out=Y1T[up][ec][:], in0=Y1T[up][ec][:], in1=xring[xs][:], op=ALU.add), inc="vv")
                xring_free[xs] = xa
                xa_fold[up][ec] = xa
                drain_one()
            z2_done_r[up] = z2rec
            bc2_free[up] = p.cnt["vv"]
            return z2rec

        def kcm_only(u, z2rec):
            nonlocal kr_free
            for fc in range(32):
                fp = fc % 2
                p.wait("tensor", "wk", wkey_done)
                p.wait("tensor", "ss", z2rec)
                p.wait("tensor", "vv", k4_free_v[fp])
                p.wait("tensor", "ss", k4_free_s[fp])
                mk = 0
                for kc in range(C):
                    mk = p.op("tensor", lambda E, fc=fc, kc=kc, fp=fp, st=(kc == 0), sp=(kc == 7): E.matmul(
                        PS_k4[fp][:, :T4], wkey_sb[fc][:, kc * 128:(kc + 1) * 128],
                        z2t[kc][:], start=st, stop=sp), inc="mm" if kc == 7 else None)
                p.wait("vector", "mm", mk)
                p.wait("vector", "ss", kr_free)
                vkr = p.op("vector", lambda E, fp=fp: E.tensor_single_scalar(
                    out=KRb[:], in_=PS_k4[fp][:, :T4], scalar=0.0, op=ALU.max), inc="vv")
                k4_free_v[fp] = vkr
                p.wait("scalar", "vv", vkr)
                p.wait("scalar", "mm", ks_free[fc])
                sks = p.op("scalar", lambda E, fc=fc: E.activation(
                    out=KSb[fc][:], in_=KRb[:], func=AF.Square), inc="ss")
                kr_free = sks
                drain_one()
            return sks

        def rcm_sig_half(u, dh):
            last = 0
            for pair in range(2):
                mrs = [0] * 2
                for ecl2 in range(2):
                    ec = dh * 4 + pair * 2 + ecl2
                    ep = ec % 2
                    p.wait("tensor", "wr", wrec_done)
                    p.wait("tensor", "vv", k4_free_v[ep])
                    p.wait("tensor", "ss", k4_free_s[ep])
                    mr = 0
                    for kc in range(C):
                        mr = p.op("tensor", lambda E, kc=kc, ec=ec, ep=ep, st=(kc == 0), sp=(kc == 7): E.matmul(
                            PS_k4[ep][:, :T4], wrec_sb[ec][:, kc * 128:(kc + 1) * 128],
                            z2t[kc][:], start=st, stop=sp), inc="mm" if kc == 7 else None)
                    k4_free_v[ep] = 0
                    mrs[ecl2] = mr
                srs = [0, 0]
                for ecl2 in range(2):
                    ec = dh * 4 + pair * 2 + ecl2
                    ep = ec % 2
                    slot = dh * 4 + pair * 2 + ecl2
                    p.wait("scalar", "mm", mrs[ecl2])
                    p.wait("scalar", "vv", rc2_free[slot])
                    srs[ecl2] = p.op("scalar", lambda E, ep=ep, slot=slot: E.activation(
                        out=RC2[slot][:], in_=PS_k4[ep][:, :T4], func=AF.Sigmoid, scale=1.0), inc="ss")
                    k4_free_s[ep] = srs[ecl2]
                    rc2_sig[slot] = srs[ecl2]
                    last = srs[ecl2]
            return last

        def vcm_pass(u, dh, ks_ready, bankfree_vv):
            p.wait("tensor", "ss", ks_ready)
            p.wait("tensor", "vv", bankfree_vv)
            mvcm = 0
            for fc in range(32):
                p.wait("tensor", f"wv{fc // 4}", 64)
                for dcl in range(4):
                    dc = dh * 4 + dcl
                    mvcm = p.op("tensor", lambda E, fc=fc, dc=dc, dcl=dcl,
                                st=(fc == 0), sp=(fc == 31): E.matmul(
                        PS_v4[dcl][:, :T4],
                        wval_sb[fc][:, dc * 128:(dc + 1) * 128], KSb[fc][:],
                        start=st, stop=sp), inc="mm" if dcl == 3 else None)
                if dh == 1:
                    ks_free[fc] = mvcm
            return mvcm

        def combine_half(u, dh, mvcm):
            nonlocal vcm_free, vcmA_free, s3_read_gate
            up = u % 2
            lastvx1 = 0
            if dh == 0:
                drain_one()
                drain_one()
            bank_evac = 0
            for ecl in range(4):
                slot = dh * 4 + ecl
                p.wait("vector", "mm", mvcm)
                p.wait("vector", "ss", rc2_sig[slot])
                bank_evac = p.op("vector", lambda E, slot=slot, ecl=ecl: E.tensor_tensor(
                    out=RC2[slot][:], in0=PS_v4[ecl][:, :T4], in1=RC2[slot][:], op=ALU.mult), inc="vv")
            if dh == 0:
                vcmA_free = bank_evac
            else:
                vcm_free = bank_evac
            for ecl in range(4):
                ec = dh * 4 + ecl
                slot = dh * 4 + ecl
                xs4 = ec % 2
                p.wait("vector", "ss", x1t_cons[xs4])
                p.wait("vector", f"x1w{xs4}", x1spill[xs4])
                vx1 = p.op("vector", lambda E, slot=slot, up=up, ec=ec, xs4=xs4: E.tensor_tensor(
                    out=X1T[xs4][:], in0=RC2[slot][:], in1=Y1T[up][ec][:], op=ALU.add), inc="vv")
                rc2_free[slot] = vx1
                lastvx1 = vx1
                if ec == C - 1:
                    y1t_free[up] = vx1
                p.wait("sync", "vv", vx1)
                x1da = p.op("sync", lambda E, u=u, ec=ec, xs4=xs4: E.dma_start(
                    out=x1sp[ec * 128:(ec + 1) * 128, u * T4:(u + 1) * T4], in_=X1T[xs4][:]), inc=p.chan(f"x1w{xs4}"))
                x1spill[xs4] = x1da
                sl4 = ec % 4
                p.wait("vector", "mm", xb4_free[sl4])
                vb = p.op("vector", lambda E, sl4=sl4, xs4=xs4: E.tensor_copy(out=xb4_r[sl4][:], in_=X1T[xs4][:]), inc="vv")
                p.wait("scalar", "vv", vx1)
                p.wait("scalar", "mm", xq4_free[sl4])
                sq = p.op("scalar", lambda E, sl4=sl4, xs4=xs4: E.activation(
                    out=xq4_r[sl4][:], in_=X1T[xs4][:], func=AF.Square), inc="ss")
                x1t_cons[xs4] = sq
                vb_sem[sl4] = vb
                sq_sem[sl4] = sq
                drain_one()
            return lastvx1

        def stats_mms(u, dh, gate=0):
            mmv = 0
            for ecl in range(4):
                ec = dh * 4 + ecl
                sl4 = ecl
                p.wait("tensor", "vv", vb_sem[sl4])
                p.wait("tensor", "ss", sq_sem[sl4])
                if ec == 0:
                    p.wait("tensor", "vv", gate)
                p.op("tensor", lambda E, sl4=sl4, st=(ec == 0), sp=(ec == 7): E.matmul(
                    ps_s3[0][0:1, :T4], ones_bf[:], xb4_r[sl4][:], start=st, stop=sp))
                mmv = p.op("tensor", lambda E, sl4=sl4, st=(ec == 0), sp=(ec == 7): E.matmul(
                    ps_s3[1][0:1, :T4], ones_bf[:], xq4_r[sl4][:], start=st, stop=sp), inc="mm")
                xb4_free[sl4] = mmv
                xq4_free[sl4] = mmv
                if ec == 7:
                    st3[u] = mmv
            return mmv

        z2r = stage_head(0)
        ksr = kcm_only(0, z2r)
        rcm_sig_half(0, 0)
        p.wait("gpsimd", "mm", ph3_end_mm)
        for fc in range(32):
            p.op("gpsimd", lambda E, fc=fc: E.dma_start(
                out=wval_sb[fc][:], in_=wval4[fc].rearrange("p a b -> p (a b)")), inc=p.chan(f"wv{fc // 4}"))
        for u in range(NT4):
            mvcmA = vcm_pass(u, 0, ksr, vcm_free)
            if u >= 1:
                stats_mms(u - 1, 1)
                pending.extend(smalls3_chain(u - 1, st3[u - 1]))
            rcm_sig_half(u, 1)
            z2t_free_mm = p.cnt["mm"]
            combine_half(u, 0, mvcmA)
            s3gate = p.cnt["vv"]
            if u + 1 < NT4:
                z2r = stage_head(u + 1)
            mvcmB = vcm_pass(u, 1, ksr, vcmA_free)
            stats_mms(u, 0, s3gate)
            if u + 1 < NT4:
                ksr = kcm_only(u + 1, z2r)
                rcm_sig_half(u + 1, 0)
            combine_half(u, 1, mvcmB)
            rows3_done[u] = None
        stats_mms(NT4 - 1, 1)
        pending.extend(smalls3_chain(NT4 - 1, st3[NT4 - 1]))
        drain_all_padded()
        rows3_all_done = p.cnt["rw3"]
        for u in range(NT4):
            rows3_done[u] = rows3_all_done
        ph4.close()
        ph4w.close()
        barrier_all()

        ph5 = ExitStack()
        sb5 = lambda n_, s_, d_=F32: ph5.enter_context(nc.sbuf_tensor(n_, list(s_), d_))
        ps5 = lambda n_, s_: ph5.enter_context(nc.psum_tensor(n_, list(s_), F32))
        OUT = [sb5(f"OUT{ec}", (128, HS)) for ec in range(C)]
        y3h = [sb5(f"y3h{ec}", (128, HS), BF16) for ec in range(C)]
        KEY = [sb5(f"KEY{i}", (128, 4096), BF16) for i in range(4)]
        VAL = [sb5(f"VAL{i}", (128, 4096), BF16) for i in range(4)]
        GFM = sb5("GFM", (32, HS))
        BC3r = [sb5(f"BC3r{i}", (128, T5)) for i in range(2)]
        BC3m = [sb5(f"BC3m{i}", (128, T5)) for i in range(2)]
        tmp5 = [sb5(f"tmp5{i}", (128, T5)) for i in range(2)]
        Gb = [sb5(f"Gb{i}", (128, 32)) for i in range(4)]
        GCb = [sb5(f"GCb{i}", (128, 32)) for i in range(4)]
        Mb = [sb5(f"Mb{i}", (128, 1)) for i in range(4)]
        MKb = [sb5(f"MKb{i}", (128, 32)) for i in range(4)]
        GFb = [sb5(f"GFb{i}", (128, 32)) for i in range(4)]
        GB = [sb5(f"GB{i}", (128, T5B)) for i in range(4)]
        SC = [[[sb5(f"SC{par}_{el}_{fc}", (128, T5B), BF16) for fc in range(4)] for el in range(GE)] for par in range(2)]
        PS_sc = [ps5(f"PS_sc{i}", (128, 512)) for i in range(2)]
        PS_out = [ps5(f"PS_out{i}", (128, 512)) for i in range(4)]

        kv_done = [0] * NE
        kv_free = [0] * 4
        out_done = 0
        p.chan("outw")
        gb_free = [0] * 4
        sc_free = [[[0] * 4 for _ in range(GE)] for _ in range(2)]
        sc_ready = [[[0] * 4 for _ in range(GE)] for _ in range(2)]
        sc_bank_free = [0] * 2
        sc_bank_free_ss = [0] * 2
        gdr_done_prev = 0
        gbatch_free = [0] * 4
        tr_read = 0
        outadd_last = 0
        for h in range(2):
            hoff = h * HS
            outacc_da = 0
            for ec2 in range(2):
                p.wait("sync", f"x1w{ec2}", x1spill[ec2])
            for ec in range(C):
                if h == 1:
                    p.wait("sync", "outw", out_done)
                    p.wait("sync", "vv", outadd_last)
                outacc_da = p.op("sync", lambda E, ec=ec, hoff=hoff: E.dma_start(
                    out=OUT[ec][:], in_=x1sp[ec * 128:(ec + 1) * 128, hoff:hoff + HS]), inc=p.chan("outld"))
            bc3_free = [0] * 2
            y3rec = [[0] * C for _ in range(NT5)]
            for u in range(NT5):
                ug = (hoff + u * T5) // T4
                up = u % 2
                p.wait("gpsimd", "rw3", rows3_done[ug])
                p.wait("gpsimd", "vv", bc3_free[up])
                p.op("gpsimd", lambda E, up=up, hoff=hoff, u=u: E.dma_start(
                    out=BC3r[up][:], in_=rows[3][0:1, hoff + u * T5:hoff + (u + 1) * T5].broadcast_to([128, T5])), inc=p.chan(f"bc3{up}"))
                bc_da = p.op("gpsimd", lambda E, up=up, hoff=hoff, u=u: E.dma_start(
                    out=BC3m[up][:], in_=rows[3][1:2, hoff + u * T5:hoff + (u + 1) * T5].broadcast_to([128, T5])), inc=f"bc3{up}")
                for ec in range(C):
                    tb = ec % 2
                    p.wait("vector", "outld", outacc_da)
                    p.wait("vector", f"bc3{up}", bc_da)
                    p.op("vector", lambda E, ec=ec, u=u, up=up, tb=tb: E.tensor_tensor(
                        out=tmp5[tb][:], in0=OUT[ec][:, u * T5:(u + 1) * T5], in1=BC3r[up][:], op=ALU.mult))
                    y3rec[u][ec] = p.op("vector", lambda E, ec=ec, u=u, up=up, tb=tb: E.tensor_tensor(
                        out=y3h[ec][:, u * T5:(u + 1) * T5], in0=tmp5[tb][:], in1=BC3m[up][:], op=ALU.subtract), inc="vv")
                bc3_free[up] = y3rec[u][C - 1]
            gfm_last = 0
            for scb in range(0, NSC, 4):
                gsig = [0] * 4
                for j in range(4):
                    sc = scb + j
                    sp5 = sc % 2
                    u5 = (sc * 128) // T5
                    p.wait("tensor", "vv", y3rec[u5][C - 1])
                    p.wait("tensor", "preg", pre_dw)
                    p.wait("tensor", "vv", sc_bank_free[sp5])
                    p.wait("tensor", "ss", sc_bank_free_ss[sp5])
                    ms = 0
                    for kc in range(C):
                        ms = p.op("tensor", lambda E, sc=sc, kc=kc, sp5=sp5, st=(kc == 0), sp=(kc == 7): E.matmul(
                            PS_sc[sp5][:, 0:32], y3h[kc][:, sc * 128:(sc + 1) * 128],
                            selT_sb[:, kc * 32:(kc + 1) * 32], start=st, stop=sp),
                            inc="mm" if kc == 7 else None)
                    g4 = sc % 4
                    p.wait("scalar", "mm", ms)
                    p.wait("scalar", "vv", gbatch_free[g4])
                    gsig[j] = p.op("scalar", lambda E, sp5=sp5, g4=g4: E.activation(
                        out=Gb[g4][:], in_=PS_sc[sp5][:, 0:32], func=AF.Sigmoid, scale=1.0), inc="ss")
                    sc_bank_free_ss[sp5] = gsig[j]
                chains4 = []
                for j in range(4):
                    sc = scb + j
                    g4 = sc % 4
                    q = []
                    q.append(("vector", lambda E, g4=g4: E.tensor_copy(
                        out=GCb[g4][:], in_=Gb[g4][:]), "vv", [("ss", gsig[j])]))
                    for it in range(4):
                        q.append(("vector", lambda E, g4=g4: E.reduce_max(
                            out=Mb[g4][:], in_=GCb[g4][:], axis=AX.X), "vv", []))
                        if it < 3:
                            q.append(("vector", lambda E, g4=g4: E.tensor_tensor(
                                out=MKb[g4][:], in0=GCb[g4][:], in1=Mb[g4][:, 0:1].broadcast_to([128, 32]),
                                op=ALU.is_ge), "vv", []))
                            q.append(("vector", lambda E, g4=g4: E.scalar_tensor_tensor(
                                out=GCb[g4][:], in0=MKb[g4][:], scalar=-1e30, in1=GCb[g4][:],
                                op0=ALU.mult, op1=ALU.add), "vv", []))
                    q.append(("vector", lambda E, g4=g4: E.tensor_tensor(
                        out=MKb[g4][:], in0=Gb[g4][:], in1=Mb[g4][:, 0:1].broadcast_to([128, 32]),
                        op=ALU.is_ge), "vv", []))
                    q.append(("vector", lambda E, g4=g4: E.tensor_tensor(
                        out=GFb[g4][:], in0=Gb[g4][:], in1=MKb[g4][:], op=ALU.mult), "vv", []))
                    chains4.append(q)
                gf_rec = [0] * 4
                ncs5 = len(chains4[0])
                for si in range(ncs5):
                    for j in range(4):
                        eng, make, inc, waits = chains4[j][si]
                        for s_, v_ in waits:
                            p.wait(eng, s_, p.cnt[s_] if v_ is None else v_)
                        r_ = p.op(eng, make, inc=inc)
                        if si == ncs5 - 1:
                            gf_rec[j] = r_
                for j in range(4):
                    sc = scb + j
                    g4 = sc % 4
                    p.wait("tensor", "vv", gf_rec[j])
                    p.wait("tensor", "ss", tr_read)
                    p.wait("tensor", "vv", outadd_last)
                    mtr = p.op("tensor", lambda E, g4=g4: E.transpose(
                        out=PS_out[0][0:32, 0:128], in_=GFb[g4][:], identity=ident[:]), inc="mm")
                    p.wait("scalar", "mm", mtr)
                    p.wait("scalar", "gdr", gdr_done_prev)
                    tr_read = p.op("scalar", lambda E, sc=sc: E.activation(
                        out=GFM[0:32, sc * 128:(sc + 1) * 128], in_=PS_out[0][0:32, 0:128], func=AF.Copy), inc="ss")
                    gbatch_free[g4] = gf_rec[j]
                gfm_last = tr_read
            p.wait("sync", "ss", gfm_last)
            gdr_done = p.op("sync", lambda E, hoff=hoff: E.dma_start(
                out=gdr[0:32, hoff:hoff + HS], in_=GFM[:]), inc=p.chan("gdr"))
            gdr_done_prev = gdr_done
            seq = [(g, t) for g in range(NG) for t in range(NT5B)]

            def kv_load(g):
                for el in range(GE):
                    e = g * GE + el
                    slot = e % 4
                    p.wait("gpsimd", "mm", kv_free[slot])
                    p.op("gpsimd", lambda E, e=e, slot=slot: E.dma_start(
                        out=KEY[slot][:], in_=keysT[e]), inc=p.chan(f"kv{slot}"))
                    kv_done[e] = p.op("gpsimd", lambda E, e=e, slot=slot: E.dma_start(
                        out=VAL[slot][:], in_=valsT[e]), inc=f"kv{slot}")

            def front(i):
                g, t = seq[i]
                par = i % 2
                for el in range(GE):
                    e = g * GE + el
                    slot = e % 4
                    gbs = (i * GE + el) % 4
                    p.wait("gpsimd", "gdr", gdr_done)
                    p.wait("gpsimd", "vv", gb_free[gbs])
                    gb_da = p.op("gpsimd", lambda E, e=e, gbs=gbs, hoff=hoff, t=t: E.dma_start(
                        out=GB[gbs][:], in_=gdr[e:e + 1, hoff + t * T5B:hoff + (t + 1) * T5B].broadcast_to([128, T5B])), inc=p.chan(f"gb{gbs}"))
                    for fc in range(4):
                        fp = fc % 2
                        p.wait("tensor", f"kv{slot}", kv_done[e])
                        p.wait("tensor", "vv", y3rec[2 * t + 1][C - 1])
                        p.wait("tensor", "vv", sc_bank_free[fp])
                        p.wait("tensor", "ss", sc_bank_free_ss[fp])
                        mk = 0
                        for kc in range(C):
                            mk = p.op("tensor", lambda E, slot=slot, kc=kc, fc=fc, t=t, fp=fp,
                                      st=(kc == 0), sp=(kc == 7): E.matmul(
                                PS_sc[fp][:, 0:T5B],
                                KEY[slot][:, kc * 512 + fc * 128: kc * 512 + (fc + 1) * 128],
                                y3h[kc][:, t * T5B:(t + 1) * T5B], start=st, stop=sp),
                                inc="mm" if kc == 7 else None)
                        p.wait("vector", "mm", mk)
                        p.wait("vector", f"gb{gbs}", gb_da)
                        p.wait("vector", "mm", sc_free[par][el][fc])
                        vsc = p.op("vector", lambda E, par=par, el=el, fc=fc, fp=fp, gbs=gbs: E.scalar_tensor_tensor(
                            out=SC[par][el][fc][:], in0=PS_sc[fp][:, 0:T5B],
                            scalar=0.0, in1=GB[gbs][:], op0=ALU.max, op1=ALU.mult), inc="vv")
                        sc_bank_free[fp] = vsc
                        if fc == 3:
                            gb_free[gbs] = vsc
                        sc_ready[par][el][fc] = vsc
                if t == 1 and g + 1 < NG:
                    kv_load(g + 1)

            def values_pass(i, dh):
                nonlocal outadd_last, mvals_last
                g, t = seq[i]
                par = i % 2
                p.wait("tensor", "vv", sc_ready[par][GE - 1][3])
                p.wait("tensor", "ss", gfm_last)
                mvals = 0
                mvals_dcl = [0] * 4
                for el in range(GE):
                    for fc in range(4):
                        for dcl in range(4):
                            dc = dh * 4 + dcl
                            if el == 0 and fc == 0:
                                p.wait("tensor", "vv", outadd_done[dcl])
                            mvals = p.op("tensor", lambda E, par=par, el=el, slot2=(g * GE + el) % 4,
                                         fc=fc, dc=dc, dcl=dcl,
                                         st=(el == 0 and fc == 0), sp=(el == GE - 1 and fc == 3): E.matmul(
                                PS_out[dcl][:, 0:T5B],
                                VAL[slot2][:, fc * 1024 + dc * 128: fc * 1024 + (dc + 1) * 128],
                                SC[par][el][fc][:], start=st, stop=sp), inc="mm")
                            if el == GE - 1 and fc == 3:
                                mvals_dcl[dcl] = mvals
                for dcl in range(4):
                    dc = dh * 4 + dcl
                    p.wait("vector", "mm", mvals_dcl[dcl])
                    outadd_last = outadd_done[dcl] = p.op("vector", lambda E, dc=dc, dcl=dcl, t=t: E.tensor_tensor(
                        out=OUT[dc][:, t * T5B:(t + 1) * T5B], in0=OUT[dc][:, t * T5B:(t + 1) * T5B],
                        in1=PS_out[dcl][:, 0:T5B], op=ALU.add), inc="vv")
                mvals_last = mvals
                if dh == 1:
                    for el in range(GE):
                        for fc in range(4):
                            sc_free[par][el][fc] = mvals
                        kv_free[(g * GE + el) % 4] = mvals

            kv_load(0)
            mvals_last = 0
            outadd_done = [outadd_last] * 4
            front(0)
            for i in range(len(seq)):
                values_pass(i, 0)
                if i + 1 < len(seq):
                    front(i + 1)
                values_pass(i, 1)
            for ec in range(C):
                p.wait("sync", "vv", outadd_last)
                out_done = p.op("sync", lambda E, ec=ec, hoff=hoff: E.dma_start(
                    out=outT[ec * 128:(ec + 1) * 128, hoff:hoff + HS], in_=OUT[ec][:]), inc=p.chan("outw"))
        ph5.close()

        for s_name in list(p.cnt):
            p.wait("sync", s_name, p.cnt[s_name])

        @block.sync
        def _(E):
            p.replay("sync", E)

        @block.vector
        def _(E):
            p.replay("vector", E)

        @block.scalar
        def _(E):
            p.replay("scalar", E)

        @block.tensor
        def _(E):
            p.replay("tensor", E)

        @block.gpsimd
        def _(E):
            p.replay("gpsimd", E)

    es.close()
    return nc


def _host_layouts(inputs):
    f32 = np.float32
    A = lambda x: np.ascontiguousarray(np.asarray(x), dtype=f32)
    _ = inputs["L0_attn_time_first"]

    def t4(W):
        return np.ascontiguousarray(A(W).T.reshape(8, 128, 8, 128).transpose(2, 1, 0, 3))

    d = {}
    d["wr4"] = t4(inputs["L0_attn_Wr"])
    d["ww4"] = t4(inputs["L0_attn_Ww"])
    d["wk4"] = t4(inputs["L0_attn_Wk"])
    d["wv4"] = t4(inputs["L0_attn_Wv"])
    d["wo4"] = t4(inputs["L0_attn_Wo"])
    d["wrec4"] = t4(inputs["L0_attn_Wrec"])
    d["wkey4"] = np.ascontiguousarray(A(inputs["L0_attn_Wkey"]).T.reshape(8, 128, 32, 128).transpose(2, 1, 0, 3))
    d["wval4"] = np.ascontiguousarray(A(inputs["L0_attn_Wval"]).T.reshape(32, 128, 8, 128))
    d["selT3"] = np.ascontiguousarray(
        A(inputs["L0_ffn_expert_sel"]).T.reshape(8, 128, 32).transpose(1, 0, 2)).reshape(128, 256)
    d["keysT"] = np.ascontiguousarray(
        A(inputs["L0_ffn_keys"]).reshape(32, 8, 128, 512).transpose(0, 2, 1, 3)).reshape(32, 128, 4096)
    d["valsT"] = np.ascontiguousarray(
        A(inputs["L0_ffn_values"]).reshape(32, 4, 128, 1024).transpose(0, 2, 1, 3)).reshape(32, 128, 4096)
    for nm, key in (("g1v", "L0_attn_ln1_g"), ("b1v", "L0_attn_ln1_b"),
                    ("g2v", "L0_attn_ln2_g"), ("b2v", "L0_attn_ln2_b"),
                    ("tdv", "L0_attn_time_decay")):
        d[nm] = np.ascontiguousarray(A(inputs[key]).reshape(8, 128).T)
    d["onescol_bf"] = np.ones((128, 1), dtype=ml_dtypes.bfloat16)
    d["zcol_in"] = np.zeros((128, 1), dtype=f32)
    d["onecol_in"] = np.ones((128, 1), dtype=f32)
    d["onesrow_in"] = np.ones((1, 128), dtype=f32)
    d["epscol_in"] = np.full((128, 1), 1e-8, dtype=f32)
    d["ident_in"] = np.eye(128, dtype=f32)
    return d


def kernel(**inputs):
    nc = build_program()
    shared = _host_layouts(inputs)
    x = np.asarray(inputs["L0_attn_x"], dtype=np.float32)
    in_maps = []
    for b in range(8):
        m = dict(shared)
        m["xT"] = np.ascontiguousarray(x[b].T)
        in_maps.append(m)
    res = run_bass_kernel_spmd(nc, in_maps, core_ids=list(range(8)))
    out = np.stack([np.asarray(r["outT"]).T for r in res.results], axis=0)
    return np.ascontiguousarray(out, dtype=np.float32)
